# version 02
# baseline (speedup 1.0000x reference)
import contextlib
import numpy as np
import ml_dtypes
import concourse.bass as bass
import concourse.mybir as mybir
from concourse.bass_utils import run_bass_kernel_spmd

LAST_NSM = {}

F32 = mybir.dt.float32
BF16 = mybir.dt.bfloat16
AF = mybir.ActivationFunctionType
OP = mybir.AluOpType
AX = mybir.AxisListType
BF = ml_dtypes.bfloat16
EPS = 1e-5
DS = bass.ds
TS = bass.ts


class CFG:
    def __init__(self, DM, L, LC, F):
        self.DM = DM
        self.DI = 2 * DM
        self.DH = DM
        self.L = L
        self.LH = L // 2
        self.LC = LC
        self.NLC = L // LC
        self.NST = 16
        self.R = 64
        self.RK2 = self.R + 2 * self.NST
        self.E = 16
        self.S = 64
        self.ES = self.E * self.S
        self.F = F
        self.DO = DM
        self.KC = DM // 128
        self.DT = self.DH // 128
        self.ET = self.ES // 128
        self.FT = F // 128
        self.LT = self.LH // 128
        self.W = min(512, L)
        self.NW = L // self.W
        self.WH = min(512, self.LH)
        self.NWH = self.LH // self.WH
        self.NLB = 2 if self.LH >= 256 else 1
        self.LB = self.LH // self.NLB
        self.LTB = self.LB // 128
        self.WB = min(512, self.LB)
        self.NWB = self.LB // self.WB
        assert DM % 128 == 0 and L % (2 * LC) == 0 and LC % 128 == 0
        assert self.LH % 128 == 0 and F % 128 == 0 and self.ET == 8


CFG_FULL = CFG(DM=1024, L=4096, LC=1024, F=2048)


def build(C: CFG, dbg=False, races=False):
    nc = bass.Bass("TRN2", target_bir_lowering=False, detect_race_conditions=races)
    dtn = nc.dram_tensor

    def ein(name, shape):
        return dtn(name, shape, F32, kind="ExternalInput").ap()

    i_x = ein("i_x", [C.L, C.DM])
    i_xh = ein("i_xh", [C.LH, C.DM])
    i_inw = ein("i_inw", [2 * C.DH, C.DM])
    i_convw = ein("i_convw", [128, C.DT * 4])
    i_convb = ein("i_convb", [128, C.DT])
    i_xprojT = ein("i_xprojT", [C.DH, C.RK2])
    i_dtwT = ein("i_dtwT", [C.R, C.DH])
    i_dtb = ein("i_dtb", [128, C.DT])
    i_alog = ein("i_alog", [128, C.DT * C.NST])
    i_dv = ein("i_dv", [128, C.DT])
    i_outwT = ein("i_outwT", [C.DH, C.DO])
    i_normg = ein("i_normg", [128, C.KC])
    i_normb = ein("i_normb", [128, C.KC])
    i_slotg = ein("i_slotg", [128, C.KC])
    i_slotb = ein("i_slotb", [128, C.KC])
    i_scale = ein("i_scale", [1, 1])
    i_seT = ein("i_seT", [C.DM, C.ES])
    i_wg = ein("i_wg", [2, C.DM, C.F])
    i_wu = ein("i_wu", [2, C.DM, C.F])
    i_wd = ein("i_wd", [2, C.F, C.DO])
    i_bgT = ein("i_bgT", [128, 2 * C.FT])
    i_buT = ein("i_buT", [128, 2 * C.FT])
    i_bd = ein("i_bd", [2, C.DO])
    i_eyeb = dtn("i_eyeb", [128, 128], BF16, kind="ExternalInput").ap()
    i_eyef = dtn("i_eyef", [128, 128], F32, kind="ExternalInput").ap()
    o_y = dtn("o_y", [C.LH, C.DM], F32, kind="ExternalOutput").ap()

    xb = dtn("xb", [C.L, C.DM], BF16).ap()
    xhb = dtn("xhb", [C.LH, C.DM], BF16).ap()
    inwb = dtn("inwb", [2 * C.DH, C.DM], BF16).ap()
    xprojTb = dtn("xprojTb", [C.DH, C.RK2], BF16).ap()
    dtwTb = dtn("dtwTb", [C.R, C.DH], BF16).ap()
    outwTb = dtn("outwTb", [C.DH, C.DO], BF16).ap()
    seTb = dtn("seTb", [C.DM, C.ES], BF16).ap()
    wgb = dtn("wgb", [2, C.DM, C.F], BF16).ap()
    wub = dtn("wub", [2, C.DM, C.F], BF16).ap()
    wdb = dtn("wdb", [2, C.F, C.DO], BF16).ap()
    dk = dict(kind="ExternalOutput") if dbg else {}
    xx_h = dtn("xx_h", [C.DH, C.L], BF16, **dk).ap()
    xr1_h = dtn("xr1_h", [C.DM, C.LH], BF16, **dk).ap()
    delta_h = dtn("delta_h", [C.DH, C.L], BF16, **dk).ap()
    u_h = dtn("u_h", [C.DH, C.L], BF16, **dk).ap()
    sres_h = dtn("sres_h", [C.DH, C.L], BF16, **dk).ap()
    y3_h = dtn("y3_h", [C.DH, C.L], BF16, **dk).ap()
    brow_h = dtn("brow_h", [2 * C.NST, C.L], BF16).ap()
    murow_h = dtn("murow_h", [2, max(C.LH, C.ES)], F32).ap()
    b2_h = dtn("b2_h", [C.LT, 128], F32).ap()
    cin0 = dtn("cin0", [C.RK2, C.L], F32).ap()
    cout0 = dtn("cout0", [8 * C.RK2, C.L], F32).ap()
    cin1 = dtn("cin1", [C.DO, C.L], BF16).ap()
    cout1 = dtn("cout1", [8 * C.DO, C.L], BF16).ap()
    cin2 = dtn("cin2", [128, 4 * C.ET], F32).ap()
    cout2 = dtn("cout2", [8 * 128, 4 * C.ET], F32).ap()
    cin3 = dtn("cin3", [C.DM, C.ES], BF16).ap()
    cout3 = dtn("cout3", [8 * C.DM, C.ES], BF16).ap()
    cin4 = dtn("cin4", [512, C.DO], BF16).ap()
    cout4 = dtn("cout4", [8 * 512, C.DO], BF16).ap()
    RG8 = [[0, 1, 2, 3, 4, 5, 6, 7]]

    top = contextlib.ExitStack()
    live = {}

    def mksb(st_, name, p, f, dty, side=None):
        h = nc.sbuf_tensor(name, [p, f], dty, side=side, unsafe_allow_realloc=True)
        t = h.__enter__()
        live[name] = h
        return t

    def fre(*names):
        for n in names:
            live.pop(n).__exit__(None, None, None)

    def mkps(st_, name, p, f, dty=F32):
        return st_.enter_context(nc.psum_tensor(name, [p, f], dty))

    def mksem(st_, name):
        return st_.enter_context(nc.semaphore(name))

    convw = mksb(top, "convw", 128, C.DT * 4, F32)
    convb = mksb(top, "convb", 128, C.DT, F32)
    dtb = mksb(top, "dtb", 128, C.DT, F32)
    alog = mksb(top, "alog", 128, C.DT * C.NST, F32)
    Amat = mksb(top, "Amat", 128, C.DT * C.NST, F32)
    dvs = mksb(top, "dvs", 128, C.DT, F32)
    normg = mksb(top, "normg", 128, C.KC, F32)
    normb = mksb(top, "normb", 128, C.KC, F32)
    sgsc = mksb(top, "sgsc", 128, C.KC, F32)
    sbsc = mksb(top, "sbsc", 128, C.KC, F32)
    bgT = mksb(top, "bgT", 128, 2 * C.FT, F32)
    buT = mksb(top, "buT", 128, 2 * C.FT, F32)
    eyeb = mksb(top, "eyeb", 128, 128, BF16)
    eyef = mksb(top, "eyef", 128, 128, F32)
    ones1 = mksb(top, "ones1", 128, 1, BF16)
    scale_bc = mksb(top, "scale_bc", 128, 1, F32)
    tiny = mksb(top, "tiny", 128, 64, F32)
    csts = mksb(top, "csts", 128, 2, F32)
    mrow = mksb(top, "mrow", 128, max(C.LH, C.ES), BF16)
    rrow = mksb(top, "rrow", 128, max(C.LH, C.ES), BF16)
    row1 = mksb(top, "row1", 1, 2 * max(C.LH, C.ES), F32)

    psA = mkps(top, "psA", 128, 512)
    psB = mkps(top, "psB", 128, 512)
    psSa = mkps(top, "psSa", 128, 512)
    psSb = mkps(top, "psSb", 128, 512)
    psTa = mkps(top, "psTa", 128, 512, BF16)
    psTb = mkps(top, "psTb", 128, 512, BF16)
    psTf = mkps(top, "psTf", 128, 512)

    sg = mksem(top, "sg")
    sh = mksem(top, "sh")
    st = mksem(top, "st")
    sv = mksem(top, "sv")
    ss = mksem(top, "ss")
    scc = mksem(top, "scc")
    ph = mksem(top, "ph")
    pipe_names = ["q1", "q2", "q3", "q4", "q5", "q6", "q7", "q8",
                  "dg0", "dg1", "dg2", "dg3", "dg4", "dg5", "dg6"]
    Q = {n: mksem(top, n) for n in pipe_names}
    qc = {n: 0 for n in pipe_names}

    cnt = {id(sg): 0, id(sh): 0, id(st): 0, id(sv): 0, id(ss): 0, id(scc): 0}
    nsm = {}
    nbar = [0]

    NSD = 8
    SD = [mksem(top, f"sd{i}") for i in range(NSD)]
    sdc = [0] * NSD
    sdbar = [0] * NSD
    qbar = {n: 0 for n in pipe_names}
    sdp = [0]
    srace = {n: mksem(top, f"sr_{n}") for n in
             ("Vector", "Scalar", "Tensor", "Sync", "GpSimd", "Pool")}
    cnt_sr = {n: 0 for n in srace}

    class Stream:
        def __init__(self, npar):
            self.sems = [sdp[0] + p for p in range(npar)]
            sdp[0] += npar
            assert sdp[0] <= NSD, "stream pool exhausted in phase window"
            self.npar = npar
            self.val = {}
            self._preset = False

        def preset(self, nsteps, per=1):
            shadow = {s: sdc[s] for s in self.sems}
            for k in range(nsteps):
                si = self.sems[k % self.npar]
                shadow[si] += 16 * per
                self.val[k] = shadow[si]
            self._preset = True

        def inc(self, instr, k):
            si = self.sems[k % self.npar]
            sdc[si] += 16
            if self._preset:
                assert sdc[si] <= self.val[k], "stream preset mismatch"
            else:
                self.val[k] = sdc[si]
            instr.then_inc(SD[si], 16)

        def wait(self, eng, k):
            eng.wait_ge(SD[self.sems[k % self.npar]], self.val[k])

        def wait_all(self, eng):
            for s in self.sems:
                if sdc[s] > sdbar[s]:
                    eng.wait_ge(SD[s], sdc[s])

    blk = top.enter_context(nc.Block())

    def inc(instr, s, n=1):
        instr.then_inc(s, n)
        cnt[id(s)] += n
        return cnt[id(s)]

    def qinc(instr, name):
        instr.then_inc(Q[name], 1)
        qc[name] += 1
        return qc[name]

    def qinc16(instr, name):
        instr.then_inc(Q[name], 16)
        qc[name] += 16
        return qc[name]

    def on(which, fn):
        getattr(blk, which)(fn)

    def ng(eng, instr=None, sem=None, val=None):
        if not races:
            return
        if val is None:
            en = next((n for n in srace if n in type(eng).__name__), "Sync")
            cnt_sr[en] += 1
            instr.then_inc(srace[en], 1)
            sem, val = srace[en], cnt_sr[en]
        eng.wait_ge(sem, val)

    def dbg_dump(name, ap_fn, sem_, val_fn):
        if not dbg:
            return
        p, f = ap_fn().partition_size(), ap_fn().free_size()
        od = dtn("dbg_" + name, [p, f], ap_fn().dtype, kind="ExternalOutput").ap()

        def _d(s):
            s.wait_ge(sem_, val_fn())
            inc(s.dma_start(out=od[:], in_=ap_fn()), sh, 16)
        on("sync", _d)

    def barrier():
        nbar[0] += 1
        t_ = 5 * nbar[0]
        shv, sgv = cnt[id(sh)], cnt[id(sg)]
        sdv = list(sdc)
        qv = dict(qc)
        for wn in ("tensor", "vector", "scalar", "sync", "gpsimd"):
            def f(eng, _t=t_, _wn=wn):
                if _wn == "sync":
                    eng.wait_ge(sh, shv)
                    eng.wait_ge(scc, cnt[id(scc)])
                    for i_ in range(NSD):
                        if sdv[i_] > sdbar[i_]:
                            eng.wait_ge(SD[i_], sdv[i_])
                    for qn_ in pipe_names:
                        if qv[qn_] > qbar[qn_]:
                            eng.wait_ge(Q[qn_], qv[qn_])
                if _wn == "gpsimd":
                    eng.wait_ge(sg, sgv)
                eng.drain()
                eng.sem_inc(ph, 1)
                eng.wait_ge(ph, _t)
            on(wn, f)
        sdbar[:] = sdv
        qbar.update(qv)
        sdp[0] = 0

    ph0 = contextlib.ExitStack()
    xprojT = mksb(top, "xprojT", 128, C.DT * C.RK2, BF16)
    dtwT = mksb(top, "dtwT", C.R, C.DH, BF16)
    xT = mksb(top, "xT", 128, C.KC * C.L, BF16)
    inwT = mksb(top, "inwT", 128, C.KC * 2 * C.DH, BF16)

    def p0_g(g):
        for ds_, sr in ((xb, i_x), (xhb, i_xh), (inwb, i_inw), (xprojTb, i_xprojT),
                        (dtwTb, i_dtwT)):
            qinc16(g.dma_start(out=ds_[:], in_=sr[:]), "dg0")
        for ds_, sr in ((outwTb, i_outwT), (seTb, i_seT),
                        (wgb, i_wg), (wub, i_wu), (wdb, i_wd)):
            qinc16(g.dma_start(out=ds_[:], in_=sr[:]), "dg1")
    on("gpsimd", p0_g)

    def p0_s(s):
        for sr, ds_ in ((i_convw, convw), (i_convb, convb), (i_dtb, dtb),
                        (i_alog, alog), (i_dv, dvs), (i_normg, normg),
                        (i_normb, normb), (i_bgT, bgT), (i_buT, buT),
                        (i_eyeb, eyeb), (i_eyef, eyef)):
            inc(s.dma_start(out=ds_[:], in_=sr[:]), sh, 16)
        inc(s.dma_start(out=scale_bc[:], in_=i_scale[:].broadcast_to((128, 1))), sh, 16)
        inc(s.dma_start(out=tiny[:, 0:C.KC], in_=i_slotg[:]), sh, 16)
        inc(s.dma_start(out=tiny[:, C.KC:2 * C.KC], in_=i_slotb[:]), sh, 16)
        s.wait_ge(Q["dg0"], 80)
        for kc in range(C.KC):
            inc(s.dma_start_transpose(out=xT[:, kc * C.L:(kc + 1) * C.L],
                                      in_=xb[:, kc * 128:(kc + 1) * 128]), sh, 16)
        for kc in range(C.KC):
            inc(s.dma_start_transpose(out=inwT[:, kc * 2 * C.DH:(kc + 1) * 2 * C.DH],
                                      in_=inwb[:, kc * 128:(kc + 1) * 128]), sh, 16)
        for dti in range(C.DT):
            inc(s.dma_start(out=xprojT[:, dti * C.RK2:(dti + 1) * C.RK2],
                            in_=xprojTb[dti * 128:(dti + 1) * 128, :]), sh, 16)
        inc(s.dma_start(out=dtwT[:], in_=dtwTb[:]), sh, 16)
    on("sync", p0_s)
    P0H = cnt[id(sh)]

    REG = {}

    def p0_s2(s):
        pid = s.partition_id()
        REG["pid"] = pid
        REG["pxor"] = s.compute_val((pid + 4) % 8)
        REG["jj"] = s.compute_val(pid // 4)
        REG["breg"] = s.compute_val(pid % 4)
        REG["jcol"] = s.compute_val((pid // 4) * C.LH)
    on("sync", p0_s2)

    def p0_sc(sc):
        sc.wait_ge(sh, P0H)
        inc(sc.activation(Amat[:], alog[:], AF.Exp), ss)
    on("scalar", p0_sc)
    p0ss = cnt[id(ss)]

    def p0_v(v):
        v.memset(ones1[:], 1.0)
        v.memset(csts[:, 0:1], EPS)
        v.memset(csts[:, 1:2], -1.0)
        v.wait_ge(sh, P0H)
        v.tensor_scalar_mul(sgsc[:], tiny[:, 0:C.KC], scale_bc[:, 0:1])
        v.tensor_scalar_mul(sbsc[:], tiny[:, C.KC:2 * C.KC], scale_bc[:, 0:1])
        v.wait_ge(ss, p0ss)
        inc(v.tensor_scalar_mul(Amat[:], Amat[:], -1.0), sv)
    on("vector", p0_v)

    barrier()

    ev1 = mksb(top, "ev1", 128, 4 * 512, BF16)
    P1SS0 = cnt[id(ss)]
    sp1 = Stream(4)
    sp1.preset(2 * C.DT * C.NW, 1)

    def p1_t(t):
        t.wait_ge(sh, P0H)
        i = 0
        for ot in range(2 * C.DT):
            for q in range(C.NW):
                pst = psA if i % 2 == 0 else psB
                if i >= 2:
                    t.wait_ge(ss, P1SS0 + i - 1)
                for kc in range(C.KC):
                    mm = t.matmul(pst[:, 0:C.W],
                                  inwT[:, kc * 2 * C.DH + ot * 128:kc * 2 * C.DH + (ot + 1) * 128],
                                  xT[:, kc * C.L + q * C.W:kc * C.L + (q + 1) * C.W],
                                  start=(kc == 0), stop=(kc == C.KC - 1))
                nsm[("p1m", i)] = inc(mm, st)
                i += 1
    on("tensor", p1_t)

    def p1_sc(sc):
        i = 0
        for ot in range(2 * C.DT):
            for q in range(C.NW):
                pst = psA if i % 2 == 0 else psB
                sc.wait_ge(st, nsm[("p1m", i)])
                if i >= 4:
                    sp1.wait(sc, i - 4)
                so = (i % 4) * 512
                if ot < C.DT:
                    e = sc.copy(ev1[:, so:so + C.W], pst[:, 0:C.W])
                else:
                    e = sc.activation(ev1[:, so:so + C.W], pst[:, 0:C.W], AF.Silu)
                nsm[("p1e", i)] = inc(e, ss)
                i += 1
    on("scalar", p1_sc)

    def p1_s(s):
        i = 0
        for ot in range(2 * C.DT):
            for q in range(C.NW):
                s.wait_ge(ss, nsm[("p1e", i)])
                so = (i % 4) * 512
                dst = xx_h if ot < C.DT else sres_h
                rt = ot if ot < C.DT else ot - C.DT
                e = s.dma_start(out=dst[rt * 128:(rt + 1) * 128, q * C.W:(q + 1) * C.W],
                                in_=ev1[:, so:so + C.W])
                sp1.inc(e, i)
                i += 1
    on("sync", p1_s)

    barrier()
    fre("ev1", "inwT", "xT")

    uu = mksb(top, "uu", 128, 2 * C.L, BF16)
    xxs = mksb(top, "xxs", 128, 2 * (4 + C.L), BF16)

    def p2_v0(v):
        v.memset(xxs[:, 0:4], 0.0)
        inc(v.memset(xxs[:, 4 + C.L:8 + C.L], 0.0), sv)
    on("vector", p2_v0)
    p2pad = cnt[id(sv)]
    P2VB = cnt[id(sv)]

    sx2 = Stream(2)
    su2 = Stream(2)
    sx2.preset(C.DT, 1)
    su2.preset(C.DT, 1)

    def p2_v(v):
        for dti in range(C.DT):
            base = (dti % 2) * (4 + C.L) + 4
            so = (dti % 2) * C.L
            tmp = uu[:, so:so + C.L]
            sx2.wait(v, dti)
            if dti >= 2:
                su2.wait(v, dti - 2)
            e = v.tensor_scalar_mul(tmp, xxs[:, base - 2:base - 2 + C.L],
                                    convw[:, dti * 4 + 1:dti * 4 + 2])
            ng(v, e)
            e = v.scalar_tensor_tensor(tmp, xxs[:, base - 3:base - 3 + C.L],
                                       convw[:, dti * 4 + 0:dti * 4 + 1], tmp, OP.mult, OP.add)
            ng(v, e)
            e = v.scalar_tensor_tensor(tmp, xxs[:, base - 1:base - 1 + C.L],
                                       convw[:, dti * 4 + 2:dti * 4 + 3], tmp, OP.mult, OP.add)
            ng(v, e)
            e = v.scalar_tensor_tensor(tmp, xxs[:, base:base + C.L],
                                       convw[:, dti * 4 + 3:dti * 4 + 4], tmp, OP.mult, OP.add)
            nsm[("p2c", dti)] = inc(e, sv)
    on("vector", p2_v)

    def p2_sc(sc):
        for dti in range(C.DT):
            so = (dti % 2) * C.L
            tmp = uu[:, so:so + C.L]
            sc.wait_ge(sv, nsm[("p2c", dti)])
            nsm[("p2u", dti)] = inc(
                sc.activation(tmp, tmp, AF.Silu, bias=convb[:, dti:dti + 1]), ss)
    on("scalar", p2_sc)

    def p2_sync(s):
        s.wait_ge(sv, p2pad)
        for k in range(C.DT + 2):
            if k < C.DT:
                if k >= 2:
                    s.wait_ge(sv, P2VB + k - 1)
                e = s.dma_start(out=xxs[:, (k % 2) * (4 + C.L) + 4:
                                        (k % 2) * (4 + C.L) + 4 + C.L],
                                in_=xx_h[k * 128:(k + 1) * 128, :])
                sx2.inc(e, k)
            if k >= 2:
                s.wait_ge(ss, nsm[("p2u", k - 2)])
                e = s.dma_start(out=u_h[(k - 2) * 128:(k - 1) * 128, :],
                                in_=uu[:, ((k - 2) % 2) * C.L:((k - 2) % 2) * C.L + C.L])
                su2.inc(e, k - 2)
    on("sync", p2_sync)

    barrier()
    fre("xxs", "uu")

    xdbl = mksb(top, "xdbl", C.RK2, C.L, F32)
    xdbl2 = mksb(top, "xdbl2", C.RK2, C.L, F32)
    ut3 = mksb(top, "ut3", 128, 2 * C.DT * C.W, BF16)
    P3SV0 = cnt[id(sv)]
    s3l = Stream(2)

    def p3_s(s):
        for q in range(C.NW):
            so = (q % 2) * C.DT * C.W
            if q >= 2:
                s.wait_ge(sv, P3SV0 + q - 1)
            for dti in range(C.DT):
                e = s.dma_start(out=ut3[:, so + dti * C.W:so + (dti + 1) * C.W],
                                in_=u_h[dti * 128:(dti + 1) * 128, q * C.W:(q + 1) * C.W])
                s3l.inc(e, q)
    on("sync", p3_s)

    def p3_t(t):
        for q in range(C.NW):
            so = (q % 2) * C.DT * C.W
            pst = psA if q % 2 == 0 else psB
            s3l.wait(t, q)
            if q >= 2:
                t.wait_ge(sv, P3SV0 + q - 1)
            for dti in range(C.DT):
                mm = t.matmul(pst[0:C.RK2, 0:C.W],
                              xprojT[:, dti * C.RK2:(dti + 1) * C.RK2],
                              ut3[:, so + dti * C.W:so + (dti + 1) * C.W],
                              start=(dti == 0), stop=(dti == C.DT - 1))
            nsm[("p3m", q)] = inc(mm, st)
    on("tensor", p3_t)

    def p3_v(v):
        for q in range(C.NW):
            pst = psA if q % 2 == 0 else psB
            v.wait_ge(st, nsm[("p3m", q)])
            e = v.tensor_copy(xdbl[:, q * C.W:(q + 1) * C.W], pst[0:C.RK2, 0:C.W])
            nsm[("p3e", q)] = inc(e, sv)
    on("vector", p3_v)

    def p3_s2(s):
        s.wait_ge(sv, nsm[("p3e", C.NW - 1)])
        nsm["p3stage"] = inc(s.dma_start(out=cin0[:], in_=xdbl[:]), sh, 16)
    on("sync", p3_s2)

    def p3_g(g):
        g.wait_ge(sh, nsm["p3stage"])
        nsm["cc0"] = inc(g.collective_compute(
            "AllGather", OP.bypass, replica_groups=RG8, ins=[cin0[:]], outs=[cout0[:]]), scc)
    on("gpsimd", p3_g)

    def p3_s3(s):
        s.wait_ge(scc, nsm["cc0"])
        nsm["p3pl"] = inc(
            s.dma_start(out=xdbl2[:], in_=cout0[TS(REG["pxor"], C.RK2), :]), sh, 16)
    on("sync", p3_s3)

    xdblb = mksb(top, "xdblb", C.R, C.L, BF16)

    def p3_v2(v):
        v.wait_ge(sh, nsm["p3pl"])
        e = v.tensor_add(xdbl[:], xdbl[:], xdbl2[:])
        ng(v, e)
        nsm["p3add"] = inc(v.tensor_copy(xdblb[:], xdbl[0:C.R, :]), sv)
    on("vector", p3_v2)

    def p3_g2(g):
        g.wait_ge(sv, nsm["p3add"])
        qinc16(g.dma_start(out=brow_h[:], in_=xdbl[C.R:C.RK2, :]), "dg2")
    on("gpsimd", p3_g2)

    barrier()

    sc1 = mksb(top, "sc1", 128, C.W, F32)
    sc2 = mksb(top, "sc2", 128, C.W, F32)
    sc3 = mksb(top, "sc3", 128, 2 * C.W, BF16)
    P4SV0a = cnt[id(sv)]
    s4d = Stream(2)
    s4d.preset(C.DT * C.NW, 1)

    def p4_t(t):
        i = 0
        for dti in range(C.DT):
            for q in range(C.NW):
                pst = psA if i % 2 == 0 else psB
                if i >= 2:
                    t.wait_ge(sv, P4SV0a + 2 * (i - 1) + 1)
                mm = t.matmul(pst[:, 0:C.W], dtwT[:, dti * 128:(dti + 1) * 128],
                              xdblb[:, q * C.W:(q + 1) * C.W], start=True, stop=True)
                nsm[("p4m", i)] = inc(mm, st)
                i += 1
    on("tensor", p4_t)

    def p4_v_joint(v):
        i = 0
        for dti in range(C.DT):
            for q in range(C.NW):
                pst = psA if i % 2 == 0 else psB
                v.wait_ge(st, nsm[("p4m", i)])
                if i >= 1:
                    ng(v, sem=sv, val=nsm[("p4d", i - 1)])
                e = v.tensor_scalar_add(sc1[:, 0:C.W], pst[:, 0:C.W], dtb[:, dti:dti + 1])
                nsm[("p4z", i)] = inc(e, sv)
                v.wait_ge(ss, P4SS0[0] + i + 1)
                if i >= 2:
                    s4d.wait(v, i - 2)
                so3 = (i % 2) * C.W
                e = v.tensor_scalar_max(sc1[:, 0:C.W], sc1[:, 0:C.W], 0.0)
                ng(v, e)
                e = v.tensor_add(sc3[:, so3:so3 + C.W], sc1[:, 0:C.W], sc2[:, 0:C.W])
                nsm[("p4d", i)] = inc(e, sv)
                i += 1
    P4SS0 = [None]
    P4SV0 = [None]

    def p4_sc2(sc):
        i = 0
        for dti in range(C.DT):
            for q in range(C.NW):
                sc.wait_ge(sv, P4SV0[0] + 2 * i + 1)
                e = sc.activation(sc2[:, 0:C.W], sc1[:, 0:C.W], AF.Abs)
                ng(sc, e, sem=ss)
                e = sc.activation(sc2[:, 0:C.W], sc2[:, 0:C.W], AF.Exp, scale=csts[:, 1:2])
                ng(sc, e, sem=ss)
                inc(sc.activation(sc2[:, 0:C.W], sc2[:, 0:C.W], AF.Ln, bias=1.0), ss)
                i += 1

    P4SV0[0] = cnt[id(sv)]
    P4SS0[0] = cnt[id(ss)]
    on("scalar", p4_sc2)
    on("vector", p4_v_joint)

    def p4_s(s):
        i = 0
        for dti in range(C.DT):
            for q in range(C.NW):
                s.wait_ge(sv, nsm[("p4d", i)])
                so3 = (i % 2) * C.W
                e = s.dma_start(out=delta_h[dti * 128:(dti + 1) * 128, q * C.W:(q + 1) * C.W],
                                in_=sc3[:, so3:so3 + C.W])
                s4d.inc(e, i)
                i += 1
    on("sync", p4_s)

    barrier()
    fre("sc3", "sc2", "sc1", "xdblb", "ut3", "xdbl2", "xdbl", "dtwT", "xprojT")

    d_lq = mksb(top, "d_lq", 128, C.DT * C.LC, BF16)
    du_lq = mksb(top, "du_lq", 128, C.DT * C.LC, BF16)
    y3_lq = mksb(top, "y3_lq", 128, C.DT * C.LC, BF16)
    sr_lq = mksb(top, "sr_lq", 128, C.DT * C.LC, BF16)
    u_lq = mksb(top, "u_lq", 128, C.DT * C.LC, BF16)
    bsl = mksb(top, "bsl", 128, 2 * C.LC, BF16)
    csl = mksb(top, "csl", 128, 2 * C.LC, BF16)
    dasl = mksb(top, "dasl", 128, 2 * C.LC, BF16)
    dbsl = mksb(top, "dbsl", 128, 2 * C.LC, BF16)
    hsl = mksb(top, "hsl", 128, 2 * C.LC, BF16)
    carry = mksb(top, "carry", 128, C.DT * C.NST, F32)

    for lq in range(C.NLC):
        SV0 = [None]
        SS0 = [None]
        q7_0 = qc["q7"]
        s6a = Stream(1)
        s6b = Stream(1)
        s6c = Stream(2)

        def p6_load(s, _lq=lq):
            s.wait_ge(Q["dg2"], 16)
            for dti in range(C.DT):
                s6a.inc(s.dma_start(out=d_lq[:, dti * C.LC:(dti + 1) * C.LC],
                                    in_=delta_h[dti * 128:(dti + 1) * 128,
                                                _lq * C.LC:(_lq + 1) * C.LC]), 0)
            for dti in range(C.DT):
                s6a.inc(s.dma_start(out=u_lq[:, dti * C.LC:(dti + 1) * C.LC],
                                    in_=u_h[dti * 128:(dti + 1) * 128,
                                            _lq * C.LC:(_lq + 1) * C.LC]), 0)
            for dti in range(C.DT):
                s6b.inc(s.dma_start(out=sr_lq[:, dti * C.LC:(dti + 1) * C.LC],
                                    in_=sres_h[dti * 128:(dti + 1) * 128,
                                               _lq * C.LC:(_lq + 1) * C.LC]), 0)
            for n in range(C.NST):
                sl = n % 2
                if n >= 2:
                    s.wait_ge(Q["q7"], q7_0 + (n - 1) * C.DT)
                s6c.inc(s.dma_start(out=bsl[:, sl * C.LC:(sl + 1) * C.LC],
                                    in_=brow_h[n:n + 1, _lq * C.LC:(_lq + 1) * C.LC]
                                    .broadcast_to((128, C.LC))), n)
                e = s.dma_start(out=csl[:, sl * C.LC:(sl + 1) * C.LC],
                                in_=brow_h[C.NST + n:C.NST + n + 1, _lq * C.LC:(_lq + 1) * C.LC]
                                .broadcast_to((128, C.LC)))
                s6c.inc(e, n)
        on("sync", p6_load)

        def p6_da(sc, _lq=lq):
            SS0[0] = cnt[id(ss)]
            s6a.wait(sc, 0)
            i = 0
            for n in range(C.NST):
                for dti in range(C.DT):
                    sl = i % 2
                    if i >= 2:
                        sc.wait_ge(Q["q7"], q7_0 + i - 1)
                    e = sc.activation(dasl[:, sl * C.LC:(sl + 1) * C.LC],
                                      d_lq[:, dti * C.LC:(dti + 1) * C.LC], AF.Exp,
                                      scale=Amat[:, dti * C.NST + n:dti * C.NST + n + 1])
                    inc(e, ss)
                    i += 1
        on("scalar", p6_da)

        def p6_v(v, _lq=lq):
            s6a.wait(v, 0)
            for dti in range(C.DT):
                e = v.tensor_mul(du_lq[:, dti * C.LC:(dti + 1) * C.LC],
                                 d_lq[:, dti * C.LC:(dti + 1) * C.LC],
                                 u_lq[:, dti * C.LC:(dti + 1) * C.LC])
                ng(v, e)
                e = v.tensor_scalar_mul(y3_lq[:, dti * C.LC:(dti + 1) * C.LC],
                                        u_lq[:, dti * C.LC:(dti + 1) * C.LC],
                                        dvs[:, dti:dti + 1])
                ng(v, e)
            i = 0
            for n in range(C.NST):
                s6c.wait(v, n)
                sl2 = (n % 2) * C.LC
                for dti in range(C.DT):
                    sl = (i % 2) * C.LC
                    v.wait_ge(ss, SS0[0] + i + 1)
                    if races and i >= 1:
                        v.wait_ge(Q["q7"], q7_0 + i)
                    ci = dti * C.NST + n
                    e = v.tensor_mul(dbsl[:, sl:sl + C.LC],
                                     du_lq[:, dti * C.LC:(dti + 1) * C.LC],
                                     bsl[:, sl2:sl2 + C.LC])
                    init = 0.0 if _lq == 0 else carry[:, ci:ci + 1]
                    ng(v, e)
                    e = v.tensor_tensor_scan(hsl[:, sl:sl + C.LC], dasl[:, sl:sl + C.LC],
                                             dbsl[:, sl:sl + C.LC], init, OP.mult, OP.add)
                    ng(v, e)
                    e = v.tensor_copy(carry[:, ci:ci + 1], hsl[:, sl + C.LC - 1:sl + C.LC])
                    ng(v, e)
                    e = v.tensor_mul(dbsl[:, sl:sl + C.LC], hsl[:, sl:sl + C.LC],
                                     csl[:, sl2:sl2 + C.LC])
                    ng(v, e)
                    e = v.tensor_add(y3_lq[:, dti * C.LC:(dti + 1) * C.LC],
                                     y3_lq[:, dti * C.LC:(dti + 1) * C.LC],
                                     dbsl[:, sl:sl + C.LC])
                    qinc(e, "q7")
                    i += 1
            s6b.wait(v, 0)
            if races:
                v.wait_ge(Q["q7"], q7_0 + C.NST * C.DT)
            for dti in range(C.DT):
                e = v.tensor_mul(y3_lq[:, dti * C.LC:(dti + 1) * C.LC],
                                 y3_lq[:, dti * C.LC:(dti + 1) * C.LC],
                                 sr_lq[:, dti * C.LC:(dti + 1) * C.LC])
                if dti < C.DT - 1:
                    ng(v, e)
            nsm[("p6y3", _lq)] = inc(e, sv)
        on("vector", p6_v)

        def p6_spill(s, _lq=lq):
            s.wait_ge(sv, nsm[("p6y3", _lq)])
            for dti in range(C.DT):
                inc(s.dma_start(out=y3_h[dti * 128:(dti + 1) * 128, _lq * C.LC:(_lq + 1) * C.LC],
                                in_=y3_lq[:, dti * C.LC:(dti + 1) * C.LC]), sh, 16)
        on("sync", p6_spill)
        barrier()

    fre("carry", "hsl", "dbsl", "dasl", "csl", "bsl", "u_lq", "sr_lq", "y3_lq",
        "du_lq", "d_lq")

    outwT = mksb(top, "outwT", 128, C.DT * C.DO, BF16)
    y3s = mksb(top, "y3s", 128, 2 * C.DT * C.W, BF16)
    evb = mksb(top, "evb", 128, 2 * 512, BF16)
    P7SS0 = cnt[id(ss)]
    s7w = Stream(1)
    s7y = Stream(2)
    s7s = Stream(2)
    s7y.preset(C.NW, C.DT)
    s7s.preset(C.NW * C.KC, 1)

    def p7_s0(s):
        s.wait_ge(Q["dg1"], 80)
        for dti in range(C.DT):
            s7w.inc(s.dma_start(out=outwT[:, dti * C.DO:(dti + 1) * C.DO],
                                in_=outwTb[dti * 128:(dti + 1) * 128, :]), 0)
        for q in range(min(2, C.NW)):
            so = (q % 2) * C.DT * C.W
            for dti in range(C.DT):
                s7y.inc(s.dma_start(out=y3s[:, so + dti * C.W:so + (dti + 1) * C.W],
                                    in_=y3_h[dti * 128:(dti + 1) * 128,
                                             q * C.W:(q + 1) * C.W]), q)
    on("sync", p7_s0)

    def p7_t(t):
        s7w.wait(t, 0)
        i = 0
        for q in range(C.NW):
            so = (q % 2) * C.DT * C.W
            s7y.wait(t, q)
            for ot in range(C.KC):
                pst = psA if i % 2 == 0 else psB
                if i >= 2:
                    t.wait_ge(ss, P7SS0 + i - 1)
                for dti in range(C.DT):
                    mm = t.matmul(pst[:, 0:C.W],
                                  outwT[:, dti * C.DO + ot * 128:dti * C.DO + (ot + 1) * 128],
                                  y3s[:, so + dti * C.W:so + (dti + 1) * C.W],
                                  start=(dti == 0), stop=(dti == C.DT - 1))
                nsm[("p7m", i)] = inc(mm, st)
                i += 1
    on("tensor", p7_t)

    def p7_sc(sc):
        i = 0
        for q in range(C.NW):
            for ot in range(C.KC):
                pst = psA if i % 2 == 0 else psB
                sc.wait_ge(st, nsm[("p7m", i)])
                if i >= 2:
                    s7s.wait(sc, i - 2)
                so = (i % 2) * 512
                e = sc.copy(evb[:, so:so + C.W], pst[:, 0:C.W])
                nsm[("p7e", i)] = inc(e, ss)
                i += 1
    on("scalar", p7_sc)

    def p7_s1(s):
        i = 0
        for q in range(C.NW):
            for ot in range(C.KC):
                s.wait_ge(ss, nsm[("p7e", i)])
                so = (i % 2) * 512
                e = s.dma_start(out=cin1[ot * 128:(ot + 1) * 128, q * C.W:(q + 1) * C.W],
                                in_=evb[:, so:so + C.W])
                s7s.inc(e, i)
                i += 1
            if q + 2 < C.NW:
                so = (q % 2) * C.DT * C.W
                for dti in range(C.DT):
                    s7y.inc(s.dma_start(
                        out=y3s[:, so + dti * C.W:so + (dti + 1) * C.W],
                        in_=y3_h[dti * 128:(dti + 1) * 128,
                                 (q + 2) * C.W:(q + 3) * C.W]), q + 2)
    on("sync", p7_s1)

    def p7_g(g):
        s7s.wait_all(g)
        nsm["cc1"] = inc(g.collective_compute(
            "AllGather", OP.bypass, replica_groups=RG8, ins=[cin1[:]], outs=[cout1[:]]), scc)
    on("gpsimd", p7_g)

    barrier()
    fre("evb", "y3s", "outwT")

    combT = mksb(top, "combT", 128, C.ET * C.LH, BF16, side="right")
    xr1 = mksb(top, "xr1", 128, C.KC * C.LH, BF16, side="right")
    xsq = mksb(top, "xsq", 128, C.KC * max(C.LH, C.ES), BF16)
    pme = mksb(top, "pme", 128, C.KC * C.LH, BF16)
    ppt = mksb(top, "ppt", 128, C.KC * C.LH, BF16)

    s8l = Stream(1)

    def p8_s0(s):
        s.wait_ge(scc, nsm["cc1"])
        pid, pxor, jcol = REG["pid"], REG["pxor"], REG["jcol"]
        for kc in range(C.KC):
            s8l.inc(s.dma_start_transpose(out=xr1[:, kc * C.LH:(kc + 1) * C.LH],
                                          in_=xhb[:, kc * 128:(kc + 1) * 128]), 0)
        c1r = cout1.rearrange("(c k p) l -> c p k l", c=8, k=C.KC)
        pme_r = pme[:].rearrange("p (k l) -> p k l", k=C.KC)
        ppt_r = ppt[:].rearrange("p (k l) -> p k l", k=C.KC)
        s8l.inc(s.dma_start(out=pme_r,
                            in_=c1r[DS(pid, 1), :, :, DS(jcol, C.LH)].squeeze(0)), 0)
        s8l.inc(s.dma_start(out=ppt_r,
                            in_=c1r[DS(pxor, 1), :, :, DS(jcol, C.LH)].squeeze(0)), 0)
    on("sync", p8_s0)

    def p8_v(v):
        s8l.wait(v, 0)
        for kc in range(C.KC):
            e = v.tensor_add(pme[:, kc * C.LH:(kc + 1) * C.LH],
                             pme[:, kc * C.LH:(kc + 1) * C.LH],
                             ppt[:, kc * C.LH:(kc + 1) * C.LH])
            ng(v, e)
            e = v.tensor_add(xr1[:, kc * C.LH:(kc + 1) * C.LH],
                             xr1[:, kc * C.LH:(kc + 1) * C.LH],
                             pme[:, kc * C.LH:(kc + 1) * C.LH])
        nsm["p8res"] = inc(e, sv)
    on("vector", p8_v)

    LNI = [0]

    def ln_feat(xap, outap, ncols, gain, bias, pre_sem, pre_val):
        li = LNI[0]
        LNI[0] += 1
        NWS = ncols // min(512, ncols)
        WS = min(512, ncols)
        sln = Stream(1)

        def l_sc(sc):
            sc.wait_ge(pre_sem, pre_val)
            for kc in range(C.KC):
                e = sc.activation(xsq[:, kc * ncols:(kc + 1) * ncols],
                                  xap[:, kc * ncols:(kc + 1) * ncols], AF.Square)
            nsm[("lnsq", li)] = inc(e, ss)
        on("scalar", l_sc)

        q1b = qc["q1"]

        def l_t(t):
            t.wait_ge(ss, nsm[("lnsq", li)])
            for w in range(NWS):
                if w >= 1:
                    t.wait_ge(Q["q1"], q1b + w)
                pa = psSa[0:1, 0:WS]
                pb = psSb[0:1, 0:WS]
                for kc in range(C.KC):
                    t.matmul(pa, ones1[:], xap[:, kc * ncols + w * WS:kc * ncols + (w + 1) * WS],
                             start=(kc == 0), stop=(kc == C.KC - 1))
                for kc in range(C.KC):
                    mm = t.matmul(pb, ones1[:],
                                  xsq[:, kc * ncols + w * WS:kc * ncols + (w + 1) * WS],
                                  start=(kc == 0), stop=(kc == C.KC - 1))
                nsm[("lnst", li, w)] = inc(mm, st)
        on("tensor", l_t)

        def l_v(v):
            for w in range(NWS):
                v.wait_ge(st, nsm[("lnst", li, w)])
                pa = psSa[0:1, 0:WS]
                pb = psSb[0:1, 0:WS]
                e = v.tensor_scalar_mul(row1[0:1, w * WS:(w + 1) * WS], pa, 1.0 / C.DM)
                ng(v, e)
                e = v.tensor_scalar_mul(row1[0:1, ncols + w * WS:ncols + (w + 1) * WS],
                                        pb, 1.0 / C.DM)
                qinc(e, "q1")
                ng(v, sem=Q["q1"], val=qc["q1"])
            t = inc(v.tensor_mul(mrow[0:1, 0:ncols], row1[0:1, 0:ncols],
                                 row1[0:1, 0:ncols]), sv)
            v.wait_ge(sv, t)
            e = v.tensor_sub(row1[0:1, ncols:2 * ncols], row1[0:1, ncols:2 * ncols],
                             mrow[0:1, 0:ncols])
            nsm[("lnvar", li)] = inc(e, sv)
        on("vector", l_v)

        def l_sc2(sc):
            sc.wait_ge(sv, nsm[("lnvar", li)])
            e = sc.activation(row1[0:1, ncols:2 * ncols], row1[0:1, ncols:2 * ncols],
                              AF.Sqrt, bias=csts[0:1, 0:1])
            nsm[("lnsd", li)] = inc(e, ss)
        on("scalar", l_sc2)

        def l_v2(v):
            v.wait_ge(ss, nsm[("lnsd", li)])
            e = v.reciprocal(row1[0:1, ncols:2 * ncols], row1[0:1, ncols:2 * ncols])
            nsm[("lnr", li)] = inc(e, sv)
        on("vector", l_v2)

        def l_s(s):
            s.wait_ge(sv, nsm[("lnr", li)])
            sln.inc(s.dma_start(out=murow_h[0:1, 0:ncols], in_=row1[0:1, 0:ncols]), 0)
            e = s.dma_start(out=murow_h[1:2, 0:ncols], in_=row1[0:1, ncols:2 * ncols])
            sln.inc(e, 0)
        on("sync", l_s)

        dgn = "dg3" if li == 0 else "dg4"

        def l_g(g):
            sln.wait(g, 0)
            qinc16(g.dma_start(out=mrow[:, 0:ncols],
                               in_=murow_h[0:1, 0:ncols].broadcast_to((128, ncols))), dgn)
            qinc16(g.dma_start(out=rrow[:, 0:ncols],
                               in_=murow_h[1:2, 0:ncols].broadcast_to((128, ncols))), dgn)
        on("gpsimd", l_g)

        def l_v3(v):
            v.wait_ge(Q[dgn], 32)
            for kc in range(C.KC):
                e = v.tensor_sub(outap[:, kc * ncols:(kc + 1) * ncols],
                                 xap[:, kc * ncols:(kc + 1) * ncols], mrow[:, 0:ncols])
                ng(v, e)
                e = v.tensor_mul(outap[:, kc * ncols:(kc + 1) * ncols],
                                 outap[:, kc * ncols:(kc + 1) * ncols], rrow[:, 0:ncols])
                if gain is not None:
                    ng(v, e)
                    e = v.tensor_scalar(outap[:, kc * ncols:(kc + 1) * ncols],
                                        outap[:, kc * ncols:(kc + 1) * ncols],
                                        gain[:, kc:kc + 1], bias[:, kc:kc + 1],
                                        OP.mult, OP.add)
            nsm[("lndone", li)] = inc(e, sv)
            return
        on("vector", l_v3)
        return li

    li1 = ln_feat(xr1, xr1, C.LH, None, None, sv, nsm["p8res"])
    barrier()
    fre("ppt", "pme")

    s9p = Stream(1)

    def p9_spill(s):
        s.wait_ge(sv, nsm[("lndone", li1)])
        for kc in range(C.KC):
            e = s.dma_start(out=xr1_h[kc * 128:(kc + 1) * 128, :],
                            in_=xr1[:, kc * C.LH:(kc + 1) * C.LH])
            s9p.inc(e, 0)
    on("sync", p9_spill)

    def p9_gate(v):
        s9p.wait(v, 0)
        inc(v.engine_nop(), sv)
    on("vector", p9_gate)
    li2 = ln_feat(xr1, xr1, C.LH, normg, normb, sv, cnt[id(sv)])
    xln = xr1
    barrier()

    seS = mksb(top, "seS", 128, C.KC * C.ES, BF16, side="right")

    sseL = Stream(1)
    sseM = Stream(1)

    def pse_s(s):
        s.wait_ge(Q["dg1"], 80)
        for kc in range(C.KC):
            sseL.inc(s.dma_start(out=seS[:, kc * C.ES:(kc + 1) * C.ES],
                                 in_=seTb[kc * 128:(kc + 1) * 128, :]), 0)
    on("sync", pse_s)

    sesq = xsq

    def pse_sq(sc):
        sseL.wait(sc, 0)
        for kc in range(C.KC):
            e = sc.activation(sesq[:, kc * C.ES:(kc + 1) * C.ES],
                              seS[:, kc * C.ES:(kc + 1) * C.ES], AF.Square)
        nsm["psesq"] = inc(e, ss)
    on("scalar", pse_sq)

    q1c = qc["q1"]

    def pse_t(t):
        t.wait_ge(ss, nsm["psesq"])
        for w in range(C.ES // 512):
            if w >= 1:
                t.wait_ge(Q["q1"], q1c + w)
            pa = psSa[0:1, 0:512]
            pb = psSb[0:1, 0:512]
            for kc in range(C.KC):
                t.matmul(pa, ones1[:], seS[:, kc * C.ES + w * 512:kc * C.ES + (w + 1) * 512],
                         start=(kc == 0), stop=(kc == C.KC - 1))
            for kc in range(C.KC):
                mm = t.matmul(pb, ones1[:], sesq[:, kc * C.ES + w * 512:kc * C.ES + (w + 1) * 512],
                              start=(kc == 0), stop=(kc == C.KC - 1))
            nsm[("psest", w)] = inc(mm, st)
    on("tensor", pse_t)

    def pse_v(v):
        for w in range(C.ES // 512):
            v.wait_ge(st, nsm[("psest", w)])
            pa = psSa[0:1, 0:512]
            pb = psSb[0:1, 0:512]
            e = v.tensor_scalar_mul(row1[0:1, w * 512:(w + 1) * 512], pa, 1.0 / C.DM)
            ng(v, e)
            e = v.tensor_scalar_mul(row1[0:1, C.ES + w * 512:C.ES + (w + 1) * 512],
                                    pb, 1.0 / C.DM)
            qinc(e, "q1")
            ng(v, sem=Q["q1"], val=qc["q1"])
        e = v.tensor_mul(mrow[0:1, 0:C.ES], row1[0:1, 0:C.ES], row1[0:1, 0:C.ES])
        ng(v, e)
        e = v.tensor_sub(row1[0:1, C.ES:2 * C.ES], row1[0:1, C.ES:2 * C.ES],
                         mrow[0:1, 0:C.ES])
        nsm["psevar"] = inc(e, sv)
    on("vector", pse_v)

    def pse_sc2(sc):
        sc.wait_ge(sv, nsm["psevar"])
        e = sc.activation(row1[0:1, C.ES:2 * C.ES], row1[0:1, C.ES:2 * C.ES],
                          AF.Sqrt, bias=csts[0:1, 0:1])
        nsm["psesd"] = inc(e, ss)
    on("scalar", pse_sc2)

    def pse_v2(v):
        v.wait_ge(ss, nsm["psesd"])
        e = v.reciprocal(row1[0:1, C.ES:2 * C.ES], row1[0:1, C.ES:2 * C.ES])
        nsm["pser"] = inc(e, sv)
    on("vector", pse_v2)

    def pse_s2(s):
        s.wait_ge(sv, nsm["pser"])
        sseM.inc(s.dma_start(out=murow_h[0:1, 0:C.ES], in_=row1[0:1, 0:C.ES]), 0)
        e = s.dma_start(out=murow_h[1:2, 0:C.ES], in_=row1[0:1, C.ES:2 * C.ES])
        sseM.inc(e, 0)
    on("sync", pse_s2)

    def pse_g2(g):
        sseM.wait(g, 0)
        qinc16(g.dma_start(out=mrow[:, 0:C.ES],
                           in_=murow_h[0:1, 0:C.ES].broadcast_to((128, C.ES))), "dg5")
        qinc16(g.dma_start(out=rrow[:, 0:C.ES],
                           in_=murow_h[1:2, 0:C.ES].broadcast_to((128, C.ES))), "dg5")
    on("gpsimd", pse_g2)

    def pse_v3(v):
        v.wait_ge(Q["dg5"], 32)
        for kc in range(C.KC):
            e = v.tensor_sub(seS[:, kc * C.ES:(kc + 1) * C.ES],
                             seS[:, kc * C.ES:(kc + 1) * C.ES], mrow[:, 0:C.ES])
            ng(v, e)
            e = v.tensor_mul(seS[:, kc * C.ES:(kc + 1) * C.ES],
                             seS[:, kc * C.ES:(kc + 1) * C.ES], rrow[:, 0:C.ES])
            ng(v, e)
            e = v.tensor_scalar(seS[:, kc * C.ES:(kc + 1) * C.ES],
                                seS[:, kc * C.ES:(kc + 1) * C.ES],
                                sgsc[:, kc:kc + 1], sbsc[:, kc:kc + 1], OP.mult, OP.add)
        nsm["psedone"] = inc(e, sv)
    on("vector", pse_v3)
    dbg_dump("seS", lambda: seS[:], sv, lambda: nsm["psedone"])

    barrier()


    NLB, LB, LTB, WB, NWB = C.NLB, C.LB, C.LTB, C.WB, C.NWB
    fre("xsq")
    disp = mksb(top, "disp", 128, C.ET * C.LH, BF16)
    lgf = mksb(top, "lgf", 128, C.ET * LB, F32)
    dstat = mksb(top, "dstat", 128, 4 * C.ET, F32)
    pstat = mksb(top, "pstat", 128, 4 * C.ET, F32)
    b2c = mksb(top, "b2c", 128, max(C.LT, 2), F32)
    b2sb = mksb(top, "b2sb", max(C.LT, 1), 128, F32)
    lrow = mksb(top, "lrow", 128, 2 * C.ES, F32)
    brow_f = mksb(top, "brow_f", 128, LB, F32)

    def pfe_v0(v):
        nsm["pfedm"] = inc(v.memset(dstat[:], 0.0), sv)
    on("vector", pfe_v0)

    for lb in range(NLB):
        PLGSS0 = cnt[id(ss)]

        def plg_t(t, _lb=lb, _b=PLGSS0):
            i = 0
            for et in range(C.ET):
                for w in range(NWB):
                    pst = psA if i % 2 == 0 else psB
                    if i >= 2:
                        t.wait_ge(ss, _b + i - 1)
                    for kc in range(C.KC):
                        mm = t.matmul(
                            pst[:, 0:WB],
                            seS[:, kc * C.ES + et * 128:kc * C.ES + (et + 1) * 128],
                            xln[:, kc * C.LH + _lb * LB + w * WB:
                                kc * C.LH + _lb * LB + (w + 1) * WB],
                            start=(kc == 0), stop=(kc == C.KC - 1))
                    nsm[("plgm", _lb, i)] = inc(mm, st)
                    i += 1
        on("tensor", plg_t)

        def plg_sc(sc, _lb=lb):
            i = 0
            for et in range(C.ET):
                for w in range(NWB):
                    pst = psA if i % 2 == 0 else psB
                    sc.wait_ge(st, nsm[("plgm", _lb, i)])
                    e = sc.copy(lgf[:, et * LB + w * WB:et * LB + (w + 1) * WB],
                                pst[:, 0:WB])
                    nsm[("plge", _lb, i)] = inc(e, ss)
                    i += 1
        on("scalar", plg_sc)

        def pds_v(v, _lb=lb):
            v.wait_ge(ss, nsm[("plge", _lb, C.ET * NWB - 1)])
            ng(v, sem=sv, val=nsm["pfedm"])
            for et in range(C.ET):
                e = v.reduce_max(dstat[:, _lb * C.ET + et:_lb * C.ET + et + 1],
                                 lgf[:, et * LB:(et + 1) * LB], axis=AX.X, negate=True)
            nsm[("pdsm", _lb)] = inc(e, sv)
        on("vector", pds_v)

        def pds_sc(sc, _lb=lb):
            sc.wait_ge(sv, nsm[("pdsm", _lb)])
            for et in range(C.ET):
                e = sc.activation(
                    disp[:, et * C.LH + _lb * LB:et * C.LH + (_lb + 1) * LB],
                    lgf[:, et * LB:(et + 1) * LB], AF.Exp,
                    bias=dstat[:, _lb * C.ET + et:_lb * C.ET + et + 1],
                    accum_out=dstat[:, 2 * C.ET + _lb * C.ET + et:
                                    2 * C.ET + _lb * C.ET + et + 1])
            nsm[("pdse", _lb)] = inc(e, ss)
        on("scalar", pds_sc)

        NG4 = C.ET // 4
        PCBSV0 = [None]
        for ltl in range(LTB):
            ltg = lb * LTB + ltl

            def pcb_t(t, _lb=lb, _ltl=ltl, _ltg=ltg):
                if _ltl == 0:
                    PCBSV0[0] = cnt[id(sv)]
                t.wait_ge(ss, nsm[("pdse", _lb)])
                for g8 in range(NG4):
                    if g8 > 0:
                        t.wait_ge(sv, PCBSV0[0] + _ltl * (NG4 + 2) + g8)
                    elif _ltl > 0:
                        t.wait_ge(sv, PCBSV0[0] + (_ltl - 1) * (NG4 + 2) + NG4 + 1)
                    for q4 in range(4):
                        et = g8 * 4 + q4
                        tr = t.transpose(psTf[:, q4 * 128:(q4 + 1) * 128],
                                         lgf[:, et * LB + _ltl * 128:
                                             et * LB + (_ltl + 1) * 128],
                                         eyef[:])
                    nsm[("pcbt", _ltg, g8)] = inc(tr, st)
            on("tensor", pcb_t)

            def pcb_v(v, _ltg=ltg):
                for g8 in range(NG4):
                    v.wait_ge(st, nsm[("pcbt", _ltg, g8)])
                    e = v.tensor_copy(lrow[:, g8 * 512:(g8 + 1) * 512], psTf[:])
                    inc(e, sv)
                ng(v, sem=sv, val=cnt[id(sv)])
                e = v.reduce_max(b2c[:, _ltg:_ltg + 1], lrow[:, 0:C.ES],
                                 axis=AX.X, negate=True)
                nsm[("pcbm", _ltg)] = inc(e, sv)
            on("vector", pcb_v)

            def pcb_sc(sc, _ltg=ltg):
                sc.wait_ge(sv, nsm[("pcbm", _ltg)])
                t = inc(sc.activation(lrow[:, C.ES:2 * C.ES], lrow[:, 0:C.ES], AF.Exp,
                                      bias=b2c[:, _ltg:_ltg + 1],
                                      accum_out=tiny[:, 40:41]), ss)
                sc.wait_ge(ss, t)
                e = sc.activation(tiny[:, 41:42], tiny[:, 40:41], AF.Ln)
                nsm[("pcbz", _ltg)] = inc(e, ss)
            on("scalar", pcb_sc)

            def pcb_v2(v, _ltg=ltg):
                v.wait_ge(ss, nsm[("pcbz", _ltg)])
                e = v.tensor_sub(b2c[:, _ltg:_ltg + 1], b2c[:, _ltg:_ltg + 1],
                                 tiny[:, 41:42])
                nsm[("pcbb", _ltg)] = inc(e, sv)
            on("vector", pcb_v2)

        def pcb_t3(t, _lb=lb):
            t.wait_ge(sv, nsm[("pcbb", (_lb + 1) * LTB - 1)])
            tr = t.transpose(psTf[0:LTB, 0:128], b2c[:, _lb * LTB:(_lb + 1) * LTB],
                             eyef[:])
            nsm[("pcbtr", _lb)] = inc(tr, st)
        on("tensor", pcb_t3)

        def pcb_v4(v, _lb=lb):
            v.wait_ge(st, nsm[("pcbtr", _lb)])
            e = v.tensor_copy(b2sb[0:LTB, :], psTf[0:LTB, 0:128])
            nsm[("pcbsb", _lb)] = inc(e, sv)
        on("vector", pcb_v4)

        scb = Stream(1)

        def pcb_s(s, _lb=lb):
            s.wait_ge(sv, nsm[("pcbsb", _lb)])
            e = s.dma_start(out=b2_h[_lb * LTB:(_lb + 1) * LTB, :], in_=b2sb[0:LTB, :])
            scb.inc(e, 0)
        on("sync", pcb_s)

        def pcb_g(g, _lb=lb):
            scb.wait(g, 0)
            qinc16(g.dma_start(
                out=brow_f[:],
                in_=b2_h[_lb * LTB:(_lb + 1) * LTB, :]
                .rearrange("a b -> (a b)").unsqueeze(0).broadcast_to((128, LB))), "dg6")
        on("gpsimd", pcb_g)

        def pcb_v5(v, _lb=lb):
            v.wait_ge(Q["dg6"], (_lb + 1) * 16)
            for et in range(C.ET):
                e = v.scalar_tensor_tensor(lgf[:, et * LB:(et + 1) * LB],
                                           lgf[:, et * LB:(et + 1) * LB], 1.0,
                                           brow_f[:], OP.mult, OP.add)
            nsm[("pcbadd", _lb)] = inc(e, sv)
        on("vector", pcb_v5)

        def pcb_sc5(sc, _lb=lb):
            sc.wait_ge(sv, nsm[("pcbadd", _lb)])
            for et in range(C.ET):
                e = sc.activation(combT[:, et * C.LH + _lb * LB:et * C.LH + (_lb + 1) * LB],
                                  lgf[:, et * LB:(et + 1) * LB], AF.Exp)
            nsm[("pcomb", _lb)] = inc(e, ss)
        on("scalar", pcb_sc5)
        barrier()

    def pdn_s(s):
        s.wait_ge(ss, nsm[("pdse", NLB - 1)])
        nsm["pdst"] = inc(s.dma_start(out=cin2[:], in_=dstat[:]), sh, 16)
    on("sync", pdn_s)

    def pdn_g(g):
        g.wait_ge(sh, nsm["pdst"])
        nsm["cc2"] = inc(g.collective_compute(
            "AllGather", OP.bypass, replica_groups=RG8, ins=[cin2[:]], outs=[cout2[:]]), scc)
    on("gpsimd", pdn_g)

    def pdn_s2(s):
        s.wait_ge(scc, nsm["cc2"])
        nsm["pdsp"] = inc(s.dma_start(out=pstat[:], in_=cout2[TS(REG["pxor"], 128), :]),
                          sh, 16)
    on("sync", pdn_s2)

    def pdn_v(v):
        v.wait_ge(sh, nsm["pdsp"])
        ET = C.ET
        if NLB == 2:
            v.tensor_tensor(tiny[:, 0:ET], dstat[:, 0:ET], dstat[:, ET:2 * ET], OP.min)
            t = inc(v.tensor_tensor(tiny[:, ET:2 * ET], pstat[:, 0:ET],
                                    pstat[:, ET:2 * ET], OP.min), sv)
            v.wait_ge(sv, t)
            t = inc(v.tensor_tensor(tiny[:, 0:ET], tiny[:, 0:ET],
                                    tiny[:, ET:2 * ET], OP.min), sv)
            v.wait_ge(sv, t)
        else:
            t = inc(v.tensor_tensor(tiny[:, 0:ET], dstat[:, 0:ET], pstat[:, 0:ET],
                                    OP.min), sv)
            v.wait_ge(sv, t)
        for lb in range(NLB):
            v.tensor_sub(tiny[:, (1 + lb) * ET:(2 + lb) * ET], tiny[:, 0:ET],
                         dstat[:, lb * ET:(lb + 1) * ET])
            e = v.tensor_sub(tiny[:, (1 + NLB + lb) * ET:(2 + NLB + lb) * ET],
                             tiny[:, 0:ET], pstat[:, lb * ET:(lb + 1) * ET])
        nsm["pdnd"] = inc(e, sv)
    on("vector", pdn_v)

    def pdn_sc(sc):
        sc.wait_ge(sv, nsm["pdnd"])
        e = sc.activation(tiny[:, C.ET:(1 + 2 * NLB) * C.ET],
                          tiny[:, C.ET:(1 + 2 * NLB) * C.ET], AF.Exp)
        nsm["pdnf"] = inc(e, ss)
    on("scalar", pdn_sc)

    def pdn_v2(v):
        v.wait_ge(ss, nsm["pdnf"])
        ET = C.ET

        def step(ins):
            t = inc(ins, sv)
            v.wait_ge(sv, t)
        step(v.tensor_mul(tiny[:, (1 + 2 * NLB) * ET:(2 + 2 * NLB) * ET],
                          dstat[:, 2 * ET:3 * ET], tiny[:, ET:2 * ET]))
        for lb in range(NLB):
            if lb > 0:
                step(v.tensor_mul(tiny[:, (2 + 2 * NLB) * ET:(3 + 2 * NLB) * ET],
                                  dstat[:, (2 + lb) * ET:(3 + lb) * ET],
                                  tiny[:, (1 + lb) * ET:(2 + lb) * ET]))
                step(v.tensor_add(tiny[:, (1 + 2 * NLB) * ET:(2 + 2 * NLB) * ET],
                                  tiny[:, (1 + 2 * NLB) * ET:(2 + 2 * NLB) * ET],
                                  tiny[:, (2 + 2 * NLB) * ET:(3 + 2 * NLB) * ET]))
            step(v.tensor_mul(tiny[:, (2 + 2 * NLB) * ET:(3 + 2 * NLB) * ET],
                              pstat[:, (2 + lb) * ET:(3 + lb) * ET],
                              tiny[:, (1 + NLB + lb) * ET:(2 + NLB + lb) * ET]))
            step(v.tensor_add(tiny[:, (1 + 2 * NLB) * ET:(2 + 2 * NLB) * ET],
                              tiny[:, (1 + 2 * NLB) * ET:(2 + 2 * NLB) * ET],
                              tiny[:, (2 + 2 * NLB) * ET:(3 + 2 * NLB) * ET]))
        step(v.reciprocal(tiny[:, (1 + 2 * NLB) * ET:(2 + 2 * NLB) * ET],
                          tiny[:, (1 + 2 * NLB) * ET:(2 + 2 * NLB) * ET]))
        for lb in range(NLB):
            step(v.tensor_mul(tiny[:, (1 + lb) * ET:(2 + lb) * ET],
                              tiny[:, (1 + lb) * ET:(2 + lb) * ET],
                              tiny[:, (1 + 2 * NLB) * ET:(2 + 2 * NLB) * ET]))
        for lb in range(NLB):
            for et in range(C.ET):
                e = v.tensor_scalar_mul(
                    disp[:, et * C.LH + lb * LB:et * C.LH + (lb + 1) * LB],
                    disp[:, et * C.LH + lb * LB:et * C.LH + (lb + 1) * LB],
                    tiny[:, (1 + lb) * ET + et:(1 + lb) * ET + et + 1])
        nsm["pdisp"] = inc(e, sv)
    on("vector", pdn_v2)

    dbg_dump("disp", lambda: disp[:], sv, lambda: nsm["pdisp"])
    dbg_dump("dispu", lambda: disp[:], ss, lambda: nsm[("pdse", NLB - 1)])
    dbg_dump("pstat", lambda: pstat[:], sh, lambda: nsm["pdsp"])
    dbg_dump("tiny", lambda: tiny[:], sv, lambda: nsm["pdisp"])
    dbg_dump("combT", lambda: combT[:], ss, lambda: nsm[("pcomb", NLB - 1)])
    dbg_dump("dstat", lambda: dstat[:], ss, lambda: nsm[("pdse", NLB - 1)])
    barrier()
    fre("brow_f", "lrow", "b2sb", "b2c", "pstat", "dstat", "lgf", "seS")

    xlnT = mksb(top, "xlnT", 128, C.LT * C.DM, BF16)
    dispT = mksb(top, "dispT", 128, C.LT * C.ES, BF16)
    q4b = qc["q4"]

    def ptr_t(t):
        gi = 0
        for (src, nt) in ((xln, C.KC), (disp, C.ET)):
            for lt in range(C.LT):
                for g4 in range(0, nt, 4):
                    w = min(4, nt - g4)
                    if gi >= 2:
                        t.wait_ge(Q["q4"], q4b + gi - 1)
                    pT = psTa if gi % 2 == 0 else psTb
                    for q4 in range(w):
                        tt = g4 + q4
                        tr = t.transpose(pT[:, q4 * 128:(q4 + 1) * 128],
                                         src[:, tt * C.LH + lt * 128:tt * C.LH + (lt + 1) * 128],
                                         eyeb[:])
                    nsm[("ptr", gi)] = inc(tr, st)
                    gi += 1
    on("tensor", ptr_t)

    def ptr_v(v):
        gi = 0
        for (dst, nt, tl) in ((xlnT, C.KC, C.DM), (dispT, C.ET, C.ES)):
            for lt in range(C.LT):
                for g4 in range(0, nt, 4):
                    w = min(4, nt - g4)
                    v.wait_ge(st, nsm[("ptr", gi)])
                    pT = psTa if gi % 2 == 0 else psTb
                    e = v.tensor_copy(dst[:, lt * tl + g4 * 128:lt * tl + (g4 + w) * 128],
                                      pT[:, 0:w * 128])
                    qinc(e, "q4")
                    gi += 1
    on("vector", ptr_v)

    barrier()
    fre("xr1")

    slotp = mksb(top, "slotp", 128, C.KC * C.ES, BF16)
    PSLSS0 = cnt[id(ss)]

    def psl_t(t):
        i = 0
        for kc in range(C.KC):
            for w in range(C.ES // 512):
                pst = psA if i % 2 == 0 else psB
                if i >= 2:
                    t.wait_ge(ss, PSLSS0 + i - 1)
                for lt in range(C.LT):
                    mm = t.matmul(pst[:, 0:512],
                                  xlnT[:, lt * C.DM + kc * 128:lt * C.DM + (kc + 1) * 128],
                                  dispT[:, lt * C.ES + w * 512:lt * C.ES + (w + 1) * 512],
                                  start=(lt == 0), stop=(lt == C.LT - 1))
                nsm[("pslm", i)] = inc(mm, st)
                i += 1
    on("tensor", psl_t)

    def psl_sc(sc):
        i = 0
        for kc in range(C.KC):
            for w in range(C.ES // 512):
                pst = psA if i % 2 == 0 else psB
                sc.wait_ge(st, nsm[("pslm", i)])
                e = sc.copy(slotp[:, kc * C.ES + w * 512:kc * C.ES + (w + 1) * 512],
                            pst[:, 0:512])
                nsm[("psle", i)] = inc(e, ss)
                i += 1
    on("scalar", psl_sc)

    def psl_s(s):
        s.wait_ge(ss, nsm[("psle", C.KC * (C.ES // 512) - 1)])
        for kc in range(C.KC):
            e = s.dma_start(out=cin3[kc * 128:(kc + 1) * 128, :],
                            in_=slotp[:, kc * C.ES:(kc + 1) * C.ES])
            inc(e, sh, 16)
        nsm["pslst"] = cnt[id(sh)]
    on("sync", psl_s)

    def psl_g(g):
        g.wait_ge(sh, nsm["pslst"])
        nsm["cc3"] = inc(g.collective_compute(
            "AllGather", OP.bypass, replica_groups=RG8, ins=[cin3[:]], outs=[cout3[:]]), scc)
    on("gpsimd", psl_g)

    barrier()
    fre("slotp", "dispT", "xlnT", "disp")

    sown = mksb(top, "sown", 128, C.KC * 512, BF16)
    srec = mksb(top, "srec", 128, 8 * C.KC * 128, BF16)

    def psg_s(s):
        s.wait_ge(scc, nsm["cc3"])
        pid = REG["pid"]
        c3r = cout3.rearrange("(g k p) e -> p g k e", g=8, k=C.KC)
        srec_r = srec[:].rearrange("p (g k e) -> p g k e", g=8, k=C.KC)
        e = s.dma_start(out=srec_r, in_=c3r[:, :, :, TS(pid, 128)])
        inc(e, sh, 16)
        nsm["psgl"] = cnt[id(sh)]
    on("sync", psg_s)

    def psg_v(v):
        v.wait_ge(sh, nsm["psgl"])
        for bb in range(4):
            for kc in range(C.KC):
                e = v.tensor_add(
                    sown[:, kc * 512 + bb * 128:kc * 512 + (bb + 1) * 128],
                    srec[:, (bb * C.KC + kc) * 128:(bb * C.KC + kc + 1) * 128],
                    srec[:, ((bb + 4) * C.KC + kc) * 128:((bb + 4) * C.KC + kc + 1) * 128])
        nsm["psown"] = inc(e, sv)
    on("vector", psg_v)
    dbg_dump("sown", lambda: sown[:], sv, lambda: nsm["psown"])

    barrier()
    fre("srec")

    wgt = mksb(top, "wgt", 128, C.KC * C.F, BF16)
    wut = mksb(top, "wut", 128, C.KC * C.F, BF16)
    wdt = mksb(top, "wdt", 128, C.FT * C.DO, BF16)
    gat = mksb(top, "gat", 128, C.FT * 256, BF16)
    upt = mksb(top, "upt", 128, C.FT * 256, BF16)
    eob = mksb(top, "eob", 64, 2 * C.DO, BF16)
    bdbc = mksb(top, "bdbc", 128, 2 * C.DO, F32)

    def slot_rhs(kc, e_):
        ap = sown[:, kc * 512:(kc + 1) * 512]
        ap = ap.rearrange("p (b s) -> p b s", b=4)
        return ap[:, :, e_ * 64:(e_ + 1) * 64]

    sBD = Stream(1)
    sWt = Stream(2)
    sE4 = Stream(1)
    sE4.preset(8, 1)

    def pex_s0(s):
        sBD.inc(s.dma_start(out=bdbc[:, 0:C.DO],
                            in_=i_bd[0:1, :].broadcast_to((128, C.DO))), 0)
        e = s.dma_start(out=bdbc[:, C.DO:2 * C.DO],
                        in_=i_bd[1:2, :].broadcast_to((128, C.DO)))
        sBD.inc(e, 0)
    on("sync", pex_s0)

    NDOW = max(1, C.DO // 512)
    WDO = min(512, C.DO)
    for e_i in range(2):
        def pex_sw(s, _e=e_i):
            if _e == 1:
                s.wait_ge(st, nsm[("peom", 0, 3)])
            for kc in range(C.KC):
                sWt.inc(s.dma_start(out=wgt[:, kc * C.F:(kc + 1) * C.F],
                                    in_=wgb[_e, kc * 128:(kc + 1) * 128, :]), _e)
                sWt.inc(s.dma_start(out=wut[:, kc * C.F:(kc + 1) * C.F],
                                    in_=wub[_e, kc * 128:(kc + 1) * 128, :]), _e)
            for ft in range(C.FT):
                sWt.inc(s.dma_start(out=wdt[:, ft * C.DO:(ft + 1) * C.DO],
                                    in_=wdb[_e, ft * 128:(ft + 1) * 128, :]), _e)
        on("sync", pex_sw)

        PEXSS0 = cnt[id(ss)]

        def pex_t(t, _e=e_i, _b=PEXSS0):
            sWt.wait(t, _e)
            t.wait_ge(sv, nsm["psown"])
            i = 0
            for wsb in (wgt, wut):
                for ft in range(C.FT):
                    pst = psA if i % 2 == 0 else psB
                    if i >= 2:
                        t.wait_ge(ss, _b + i - 1)
                    for kc in range(C.KC):
                        mm = t.matmul(pst[:, 0:256],
                                      wsb[:, kc * C.F + ft * 128:kc * C.F + (ft + 1) * 128],
                                      slot_rhs(kc, _e),
                                      start=(kc == 0), stop=(kc == C.KC - 1))
                    nsm[("pexm", _e, i)] = inc(mm, st)
                    i += 1
        on("tensor", pex_t)

        def pex_sc(sc, _e=e_i):
            i = 0
            for which in ("g", "u"):
                for ft in range(C.FT):
                    pst = psA if i % 2 == 0 else psB
                    sc.wait_ge(st, nsm[("pexm", _e, i)])
                    dst = gat if which == "g" else upt
                    bias = bgT if which == "g" else buT
                    fn = AF.Gelu if which == "g" else AF.Identity
                    e = sc.activation(dst[:, ft * 256:(ft + 1) * 256], pst[:, 0:256], fn,
                                      bias=bias[:, _e * C.FT + ft:_e * C.FT + ft + 1])
                    nsm[("pexe", _e, i)] = inc(e, ss)
                    i += 1
        on("scalar", pex_sc)

        def pex_v(v, _e=e_i):
            v.wait_ge(ss, nsm[("pexe", _e, 2 * C.FT - 1)])
            e = v.tensor_mul(gat[:], gat[:], upt[:])
            nsm[("pexgu", _e)] = inc(e, sv)
        on("vector", pex_v)

        PEXSV0 = cnt[id(sv)]

        def pex_t2(t, _e=e_i, _b=PEXSV0):
            t.wait_ge(sv, nsm[("pexgu", _e)])
            for bb in range(4):
                if bb >= 1:
                    t.wait_ge(sv, _b + bb)
                for w2 in range(NDOW):
                    pst = psA if w2 == 0 else psB
                    for ft in range(C.FT):
                        mm = t.matmul(pst[0:64, 0:WDO],
                                      gat[:, ft * 256 + bb * 64:ft * 256 + (bb + 1) * 64],
                                      wdt[:, ft * C.DO + w2 * WDO:ft * C.DO + (w2 + 1) * WDO],
                                      start=(ft == 0), stop=(ft == C.FT - 1))
                nsm[("peom", _e, bb)] = inc(mm, st)
        on("tensor", pex_t2)

        def pex_v2(v, _e=e_i):
            sBD.wait(v, 0)
            for bb in range(4):
                v.wait_ge(st, nsm[("peom", _e, bb)])
                if _e * 4 + bb >= 1:
                    sE4.wait(v, _e * 4 + bb - 1)
                for w2 in range(NDOW):
                    pst = psA if w2 == 0 else psB
                    e = v.scalar_tensor_tensor(
                        eob[0:64, w2 * WDO:(w2 + 1) * WDO], pst[0:64, 0:WDO], 1.0,
                        bdbc[0:64, _e * C.DO + w2 * WDO:_e * C.DO + (w2 + 1) * WDO],
                        OP.mult, OP.add)
                nsm[("peoe", _e, bb)] = inc(e, sv)
        on("vector", pex_v2)

        def pex_s2(s, _e=e_i):
            for bb in range(4):
                s.wait_ge(sv, nsm[("peoe", _e, bb)])
                e = s.dma_start(out=cin4[bb * 128 + _e * 64:bb * 128 + (_e + 1) * 64, :],
                                in_=eob[0:64, 0:C.DO])
                sE4.inc(e, _e * 4 + bb)
        on("sync", pex_s2)

    def pex_g(g):
        sE4.wait(g, 7)
        nsm["cc4"] = inc(g.collective_compute(
            "AllGather", OP.bypass, replica_groups=RG8, ins=[cin4[:]], outs=[cout4[:]]), scc)
    on("gpsimd", pex_g)

    barrier()
    fre("bdbc", "eob", "upt", "gat", "wdt", "wut", "wgt", "sown")

    res2 = mksb(top, "res2", 128, C.KC * C.LH, BF16)
    eoutb = mksb(top, "eoutb", 128, 8 * C.DO, BF16)
    xr1b = mksb(top, "xr1b", 128, C.KC * C.LH, BF16)

    s12x = Stream(1)

    def p12_s0(s):
        for kc in range(C.KC):
            e = s.dma_start(out=xr1b[:, kc * C.LH:(kc + 1) * C.LH],
                            in_=xr1_h[kc * 128:(kc + 1) * 128, :])
            s12x.inc(e, 0)
    on("sync", p12_s0)

    def pcm_s(s):
        s.wait_ge(scc, nsm["cc4"])
        breg = REG["breg"]
        c4r = cout4.rearrange("(g b p) o -> p g b o", g=8, b=4)
        eoutb_r = eoutb[:].rearrange("p (g o) -> p g o", g=8)
        e = s.dma_start(out=eoutb_r, in_=c4r[:, :, DS(breg, 1), :].squeeze(2))
        inc(e, sh, 16)
        nsm["pcml"] = cnt[id(sh)]
    on("sync", pcm_s)

    PCMSV0 = cnt[id(sv)]

    def pcm_t(t):
        t.wait_ge(sh, nsm["pcml"])
        i = 0
        for ot in range(C.KC):
            for w in range(C.NWH):
                pst = psA if i % 2 == 0 else psB
                if i >= 2:
                    t.wait_ge(sv, PCMSV0 + i - 1)
                for sdr in range(8):
                    mm = t.matmul(pst[:, 0:C.WH],
                                  eoutb[:, sdr * C.DO + ot * 128:sdr * C.DO + (ot + 1) * 128],
                                  combT[:, sdr * C.LH + w * C.WH:sdr * C.LH + (w + 1) * C.WH],
                                  start=(sdr == 0), stop=(sdr == 7))
                nsm[("pcmm", i)] = inc(mm, st)
                i += 1
    on("tensor", pcm_t)

    def pcm_v(v):
        s12x.wait(v, 0)
        i = 0
        for ot in range(C.KC):
            for w in range(C.NWH):
                pst = psA if i % 2 == 0 else psB
                v.wait_ge(st, nsm[("pcmm", i)])
                e = v.scalar_tensor_tensor(
                    res2[:, ot * C.LH + w * C.WH:ot * C.LH + (w + 1) * C.WH],
                    pst[:, 0:C.WH], 1.0,
                    xr1b[:, ot * C.LH + w * C.WH:ot * C.LH + (w + 1) * C.WH],
                    OP.mult, OP.add)
                nsm[("pcme", i)] = inc(e, sv)
                i += 1
    on("vector", pcm_v)
    dbg_dump("eoutb", lambda: eoutb[:], sh, lambda: nsm["pcml"])
    dbg_dump("res2", lambda: res2[:], sv, lambda: nsm[("pcme", C.KC * C.NWH - 1)])

    barrier()
    fre("xr1b", "eoutb")

    r2T = mksb(top, "r2T", 128, 2 * C.DM, BF16)
    osb = mksb(top, "osb", 128, 2 * C.DM, F32)
    sqf = mksb(top, "sqf", 128, C.DM, F32)
    q1d = qc["q1"]
    q2d = qc["q2"]
    sOo = Stream(2)

    for lt in range(C.LT):
        def pf_t(t, _lt=lt):
            for g4 in range(0, C.KC, 4):
                w = min(4, C.KC - g4)
                gi = _lt * ((C.KC + 3) // 4) + g4 // 4
                if gi >= 2:
                    t.wait_ge(Q["q1"], q1d + gi - 1)
                pT = psTa if gi % 2 == 0 else psTb
                for q4 in range(w):
                    kc = g4 + q4
                    tr = t.transpose(pT[:, q4 * 128:(q4 + 1) * 128],
                                     res2[:, kc * C.LH + _lt * 128:kc * C.LH + (_lt + 1) * 128],
                                     eyeb[:])
                nsm[("pft", _lt, g4)] = inc(tr, st)
        on("tensor", pf_t)

        def pf_v(v, _lt=lt):
            so = (_lt % 2) * C.DM
            if _lt >= 2:
                sOo.wait(v, _lt - 2)
            for g4 in range(0, C.KC, 4):
                w = min(4, C.KC - g4)
                gi = _lt * ((C.KC + 3) // 4) + g4 // 4
                v.wait_ge(st, nsm[("pft", _lt, g4)])
                pT = psTa if gi % 2 == 0 else psTb
                e = v.tensor_copy(r2T[:, so + g4 * 128:so + (g4 + w) * 128], pT[:, 0:w * 128])
                qinc(e, "q1")
            ng(v, sem=Q["q1"], val=qc["q1"])
            e = v.reduce_sum(tiny[:, 9:10], r2T[:, so:so + C.DM], axis=AX.X)
            nsm[("pfc", _lt)] = inc(e, sv)
            v.wait_ge(sv, nsm[("pfc", _lt)])
        on("vector", pf_v)

        def pf_sc(sc, _lt=lt):
            so = (_lt % 2) * C.DM
            sc.wait_ge(sv, nsm[("pfc", _lt)])
            e = sc.activation(sqf[:], r2T[:, so:so + C.DM], AF.Square,
                              accum_out=tiny[:, 8:9])
            nsm[("pfsq", _lt)] = inc(e, ss)
        on("scalar", pf_sc)

        def pf_v2(v, _lt=lt):
            v.wait_ge(ss, nsm[("pfsq", _lt)])
            v.tensor_scalar_mul(tiny[:, 9:10], tiny[:, 9:10], 1.0 / C.DM)
            t = inc(v.tensor_scalar_mul(tiny[:, 8:9], tiny[:, 8:9], 1.0 / C.DM), sv)
            v.wait_ge(sv, t)
            t = inc(v.tensor_mul(tiny[:, 10:11], tiny[:, 9:10], tiny[:, 9:10]), sv)
            v.wait_ge(sv, t)
            e = v.tensor_sub(tiny[:, 10:11], tiny[:, 8:9], tiny[:, 10:11])
            nsm[("pfv", _lt)] = inc(e, sv)
        on("vector", pf_v2)

        def pf_sc2(sc, _lt=lt):
            sc.wait_ge(sv, nsm[("pfv", _lt)])
            e = sc.activation(tiny[:, 10:11], tiny[:, 10:11], AF.Sqrt, bias=csts[:, 0:1])
            nsm[("pfsd", _lt)] = inc(e, ss)
        on("scalar", pf_sc2)

        def pf_v3(v, _lt=lt):
            so = (_lt % 2) * C.DM
            v.wait_ge(ss, nsm[("pfsd", _lt)])
            t = inc(v.reciprocal(tiny[:, 10:11], tiny[:, 10:11]), sv)
            v.wait_ge(sv, t)
            e = v.tensor_scalar(osb[:, so:so + C.DM], r2T[:, so:so + C.DM],
                                tiny[:, 9:10], tiny[:, 10:11], OP.subtract, OP.mult)
            nsm[("pfe", _lt)] = inc(e, sv)
        on("vector", pf_v3)

        def pf_s(s, _lt=lt):
            so = (_lt % 2) * C.DM
            s.wait_ge(sv, nsm[("pfe", _lt)])
            e = s.dma_start(out=o_y[_lt * 128:(_lt + 1) * 128, :], in_=osb[:, so:so + C.DM])
            sOo.inc(e, _lt)
        on("sync", pf_s)

    def fin_s(s):
        sOo.wait_all(s)
        if cnt[id(sh)] > 0:
            s.wait_ge(sh, cnt[id(sh)])
    on("sync", fin_s)

    for h in reversed(list(live.values())):
        h.__exit__(None, None, None)
    live.clear()
    top.close()
    global LAST_NSM
    LAST_NSM = dict(nsm)
    return nc


def _prep_core(C, c, I):
    b, j = c % 4, c // 4
    js, je = j * C.DH, (j + 1) * C.DH
    m = {}
    m["i_x"] = I["L0_attn_x"][b]
    m["i_xh"] = I["L0_attn_x"][b][j * C.LH:(j + 1) * C.LH]
    m["i_inw"] = np.concatenate([I["L0_attn_in_w"][js:je],
                                 I["L0_attn_in_w"][C.DI + js:C.DI + je]], 0)
    m["i_convw"] = I["L0_attn_conv_w"][js:je, 0, :].reshape(
        C.DT, 128, 4).transpose(1, 0, 2).reshape(128, C.DT * 4)
    m["i_convb"] = I["L0_attn_conv_b"][js:je].reshape(C.DT, 128).T
    m["i_xprojT"] = I["L0_attn_xproj_w"][:, js:je].T
    m["i_dtwT"] = I["L0_attn_dt_w"][js:je, :].T
    m["i_dtb"] = I["L0_attn_dt_b"][js:je].reshape(C.DT, 128).T
    m["i_alog"] = I["L0_attn_A_log"][js:je].reshape(
        C.DT, 128, C.NST).transpose(1, 0, 2).reshape(128, C.DT * C.NST)
    m["i_dv"] = I["L0_attn_D"][js:je].reshape(C.DT, 128).T
    m["i_outwT"] = I["L0_attn_out_w"][:, js:je].T
    for nm, k in (("i_normg", "L0_ffn_norm_g"), ("i_normb", "L0_ffn_norm_b"),
                  ("i_slotg", "L0_ffn_slot_g"), ("i_slotb", "L0_ffn_slot_b")):
        m[nm] = I[k].reshape(C.KC, 128).T
    m["i_scale"] = I["L0_ffn_scale"].reshape(1, 1)
    m["i_seT"] = I["L0_ffn_slot_embeds"].reshape(C.ES, C.DM).T
    m["i_wg"] = I["L0_ffn_Wg"][2 * c:2 * c + 2]
    m["i_wu"] = I["L0_ffn_Wu"][2 * c:2 * c + 2]
    m["i_wd"] = I["L0_ffn_Wd"][2 * c:2 * c + 2]
    m["i_bgT"] = I["L0_ffn_bg"][2 * c:2 * c + 2].reshape(
        2, C.FT, 128).transpose(2, 0, 1).reshape(128, 2 * C.FT)
    m["i_buT"] = I["L0_ffn_bu"][2 * c:2 * c + 2].reshape(
        2, C.FT, 128).transpose(2, 0, 1).reshape(128, 2 * C.FT)
    m["i_bd"] = I["L0_ffn_bd"][2 * c:2 * c + 2]
    m["i_eyeb"] = np.eye(128, dtype=BF)
    m["i_eyef"] = np.eye(128, dtype=np.float32)
    out = {}
    for k, v in m.items():
        v = np.asarray(v)
        if v.dtype == np.float64:
            v = v.astype(np.float32)
        out[k] = np.ascontiguousarray(v)
    return out


def kernel(**inputs):
    C = CFG_FULL
    I = {k: np.asarray(v) for k, v in inputs.items()}
    nc = build(C)
    in_maps = [_prep_core(C, c, I) for c in range(8)]
    res = run_bass_kernel_spmd(nc, in_maps, core_ids=list(range(8)))
    out = np.zeros((4, C.L, C.DM), np.float32)
    for c in range(8):
        b, j = c % 4, c // 4
        out[b, j * C.LH:(j + 1) * C.LH] = res.results[c]["o_y"]
    return out


# version 03
# speedup vs baseline: 1.0538x; 1.0538x over previous
import contextlib
import numpy as np
import ml_dtypes
import concourse.bass as bass
import concourse.mybir as mybir
from concourse.bass_utils import run_bass_kernel_spmd

LAST_NSM = {}

F32 = mybir.dt.float32
BF16 = mybir.dt.bfloat16
AF = mybir.ActivationFunctionType
OP = mybir.AluOpType
AX = mybir.AxisListType
BF = ml_dtypes.bfloat16
EPS = 1e-5
DS = bass.ds
TS = bass.ts


class CFG:
    def __init__(self, DM, L, LC, F):
        self.DM = DM
        self.DI = 2 * DM
        self.DH = DM
        self.L = L
        self.LH = L // 2
        self.LC = LC
        self.NLC = L // LC
        self.NST = 16
        self.R = 64
        self.RK2 = self.R + 2 * self.NST
        self.E = 16
        self.S = 64
        self.ES = self.E * self.S
        self.F = F
        self.DO = DM
        self.KC = DM // 128
        self.DT = self.DH // 128
        self.ET = self.ES // 128
        self.FT = F // 128
        self.LT = self.LH // 128
        self.W = min(512, L)
        self.NW = L // self.W
        self.WH = min(512, self.LH)
        self.NWH = self.LH // self.WH
        self.NLB = 2 if self.LH >= 256 else 1
        self.LB = self.LH // self.NLB
        self.LTB = self.LB // 128
        self.WB = min(512, self.LB)
        self.NWB = self.LB // self.WB
        assert DM % 128 == 0 and L % (2 * LC) == 0 and LC % 128 == 0
        assert self.LH % 128 == 0 and F % 128 == 0 and self.ET == 8


CFG_FULL = CFG(DM=1024, L=4096, LC=1024, F=2048)


def build(C: CFG, dbg=False, races=False):
    nc = bass.Bass("TRN2", target_bir_lowering=False, detect_race_conditions=races)
    dtn = nc.dram_tensor

    def ein(name, shape):
        return dtn(name, shape, F32, kind="ExternalInput").ap()

    i_x = ein("i_x", [C.L, C.DM])
    i_xh = ein("i_xh", [C.LH, C.DM])
    i_inw = ein("i_inw", [2 * C.DH, C.DM])
    i_convw = ein("i_convw", [128, C.DT * 4])
    i_convb = ein("i_convb", [128, C.DT])
    i_xprojT = ein("i_xprojT", [C.DH, C.RK2])
    i_dtwT = ein("i_dtwT", [C.R, C.DH])
    i_dtb = ein("i_dtb", [128, C.DT])
    i_alog = ein("i_alog", [128, C.DT * C.NST])
    i_dv = ein("i_dv", [128, C.DT])
    i_outwT = ein("i_outwT", [C.DH, C.DO])
    i_normg = ein("i_normg", [128, C.KC])
    i_normb = ein("i_normb", [128, C.KC])
    i_slotg = ein("i_slotg", [128, C.KC])
    i_slotb = ein("i_slotb", [128, C.KC])
    i_scale = ein("i_scale", [1, 1])
    i_seT = ein("i_seT", [C.DM, C.ES])
    i_wg = ein("i_wg", [2, C.DM, C.F])
    i_wu = ein("i_wu", [2, C.DM, C.F])
    i_wd = ein("i_wd", [2, C.F, C.DO])
    i_bgT = ein("i_bgT", [128, 2 * C.FT])
    i_buT = ein("i_buT", [128, 2 * C.FT])
    i_bd = ein("i_bd", [2, C.DO])
    i_eyeb = dtn("i_eyeb", [128, 128], BF16, kind="ExternalInput").ap()
    i_eyef = dtn("i_eyef", [128, 128], F32, kind="ExternalInput").ap()
    o_y = dtn("o_y", [C.LH, C.DM], F32, kind="ExternalOutput").ap()

    xb = dtn("xb", [C.L, C.DM], BF16).ap()
    xhb = dtn("xhb", [C.LH, C.DM], BF16).ap()
    inwb = dtn("inwb", [2 * C.DH, C.DM], BF16).ap()
    xprojTb = dtn("xprojTb", [C.DH, C.RK2], BF16).ap()
    dtwTb = dtn("dtwTb", [C.R, C.DH], BF16).ap()
    outwTb = dtn("outwTb", [C.DH, C.DO], BF16).ap()
    seTb = dtn("seTb", [C.DM, C.ES], BF16).ap()
    wgb = dtn("wgb", [2, C.DM, C.F], BF16).ap()
    wub = dtn("wub", [2, C.DM, C.F], BF16).ap()
    wdb = dtn("wdb", [2, C.F, C.DO], BF16).ap()
    dk = dict(kind="ExternalOutput") if dbg else {}
    xx_h = dtn("xx_h", [C.DH, C.L], BF16, **dk).ap()
    xr1_h = dtn("xr1_h", [C.DM, C.LH], BF16, **dk).ap()
    delta_h = dtn("delta_h", [C.DH, C.L], BF16, **dk).ap()
    u_h = dtn("u_h", [C.DH, C.L], BF16, **dk).ap()
    sres_h = dtn("sres_h", [C.DH, C.L], BF16, **dk).ap()
    y3_h = dtn("y3_h", [C.DH, C.L], BF16, **dk).ap()
    brow_h = dtn("brow_h", [2 * C.NST, C.L], BF16).ap()
    murow_h = dtn("murow_h", [2, max(C.LH, C.ES)], F32).ap()
    b2_h = dtn("b2_h", [C.LT, 128], F32).ap()
    cin0 = dtn("cin0", [C.RK2, C.L], F32).ap()
    cout0 = dtn("cout0", [C.RK2, C.L], F32).ap()
    cin1 = dtn("cin1", [C.DO, C.L], BF16).ap()
    cout1 = dtn("cout1", [C.DO, C.L], BF16).ap()
    cin2 = dtn("cin2", [128, 4 * C.ET], F32).ap()
    cout2 = dtn("cout2", [8 * 128, 4 * C.ET], F32).ap()
    cin3 = dtn("cin3", [C.DM, C.ES], BF16).ap()
    cout3 = dtn("cout3", [8 * C.DM, C.ES], BF16).ap()
    cin4 = dtn("cin4", [512, C.DO], BF16).ap()
    cout4 = dtn("cout4", [8 * 512, C.DO], BF16).ap()
    RG8 = [[0, 1, 2, 3, 4, 5, 6, 7]]
    RGP = [[0, 4], [1, 5], [2, 6], [3, 7]]

    top = contextlib.ExitStack()
    live = {}

    def mksb(st_, name, p, f, dty, side=None):
        h = nc.sbuf_tensor(name, [p, f], dty, side=side, unsafe_allow_realloc=True)
        t = h.__enter__()
        live[name] = h
        return t

    def fre(*names):
        for n in names:
            live.pop(n).__exit__(None, None, None)

    def mkps(st_, name, p, f, dty=F32):
        return st_.enter_context(nc.psum_tensor(name, [p, f], dty))

    def mksem(st_, name):
        return st_.enter_context(nc.semaphore(name))

    convw = mksb(top, "convw", 128, C.DT * 4, F32)
    convb = mksb(top, "convb", 128, C.DT, F32)
    dtb = mksb(top, "dtb", 128, C.DT, F32)
    alog = mksb(top, "alog", 128, C.DT * C.NST, F32)
    Amat = mksb(top, "Amat", 128, C.DT * C.NST, F32)
    dvs = mksb(top, "dvs", 128, C.DT, F32)
    normg = mksb(top, "normg", 128, C.KC, F32)
    normb = mksb(top, "normb", 128, C.KC, F32)
    sgsc = mksb(top, "sgsc", 128, C.KC, F32)
    sbsc = mksb(top, "sbsc", 128, C.KC, F32)
    bgT = mksb(top, "bgT", 128, 2 * C.FT, F32)
    buT = mksb(top, "buT", 128, 2 * C.FT, F32)
    eyeb = mksb(top, "eyeb", 128, 128, BF16)
    eyef = mksb(top, "eyef", 128, 128, F32)
    ones1 = mksb(top, "ones1", 128, 1, BF16)
    scale_bc = mksb(top, "scale_bc", 128, 1, F32)
    tiny = mksb(top, "tiny", 128, 64, F32)
    csts = mksb(top, "csts", 128, 2, F32)
    mrow = mksb(top, "mrow", 128, max(C.LH, C.ES), BF16)
    rrow = mksb(top, "rrow", 128, max(C.LH, C.ES), BF16)
    row1 = mksb(top, "row1", 1, 2 * max(C.LH, C.ES), F32)

    psA = mkps(top, "psA", 128, 512)
    psB = mkps(top, "psB", 128, 512)
    psSa = mkps(top, "psSa", 128, 512)
    psSb = mkps(top, "psSb", 128, 512)
    psTa = mkps(top, "psTa", 128, 512, BF16)
    psTb = mkps(top, "psTb", 128, 512, BF16)
    psTf = mkps(top, "psTf", 128, 512)

    sg = mksem(top, "sg")
    sh = mksem(top, "sh")
    st = mksem(top, "st")
    sv = mksem(top, "sv")
    ss = mksem(top, "ss")
    scc = mksem(top, "scc")
    ph = mksem(top, "ph")
    pipe_names = ["q1", "q2", "q3", "q4", "q5", "q6", "q7", "q8",
                  "dg0", "dg1", "dg2", "dg3", "dg4", "dg5", "dg6"]
    Q = {n: mksem(top, n) for n in pipe_names}
    qc = {n: 0 for n in pipe_names}

    cnt = {id(sg): 0, id(sh): 0, id(st): 0, id(sv): 0, id(ss): 0, id(scc): 0}
    nsm = {}
    nbar = [0]

    NSD = 8
    SD = [mksem(top, f"sd{i}") for i in range(NSD)]
    sdc = [0] * NSD
    sdbar = [0] * NSD
    qbar = {n: 0 for n in pipe_names}
    sdp = [0]
    srace = {n: mksem(top, f"sr_{n}") for n in
             ("Vector", "Scalar", "Tensor", "Sync", "GpSimd", "Pool")}
    cnt_sr = {n: 0 for n in srace}

    class Stream:
        def __init__(self, npar):
            self.sems = [sdp[0] + p for p in range(npar)]
            sdp[0] += npar
            assert sdp[0] <= NSD, "stream pool exhausted in phase window"
            self.npar = npar
            self.val = {}
            self._preset = False

        def preset(self, nsteps, per=1):
            shadow = {s: sdc[s] for s in self.sems}
            for k in range(nsteps):
                si = self.sems[k % self.npar]
                shadow[si] += 16 * per
                self.val[k] = shadow[si]
            self._preset = True

        def inc(self, instr, k):
            si = self.sems[k % self.npar]
            sdc[si] += 16
            if self._preset:
                assert sdc[si] <= self.val[k], "stream preset mismatch"
            else:
                self.val[k] = sdc[si]
            instr.then_inc(SD[si], 16)

        def wait(self, eng, k):
            eng.wait_ge(SD[self.sems[k % self.npar]], self.val[k])

        def wait_all(self, eng):
            for s in self.sems:
                if sdc[s] > sdbar[s]:
                    eng.wait_ge(SD[s], sdc[s])

    blk = top.enter_context(nc.Block())

    def inc(instr, s, n=1):
        instr.then_inc(s, n)
        cnt[id(s)] += n
        return cnt[id(s)]

    def qinc(instr, name):
        instr.then_inc(Q[name], 1)
        qc[name] += 1
        return qc[name]

    def qinc16(instr, name):
        instr.then_inc(Q[name], 16)
        qc[name] += 16
        return qc[name]

    def on(which, fn):
        getattr(blk, which)(fn)

    def ng(eng, instr=None, sem=None, val=None):
        if not races:
            return
        if val is None:
            en = next((n for n in srace if n in type(eng).__name__), "Sync")
            cnt_sr[en] += 1
            instr.then_inc(srace[en], 1)
            sem, val = srace[en], cnt_sr[en]
        eng.wait_ge(sem, val)

    def dbg_dump(name, ap_fn, sem_, val_fn):
        if not dbg:
            return
        p, f = ap_fn().partition_size(), ap_fn().free_size()
        od = dtn("dbg_" + name, [p, f], ap_fn().dtype, kind="ExternalOutput").ap()

        def _d(s):
            s.wait_ge(sem_, val_fn())
            inc(s.dma_start(out=od[:], in_=ap_fn()), sh, 16)
        on("sync", _d)

    def barrier():
        nbar[0] += 1
        t_ = 5 * nbar[0]
        shv, sgv = cnt[id(sh)], cnt[id(sg)]
        sdv = list(sdc)
        qv = dict(qc)
        for wn in ("tensor", "vector", "scalar", "sync", "gpsimd"):
            def f(eng, _t=t_, _wn=wn):
                if _wn == "sync":
                    eng.wait_ge(sh, shv)
                    eng.wait_ge(scc, cnt[id(scc)])
                    for i_ in range(NSD):
                        if sdv[i_] > sdbar[i_]:
                            eng.wait_ge(SD[i_], sdv[i_])
                    for qn_ in pipe_names:
                        if qv[qn_] > qbar[qn_]:
                            eng.wait_ge(Q[qn_], qv[qn_])
                if _wn == "gpsimd":
                    eng.wait_ge(sg, sgv)
                eng.drain()
                eng.sem_inc(ph, 1)
                eng.wait_ge(ph, _t)
            on(wn, f)
        sdbar[:] = sdv
        qbar.update(qv)
        sdp[0] = 0

    ph0 = contextlib.ExitStack()
    xprojT = mksb(top, "xprojT", 128, C.DT * C.RK2, BF16)
    dtwT = mksb(top, "dtwT", C.R, C.DH, BF16)
    xT = mksb(top, "xT", 128, C.KC * C.L, BF16)
    inwT = mksb(top, "inwT", 128, C.KC * 2 * C.DH, BF16)

    def p0_g(g):
        gq = C.DM // 4
        for qn in range(4):
            qinc16(g.dma_start(out=xb[:, qn * gq:(qn + 1) * gq],
                               in_=i_x[:, qn * gq:(qn + 1) * gq]), "dg0")
        qinc16(g.dma_start(out=inwb[:], in_=i_inw[:]), "dg0")
        qinc16(g.dma_start(out=xprojTb[:], in_=i_xprojT[:]), "q5")
        qinc16(g.dma_start(out=dtwTb[:], in_=i_dtwT[:]), "q5")
        qinc16(g.dma_start(out=xhb[:], in_=i_xh[:]), "q6")
        for ds_, sr in ((outwTb, i_outwT), (seTb, i_seT),
                        (wgb, i_wg), (wub, i_wu), (wdb, i_wd)):
            qinc16(g.dma_start(out=ds_[:], in_=sr[:]), "dg1")
    on("gpsimd", p0_g)

    def p0_s(s):
        for sr, ds_ in ((i_convw, convw), (i_convb, convb), (i_dtb, dtb),
                        (i_alog, alog), (i_dv, dvs), (i_normg, normg),
                        (i_normb, normb), (i_bgT, bgT), (i_buT, buT),
                        (i_eyeb, eyeb), (i_eyef, eyef)):
            inc(s.dma_start(out=ds_[:], in_=sr[:]), sh, 16)
        inc(s.dma_start(out=scale_bc[:], in_=i_scale[:].broadcast_to((128, 1))), sh, 16)
        inc(s.dma_start(out=tiny[:, 0:C.KC], in_=i_slotg[:]), sh, 16)
        inc(s.dma_start(out=tiny[:, C.KC:2 * C.KC], in_=i_slotb[:]), sh, 16)
        s.wait_ge(Q["dg0"], 80)
        for kc in range(C.KC):
            inc(s.dma_start_transpose(out=xT[:, kc * C.L:(kc + 1) * C.L],
                                      in_=xb[:, kc * 128:(kc + 1) * 128]), sh, 16)
        for kc in range(C.KC):
            inc(s.dma_start_transpose(out=inwT[:, kc * 2 * C.DH:(kc + 1) * 2 * C.DH],
                                      in_=inwb[:, kc * 128:(kc + 1) * 128]), sh, 16)
    on("sync", p0_s)
    P0H = cnt[id(sh)]

    REG = {}

    def p0_s2(s):
        pid = s.partition_id()
        REG["pid"] = pid
        REG["pxor"] = s.compute_val((pid + 4) % 8)
        REG["jj"] = s.compute_val(pid // 4)
        REG["breg"] = s.compute_val(pid % 4)
        REG["jcol"] = s.compute_val((pid // 4) * C.LH)
    on("sync", p0_s2)

    def p0_sc(sc):
        sc.wait_ge(sh, P0H)
        inc(sc.activation(Amat[:], alog[:], AF.Exp), ss)
    on("scalar", p0_sc)
    p0ss = cnt[id(ss)]

    def p0_v(v):
        v.memset(ones1[:], 1.0)
        v.memset(csts[:, 0:1], EPS)
        v.memset(csts[:, 1:2], -1.0)
        v.wait_ge(sh, P0H)
        v.tensor_scalar_mul(sgsc[:], tiny[:, 0:C.KC], scale_bc[:, 0:1])
        v.tensor_scalar_mul(sbsc[:], tiny[:, C.KC:2 * C.KC], scale_bc[:, 0:1])
        v.wait_ge(ss, p0ss)
        inc(v.tensor_scalar_mul(Amat[:], Amat[:], -1.0), sv)
    on("vector", p0_v)

    barrier()

    ev1 = mksb(top, "ev1", 128, 4 * 512, BF16)
    P1SS0 = cnt[id(ss)]
    sp1 = Stream(4)
    sp1.preset(2 * C.DT * C.NW, 1)

    def p1_t(t):
        t.wait_ge(sh, P0H)
        i = 0
        for ot in range(2 * C.DT):
            for q in range(C.NW):
                pst = psA if i % 2 == 0 else psB
                if i >= 2:
                    t.wait_ge(ss, P1SS0 + i - 1)
                for kc in range(C.KC):
                    mm = t.matmul(pst[:, 0:C.W],
                                  inwT[:, kc * 2 * C.DH + ot * 128:kc * 2 * C.DH + (ot + 1) * 128],
                                  xT[:, kc * C.L + q * C.W:kc * C.L + (q + 1) * C.W],
                                  start=(kc == 0), stop=(kc == C.KC - 1))
                nsm[("p1m", i)] = inc(mm, st)
                i += 1
    on("tensor", p1_t)

    def p1_sc(sc):
        i = 0
        for ot in range(2 * C.DT):
            for q in range(C.NW):
                pst = psA if i % 2 == 0 else psB
                sc.wait_ge(st, nsm[("p1m", i)])
                if i >= 4:
                    sp1.wait(sc, i - 4)
                so = (i % 4) * 512
                if ot < C.DT:
                    e = sc.copy(ev1[:, so:so + C.W], pst[:, 0:C.W])
                else:
                    e = sc.activation(ev1[:, so:so + C.W], pst[:, 0:C.W], AF.Silu)
                nsm[("p1e", i)] = inc(e, ss)
                i += 1
    on("scalar", p1_sc)

    def p1_s(s):
        i = 0
        for ot in range(2 * C.DT):
            for q in range(C.NW):
                s.wait_ge(ss, nsm[("p1e", i)])
                so = (i % 4) * 512
                dst = xx_h if ot < C.DT else sres_h
                rt = ot if ot < C.DT else ot - C.DT
                e = s.dma_start(out=dst[rt * 128:(rt + 1) * 128, q * C.W:(q + 1) * C.W],
                                in_=ev1[:, so:so + C.W])
                sp1.inc(e, i)
                i += 1
    on("sync", p1_s)

    def p0_s_late(s):
        s.wait_ge(Q["q5"], 32)
        for dti in range(C.DT):
            inc(s.dma_start(out=xprojT[:, dti * C.RK2:(dti + 1) * C.RK2],
                            in_=xprojTb[dti * 128:(dti + 1) * 128, :]), sh, 16)
        inc(s.dma_start(out=dtwT[:], in_=dtwTb[:]), sh, 16)
    on("sync", p0_s_late)

    barrier()
    fre("ev1", "inwT", "xT")

    uu = mksb(top, "uu", 128, 2 * C.L, BF16)
    xxs = mksb(top, "xxs", 128, 2 * (4 + C.L), BF16)

    def p2_v0(v):
        v.memset(xxs[:, 0:4], 0.0)
        inc(v.memset(xxs[:, 4 + C.L:8 + C.L], 0.0), sv)
    on("vector", p2_v0)
    p2pad = cnt[id(sv)]
    P2VB = cnt[id(sv)]

    sx2 = Stream(2)
    su2 = Stream(2)
    sx2.preset(C.DT, 1)
    su2.preset(C.DT, 1)

    def p2_v(v):
        for dti in range(C.DT):
            base = (dti % 2) * (4 + C.L) + 4
            so = (dti % 2) * C.L
            tmp = uu[:, so:so + C.L]
            sx2.wait(v, dti)
            if dti >= 2:
                su2.wait(v, dti - 2)
            e = v.tensor_scalar_mul(tmp, xxs[:, base - 2:base - 2 + C.L],
                                    convw[:, dti * 4 + 1:dti * 4 + 2])
            ng(v, e)
            e = v.scalar_tensor_tensor(tmp, xxs[:, base - 3:base - 3 + C.L],
                                       convw[:, dti * 4 + 0:dti * 4 + 1], tmp, OP.mult, OP.add)
            ng(v, e)
            e = v.scalar_tensor_tensor(tmp, xxs[:, base - 1:base - 1 + C.L],
                                       convw[:, dti * 4 + 2:dti * 4 + 3], tmp, OP.mult, OP.add)
            ng(v, e)
            e = v.scalar_tensor_tensor(tmp, xxs[:, base:base + C.L],
                                       convw[:, dti * 4 + 3:dti * 4 + 4], tmp, OP.mult, OP.add)
            nsm[("p2c", dti)] = inc(e, sv)
    on("vector", p2_v)

    def p2_sc(sc):
        for dti in range(C.DT):
            so = (dti % 2) * C.L
            tmp = uu[:, so:so + C.L]
            sc.wait_ge(sv, nsm[("p2c", dti)])
            nsm[("p2u", dti)] = inc(
                sc.activation(tmp, tmp, AF.Silu, bias=convb[:, dti:dti + 1]), ss)
    on("scalar", p2_sc)

    def p2_sync(s):
        s.wait_ge(sv, p2pad)
        for k in range(C.DT + 2):
            if k < C.DT:
                if k >= 2:
                    s.wait_ge(sv, P2VB + k - 1)
                e = s.dma_start(out=xxs[:, (k % 2) * (4 + C.L) + 4:
                                        (k % 2) * (4 + C.L) + 4 + C.L],
                                in_=xx_h[k * 128:(k + 1) * 128, :])
                sx2.inc(e, k)
            if k >= 2:
                s.wait_ge(ss, nsm[("p2u", k - 2)])
                e = s.dma_start(out=u_h[(k - 2) * 128:(k - 1) * 128, :],
                                in_=uu[:, ((k - 2) % 2) * C.L:((k - 2) % 2) * C.L + C.L])
                su2.inc(e, k - 2)
    on("sync", p2_sync)

    barrier()
    fre("xxs", "uu")

    xdbl = mksb(top, "xdbl", C.RK2, C.L, F32)
    xdbl2 = mksb(top, "xdbl2", C.RK2, C.L, F32)
    ut3 = mksb(top, "ut3", 128, 2 * C.DT * C.W, BF16)
    P3SV0 = cnt[id(sv)]
    s3l = Stream(2)

    def p3_s(s):
        for q in range(C.NW):
            so = (q % 2) * C.DT * C.W
            if q >= 2:
                s.wait_ge(sv, P3SV0 + q - 1)
            for dti in range(C.DT):
                e = s.dma_start(out=ut3[:, so + dti * C.W:so + (dti + 1) * C.W],
                                in_=u_h[dti * 128:(dti + 1) * 128, q * C.W:(q + 1) * C.W])
                s3l.inc(e, q)
    on("sync", p3_s)

    def p3_t(t):
        for q in range(C.NW):
            so = (q % 2) * C.DT * C.W
            pst = psA if q % 2 == 0 else psB
            s3l.wait(t, q)
            if q >= 2:
                t.wait_ge(sv, P3SV0 + q - 1)
            for dti in range(C.DT):
                mm = t.matmul(pst[0:C.RK2, 0:C.W],
                              xprojT[:, dti * C.RK2:(dti + 1) * C.RK2],
                              ut3[:, so + dti * C.W:so + (dti + 1) * C.W],
                              start=(dti == 0), stop=(dti == C.DT - 1))
            nsm[("p3m", q)] = inc(mm, st)
    on("tensor", p3_t)

    def p3_v(v):
        for q in range(C.NW):
            pst = psA if q % 2 == 0 else psB
            v.wait_ge(st, nsm[("p3m", q)])
            e = v.tensor_copy(xdbl[:, q * C.W:(q + 1) * C.W], pst[0:C.RK2, 0:C.W])
            nsm[("p3e", q)] = inc(e, sv)
    on("vector", p3_v)

    def p3_s2(s):
        s.wait_ge(sv, nsm[("p3e", C.NW - 1)])
        nsm["p3stage"] = inc(s.dma_start(out=cin0[:], in_=xdbl[:]), sh, 16)
    on("sync", p3_s2)

    def p3_g(g):
        g.wait_ge(sh, nsm["p3stage"])
        nsm["cc0"] = inc(g.collective_compute(
            "AllReduce", OP.add, replica_groups=RGP, ins=[cin0[:]], outs=[cout0[:]]), scc)
    on("gpsimd", p3_g)

    def p3_s3(s):
        s.wait_ge(scc, nsm["cc0"])
        nsm["p3pl"] = inc(
            s.dma_start(out=xdbl[:], in_=cout0[:]), sh, 16)
    on("sync", p3_s3)

    xdblb = mksb(top, "xdblb", C.R, C.L, BF16)

    def p3_v2(v):
        v.wait_ge(sh, nsm["p3pl"])
        nsm["p3add"] = inc(v.tensor_copy(xdblb[:], xdbl[0:C.R, :]), sv)
    on("vector", p3_v2)

    def p3_g2(g):
        g.wait_ge(sv, nsm["p3add"])
        qinc16(g.dma_start(out=brow_h[:], in_=xdbl[C.R:C.RK2, :]), "dg2")
    on("gpsimd", p3_g2)

    barrier()

    sc1 = mksb(top, "sc1", 128, C.W, F32)
    sc2 = mksb(top, "sc2", 128, C.W, F32)
    sc3 = mksb(top, "sc3", 128, 2 * C.W, BF16)
    P4SV0a = cnt[id(sv)]
    s4d = Stream(2)
    s4d.preset(C.DT * C.NW, 1)

    def p4_t(t):
        i = 0
        for dti in range(C.DT):
            for q in range(C.NW):
                pst = psA if i % 2 == 0 else psB
                if i >= 2:
                    t.wait_ge(sv, P4SV0a + 2 * (i - 1) + 1)
                mm = t.matmul(pst[:, 0:C.W], dtwT[:, dti * 128:(dti + 1) * 128],
                              xdblb[:, q * C.W:(q + 1) * C.W], start=True, stop=True)
                nsm[("p4m", i)] = inc(mm, st)
                i += 1
    on("tensor", p4_t)

    def p4_v_joint(v):
        i = 0
        for dti in range(C.DT):
            for q in range(C.NW):
                pst = psA if i % 2 == 0 else psB
                v.wait_ge(st, nsm[("p4m", i)])
                if i >= 1:
                    ng(v, sem=sv, val=nsm[("p4d", i - 1)])
                e = v.tensor_scalar_add(sc1[:, 0:C.W], pst[:, 0:C.W], dtb[:, dti:dti + 1])
                nsm[("p4z", i)] = inc(e, sv)
                v.wait_ge(ss, P4SS0[0] + i + 1)
                if i >= 2:
                    s4d.wait(v, i - 2)
                so3 = (i % 2) * C.W
                e = v.tensor_scalar_max(sc1[:, 0:C.W], sc1[:, 0:C.W], 0.0)
                ng(v, e)
                e = v.tensor_add(sc3[:, so3:so3 + C.W], sc1[:, 0:C.W], sc2[:, 0:C.W])
                nsm[("p4d", i)] = inc(e, sv)
                i += 1
    P4SS0 = [None]
    P4SV0 = [None]

    def p4_sc2(sc):
        i = 0
        for dti in range(C.DT):
            for q in range(C.NW):
                sc.wait_ge(sv, P4SV0[0] + 2 * i + 1)
                e = sc.activation(sc2[:, 0:C.W], sc1[:, 0:C.W], AF.Abs)
                ng(sc, e, sem=ss)
                e = sc.activation(sc2[:, 0:C.W], sc2[:, 0:C.W], AF.Exp, scale=csts[:, 1:2])
                ng(sc, e, sem=ss)
                inc(sc.activation(sc2[:, 0:C.W], sc2[:, 0:C.W], AF.Ln, bias=1.0), ss)
                i += 1

    P4SV0[0] = cnt[id(sv)]
    P4SS0[0] = cnt[id(ss)]
    on("scalar", p4_sc2)
    on("vector", p4_v_joint)

    def p4_s(s):
        i = 0
        for dti in range(C.DT):
            for q in range(C.NW):
                s.wait_ge(sv, nsm[("p4d", i)])
                so3 = (i % 2) * C.W
                e = s.dma_start(out=delta_h[dti * 128:(dti + 1) * 128, q * C.W:(q + 1) * C.W],
                                in_=sc3[:, so3:so3 + C.W])
                s4d.inc(e, i)
                i += 1
    on("sync", p4_s)

    barrier()
    fre("sc3", "sc2", "sc1", "xdblb", "ut3", "xdbl2", "xdbl", "dtwT", "xprojT")

    d_lq = mksb(top, "d_lq", 128, C.DT * C.LC, BF16)
    du_lq = mksb(top, "du_lq", 128, C.DT * C.LC, BF16)
    y3_lq = mksb(top, "y3_lq", 128, C.DT * C.LC, BF16)
    sr_lq = mksb(top, "sr_lq", 128, C.DT * C.LC, BF16)
    u_lq = mksb(top, "u_lq", 128, C.DT * C.LC, BF16)
    bsl = mksb(top, "bsl", 128, 2 * C.LC, BF16)
    csl = mksb(top, "csl", 128, 2 * C.LC, BF16)
    dasl = mksb(top, "dasl", 128, 2 * C.LC, BF16)
    dbsl = mksb(top, "dbsl", 128, 2 * C.LC, BF16)
    hsl = mksb(top, "hsl", 128, 2 * C.LC, BF16)
    carry = mksb(top, "carry", 128, C.DT * C.NST, F32)

    for lq in range(C.NLC):
        SV0 = [None]
        SS0 = [None]
        q7_0 = qc["q7"]
        s6a = Stream(1)
        s6b = Stream(1)
        s6c = Stream(2)

        def p6_load(s, _lq=lq):
            s.wait_ge(Q["dg2"], 16)
            for dti in range(C.DT):
                s6a.inc(s.dma_start(out=d_lq[:, dti * C.LC:(dti + 1) * C.LC],
                                    in_=delta_h[dti * 128:(dti + 1) * 128,
                                                _lq * C.LC:(_lq + 1) * C.LC]), 0)
            for dti in range(C.DT):
                s6a.inc(s.dma_start(out=u_lq[:, dti * C.LC:(dti + 1) * C.LC],
                                    in_=u_h[dti * 128:(dti + 1) * 128,
                                            _lq * C.LC:(_lq + 1) * C.LC]), 0)
            for dti in range(C.DT):
                s6b.inc(s.dma_start(out=sr_lq[:, dti * C.LC:(dti + 1) * C.LC],
                                    in_=sres_h[dti * 128:(dti + 1) * 128,
                                               _lq * C.LC:(_lq + 1) * C.LC]), 0)
            for n in range(C.NST):
                sl = n % 2
                if n >= 2:
                    s.wait_ge(Q["q7"], q7_0 + (n - 1) * C.DT)
                s6c.inc(s.dma_start(out=bsl[:, sl * C.LC:(sl + 1) * C.LC],
                                    in_=brow_h[n:n + 1, _lq * C.LC:(_lq + 1) * C.LC]
                                    .broadcast_to((128, C.LC))), n)
                e = s.dma_start(out=csl[:, sl * C.LC:(sl + 1) * C.LC],
                                in_=brow_h[C.NST + n:C.NST + n + 1, _lq * C.LC:(_lq + 1) * C.LC]
                                .broadcast_to((128, C.LC)))
                s6c.inc(e, n)
        on("sync", p6_load)

        def p6_da(sc, _lq=lq):
            SS0[0] = cnt[id(ss)]
            s6a.wait(sc, 0)
            i = 0
            for n in range(C.NST):
                for dti in range(C.DT):
                    sl = i % 2
                    if i >= 2:
                        sc.wait_ge(Q["q7"], q7_0 + i - 1)
                    e = sc.activation(dasl[:, sl * C.LC:(sl + 1) * C.LC],
                                      d_lq[:, dti * C.LC:(dti + 1) * C.LC], AF.Exp,
                                      scale=Amat[:, dti * C.NST + n:dti * C.NST + n + 1])
                    inc(e, ss)
                    i += 1
        on("scalar", p6_da)

        def p6_v(v, _lq=lq):
            s6a.wait(v, 0)
            for dti in range(C.DT):
                e = v.tensor_mul(du_lq[:, dti * C.LC:(dti + 1) * C.LC],
                                 d_lq[:, dti * C.LC:(dti + 1) * C.LC],
                                 u_lq[:, dti * C.LC:(dti + 1) * C.LC])
                ng(v, e)
                e = v.tensor_scalar_mul(y3_lq[:, dti * C.LC:(dti + 1) * C.LC],
                                        u_lq[:, dti * C.LC:(dti + 1) * C.LC],
                                        dvs[:, dti:dti + 1])
                ng(v, e)
            i = 0
            for n in range(C.NST):
                s6c.wait(v, n)
                sl2 = (n % 2) * C.LC
                for dti in range(C.DT):
                    sl = (i % 2) * C.LC
                    v.wait_ge(ss, SS0[0] + i + 1)
                    if races and i >= 1:
                        v.wait_ge(Q["q7"], q7_0 + i)
                    ci = dti * C.NST + n
                    e = v.tensor_mul(dbsl[:, sl:sl + C.LC],
                                     du_lq[:, dti * C.LC:(dti + 1) * C.LC],
                                     bsl[:, sl2:sl2 + C.LC])
                    init = 0.0 if _lq == 0 else carry[:, ci:ci + 1]
                    ng(v, e)
                    e = v.tensor_tensor_scan(hsl[:, sl:sl + C.LC], dasl[:, sl:sl + C.LC],
                                             dbsl[:, sl:sl + C.LC], init, OP.mult, OP.add)
                    ng(v, e)
                    e = v.tensor_copy(carry[:, ci:ci + 1], hsl[:, sl + C.LC - 1:sl + C.LC])
                    ng(v, e)
                    e = v.tensor_mul(dbsl[:, sl:sl + C.LC], hsl[:, sl:sl + C.LC],
                                     csl[:, sl2:sl2 + C.LC])
                    ng(v, e)
                    e = v.tensor_add(y3_lq[:, dti * C.LC:(dti + 1) * C.LC],
                                     y3_lq[:, dti * C.LC:(dti + 1) * C.LC],
                                     dbsl[:, sl:sl + C.LC])
                    qinc(e, "q7")
                    i += 1
            s6b.wait(v, 0)
            if races:
                v.wait_ge(Q["q7"], q7_0 + C.NST * C.DT)
            for dti in range(C.DT):
                e = v.tensor_mul(y3_lq[:, dti * C.LC:(dti + 1) * C.LC],
                                 y3_lq[:, dti * C.LC:(dti + 1) * C.LC],
                                 sr_lq[:, dti * C.LC:(dti + 1) * C.LC])
                if dti < C.DT - 1:
                    ng(v, e)
            nsm[("p6y3", _lq)] = inc(e, sv)
        on("vector", p6_v)

        def p6_spill(s, _lq=lq):
            s.wait_ge(sv, nsm[("p6y3", _lq)])
            for dti in range(C.DT):
                inc(s.dma_start(out=y3_h[dti * 128:(dti + 1) * 128, _lq * C.LC:(_lq + 1) * C.LC],
                                in_=y3_lq[:, dti * C.LC:(dti + 1) * C.LC]), sh, 16)
        on("sync", p6_spill)
        barrier()

    fre("carry", "hsl", "dbsl", "dasl", "csl", "bsl", "u_lq", "sr_lq", "y3_lq",
        "du_lq", "d_lq")

    outwT = mksb(top, "outwT", 128, C.DT * C.DO, BF16)
    y3s = mksb(top, "y3s", 128, 2 * C.DT * C.W, BF16)
    evb = mksb(top, "evb", 128, 2 * 512, BF16)
    P7SS0 = cnt[id(ss)]
    s7w = Stream(1)
    s7y = Stream(2)
    s7s = Stream(2)
    s7y.preset(C.NW, C.DT)
    s7s.preset(C.NW * C.KC, 1)

    def p7_s0(s):
        s.wait_ge(Q["dg1"], 80)
        for dti in range(C.DT):
            s7w.inc(s.dma_start(out=outwT[:, dti * C.DO:(dti + 1) * C.DO],
                                in_=outwTb[dti * 128:(dti + 1) * 128, :]), 0)
        for q in range(min(2, C.NW)):
            so = (q % 2) * C.DT * C.W
            for dti in range(C.DT):
                s7y.inc(s.dma_start(out=y3s[:, so + dti * C.W:so + (dti + 1) * C.W],
                                    in_=y3_h[dti * 128:(dti + 1) * 128,
                                             q * C.W:(q + 1) * C.W]), q)
    on("sync", p7_s0)

    def p7_t(t):
        s7w.wait(t, 0)
        i = 0
        for q in range(C.NW):
            so = (q % 2) * C.DT * C.W
            s7y.wait(t, q)
            for ot in range(C.KC):
                pst = psA if i % 2 == 0 else psB
                if i >= 2:
                    t.wait_ge(ss, P7SS0 + i - 1)
                for dti in range(C.DT):
                    mm = t.matmul(pst[:, 0:C.W],
                                  outwT[:, dti * C.DO + ot * 128:dti * C.DO + (ot + 1) * 128],
                                  y3s[:, so + dti * C.W:so + (dti + 1) * C.W],
                                  start=(dti == 0), stop=(dti == C.DT - 1))
                nsm[("p7m", i)] = inc(mm, st)
                i += 1
    on("tensor", p7_t)

    def p7_sc(sc):
        i = 0
        for q in range(C.NW):
            for ot in range(C.KC):
                pst = psA if i % 2 == 0 else psB
                sc.wait_ge(st, nsm[("p7m", i)])
                if i >= 2:
                    s7s.wait(sc, i - 2)
                so = (i % 2) * 512
                e = sc.copy(evb[:, so:so + C.W], pst[:, 0:C.W])
                nsm[("p7e", i)] = inc(e, ss)
                i += 1
    on("scalar", p7_sc)

    def p7_s1(s):
        i = 0
        for q in range(C.NW):
            for ot in range(C.KC):
                s.wait_ge(ss, nsm[("p7e", i)])
                so = (i % 2) * 512
                e = s.dma_start(out=cin1[ot * 128:(ot + 1) * 128, q * C.W:(q + 1) * C.W],
                                in_=evb[:, so:so + C.W])
                s7s.inc(e, i)
                i += 1
            if q + 2 < C.NW:
                so = (q % 2) * C.DT * C.W
                for dti in range(C.DT):
                    s7y.inc(s.dma_start(
                        out=y3s[:, so + dti * C.W:so + (dti + 1) * C.W],
                        in_=y3_h[dti * 128:(dti + 1) * 128,
                                 (q + 2) * C.W:(q + 3) * C.W]), q + 2)
    on("sync", p7_s1)

    def p7_g(g):
        s7s.wait_all(g)
        nsm["cc1"] = inc(g.collective_compute(
            "AllReduce", OP.add, replica_groups=RGP, ins=[cin1[:]], outs=[cout1[:]]), scc)
    on("gpsimd", p7_g)

    barrier()
    fre("evb", "y3s", "outwT")

    combT = mksb(top, "combT", 128, C.ET * C.LH, BF16, side="right")
    xr1 = mksb(top, "xr1", 128, C.KC * C.LH, BF16, side="right")
    xsq = mksb(top, "xsq", 128, C.KC * max(C.LH, C.ES), BF16)
    pme = mksb(top, "pme", 128, C.KC * C.LH, BF16)
    ppt = mksb(top, "ppt", 128, C.KC * C.LH, BF16)

    s8l = Stream(1)

    def p8_s0(s):
        s.wait_ge(scc, nsm["cc1"])
        jcol = REG["jcol"]
        for kc in range(C.KC):
            s8l.inc(s.dma_start_transpose(out=xr1[:, kc * C.LH:(kc + 1) * C.LH],
                                          in_=xhb[:, kc * 128:(kc + 1) * 128]), 0)
        c1r = cout1.rearrange("(k p) l -> p k l", k=C.KC)
        pme_r = pme[:].rearrange("p (k l) -> p k l", k=C.KC)
        s8l.inc(s.dma_start(out=pme_r, in_=c1r[:, :, DS(jcol, C.LH)]), 0)
    on("sync", p8_s0)

    def p8_v(v):
        s8l.wait(v, 0)
        for kc in range(C.KC):
            e = v.tensor_add(xr1[:, kc * C.LH:(kc + 1) * C.LH],
                             xr1[:, kc * C.LH:(kc + 1) * C.LH],
                             pme[:, kc * C.LH:(kc + 1) * C.LH])
        nsm["p8res"] = inc(e, sv)
    on("vector", p8_v)

    LNI = [0]

    def ln_feat(xap, outap, ncols, gain, bias, pre_sem, pre_val):
        li = LNI[0]
        LNI[0] += 1
        NWS = ncols // min(512, ncols)
        WS = min(512, ncols)
        sln = Stream(1)

        def l_sc(sc):
            sc.wait_ge(pre_sem, pre_val)
            for kc in range(C.KC):
                e = sc.activation(xsq[:, kc * ncols:(kc + 1) * ncols],
                                  xap[:, kc * ncols:(kc + 1) * ncols], AF.Square)
            nsm[("lnsq", li)] = inc(e, ss)
        on("scalar", l_sc)

        q1b = qc["q1"]

        def l_t(t):
            t.wait_ge(ss, nsm[("lnsq", li)])
            for w in range(NWS):
                if w >= 1:
                    t.wait_ge(Q["q1"], q1b + w)
                pa = psSa[0:1, 0:WS]
                pb = psSb[0:1, 0:WS]
                for kc in range(C.KC):
                    t.matmul(pa, ones1[:], xap[:, kc * ncols + w * WS:kc * ncols + (w + 1) * WS],
                             start=(kc == 0), stop=(kc == C.KC - 1))
                for kc in range(C.KC):
                    mm = t.matmul(pb, ones1[:],
                                  xsq[:, kc * ncols + w * WS:kc * ncols + (w + 1) * WS],
                                  start=(kc == 0), stop=(kc == C.KC - 1))
                nsm[("lnst", li, w)] = inc(mm, st)
        on("tensor", l_t)

        def l_v(v):
            for w in range(NWS):
                v.wait_ge(st, nsm[("lnst", li, w)])
                pa = psSa[0:1, 0:WS]
                pb = psSb[0:1, 0:WS]
                e = v.tensor_scalar_mul(row1[0:1, w * WS:(w + 1) * WS], pa, 1.0 / C.DM)
                ng(v, e)
                e = v.tensor_scalar_mul(row1[0:1, ncols + w * WS:ncols + (w + 1) * WS],
                                        pb, 1.0 / C.DM)
                qinc(e, "q1")
                ng(v, sem=Q["q1"], val=qc["q1"])
            t = inc(v.tensor_mul(mrow[0:1, 0:ncols], row1[0:1, 0:ncols],
                                 row1[0:1, 0:ncols]), sv)
            v.wait_ge(sv, t)
            e = v.tensor_sub(row1[0:1, ncols:2 * ncols], row1[0:1, ncols:2 * ncols],
                             mrow[0:1, 0:ncols])
            nsm[("lnvar", li)] = inc(e, sv)
        on("vector", l_v)

        def l_sc2(sc):
            sc.wait_ge(sv, nsm[("lnvar", li)])
            e = sc.activation(row1[0:1, ncols:2 * ncols], row1[0:1, ncols:2 * ncols],
                              AF.Sqrt, bias=csts[0:1, 0:1])
            nsm[("lnsd", li)] = inc(e, ss)
        on("scalar", l_sc2)

        def l_v2(v):
            v.wait_ge(ss, nsm[("lnsd", li)])
            e = v.reciprocal(row1[0:1, ncols:2 * ncols], row1[0:1, ncols:2 * ncols])
            nsm[("lnr", li)] = inc(e, sv)
        on("vector", l_v2)

        def l_s(s):
            s.wait_ge(sv, nsm[("lnr", li)])
            sln.inc(s.dma_start(out=murow_h[0:1, 0:ncols], in_=row1[0:1, 0:ncols]), 0)
            e = s.dma_start(out=murow_h[1:2, 0:ncols], in_=row1[0:1, ncols:2 * ncols])
            sln.inc(e, 0)
        on("sync", l_s)

        dgn = "dg3" if li == 0 else "dg4"

        def l_g(g):
            sln.wait(g, 0)
            qinc16(g.dma_start(out=mrow[:, 0:ncols],
                               in_=murow_h[0:1, 0:ncols].broadcast_to((128, ncols))), dgn)
            qinc16(g.dma_start(out=rrow[:, 0:ncols],
                               in_=murow_h[1:2, 0:ncols].broadcast_to((128, ncols))), dgn)
        on("gpsimd", l_g)

        def l_v3(v):
            v.wait_ge(Q[dgn], 32)
            for kc in range(C.KC):
                e = v.tensor_sub(outap[:, kc * ncols:(kc + 1) * ncols],
                                 xap[:, kc * ncols:(kc + 1) * ncols], mrow[:, 0:ncols])
                ng(v, e)
                e = v.tensor_mul(outap[:, kc * ncols:(kc + 1) * ncols],
                                 outap[:, kc * ncols:(kc + 1) * ncols], rrow[:, 0:ncols])
                if gain is not None:
                    ng(v, e)
                    e = v.tensor_scalar(outap[:, kc * ncols:(kc + 1) * ncols],
                                        outap[:, kc * ncols:(kc + 1) * ncols],
                                        gain[:, kc:kc + 1], bias[:, kc:kc + 1],
                                        OP.mult, OP.add)
            nsm[("lndone", li)] = inc(e, sv)
            return
        on("vector", l_v3)
        return li

    li1 = ln_feat(xr1, xr1, C.LH, None, None, sv, nsm["p8res"])
    barrier()
    fre("ppt", "pme")

    s9p = Stream(1)

    def p9_spill(s):
        s.wait_ge(sv, nsm[("lndone", li1)])
        for kc in range(C.KC):
            e = s.dma_start(out=xr1_h[kc * 128:(kc + 1) * 128, :],
                            in_=xr1[:, kc * C.LH:(kc + 1) * C.LH])
            s9p.inc(e, 0)
    on("sync", p9_spill)

    def p9_gate(v):
        s9p.wait(v, 0)
        inc(v.engine_nop(), sv)
    on("vector", p9_gate)
    li2 = ln_feat(xr1, xr1, C.LH, normg, normb, sv, cnt[id(sv)])
    xln = xr1
    barrier()

    seS = mksb(top, "seS", 128, C.KC * C.ES, BF16, side="right")

    sseL = Stream(1)
    sseM = Stream(1)

    def pse_s(s):
        s.wait_ge(Q["dg1"], 80)
        for kc in range(C.KC):
            sseL.inc(s.dma_start(out=seS[:, kc * C.ES:(kc + 1) * C.ES],
                                 in_=seTb[kc * 128:(kc + 1) * 128, :]), 0)
    on("sync", pse_s)

    sesq = xsq

    def pse_sq(sc):
        sseL.wait(sc, 0)
        for kc in range(C.KC):
            e = sc.activation(sesq[:, kc * C.ES:(kc + 1) * C.ES],
                              seS[:, kc * C.ES:(kc + 1) * C.ES], AF.Square)
        nsm["psesq"] = inc(e, ss)
    on("scalar", pse_sq)

    q1c = qc["q1"]

    def pse_t(t):
        t.wait_ge(ss, nsm["psesq"])
        for w in range(C.ES // 512):
            if w >= 1:
                t.wait_ge(Q["q1"], q1c + w)
            pa = psSa[0:1, 0:512]
            pb = psSb[0:1, 0:512]
            for kc in range(C.KC):
                t.matmul(pa, ones1[:], seS[:, kc * C.ES + w * 512:kc * C.ES + (w + 1) * 512],
                         start=(kc == 0), stop=(kc == C.KC - 1))
            for kc in range(C.KC):
                mm = t.matmul(pb, ones1[:], sesq[:, kc * C.ES + w * 512:kc * C.ES + (w + 1) * 512],
                              start=(kc == 0), stop=(kc == C.KC - 1))
            nsm[("psest", w)] = inc(mm, st)
    on("tensor", pse_t)

    def pse_v(v):
        for w in range(C.ES // 512):
            v.wait_ge(st, nsm[("psest", w)])
            pa = psSa[0:1, 0:512]
            pb = psSb[0:1, 0:512]
            e = v.tensor_scalar_mul(row1[0:1, w * 512:(w + 1) * 512], pa, 1.0 / C.DM)
            ng(v, e)
            e = v.tensor_scalar_mul(row1[0:1, C.ES + w * 512:C.ES + (w + 1) * 512],
                                    pb, 1.0 / C.DM)
            qinc(e, "q1")
            ng(v, sem=Q["q1"], val=qc["q1"])
        e = v.tensor_mul(mrow[0:1, 0:C.ES], row1[0:1, 0:C.ES], row1[0:1, 0:C.ES])
        ng(v, e)
        e = v.tensor_sub(row1[0:1, C.ES:2 * C.ES], row1[0:1, C.ES:2 * C.ES],
                         mrow[0:1, 0:C.ES])
        nsm["psevar"] = inc(e, sv)
    on("vector", pse_v)

    def pse_sc2(sc):
        sc.wait_ge(sv, nsm["psevar"])
        e = sc.activation(row1[0:1, C.ES:2 * C.ES], row1[0:1, C.ES:2 * C.ES],
                          AF.Sqrt, bias=csts[0:1, 0:1])
        nsm["psesd"] = inc(e, ss)
    on("scalar", pse_sc2)

    def pse_v2(v):
        v.wait_ge(ss, nsm["psesd"])
        e = v.reciprocal(row1[0:1, C.ES:2 * C.ES], row1[0:1, C.ES:2 * C.ES])
        nsm["pser"] = inc(e, sv)
    on("vector", pse_v2)

    def pse_s2(s):
        s.wait_ge(sv, nsm["pser"])
        sseM.inc(s.dma_start(out=murow_h[0:1, 0:C.ES], in_=row1[0:1, 0:C.ES]), 0)
        e = s.dma_start(out=murow_h[1:2, 0:C.ES], in_=row1[0:1, C.ES:2 * C.ES])
        sseM.inc(e, 0)
    on("sync", pse_s2)

    def pse_g2(g):
        sseM.wait(g, 0)
        qinc16(g.dma_start(out=mrow[:, 0:C.ES],
                           in_=murow_h[0:1, 0:C.ES].broadcast_to((128, C.ES))), "dg5")
        qinc16(g.dma_start(out=rrow[:, 0:C.ES],
                           in_=murow_h[1:2, 0:C.ES].broadcast_to((128, C.ES))), "dg5")
    on("gpsimd", pse_g2)

    def pse_v3(v):
        v.wait_ge(Q["dg5"], 32)
        for kc in range(C.KC):
            e = v.tensor_sub(seS[:, kc * C.ES:(kc + 1) * C.ES],
                             seS[:, kc * C.ES:(kc + 1) * C.ES], mrow[:, 0:C.ES])
            ng(v, e)
            e = v.tensor_mul(seS[:, kc * C.ES:(kc + 1) * C.ES],
                             seS[:, kc * C.ES:(kc + 1) * C.ES], rrow[:, 0:C.ES])
            ng(v, e)
            e = v.tensor_scalar(seS[:, kc * C.ES:(kc + 1) * C.ES],
                                seS[:, kc * C.ES:(kc + 1) * C.ES],
                                sgsc[:, kc:kc + 1], sbsc[:, kc:kc + 1], OP.mult, OP.add)
        nsm["psedone"] = inc(e, sv)
    on("vector", pse_v3)
    dbg_dump("seS", lambda: seS[:], sv, lambda: nsm["psedone"])

    barrier()


    NLB, LB, LTB, WB, NWB = C.NLB, C.LB, C.LTB, C.WB, C.NWB
    fre("xsq")
    disp = mksb(top, "disp", 128, C.ET * C.LH, BF16)
    lgf = mksb(top, "lgf", 128, C.ET * LB, F32)
    dstat = mksb(top, "dstat", 128, 4 * C.ET, F32)
    pstat = mksb(top, "pstat", 128, 4 * C.ET, F32)
    b2c = mksb(top, "b2c", 128, max(C.LT, 2), F32)
    b2sb = mksb(top, "b2sb", max(C.LT, 1), 128, F32)
    lrow = mksb(top, "lrow", 128, 2 * C.ES, F32)
    brow_f = mksb(top, "brow_f", 128, LB, F32)

    def pfe_v0(v):
        nsm["pfedm"] = inc(v.memset(dstat[:], 0.0), sv)
    on("vector", pfe_v0)

    for lb in range(NLB):
        PLGSS0 = cnt[id(ss)]

        def plg_t(t, _lb=lb, _b=PLGSS0):
            i = 0
            for et in range(C.ET):
                for w in range(NWB):
                    pst = psA if i % 2 == 0 else psB
                    if i >= 2:
                        t.wait_ge(ss, _b + i - 1)
                    for kc in range(C.KC):
                        mm = t.matmul(
                            pst[:, 0:WB],
                            seS[:, kc * C.ES + et * 128:kc * C.ES + (et + 1) * 128],
                            xln[:, kc * C.LH + _lb * LB + w * WB:
                                kc * C.LH + _lb * LB + (w + 1) * WB],
                            start=(kc == 0), stop=(kc == C.KC - 1))
                    nsm[("plgm", _lb, i)] = inc(mm, st)
                    i += 1
        on("tensor", plg_t)

        def plg_sc(sc, _lb=lb):
            i = 0
            for et in range(C.ET):
                for w in range(NWB):
                    pst = psA if i % 2 == 0 else psB
                    sc.wait_ge(st, nsm[("plgm", _lb, i)])
                    e = sc.copy(lgf[:, et * LB + w * WB:et * LB + (w + 1) * WB],
                                pst[:, 0:WB])
                    nsm[("plge", _lb, i)] = inc(e, ss)
                    i += 1
        on("scalar", plg_sc)

        def pds_v(v, _lb=lb):
            v.wait_ge(ss, nsm[("plge", _lb, C.ET * NWB - 1)])
            ng(v, sem=sv, val=nsm["pfedm"])
            for et in range(C.ET):
                e = v.reduce_max(dstat[:, _lb * C.ET + et:_lb * C.ET + et + 1],
                                 lgf[:, et * LB:(et + 1) * LB], axis=AX.X, negate=True)
            nsm[("pdsm", _lb)] = inc(e, sv)
        on("vector", pds_v)

        def pds_sc(sc, _lb=lb):
            sc.wait_ge(sv, nsm[("pdsm", _lb)])
            for et in range(C.ET):
                e = sc.activation(
                    disp[:, et * C.LH + _lb * LB:et * C.LH + (_lb + 1) * LB],
                    lgf[:, et * LB:(et + 1) * LB], AF.Exp,
                    bias=dstat[:, _lb * C.ET + et:_lb * C.ET + et + 1],
                    accum_out=dstat[:, 2 * C.ET + _lb * C.ET + et:
                                    2 * C.ET + _lb * C.ET + et + 1])
            nsm[("pdse", _lb)] = inc(e, ss)
        on("scalar", pds_sc)

        NG4 = C.ET // 4
        PCBSV0 = [None]
        for ltl in range(LTB):
            ltg = lb * LTB + ltl

            def pcb_t(t, _lb=lb, _ltl=ltl, _ltg=ltg):
                if _ltl == 0:
                    PCBSV0[0] = cnt[id(sv)]
                t.wait_ge(ss, nsm[("pdse", _lb)])
                for g8 in range(NG4):
                    if g8 > 0:
                        t.wait_ge(sv, PCBSV0[0] + _ltl * (NG4 + 2) + g8)
                    elif _ltl > 0:
                        t.wait_ge(sv, PCBSV0[0] + (_ltl - 1) * (NG4 + 2) + NG4 + 1)
                    for q4 in range(4):
                        et = g8 * 4 + q4
                        tr = t.transpose(psTf[:, q4 * 128:(q4 + 1) * 128],
                                         lgf[:, et * LB + _ltl * 128:
                                             et * LB + (_ltl + 1) * 128],
                                         eyef[:])
                    nsm[("pcbt", _ltg, g8)] = inc(tr, st)
            on("tensor", pcb_t)

            def pcb_v(v, _ltg=ltg):
                for g8 in range(NG4):
                    v.wait_ge(st, nsm[("pcbt", _ltg, g8)])
                    e = v.tensor_copy(lrow[:, g8 * 512:(g8 + 1) * 512], psTf[:])
                    inc(e, sv)
                ng(v, sem=sv, val=cnt[id(sv)])
                e = v.reduce_max(b2c[:, _ltg:_ltg + 1], lrow[:, 0:C.ES],
                                 axis=AX.X, negate=True)
                nsm[("pcbm", _ltg)] = inc(e, sv)
            on("vector", pcb_v)

            def pcb_sc(sc, _ltg=ltg):
                sc.wait_ge(sv, nsm[("pcbm", _ltg)])
                t = inc(sc.activation(lrow[:, C.ES:2 * C.ES], lrow[:, 0:C.ES], AF.Exp,
                                      bias=b2c[:, _ltg:_ltg + 1],
                                      accum_out=tiny[:, 40:41]), ss)
                sc.wait_ge(ss, t)
                e = sc.activation(tiny[:, 41:42], tiny[:, 40:41], AF.Ln)
                nsm[("pcbz", _ltg)] = inc(e, ss)
            on("scalar", pcb_sc)

            def pcb_v2(v, _ltg=ltg):
                v.wait_ge(ss, nsm[("pcbz", _ltg)])
                e = v.tensor_sub(b2c[:, _ltg:_ltg + 1], b2c[:, _ltg:_ltg + 1],
                                 tiny[:, 41:42])
                nsm[("pcbb", _ltg)] = inc(e, sv)
            on("vector", pcb_v2)

        def pcb_t3(t, _lb=lb):
            t.wait_ge(sv, nsm[("pcbb", (_lb + 1) * LTB - 1)])
            tr = t.transpose(psTf[0:LTB, 0:128], b2c[:, _lb * LTB:(_lb + 1) * LTB],
                             eyef[:])
            nsm[("pcbtr", _lb)] = inc(tr, st)
        on("tensor", pcb_t3)

        def pcb_v4(v, _lb=lb):
            v.wait_ge(st, nsm[("pcbtr", _lb)])
            e = v.tensor_copy(b2sb[0:LTB, :], psTf[0:LTB, 0:128])
            nsm[("pcbsb", _lb)] = inc(e, sv)
        on("vector", pcb_v4)

        scb = Stream(1)

        def pcb_s(s, _lb=lb):
            s.wait_ge(sv, nsm[("pcbsb", _lb)])
            e = s.dma_start(out=b2_h[_lb * LTB:(_lb + 1) * LTB, :], in_=b2sb[0:LTB, :])
            scb.inc(e, 0)
        on("sync", pcb_s)

        def pcb_g(g, _lb=lb):
            scb.wait(g, 0)
            qinc16(g.dma_start(
                out=brow_f[:],
                in_=b2_h[_lb * LTB:(_lb + 1) * LTB, :]
                .rearrange("a b -> (a b)").unsqueeze(0).broadcast_to((128, LB))), "dg6")
        on("gpsimd", pcb_g)

        def pcb_v5(v, _lb=lb):
            v.wait_ge(Q["dg6"], (_lb + 1) * 16)
            for et in range(C.ET):
                e = v.scalar_tensor_tensor(lgf[:, et * LB:(et + 1) * LB],
                                           lgf[:, et * LB:(et + 1) * LB], 1.0,
                                           brow_f[:], OP.mult, OP.add)
            nsm[("pcbadd", _lb)] = inc(e, sv)
        on("vector", pcb_v5)

        def pcb_sc5(sc, _lb=lb):
            sc.wait_ge(sv, nsm[("pcbadd", _lb)])
            for et in range(C.ET):
                e = sc.activation(combT[:, et * C.LH + _lb * LB:et * C.LH + (_lb + 1) * LB],
                                  lgf[:, et * LB:(et + 1) * LB], AF.Exp)
            nsm[("pcomb", _lb)] = inc(e, ss)
        on("scalar", pcb_sc5)
        barrier()

    def pdn_s(s):
        s.wait_ge(ss, nsm[("pdse", NLB - 1)])
        nsm["pdst"] = inc(s.dma_start(out=cin2[:], in_=dstat[:]), sh, 16)
    on("sync", pdn_s)

    def pdn_g(g):
        g.wait_ge(sh, nsm["pdst"])
        nsm["cc2"] = inc(g.collective_compute(
            "AllGather", OP.bypass, replica_groups=RG8, ins=[cin2[:]], outs=[cout2[:]]), scc)
    on("gpsimd", pdn_g)

    def pdn_s2(s):
        s.wait_ge(scc, nsm["cc2"])
        nsm["pdsp"] = inc(s.dma_start(out=pstat[:], in_=cout2[TS(REG["pxor"], 128), :]),
                          sh, 16)
    on("sync", pdn_s2)

    def pdn_v(v):
        v.wait_ge(sh, nsm["pdsp"])
        ET = C.ET
        if NLB == 2:
            v.tensor_tensor(tiny[:, 0:ET], dstat[:, 0:ET], dstat[:, ET:2 * ET], OP.min)
            t = inc(v.tensor_tensor(tiny[:, ET:2 * ET], pstat[:, 0:ET],
                                    pstat[:, ET:2 * ET], OP.min), sv)
            v.wait_ge(sv, t)
            t = inc(v.tensor_tensor(tiny[:, 0:ET], tiny[:, 0:ET],
                                    tiny[:, ET:2 * ET], OP.min), sv)
            v.wait_ge(sv, t)
        else:
            t = inc(v.tensor_tensor(tiny[:, 0:ET], dstat[:, 0:ET], pstat[:, 0:ET],
                                    OP.min), sv)
            v.wait_ge(sv, t)
        for lb in range(NLB):
            v.tensor_sub(tiny[:, (1 + lb) * ET:(2 + lb) * ET], tiny[:, 0:ET],
                         dstat[:, lb * ET:(lb + 1) * ET])
            e = v.tensor_sub(tiny[:, (1 + NLB + lb) * ET:(2 + NLB + lb) * ET],
                             tiny[:, 0:ET], pstat[:, lb * ET:(lb + 1) * ET])
        nsm["pdnd"] = inc(e, sv)
    on("vector", pdn_v)

    def pdn_sc(sc):
        sc.wait_ge(sv, nsm["pdnd"])
        e = sc.activation(tiny[:, C.ET:(1 + 2 * NLB) * C.ET],
                          tiny[:, C.ET:(1 + 2 * NLB) * C.ET], AF.Exp)
        nsm["pdnf"] = inc(e, ss)
    on("scalar", pdn_sc)

    def pdn_v2(v):
        v.wait_ge(ss, nsm["pdnf"])
        ET = C.ET

        def step(ins):
            t = inc(ins, sv)
            v.wait_ge(sv, t)
        step(v.tensor_mul(tiny[:, (1 + 2 * NLB) * ET:(2 + 2 * NLB) * ET],
                          dstat[:, 2 * ET:3 * ET], tiny[:, ET:2 * ET]))
        for lb in range(NLB):
            if lb > 0:
                step(v.tensor_mul(tiny[:, (2 + 2 * NLB) * ET:(3 + 2 * NLB) * ET],
                                  dstat[:, (2 + lb) * ET:(3 + lb) * ET],
                                  tiny[:, (1 + lb) * ET:(2 + lb) * ET]))
                step(v.tensor_add(tiny[:, (1 + 2 * NLB) * ET:(2 + 2 * NLB) * ET],
                                  tiny[:, (1 + 2 * NLB) * ET:(2 + 2 * NLB) * ET],
                                  tiny[:, (2 + 2 * NLB) * ET:(3 + 2 * NLB) * ET]))
            step(v.tensor_mul(tiny[:, (2 + 2 * NLB) * ET:(3 + 2 * NLB) * ET],
                              pstat[:, (2 + lb) * ET:(3 + lb) * ET],
                              tiny[:, (1 + NLB + lb) * ET:(2 + NLB + lb) * ET]))
            step(v.tensor_add(tiny[:, (1 + 2 * NLB) * ET:(2 + 2 * NLB) * ET],
                              tiny[:, (1 + 2 * NLB) * ET:(2 + 2 * NLB) * ET],
                              tiny[:, (2 + 2 * NLB) * ET:(3 + 2 * NLB) * ET]))
        step(v.reciprocal(tiny[:, (1 + 2 * NLB) * ET:(2 + 2 * NLB) * ET],
                          tiny[:, (1 + 2 * NLB) * ET:(2 + 2 * NLB) * ET]))
        for lb in range(NLB):
            step(v.tensor_mul(tiny[:, (1 + lb) * ET:(2 + lb) * ET],
                              tiny[:, (1 + lb) * ET:(2 + lb) * ET],
                              tiny[:, (1 + 2 * NLB) * ET:(2 + 2 * NLB) * ET]))
        for lb in range(NLB):
            for et in range(C.ET):
                e = v.tensor_scalar_mul(
                    disp[:, et * C.LH + lb * LB:et * C.LH + (lb + 1) * LB],
                    disp[:, et * C.LH + lb * LB:et * C.LH + (lb + 1) * LB],
                    tiny[:, (1 + lb) * ET + et:(1 + lb) * ET + et + 1])
        nsm["pdisp"] = inc(e, sv)
    on("vector", pdn_v2)

    dbg_dump("disp", lambda: disp[:], sv, lambda: nsm["pdisp"])
    dbg_dump("dispu", lambda: disp[:], ss, lambda: nsm[("pdse", NLB - 1)])
    dbg_dump("pstat", lambda: pstat[:], sh, lambda: nsm["pdsp"])
    dbg_dump("tiny", lambda: tiny[:], sv, lambda: nsm["pdisp"])
    dbg_dump("combT", lambda: combT[:], ss, lambda: nsm[("pcomb", NLB - 1)])
    dbg_dump("dstat", lambda: dstat[:], ss, lambda: nsm[("pdse", NLB - 1)])
    barrier()
    fre("brow_f", "lrow", "b2sb", "b2c", "pstat", "dstat", "lgf", "seS")

    xlnT = mksb(top, "xlnT", 128, C.LT * C.DM, BF16)
    dispT = mksb(top, "dispT", 128, C.LT * C.ES, BF16)
    q4b = qc["q4"]

    def ptr_t(t):
        gi = 0
        for (src, nt) in ((xln, C.KC), (disp, C.ET)):
            for lt in range(C.LT):
                for g4 in range(0, nt, 4):
                    w = min(4, nt - g4)
                    if gi >= 2:
                        t.wait_ge(Q["q4"], q4b + gi - 1)
                    pT = psTa if gi % 2 == 0 else psTb
                    for q4 in range(w):
                        tt = g4 + q4
                        tr = t.transpose(pT[:, q4 * 128:(q4 + 1) * 128],
                                         src[:, tt * C.LH + lt * 128:tt * C.LH + (lt + 1) * 128],
                                         eyeb[:])
                    nsm[("ptr", gi)] = inc(tr, st)
                    gi += 1
    on("tensor", ptr_t)

    def ptr_v(v):
        gi = 0
        for (dst, nt, tl) in ((xlnT, C.KC, C.DM), (dispT, C.ET, C.ES)):
            for lt in range(C.LT):
                for g4 in range(0, nt, 4):
                    w = min(4, nt - g4)
                    v.wait_ge(st, nsm[("ptr", gi)])
                    pT = psTa if gi % 2 == 0 else psTb
                    e = v.tensor_copy(dst[:, lt * tl + g4 * 128:lt * tl + (g4 + w) * 128],
                                      pT[:, 0:w * 128])
                    qinc(e, "q4")
                    gi += 1
    on("vector", ptr_v)

    barrier()
    fre("xr1")

    slotp = mksb(top, "slotp", 128, C.KC * C.ES, BF16)
    PSLSS0 = cnt[id(ss)]

    def psl_t(t):
        i = 0
        for kc in range(C.KC):
            for w in range(C.ES // 512):
                pst = psA if i % 2 == 0 else psB
                if i >= 2:
                    t.wait_ge(ss, PSLSS0 + i - 1)
                for lt in range(C.LT):
                    mm = t.matmul(pst[:, 0:512],
                                  xlnT[:, lt * C.DM + kc * 128:lt * C.DM + (kc + 1) * 128],
                                  dispT[:, lt * C.ES + w * 512:lt * C.ES + (w + 1) * 512],
                                  start=(lt == 0), stop=(lt == C.LT - 1))
                nsm[("pslm", i)] = inc(mm, st)
                i += 1
    on("tensor", psl_t)

    def psl_sc(sc):
        i = 0
        for kc in range(C.KC):
            for w in range(C.ES // 512):
                pst = psA if i % 2 == 0 else psB
                sc.wait_ge(st, nsm[("pslm", i)])
                e = sc.copy(slotp[:, kc * C.ES + w * 512:kc * C.ES + (w + 1) * 512],
                            pst[:, 0:512])
                nsm[("psle", i)] = inc(e, ss)
                i += 1
    on("scalar", psl_sc)

    def psl_s(s):
        s.wait_ge(ss, nsm[("psle", C.KC * (C.ES // 512) - 1)])
        for kc in range(C.KC):
            e = s.dma_start(out=cin3[kc * 128:(kc + 1) * 128, :],
                            in_=slotp[:, kc * C.ES:(kc + 1) * C.ES])
            inc(e, sh, 16)
        nsm["pslst"] = cnt[id(sh)]
    on("sync", psl_s)

    def psl_g(g):
        g.wait_ge(sh, nsm["pslst"])
        nsm["cc3"] = inc(g.collective_compute(
            "AllGather", OP.bypass, replica_groups=RG8, ins=[cin3[:]], outs=[cout3[:]]), scc)
    on("gpsimd", psl_g)

    barrier()
    fre("slotp", "dispT", "xlnT", "disp")

    sown = mksb(top, "sown", 128, C.KC * 512, BF16)
    srec = mksb(top, "srec", 128, 8 * C.KC * 128, BF16)

    def psg_s(s):
        s.wait_ge(scc, nsm["cc3"])
        pid = REG["pid"]
        c3r = cout3.rearrange("(g k p) e -> p g k e", g=8, k=C.KC)
        srec_r = srec[:].rearrange("p (g k e) -> p g k e", g=8, k=C.KC)
        e = s.dma_start(out=srec_r, in_=c3r[:, :, :, TS(pid, 128)])
        inc(e, sh, 16)
        nsm["psgl"] = cnt[id(sh)]
    on("sync", psg_s)

    def psg_v(v):
        v.wait_ge(sh, nsm["psgl"])
        for bb in range(4):
            for kc in range(C.KC):
                e = v.tensor_add(
                    sown[:, kc * 512 + bb * 128:kc * 512 + (bb + 1) * 128],
                    srec[:, (bb * C.KC + kc) * 128:(bb * C.KC + kc + 1) * 128],
                    srec[:, ((bb + 4) * C.KC + kc) * 128:((bb + 4) * C.KC + kc + 1) * 128])
        nsm["psown"] = inc(e, sv)
    on("vector", psg_v)
    dbg_dump("sown", lambda: sown[:], sv, lambda: nsm["psown"])

    barrier()
    fre("srec")

    wgt = mksb(top, "wgt", 128, C.KC * C.F, BF16)
    wut = mksb(top, "wut", 128, C.KC * C.F, BF16)
    wdt = mksb(top, "wdt", 128, C.FT * C.DO, BF16)
    gat = mksb(top, "gat", 128, C.FT * 256, BF16)
    upt = mksb(top, "upt", 128, C.FT * 256, BF16)
    eob = mksb(top, "eob", 64, 2 * C.DO, BF16)
    bdbc = mksb(top, "bdbc", 128, 2 * C.DO, F32)

    def slot_rhs(kc, e_):
        ap = sown[:, kc * 512:(kc + 1) * 512]
        ap = ap.rearrange("p (b s) -> p b s", b=4)
        return ap[:, :, e_ * 64:(e_ + 1) * 64]

    sBD = Stream(1)
    sWt = Stream(2)
    sE4 = Stream(1)
    sE4.preset(8, 1)

    def pex_s0(s):
        sBD.inc(s.dma_start(out=bdbc[:, 0:C.DO],
                            in_=i_bd[0:1, :].broadcast_to((128, C.DO))), 0)
        e = s.dma_start(out=bdbc[:, C.DO:2 * C.DO],
                        in_=i_bd[1:2, :].broadcast_to((128, C.DO)))
        sBD.inc(e, 0)
    on("sync", pex_s0)

    NDOW = max(1, C.DO // 512)
    WDO = min(512, C.DO)
    for e_i in range(2):
        def pex_sw(s, _e=e_i):
            if _e == 1:
                s.wait_ge(st, nsm[("peom", 0, 3)])
            for kc in range(C.KC):
                sWt.inc(s.dma_start(out=wgt[:, kc * C.F:(kc + 1) * C.F],
                                    in_=wgb[_e, kc * 128:(kc + 1) * 128, :]), _e)
                sWt.inc(s.dma_start(out=wut[:, kc * C.F:(kc + 1) * C.F],
                                    in_=wub[_e, kc * 128:(kc + 1) * 128, :]), _e)
            for ft in range(C.FT):
                sWt.inc(s.dma_start(out=wdt[:, ft * C.DO:(ft + 1) * C.DO],
                                    in_=wdb[_e, ft * 128:(ft + 1) * 128, :]), _e)
        on("sync", pex_sw)

        PEXSS0 = cnt[id(ss)]

        def pex_t(t, _e=e_i, _b=PEXSS0):
            sWt.wait(t, _e)
            t.wait_ge(sv, nsm["psown"])
            i = 0
            for wsb in (wgt, wut):
                for ft in range(C.FT):
                    pst = psA if i % 2 == 0 else psB
                    if i >= 2:
                        t.wait_ge(ss, _b + i - 1)
                    for kc in range(C.KC):
                        mm = t.matmul(pst[:, 0:256],
                                      wsb[:, kc * C.F + ft * 128:kc * C.F + (ft + 1) * 128],
                                      slot_rhs(kc, _e),
                                      start=(kc == 0), stop=(kc == C.KC - 1))
                    nsm[("pexm", _e, i)] = inc(mm, st)
                    i += 1
        on("tensor", pex_t)

        def pex_sc(sc, _e=e_i):
            i = 0
            for which in ("g", "u"):
                for ft in range(C.FT):
                    pst = psA if i % 2 == 0 else psB
                    sc.wait_ge(st, nsm[("pexm", _e, i)])
                    dst = gat if which == "g" else upt
                    bias = bgT if which == "g" else buT
                    fn = AF.Gelu if which == "g" else AF.Identity
                    e = sc.activation(dst[:, ft * 256:(ft + 1) * 256], pst[:, 0:256], fn,
                                      bias=bias[:, _e * C.FT + ft:_e * C.FT + ft + 1])
                    nsm[("pexe", _e, i)] = inc(e, ss)
                    i += 1
        on("scalar", pex_sc)

        def pex_v(v, _e=e_i):
            v.wait_ge(ss, nsm[("pexe", _e, 2 * C.FT - 1)])
            e = v.tensor_mul(gat[:], gat[:], upt[:])
            nsm[("pexgu", _e)] = inc(e, sv)
        on("vector", pex_v)

        PEXSV0 = cnt[id(sv)]

        def pex_t2(t, _e=e_i, _b=PEXSV0):
            t.wait_ge(sv, nsm[("pexgu", _e)])
            for bb in range(4):
                if bb >= 1:
                    t.wait_ge(sv, _b + bb)
                for w2 in range(NDOW):
                    pst = psA if w2 == 0 else psB
                    for ft in range(C.FT):
                        mm = t.matmul(pst[0:64, 0:WDO],
                                      gat[:, ft * 256 + bb * 64:ft * 256 + (bb + 1) * 64],
                                      wdt[:, ft * C.DO + w2 * WDO:ft * C.DO + (w2 + 1) * WDO],
                                      start=(ft == 0), stop=(ft == C.FT - 1))
                nsm[("peom", _e, bb)] = inc(mm, st)
        on("tensor", pex_t2)

        def pex_v2(v, _e=e_i):
            sBD.wait(v, 0)
            for bb in range(4):
                v.wait_ge(st, nsm[("peom", _e, bb)])
                if _e * 4 + bb >= 1:
                    sE4.wait(v, _e * 4 + bb - 1)
                for w2 in range(NDOW):
                    pst = psA if w2 == 0 else psB
                    e = v.scalar_tensor_tensor(
                        eob[0:64, w2 * WDO:(w2 + 1) * WDO], pst[0:64, 0:WDO], 1.0,
                        bdbc[0:64, _e * C.DO + w2 * WDO:_e * C.DO + (w2 + 1) * WDO],
                        OP.mult, OP.add)
                nsm[("peoe", _e, bb)] = inc(e, sv)
        on("vector", pex_v2)

        def pex_s2(s, _e=e_i):
            for bb in range(4):
                s.wait_ge(sv, nsm[("peoe", _e, bb)])
                e = s.dma_start(out=cin4[bb * 128 + _e * 64:bb * 128 + (_e + 1) * 64, :],
                                in_=eob[0:64, 0:C.DO])
                sE4.inc(e, _e * 4 + bb)
        on("sync", pex_s2)

    def pex_g(g):
        sE4.wait(g, 7)
        nsm["cc4"] = inc(g.collective_compute(
            "AllGather", OP.bypass, replica_groups=RG8, ins=[cin4[:]], outs=[cout4[:]]), scc)
    on("gpsimd", pex_g)

    barrier()
    fre("bdbc", "eob", "upt", "gat", "wdt", "wut", "wgt", "sown")

    res2 = mksb(top, "res2", 128, C.KC * C.LH, BF16)
    eoutb = mksb(top, "eoutb", 128, 8 * C.DO, BF16)
    xr1b = mksb(top, "xr1b", 128, C.KC * C.LH, BF16)

    s12x = Stream(1)

    def p12_s0(s):
        for kc in range(C.KC):
            e = s.dma_start(out=xr1b[:, kc * C.LH:(kc + 1) * C.LH],
                            in_=xr1_h[kc * 128:(kc + 1) * 128, :])
            s12x.inc(e, 0)
    on("sync", p12_s0)

    def pcm_s(s):
        s.wait_ge(scc, nsm["cc4"])
        breg = REG["breg"]
        c4r = cout4.rearrange("(g b p) o -> p g b o", g=8, b=4)
        eoutb_r = eoutb[:].rearrange("p (g o) -> p g o", g=8)
        e = s.dma_start(out=eoutb_r, in_=c4r[:, :, DS(breg, 1), :].squeeze(2))
        inc(e, sh, 16)
        nsm["pcml"] = cnt[id(sh)]
    on("sync", pcm_s)

    PCMSV0 = cnt[id(sv)]

    def pcm_t(t):
        t.wait_ge(sh, nsm["pcml"])
        i = 0
        for ot in range(C.KC):
            for w in range(C.NWH):
                pst = psA if i % 2 == 0 else psB
                if i >= 2:
                    t.wait_ge(sv, PCMSV0 + i - 1)
                for sdr in range(8):
                    mm = t.matmul(pst[:, 0:C.WH],
                                  eoutb[:, sdr * C.DO + ot * 128:sdr * C.DO + (ot + 1) * 128],
                                  combT[:, sdr * C.LH + w * C.WH:sdr * C.LH + (w + 1) * C.WH],
                                  start=(sdr == 0), stop=(sdr == 7))
                nsm[("pcmm", i)] = inc(mm, st)
                i += 1
    on("tensor", pcm_t)

    def pcm_v(v):
        s12x.wait(v, 0)
        i = 0
        for ot in range(C.KC):
            for w in range(C.NWH):
                pst = psA if i % 2 == 0 else psB
                v.wait_ge(st, nsm[("pcmm", i)])
                e = v.scalar_tensor_tensor(
                    res2[:, ot * C.LH + w * C.WH:ot * C.LH + (w + 1) * C.WH],
                    pst[:, 0:C.WH], 1.0,
                    xr1b[:, ot * C.LH + w * C.WH:ot * C.LH + (w + 1) * C.WH],
                    OP.mult, OP.add)
                nsm[("pcme", i)] = inc(e, sv)
                i += 1
    on("vector", pcm_v)
    dbg_dump("eoutb", lambda: eoutb[:], sh, lambda: nsm["pcml"])
    dbg_dump("res2", lambda: res2[:], sv, lambda: nsm[("pcme", C.KC * C.NWH - 1)])

    barrier()
    fre("xr1b", "eoutb")

    r2T = mksb(top, "r2T", 128, 2 * C.DM, BF16)
    osb = mksb(top, "osb", 128, 2 * C.DM, F32)
    sqf = mksb(top, "sqf", 128, C.DM, F32)
    q1d = qc["q1"]
    q2d = qc["q2"]
    sOo = Stream(2)

    for lt in range(C.LT):
        def pf_t(t, _lt=lt):
            for g4 in range(0, C.KC, 4):
                w = min(4, C.KC - g4)
                gi = _lt * ((C.KC + 3) // 4) + g4 // 4
                if gi >= 2:
                    t.wait_ge(Q["q1"], q1d + gi - 1)
                pT = psTa if gi % 2 == 0 else psTb
                for q4 in range(w):
                    kc = g4 + q4
                    tr = t.transpose(pT[:, q4 * 128:(q4 + 1) * 128],
                                     res2[:, kc * C.LH + _lt * 128:kc * C.LH + (_lt + 1) * 128],
                                     eyeb[:])
                nsm[("pft", _lt, g4)] = inc(tr, st)
        on("tensor", pf_t)

        def pf_v(v, _lt=lt):
            so = (_lt % 2) * C.DM
            if _lt >= 2:
                sOo.wait(v, _lt - 2)
            for g4 in range(0, C.KC, 4):
                w = min(4, C.KC - g4)
                gi = _lt * ((C.KC + 3) // 4) + g4 // 4
                v.wait_ge(st, nsm[("pft", _lt, g4)])
                pT = psTa if gi % 2 == 0 else psTb
                e = v.tensor_copy(r2T[:, so + g4 * 128:so + (g4 + w) * 128], pT[:, 0:w * 128])
                qinc(e, "q1")
            ng(v, sem=Q["q1"], val=qc["q1"])
            e = v.reduce_sum(tiny[:, 9:10], r2T[:, so:so + C.DM], axis=AX.X)
            nsm[("pfc", _lt)] = inc(e, sv)
            v.wait_ge(sv, nsm[("pfc", _lt)])
        on("vector", pf_v)

        def pf_sc(sc, _lt=lt):
            so = (_lt % 2) * C.DM
            sc.wait_ge(sv, nsm[("pfc", _lt)])
            e = sc.activation(sqf[:], r2T[:, so:so + C.DM], AF.Square,
                              accum_out=tiny[:, 8:9])
            nsm[("pfsq", _lt)] = inc(e, ss)
        on("scalar", pf_sc)

        def pf_v2(v, _lt=lt):
            v.wait_ge(ss, nsm[("pfsq", _lt)])
            v.tensor_scalar_mul(tiny[:, 9:10], tiny[:, 9:10], 1.0 / C.DM)
            t = inc(v.tensor_scalar_mul(tiny[:, 8:9], tiny[:, 8:9], 1.0 / C.DM), sv)
            v.wait_ge(sv, t)
            t = inc(v.tensor_mul(tiny[:, 10:11], tiny[:, 9:10], tiny[:, 9:10]), sv)
            v.wait_ge(sv, t)
            e = v.tensor_sub(tiny[:, 10:11], tiny[:, 8:9], tiny[:, 10:11])
            nsm[("pfv", _lt)] = inc(e, sv)
        on("vector", pf_v2)

        def pf_sc2(sc, _lt=lt):
            sc.wait_ge(sv, nsm[("pfv", _lt)])
            e = sc.activation(tiny[:, 10:11], tiny[:, 10:11], AF.Sqrt, bias=csts[:, 0:1])
            nsm[("pfsd", _lt)] = inc(e, ss)
        on("scalar", pf_sc2)

        def pf_v3(v, _lt=lt):
            so = (_lt % 2) * C.DM
            v.wait_ge(ss, nsm[("pfsd", _lt)])
            t = inc(v.reciprocal(tiny[:, 10:11], tiny[:, 10:11]), sv)
            v.wait_ge(sv, t)
            e = v.tensor_scalar(osb[:, so:so + C.DM], r2T[:, so:so + C.DM],
                                tiny[:, 9:10], tiny[:, 10:11], OP.subtract, OP.mult)
            nsm[("pfe", _lt)] = inc(e, sv)
        on("vector", pf_v3)

        def pf_s(s, _lt=lt):
            so = (_lt % 2) * C.DM
            s.wait_ge(sv, nsm[("pfe", _lt)])
            e = s.dma_start(out=o_y[_lt * 128:(_lt + 1) * 128, :], in_=osb[:, so:so + C.DM])
            sOo.inc(e, _lt)
        on("sync", pf_s)

    def fin_s(s):
        sOo.wait_all(s)
        if cnt[id(sh)] > 0:
            s.wait_ge(sh, cnt[id(sh)])
    on("sync", fin_s)

    for h in reversed(list(live.values())):
        h.__exit__(None, None, None)
    live.clear()
    top.close()
    global LAST_NSM
    LAST_NSM = dict(nsm)
    return nc


def _prep_core(C, c, I):
    b, j = c % 4, c // 4
    js, je = j * C.DH, (j + 1) * C.DH
    m = {}
    m["i_x"] = I["L0_attn_x"][b]
    m["i_xh"] = I["L0_attn_x"][b][j * C.LH:(j + 1) * C.LH]
    m["i_inw"] = np.concatenate([I["L0_attn_in_w"][js:je],
                                 I["L0_attn_in_w"][C.DI + js:C.DI + je]], 0)
    m["i_convw"] = I["L0_attn_conv_w"][js:je, 0, :].reshape(
        C.DT, 128, 4).transpose(1, 0, 2).reshape(128, C.DT * 4)
    m["i_convb"] = I["L0_attn_conv_b"][js:je].reshape(C.DT, 128).T
    m["i_xprojT"] = I["L0_attn_xproj_w"][:, js:je].T
    m["i_dtwT"] = I["L0_attn_dt_w"][js:je, :].T
    m["i_dtb"] = I["L0_attn_dt_b"][js:je].reshape(C.DT, 128).T
    m["i_alog"] = I["L0_attn_A_log"][js:je].reshape(
        C.DT, 128, C.NST).transpose(1, 0, 2).reshape(128, C.DT * C.NST)
    m["i_dv"] = I["L0_attn_D"][js:je].reshape(C.DT, 128).T
    m["i_outwT"] = I["L0_attn_out_w"][:, js:je].T
    for nm, k in (("i_normg", "L0_ffn_norm_g"), ("i_normb", "L0_ffn_norm_b"),
                  ("i_slotg", "L0_ffn_slot_g"), ("i_slotb", "L0_ffn_slot_b")):
        m[nm] = I[k].reshape(C.KC, 128).T
    m["i_scale"] = I["L0_ffn_scale"].reshape(1, 1)
    m["i_seT"] = I["L0_ffn_slot_embeds"].reshape(C.ES, C.DM).T
    m["i_wg"] = I["L0_ffn_Wg"][2 * c:2 * c + 2]
    m["i_wu"] = I["L0_ffn_Wu"][2 * c:2 * c + 2]
    m["i_wd"] = I["L0_ffn_Wd"][2 * c:2 * c + 2]
    m["i_bgT"] = I["L0_ffn_bg"][2 * c:2 * c + 2].reshape(
        2, C.FT, 128).transpose(2, 0, 1).reshape(128, 2 * C.FT)
    m["i_buT"] = I["L0_ffn_bu"][2 * c:2 * c + 2].reshape(
        2, C.FT, 128).transpose(2, 0, 1).reshape(128, 2 * C.FT)
    m["i_bd"] = I["L0_ffn_bd"][2 * c:2 * c + 2]
    m["i_eyeb"] = np.eye(128, dtype=BF)
    m["i_eyef"] = np.eye(128, dtype=np.float32)
    out = {}
    for k, v in m.items():
        v = np.asarray(v)
        if v.dtype == np.float64:
            v = v.astype(np.float32)
        out[k] = np.ascontiguousarray(v)
    return out


def kernel(**inputs):
    C = CFG_FULL
    I = {k: np.asarray(v) for k, v in inputs.items()}
    nc = build(C)
    in_maps = [_prep_core(C, c, I) for c in range(8)]
    res = run_bass_kernel_spmd(nc, in_maps, core_ids=list(range(8)))
    out = np.zeros((4, C.L, C.DM), np.float32)
    for c in range(8):
        b, j = c % 4, c // 4
        out[b, j * C.LH:(j + 1) * C.LH] = res.results[c]["o_y"]
    return out


# version 04
# speedup vs baseline: 1.0690x; 1.0144x over previous
import contextlib
import numpy as np
import ml_dtypes
import concourse.bass as bass
import concourse.mybir as mybir
from concourse.bass_utils import run_bass_kernel_spmd

LAST_NSM = {}

F32 = mybir.dt.float32
BF16 = mybir.dt.bfloat16
AF = mybir.ActivationFunctionType
OP = mybir.AluOpType
AX = mybir.AxisListType
BF = ml_dtypes.bfloat16
EPS = 1e-5
DS = bass.ds
TS = bass.ts


class CFG:
    def __init__(self, DM, L, LC, F):
        self.DM = DM
        self.DI = 2 * DM
        self.DH = DM
        self.L = L
        self.LH = L // 2
        self.LC = LC
        self.NLC = L // LC
        self.NST = 16
        self.R = 64
        self.RK2 = self.R + 2 * self.NST
        self.E = 16
        self.S = 64
        self.ES = self.E * self.S
        self.F = F
        self.DO = DM
        self.KC = DM // 128
        self.DT = self.DH // 128
        self.ET = self.ES // 128
        self.FT = F // 128
        self.LT = self.LH // 128
        self.W = min(512, L)
        self.NW = L // self.W
        self.WH = min(512, self.LH)
        self.NWH = self.LH // self.WH
        self.NLB = 2 if self.LH >= 256 else 1
        self.LB = self.LH // self.NLB
        self.LTB = self.LB // 128
        self.WB = min(512, self.LB)
        self.NWB = self.LB // self.WB
        assert DM % 128 == 0 and L % (2 * LC) == 0 and LC % 128 == 0
        assert self.LH % 128 == 0 and F % 128 == 0 and self.ET == 8


CFG_FULL = CFG(DM=1024, L=4096, LC=1024, F=2048)


def build(C: CFG, dbg=False, races=False):
    nc = bass.Bass("TRN2", target_bir_lowering=False, detect_race_conditions=races)
    dtn = nc.dram_tensor

    def ein(name, shape):
        return dtn(name, shape, F32, kind="ExternalInput").ap()

    i_x = ein("i_x", [C.L, C.DM])
    i_xh = ein("i_xh", [C.LH, C.DM])
    i_inw = ein("i_inw", [2 * C.DH, C.DM])
    i_convw = ein("i_convw", [128, C.DT * 4])
    i_convb = ein("i_convb", [128, C.DT])
    i_xprojT = ein("i_xprojT", [C.DH, C.RK2])
    i_dtwT = ein("i_dtwT", [C.R, C.DH])
    i_dtb = ein("i_dtb", [128, C.DT])
    i_alog = ein("i_alog", [128, C.DT * C.NST])
    i_dv = ein("i_dv", [128, C.DT])
    i_outwT = ein("i_outwT", [C.DH, C.DO])
    i_normg = ein("i_normg", [128, C.KC])
    i_normb = ein("i_normb", [128, C.KC])
    i_slotg = ein("i_slotg", [128, C.KC])
    i_slotb = ein("i_slotb", [128, C.KC])
    i_scale = ein("i_scale", [1, 1])
    i_seT = ein("i_seT", [C.DM, C.ES])
    i_wg = ein("i_wg", [2, C.DM, C.F])
    i_wu = ein("i_wu", [2, C.DM, C.F])
    i_wd = ein("i_wd", [2, C.F, C.DO])
    i_bgT = ein("i_bgT", [128, 2 * C.FT])
    i_buT = ein("i_buT", [128, 2 * C.FT])
    i_bd = ein("i_bd", [2, C.DO])
    i_eyeb = dtn("i_eyeb", [128, 128], BF16, kind="ExternalInput").ap()
    i_eyef = dtn("i_eyef", [128, 128], F32, kind="ExternalInput").ap()
    o_y = dtn("o_y", [C.LH, C.DM], F32, kind="ExternalOutput").ap()

    xb = dtn("xb", [C.L, C.DM], BF16).ap()
    xhb = dtn("xhb", [C.LH, C.DM], BF16).ap()
    inwb = dtn("inwb", [2 * C.DH, C.DM], BF16).ap()
    xprojTb = dtn("xprojTb", [C.DH, C.RK2], BF16).ap()
    dtwTb = dtn("dtwTb", [C.R, C.DH], BF16).ap()
    outwTb = dtn("outwTb", [C.DH, C.DO], BF16).ap()
    seTb = dtn("seTb", [C.DM, C.ES], BF16).ap()
    wgb = dtn("wgb", [2, C.DM, C.F], BF16).ap()
    wub = dtn("wub", [2, C.DM, C.F], BF16).ap()
    wdb = dtn("wdb", [2, C.F, C.DO], BF16).ap()
    dk = dict(kind="ExternalOutput") if dbg else {}
    xx_h = dtn("xx_h", [C.DH, C.L], BF16, **dk).ap()
    xr1_h = dtn("xr1_h", [C.DM, C.LH], BF16, **dk).ap()
    delta_h = dtn("delta_h", [C.DH, C.L], BF16, **dk).ap()
    u_h = dtn("u_h", [C.DH, C.L], BF16, **dk).ap()
    sres_h = dtn("sres_h", [C.DH, C.L], BF16, **dk).ap()
    y3_h = dtn("y3_h", [C.DH, C.L], BF16, **dk).ap()
    brow_h = dtn("brow_h", [2 * C.NST, C.L], BF16).ap()
    murow_h = dtn("murow_h", [2, max(C.LH, C.ES)], F32).ap()
    b2_h = dtn("b2_h", [C.LT, 128], F32).ap()
    cin0 = dtn("cin0", [C.RK2, C.L], F32).ap()
    cout0 = dtn("cout0", [C.RK2, C.L], F32).ap()
    cin1 = dtn("cin1", [C.DO, C.L], BF16).ap()
    cout1 = dtn("cout1", [C.DO, C.L], BF16).ap()
    cin2 = dtn("cin2", [128, 4 * C.ET], F32).ap()
    cout2 = dtn("cout2", [8 * 128, 4 * C.ET], F32).ap()
    cin3 = dtn("cin3", [C.DM, C.ES], BF16).ap()
    cout3 = dtn("cout3", [8 * C.DM, C.ES], BF16).ap()
    cin4 = dtn("cin4", [512, C.DO], BF16).ap()
    cout4 = dtn("cout4", [8 * 512, C.DO], BF16).ap()
    RG8 = [[0, 1, 2, 3, 4, 5, 6, 7]]
    RGP = [[0, 4], [1, 5], [2, 6], [3, 7]]

    top = contextlib.ExitStack()
    live = {}

    def mksb(st_, name, p, f, dty, side=None):
        h = nc.sbuf_tensor(name, [p, f], dty, side=side, unsafe_allow_realloc=True)
        t = h.__enter__()
        live[name] = h
        return t

    def fre(*names):
        for n in names:
            live.pop(n).__exit__(None, None, None)

    def mkps(st_, name, p, f, dty=F32):
        return st_.enter_context(nc.psum_tensor(name, [p, f], dty))

    def mksem(st_, name):
        return st_.enter_context(nc.semaphore(name))

    convw = mksb(top, "convw", 128, C.DT * 4, F32)
    convb = mksb(top, "convb", 128, C.DT, F32)
    dtb = mksb(top, "dtb", 128, C.DT, F32)
    alog = mksb(top, "alog", 128, C.DT * C.NST, F32)
    Amat = mksb(top, "Amat", 128, C.DT * C.NST, F32)
    dvs = mksb(top, "dvs", 128, C.DT, F32)
    normg = mksb(top, "normg", 128, C.KC, F32)
    normb = mksb(top, "normb", 128, C.KC, F32)
    sgsc = mksb(top, "sgsc", 128, C.KC, F32)
    sbsc = mksb(top, "sbsc", 128, C.KC, F32)
    bgT = mksb(top, "bgT", 128, 2 * C.FT, F32)
    buT = mksb(top, "buT", 128, 2 * C.FT, F32)
    eyeb = mksb(top, "eyeb", 128, 128, BF16)
    eyef = mksb(top, "eyef", 128, 128, F32)
    ones1 = mksb(top, "ones1", 128, 1, BF16)
    scale_bc = mksb(top, "scale_bc", 128, 1, F32)
    tiny = mksb(top, "tiny", 128, 64, F32)
    csts = mksb(top, "csts", 128, 2, F32)
    mrow = mksb(top, "mrow", 128, max(C.LH, C.ES), BF16)
    rrow = mksb(top, "rrow", 128, max(C.LH, C.ES), BF16)
    row1 = mksb(top, "row1", 1, 2 * max(C.LH, C.ES), F32)

    psA = mkps(top, "psA", 128, 512)
    psB = mkps(top, "psB", 128, 512)
    psSa = mkps(top, "psSa", 128, 512)
    psSb = mkps(top, "psSb", 128, 512)
    psTa = mkps(top, "psTa", 128, 512, BF16)
    psTb = mkps(top, "psTb", 128, 512, BF16)
    psTf = mkps(top, "psTf", 128, 512)

    sg = mksem(top, "sg")
    sh = mksem(top, "sh")
    st = mksem(top, "st")
    sv = mksem(top, "sv")
    ss = mksem(top, "ss")
    scc = mksem(top, "scc")
    ph = mksem(top, "ph")
    pipe_names = ["q1", "q2", "q3", "q4", "q5", "q6", "q7", "q8",
                  "dg0", "dg1", "dg2", "dg3", "dg4", "dg5", "dg6"]
    Q = {n: mksem(top, n) for n in pipe_names}
    qc = {n: 0 for n in pipe_names}

    cnt = {id(sg): 0, id(sh): 0, id(st): 0, id(sv): 0, id(ss): 0, id(scc): 0}
    nsm = {}
    nbar = [0]

    NSD = 8
    SD = [mksem(top, f"sd{i}") for i in range(NSD)]
    sdc = [0] * NSD
    sdbar = [0] * NSD
    qbar = {n: 0 for n in pipe_names}
    sdp = [0]
    srace = {n: mksem(top, f"sr_{n}") for n in
             ("Vector", "Scalar", "Tensor", "Sync", "GpSimd", "Pool")}
    cnt_sr = {n: 0 for n in srace}

    class Stream:
        def __init__(self, npar):
            self.sems = [sdp[0] + p for p in range(npar)]
            sdp[0] += npar
            assert sdp[0] <= NSD, "stream pool exhausted in phase window"
            self.npar = npar
            self.val = {}
            self._preset = False

        def preset(self, nsteps, per=1):
            shadow = {s: sdc[s] for s in self.sems}
            for k in range(nsteps):
                si = self.sems[k % self.npar]
                shadow[si] += 16 * per
                self.val[k] = shadow[si]
            self._preset = True

        def inc(self, instr, k):
            si = self.sems[k % self.npar]
            sdc[si] += 16
            if self._preset:
                assert sdc[si] <= self.val[k], "stream preset mismatch"
            else:
                self.val[k] = sdc[si]
            instr.then_inc(SD[si], 16)

        def wait(self, eng, k):
            eng.wait_ge(SD[self.sems[k % self.npar]], self.val[k])

        def wait_all(self, eng):
            for s in self.sems:
                if sdc[s] > sdbar[s]:
                    eng.wait_ge(SD[s], sdc[s])

    blk = top.enter_context(nc.Block())

    def inc(instr, s, n=1):
        instr.then_inc(s, n)
        cnt[id(s)] += n
        return cnt[id(s)]

    def qinc(instr, name):
        instr.then_inc(Q[name], 1)
        qc[name] += 1
        return qc[name]

    def qinc16(instr, name):
        instr.then_inc(Q[name], 16)
        qc[name] += 16
        return qc[name]

    def on(which, fn):
        getattr(blk, which)(fn)

    def ng(eng, instr=None, sem=None, val=None):
        if not races:
            return
        if val is None:
            en = next((n for n in srace if n in type(eng).__name__), "Sync")
            cnt_sr[en] += 1
            instr.then_inc(srace[en], 1)
            sem, val = srace[en], cnt_sr[en]
        eng.wait_ge(sem, val)

    def dbg_dump(name, ap_fn, sem_, val_fn):
        if not dbg:
            return
        p, f = ap_fn().partition_size(), ap_fn().free_size()
        od = dtn("dbg_" + name, [p, f], ap_fn().dtype, kind="ExternalOutput").ap()

        def _d(s):
            s.wait_ge(sem_, val_fn())
            inc(s.dma_start(out=od[:], in_=ap_fn()), sh, 16)
        on("sync", _d)

    def barrier():
        nbar[0] += 1
        t_ = 5 * nbar[0]
        shv, sgv = cnt[id(sh)], cnt[id(sg)]
        sdv = list(sdc)
        qv = dict(qc)
        for wn in ("tensor", "vector", "scalar", "sync", "gpsimd"):
            def f(eng, _t=t_, _wn=wn):
                if _wn == "sync":
                    eng.wait_ge(sh, shv)
                    eng.wait_ge(scc, cnt[id(scc)])
                    for i_ in range(NSD):
                        if sdv[i_] > sdbar[i_]:
                            eng.wait_ge(SD[i_], sdv[i_])
                    for qn_ in pipe_names:
                        if qv[qn_] > qbar[qn_]:
                            eng.wait_ge(Q[qn_], qv[qn_])
                if _wn == "gpsimd":
                    eng.wait_ge(sg, sgv)
                eng.drain()
                eng.sem_inc(ph, 1)
                eng.wait_ge(ph, _t)
            on(wn, f)
        sdbar[:] = sdv
        qbar.update(qv)
        sdp[0] = 0

    ph0 = contextlib.ExitStack()
    xprojT = mksb(top, "xprojT", 128, C.DT * C.RK2, BF16)
    dtwT = mksb(top, "dtwT", C.R, C.DH, BF16)
    xT = mksb(top, "xT", 128, C.KC * C.L, BF16)
    inwT = mksb(top, "inwT", 128, C.KC * 2 * C.DH, BF16)

    def p0_g(g):
        gq = C.DM // 4
        for qn in range(4):
            qinc16(g.dma_start(out=xb[:, qn * gq:(qn + 1) * gq],
                               in_=i_x[:, qn * gq:(qn + 1) * gq]), "dg0")
        qinc16(g.dma_start(out=inwb[:], in_=i_inw[:]), "dg0")
        qinc16(g.dma_start(out=xprojTb[:], in_=i_xprojT[:]), "q5")
        qinc16(g.dma_start(out=dtwTb[:], in_=i_dtwT[:]), "q5")
        qinc16(g.dma_start(out=xhb[:], in_=i_xh[:]), "q6")
        for ds_, sr in ((outwTb, i_outwT), (seTb, i_seT),
                        (wgb, i_wg), (wub, i_wu), (wdb, i_wd)):
            qinc16(g.dma_start(out=ds_[:], in_=sr[:]), "dg1")
    on("gpsimd", p0_g)

    def p0_s(s):
        for sr, ds_ in ((i_convw, convw), (i_convb, convb), (i_dtb, dtb),
                        (i_alog, alog), (i_dv, dvs), (i_normg, normg),
                        (i_normb, normb), (i_bgT, bgT), (i_buT, buT),
                        (i_eyeb, eyeb), (i_eyef, eyef)):
            inc(s.dma_start(out=ds_[:], in_=sr[:]), sh, 16)
        inc(s.dma_start(out=scale_bc[:], in_=i_scale[:].broadcast_to((128, 1))), sh, 16)
        inc(s.dma_start(out=tiny[:, 0:C.KC], in_=i_slotg[:]), sh, 16)
        inc(s.dma_start(out=tiny[:, C.KC:2 * C.KC], in_=i_slotb[:]), sh, 16)
        s.wait_ge(Q["dg0"], 80)
        for kc in range(C.KC):
            inc(s.dma_start_transpose(out=xT[:, kc * C.L:(kc + 1) * C.L],
                                      in_=xb[:, kc * 128:(kc + 1) * 128]), sh, 16)
        for kc in range(C.KC):
            inc(s.dma_start_transpose(out=inwT[:, kc * 2 * C.DH:(kc + 1) * 2 * C.DH],
                                      in_=inwb[:, kc * 128:(kc + 1) * 128]), sh, 16)
    on("sync", p0_s)
    P0H = cnt[id(sh)]

    REG = {}

    def p0_s2(s):
        pid = s.partition_id()
        REG["pid"] = pid
        REG["pxor"] = s.compute_val((pid + 4) % 8)
        REG["jj"] = s.compute_val(pid // 4)
        REG["breg"] = s.compute_val(pid % 4)
        REG["jcol"] = s.compute_val((pid // 4) * C.LH)
    on("sync", p0_s2)

    def p0_sc(sc):
        sc.wait_ge(sh, P0H)
        inc(sc.activation(Amat[:], alog[:], AF.Exp), ss)
    on("scalar", p0_sc)
    p0ss = cnt[id(ss)]

    def p0_v(v):
        v.memset(ones1[:], 1.0)
        v.memset(csts[:, 0:1], EPS)
        v.memset(csts[:, 1:2], -1.0)
        v.wait_ge(sh, P0H)
        v.tensor_scalar_mul(sgsc[:], tiny[:, 0:C.KC], scale_bc[:, 0:1])
        v.tensor_scalar_mul(sbsc[:], tiny[:, C.KC:2 * C.KC], scale_bc[:, 0:1])
        v.wait_ge(ss, p0ss)
        inc(v.tensor_scalar_mul(Amat[:], Amat[:], -1.0), sv)
    on("vector", p0_v)

    barrier()

    ev1 = mksb(top, "ev1", 128, 4 * 512, BF16)
    P1SS0 = cnt[id(ss)]
    sp1 = Stream(4)
    sp1.preset(2 * C.DT * C.NW, 1)

    def p1_t(t):
        t.wait_ge(sh, P0H)
        i = 0
        for ot in range(2 * C.DT):
            for q in range(C.NW):
                pst = psA if i % 2 == 0 else psB
                if i >= 2:
                    t.wait_ge(ss, P1SS0 + i - 1)
                for kc in range(C.KC):
                    mm = t.matmul(pst[:, 0:C.W],
                                  inwT[:, kc * 2 * C.DH + ot * 128:kc * 2 * C.DH + (ot + 1) * 128],
                                  xT[:, kc * C.L + q * C.W:kc * C.L + (q + 1) * C.W],
                                  start=(kc == 0), stop=(kc == C.KC - 1))
                nsm[("p1m", i)] = inc(mm, st)
                i += 1
    on("tensor", p1_t)

    def p1_sc(sc):
        i = 0
        for ot in range(2 * C.DT):
            for q in range(C.NW):
                pst = psA if i % 2 == 0 else psB
                sc.wait_ge(st, nsm[("p1m", i)])
                if i >= 4:
                    sp1.wait(sc, i - 4)
                so = (i % 4) * 512
                if ot < C.DT:
                    e = sc.copy(ev1[:, so:so + C.W], pst[:, 0:C.W])
                else:
                    e = sc.activation(ev1[:, so:so + C.W], pst[:, 0:C.W], AF.Silu)
                nsm[("p1e", i)] = inc(e, ss)
                i += 1
    on("scalar", p1_sc)

    def p1_s(s):
        i = 0
        for ot in range(2 * C.DT):
            for q in range(C.NW):
                s.wait_ge(ss, nsm[("p1e", i)])
                so = (i % 4) * 512
                dst = xx_h if ot < C.DT else sres_h
                rt = ot if ot < C.DT else ot - C.DT
                e = s.dma_start(out=dst[rt * 128:(rt + 1) * 128, q * C.W:(q + 1) * C.W],
                                in_=ev1[:, so:so + C.W])
                sp1.inc(e, i)
                i += 1
    on("sync", p1_s)

    def p0_s_late(s):
        s.wait_ge(Q["q5"], 32)
        for dti in range(C.DT):
            inc(s.dma_start(out=xprojT[:, dti * C.RK2:(dti + 1) * C.RK2],
                            in_=xprojTb[dti * 128:(dti + 1) * 128, :]), sh, 16)
        inc(s.dma_start(out=dtwT[:], in_=dtwTb[:]), sh, 16)
    on("sync", p0_s_late)

    barrier()
    fre("ev1", "inwT", "xT")

    uu = mksb(top, "uu", 128, 2 * C.L, BF16)
    xxs = mksb(top, "xxs", 128, 2 * (4 + C.L), BF16)

    def p2_v0(v):
        v.memset(xxs[:, 0:4], 0.0)
        inc(v.memset(xxs[:, 4 + C.L:8 + C.L], 0.0), sv)
    on("vector", p2_v0)
    p2pad = cnt[id(sv)]
    P2VB = cnt[id(sv)]

    sx2 = Stream(2)
    su2 = Stream(2)
    sx2.preset(C.DT, 1)
    su2.preset(C.DT, 1)

    def p2_v(v):
        for dti in range(C.DT):
            base = (dti % 2) * (4 + C.L) + 4
            so = (dti % 2) * C.L
            tmp = uu[:, so:so + C.L]
            sx2.wait(v, dti)
            if dti >= 2:
                su2.wait(v, dti - 2)
            e = v.tensor_scalar_mul(tmp, xxs[:, base - 2:base - 2 + C.L],
                                    convw[:, dti * 4 + 1:dti * 4 + 2])
            ng(v, e)
            e = v.scalar_tensor_tensor(tmp, xxs[:, base - 3:base - 3 + C.L],
                                       convw[:, dti * 4 + 0:dti * 4 + 1], tmp, OP.mult, OP.add)
            ng(v, e)
            e = v.scalar_tensor_tensor(tmp, xxs[:, base - 1:base - 1 + C.L],
                                       convw[:, dti * 4 + 2:dti * 4 + 3], tmp, OP.mult, OP.add)
            ng(v, e)
            e = v.scalar_tensor_tensor(tmp, xxs[:, base:base + C.L],
                                       convw[:, dti * 4 + 3:dti * 4 + 4], tmp, OP.mult, OP.add)
            nsm[("p2c", dti)] = inc(e, sv)
    on("vector", p2_v)

    def p2_sc(sc):
        for dti in range(C.DT):
            so = (dti % 2) * C.L
            tmp = uu[:, so:so + C.L]
            sc.wait_ge(sv, nsm[("p2c", dti)])
            nsm[("p2u", dti)] = inc(
                sc.activation(tmp, tmp, AF.Silu, bias=convb[:, dti:dti + 1]), ss)
    on("scalar", p2_sc)

    def p2_sync(s):
        s.wait_ge(sv, p2pad)
        for k in range(C.DT + 2):
            if k < C.DT:
                if k >= 2:
                    s.wait_ge(sv, P2VB + k - 1)
                e = s.dma_start(out=xxs[:, (k % 2) * (4 + C.L) + 4:
                                        (k % 2) * (4 + C.L) + 4 + C.L],
                                in_=xx_h[k * 128:(k + 1) * 128, :])
                sx2.inc(e, k)
            if k >= 2:
                s.wait_ge(ss, nsm[("p2u", k - 2)])
                e = s.dma_start(out=u_h[(k - 2) * 128:(k - 1) * 128, :],
                                in_=uu[:, ((k - 2) % 2) * C.L:((k - 2) % 2) * C.L + C.L])
                su2.inc(e, k - 2)
    on("sync", p2_sync)

    barrier()
    fre("xxs", "uu")

    xdbl = mksb(top, "xdbl", C.RK2, C.L, F32)
    xdbl2 = mksb(top, "xdbl2", C.RK2, C.L, F32)
    ut3 = mksb(top, "ut3", 128, 2 * C.DT * C.W, BF16)
    P3SV0 = cnt[id(sv)]
    s3l = Stream(2)

    def p3_s(s):
        for q in range(C.NW):
            so = (q % 2) * C.DT * C.W
            if q >= 2:
                s.wait_ge(sv, P3SV0 + q - 1)
            for dti in range(C.DT):
                e = s.dma_start(out=ut3[:, so + dti * C.W:so + (dti + 1) * C.W],
                                in_=u_h[dti * 128:(dti + 1) * 128, q * C.W:(q + 1) * C.W])
                s3l.inc(e, q)
    on("sync", p3_s)

    def p3_t(t):
        for q in range(C.NW):
            so = (q % 2) * C.DT * C.W
            pst = psA if q % 2 == 0 else psB
            s3l.wait(t, q)
            if q >= 2:
                t.wait_ge(sv, P3SV0 + q - 1)
            for dti in range(C.DT):
                mm = t.matmul(pst[0:C.RK2, 0:C.W],
                              xprojT[:, dti * C.RK2:(dti + 1) * C.RK2],
                              ut3[:, so + dti * C.W:so + (dti + 1) * C.W],
                              start=(dti == 0), stop=(dti == C.DT - 1))
            nsm[("p3m", q)] = inc(mm, st)
    on("tensor", p3_t)

    def p3_v(v):
        for q in range(C.NW):
            pst = psA if q % 2 == 0 else psB
            v.wait_ge(st, nsm[("p3m", q)])
            e = v.tensor_copy(xdbl[:, q * C.W:(q + 1) * C.W], pst[0:C.RK2, 0:C.W])
            nsm[("p3e", q)] = inc(e, sv)
    on("vector", p3_v)

    def p3_s2(s):
        s.wait_ge(sv, nsm[("p3e", C.NW - 1)])
        nsm["p3stage"] = inc(s.dma_start(out=cin0[:], in_=xdbl[:]), sh, 16)
    on("sync", p3_s2)

    def p3_g(g):
        g.wait_ge(sh, nsm["p3stage"])
        nsm["cc0"] = inc(g.collective_compute(
            "AllReduce", OP.add, replica_groups=RGP, ins=[cin0[:]], outs=[cout0[:]]), scc)
    on("gpsimd", p3_g)

    def p3_s3(s):
        s.wait_ge(scc, nsm["cc0"])
        nsm["p3pl"] = inc(
            s.dma_start(out=xdbl[:], in_=cout0[:]), sh, 16)
    on("sync", p3_s3)

    xdblb = mksb(top, "xdblb", C.R, C.L, BF16)

    def p3_v2(v):
        v.wait_ge(sh, nsm["p3pl"])
        nsm["p3add"] = inc(v.tensor_copy(xdblb[:], xdbl[0:C.R, :]), sv)
    on("vector", p3_v2)

    def p3_g2(g):
        g.wait_ge(sv, nsm["p3add"])
        qinc16(g.dma_start(out=brow_h[:], in_=xdbl[C.R:C.RK2, :]), "dg2")
    on("gpsimd", p3_g2)

    barrier()

    sc1 = mksb(top, "sc1", 128, C.W, F32)
    sc2 = mksb(top, "sc2", 128, C.W, F32)
    sc3 = mksb(top, "sc3", 128, 2 * C.W, BF16)
    P4SV0a = cnt[id(sv)]
    s4d = Stream(2)
    s4d.preset(C.DT * C.NW, 1)

    def p4_t(t):
        i = 0
        for dti in range(C.DT):
            for q in range(C.NW):
                pst = psA if i % 2 == 0 else psB
                if i >= 2:
                    t.wait_ge(sv, P4SV0a + 2 * (i - 1) + 1)
                mm = t.matmul(pst[:, 0:C.W], dtwT[:, dti * 128:(dti + 1) * 128],
                              xdblb[:, q * C.W:(q + 1) * C.W], start=True, stop=True)
                nsm[("p4m", i)] = inc(mm, st)
                i += 1
    on("tensor", p4_t)

    def p4_v_joint(v):
        i = 0
        for dti in range(C.DT):
            for q in range(C.NW):
                pst = psA if i % 2 == 0 else psB
                v.wait_ge(st, nsm[("p4m", i)])
                if i >= 1:
                    ng(v, sem=sv, val=nsm[("p4d", i - 1)])
                e = v.tensor_scalar_add(sc1[:, 0:C.W], pst[:, 0:C.W], dtb[:, dti:dti + 1])
                nsm[("p4z", i)] = inc(e, sv)
                v.wait_ge(ss, P4SS0[0] + i + 1)
                if i >= 2:
                    s4d.wait(v, i - 2)
                so3 = (i % 2) * C.W
                e = v.tensor_scalar_max(sc1[:, 0:C.W], sc1[:, 0:C.W], 0.0)
                ng(v, e)
                e = v.tensor_add(sc3[:, so3:so3 + C.W], sc1[:, 0:C.W], sc2[:, 0:C.W])
                nsm[("p4d", i)] = inc(e, sv)
                i += 1
    P4SS0 = [None]
    P4SV0 = [None]

    def p4_sc2(sc):
        i = 0
        for dti in range(C.DT):
            for q in range(C.NW):
                sc.wait_ge(sv, P4SV0[0] + 2 * i + 1)
                e = sc.activation(sc2[:, 0:C.W], sc1[:, 0:C.W], AF.Abs)
                ng(sc, e, sem=ss)
                e = sc.activation(sc2[:, 0:C.W], sc2[:, 0:C.W], AF.Exp, scale=csts[:, 1:2])
                ng(sc, e, sem=ss)
                inc(sc.activation(sc2[:, 0:C.W], sc2[:, 0:C.W], AF.Ln, bias=1.0), ss)
                i += 1

    P4SV0[0] = cnt[id(sv)]
    P4SS0[0] = cnt[id(ss)]
    on("scalar", p4_sc2)
    on("vector", p4_v_joint)

    def p4_s(s):
        i = 0
        for dti in range(C.DT):
            for q in range(C.NW):
                s.wait_ge(sv, nsm[("p4d", i)])
                so3 = (i % 2) * C.W
                e = s.dma_start(out=delta_h[dti * 128:(dti + 1) * 128, q * C.W:(q + 1) * C.W],
                                in_=sc3[:, so3:so3 + C.W])
                s4d.inc(e, i)
                i += 1
    on("sync", p4_s)

    barrier()
    fre("sc3", "sc2", "sc1", "xdblb", "ut3", "xdbl2", "xdbl", "dtwT", "xprojT")

    d_lq = mksb(top, "d_lq", 128, C.DT * C.LC, BF16)
    du_lq = mksb(top, "du_lq", 128, C.DT * C.LC, BF16)
    y3_lq = mksb(top, "y3_lq", 128, C.DT * C.LC, BF16)
    sr_lq = mksb(top, "sr_lq", 128, C.DT * C.LC, BF16)
    u_lq = mksb(top, "u_lq", 128, C.DT * C.LC, BF16)
    bsl = mksb(top, "bsl", 128, 2 * C.LC, BF16)
    csl = mksb(top, "csl", 128, 2 * C.LC, BF16)
    dasl = mksb(top, "dasl", 128, 2 * C.DT * C.LC, BF16)
    dbsl = mksb(top, "dbsl", 128, C.DT * C.LC, BF16)
    hsl = mksb(top, "hsl", 128, C.DT * C.LC, BF16)
    carry = mksb(top, "carry", 128, C.DT * C.NST, F32)

    def r3(ap):
        return ap.rearrange("p (t l) -> p t l", t=C.DT)

    for lq in range(C.NLC):
        SS0 = [None]
        q7_0 = qc["q7"]
        s6a = Stream(1)
        s6b = Stream(1)
        s6c = Stream(2)

        def p6_load(s, _lq=lq):
            s.wait_ge(Q["dg2"], 16)
            for dti in range(C.DT):
                s6a.inc(s.dma_start(out=d_lq[:, dti * C.LC:(dti + 1) * C.LC],
                                    in_=delta_h[dti * 128:(dti + 1) * 128,
                                                _lq * C.LC:(_lq + 1) * C.LC]), 0)
            for dti in range(C.DT):
                s6a.inc(s.dma_start(out=u_lq[:, dti * C.LC:(dti + 1) * C.LC],
                                    in_=u_h[dti * 128:(dti + 1) * 128,
                                            _lq * C.LC:(_lq + 1) * C.LC]), 0)
            for dti in range(C.DT):
                s6b.inc(s.dma_start(out=sr_lq[:, dti * C.LC:(dti + 1) * C.LC],
                                    in_=sres_h[dti * 128:(dti + 1) * 128,
                                               _lq * C.LC:(_lq + 1) * C.LC]), 0)
            for n in range(C.NST):
                sl = n % 2
                if n >= 2:
                    s.wait_ge(Q["q7"], q7_0 + (n - 1))
                s6c.inc(s.dma_start(out=bsl[:, sl * C.LC:(sl + 1) * C.LC],
                                    in_=brow_h[n:n + 1, _lq * C.LC:(_lq + 1) * C.LC]
                                    .broadcast_to((128, C.LC))), n)
                e = s.dma_start(out=csl[:, sl * C.LC:(sl + 1) * C.LC],
                                in_=brow_h[C.NST + n:C.NST + n + 1, _lq * C.LC:(_lq + 1) * C.LC]
                                .broadcast_to((128, C.LC)))
                s6c.inc(e, n)
        on("sync", p6_load)

        def p6_da(sc, _lq=lq):
            SS0[0] = cnt[id(ss)]
            s6a.wait(sc, 0)
            i = 0
            for n in range(C.NST):
                sl = n % 2
                if n >= 2:
                    sc.wait_ge(Q["q7"], q7_0 + n - 1)
                for dti in range(C.DT):
                    e = sc.activation(dasl[:, (sl * C.DT + dti) * C.LC:
                                           (sl * C.DT + dti + 1) * C.LC],
                                      d_lq[:, dti * C.LC:(dti + 1) * C.LC], AF.Exp,
                                      scale=Amat[:, dti * C.NST + n:dti * C.NST + n + 1])
                    inc(e, ss)
                    i += 1
        on("scalar", p6_da)

        def p6_v(v, _lq=lq):
            s6a.wait(v, 0)
            e = v.tensor_mul(du_lq[:], d_lq[:], u_lq[:])
            ng(v, e)
            for n in range(C.NST):
                s6c.wait(v, n)
                sl2 = (n % 2) * C.LC
                sl = (n % 2) * C.DT
                if races and n >= 1:
                    v.wait_ge(Q["q7"], q7_0 + n)
                bbc = bsl[:, sl2:sl2 + C.LC].unsqueeze(1).broadcast_to(
                    (128, C.DT, C.LC))
                cbc = csl[:, sl2:sl2 + C.LC].unsqueeze(1).broadcast_to(
                    (128, C.DT, C.LC))
                e = v.tensor_tensor(r3(dbsl[:]), r3(du_lq[:]), bbc, OP.mult)
                ng(v, e)
                for dti in range(C.DT):
                    ci = dti * C.NST + n
                    v.wait_ge(ss, SS0[0] + n * C.DT + dti + 1)
                    init = 0.0 if _lq == 0 else carry[:, ci:ci + 1]
                    e = v.tensor_tensor_scan(
                        hsl[:, dti * C.LC:(dti + 1) * C.LC],
                        dasl[:, (sl + dti) * C.LC:(sl + dti + 1) * C.LC],
                        dbsl[:, dti * C.LC:(dti + 1) * C.LC],
                        init, OP.mult, OP.add)
                    ng(v, e)
                e = v.tensor_copy(
                    carry[:].rearrange("p (t m) -> p t m", m=C.NST)[:, :, n:n + 1],
                    r3(hsl[:])[:, :, C.LC - 1:C.LC])
                ng(v, e)
                e = v.tensor_tensor(r3(dbsl[:]), r3(hsl[:]), cbc, OP.mult)
                ng(v, e)
                e = v.tensor_add(y3_lq[:], y3_lq[:], dbsl[:]) if n > 0 else \
                    v.tensor_copy(y3_lq[:], dbsl[:])
                qinc(e, "q7")
            s6b.wait(v, 0)
            if races:
                v.wait_ge(Q["q7"], q7_0 + C.NST)
            for dti in range(C.DT):
                e = v.scalar_tensor_tensor(
                    y3_lq[:, dti * C.LC:(dti + 1) * C.LC],
                    u_lq[:, dti * C.LC:(dti + 1) * C.LC],
                    dvs[:, dti:dti + 1],
                    y3_lq[:, dti * C.LC:(dti + 1) * C.LC], OP.mult, OP.add)
                ng(v, e)
            e = v.tensor_mul(y3_lq[:], y3_lq[:], sr_lq[:])
            nsm[("p6y3", _lq)] = inc(e, sv)
        on("vector", p6_v)

        def p6_spill(s, _lq=lq):
            s.wait_ge(sv, nsm[("p6y3", _lq)])
            for dti in range(C.DT):
                inc(s.dma_start(out=y3_h[dti * 128:(dti + 1) * 128, _lq * C.LC:(_lq + 1) * C.LC],
                                in_=y3_lq[:, dti * C.LC:(dti + 1) * C.LC]), sh, 16)
        on("sync", p6_spill)
        barrier()

    fre("carry", "hsl", "dbsl", "dasl", "csl", "bsl", "u_lq", "sr_lq", "y3_lq",
        "du_lq", "d_lq")

    outwT = mksb(top, "outwT", 128, C.DT * C.DO, BF16)
    y3s = mksb(top, "y3s", 128, 2 * C.DT * C.W, BF16)
    evb = mksb(top, "evb", 128, 2 * 512, BF16)
    P7SS0 = cnt[id(ss)]
    s7w = Stream(1)
    s7y = Stream(2)
    s7s = Stream(2)
    s7y.preset(C.NW, C.DT)
    s7s.preset(C.NW * C.KC, 1)

    def p7_s0(s):
        s.wait_ge(Q["dg1"], 80)
        for dti in range(C.DT):
            s7w.inc(s.dma_start(out=outwT[:, dti * C.DO:(dti + 1) * C.DO],
                                in_=outwTb[dti * 128:(dti + 1) * 128, :]), 0)
        for q in range(min(2, C.NW)):
            so = (q % 2) * C.DT * C.W
            for dti in range(C.DT):
                s7y.inc(s.dma_start(out=y3s[:, so + dti * C.W:so + (dti + 1) * C.W],
                                    in_=y3_h[dti * 128:(dti + 1) * 128,
                                             q * C.W:(q + 1) * C.W]), q)
    on("sync", p7_s0)

    def p7_t(t):
        s7w.wait(t, 0)
        i = 0
        for q in range(C.NW):
            so = (q % 2) * C.DT * C.W
            s7y.wait(t, q)
            for ot in range(C.KC):
                pst = psA if i % 2 == 0 else psB
                if i >= 2:
                    t.wait_ge(ss, P7SS0 + i - 1)
                for dti in range(C.DT):
                    mm = t.matmul(pst[:, 0:C.W],
                                  outwT[:, dti * C.DO + ot * 128:dti * C.DO + (ot + 1) * 128],
                                  y3s[:, so + dti * C.W:so + (dti + 1) * C.W],
                                  start=(dti == 0), stop=(dti == C.DT - 1))
                nsm[("p7m", i)] = inc(mm, st)
                i += 1
    on("tensor", p7_t)

    def p7_sc(sc):
        i = 0
        for q in range(C.NW):
            for ot in range(C.KC):
                pst = psA if i % 2 == 0 else psB
                sc.wait_ge(st, nsm[("p7m", i)])
                if i >= 2:
                    s7s.wait(sc, i - 2)
                so = (i % 2) * 512
                e = sc.copy(evb[:, so:so + C.W], pst[:, 0:C.W])
                nsm[("p7e", i)] = inc(e, ss)
                i += 1
    on("scalar", p7_sc)

    def p7_s1(s):
        i = 0
        for q in range(C.NW):
            for ot in range(C.KC):
                s.wait_ge(ss, nsm[("p7e", i)])
                so = (i % 2) * 512
                e = s.dma_start(out=cin1[ot * 128:(ot + 1) * 128, q * C.W:(q + 1) * C.W],
                                in_=evb[:, so:so + C.W])
                s7s.inc(e, i)
                i += 1
            if q + 2 < C.NW:
                so = (q % 2) * C.DT * C.W
                for dti in range(C.DT):
                    s7y.inc(s.dma_start(
                        out=y3s[:, so + dti * C.W:so + (dti + 1) * C.W],
                        in_=y3_h[dti * 128:(dti + 1) * 128,
                                 (q + 2) * C.W:(q + 3) * C.W]), q + 2)
    on("sync", p7_s1)

    def p7_g(g):
        s7s.wait_all(g)
        nsm["cc1"] = inc(g.collective_compute(
            "AllReduce", OP.add, replica_groups=RGP, ins=[cin1[:]], outs=[cout1[:]]), scc)
    on("gpsimd", p7_g)

    barrier()
    fre("evb", "y3s", "outwT")

    combT = mksb(top, "combT", 128, C.ET * C.LH, BF16, side="right")
    xr1 = mksb(top, "xr1", 128, C.KC * C.LH, BF16, side="right")
    xsq = mksb(top, "xsq", 128, C.KC * max(C.LH, C.ES), BF16)
    pme = mksb(top, "pme", 128, C.KC * C.LH, BF16)
    ppt = mksb(top, "ppt", 128, C.KC * C.LH, BF16)

    s8l = Stream(1)

    def p8_s0(s):
        s.wait_ge(scc, nsm["cc1"])
        jcol = REG["jcol"]
        for kc in range(C.KC):
            s8l.inc(s.dma_start_transpose(out=xr1[:, kc * C.LH:(kc + 1) * C.LH],
                                          in_=xhb[:, kc * 128:(kc + 1) * 128]), 0)
        c1r = cout1.rearrange("(k p) l -> p k l", k=C.KC)
        pme_r = pme[:].rearrange("p (k l) -> p k l", k=C.KC)
        s8l.inc(s.dma_start(out=pme_r, in_=c1r[:, :, DS(jcol, C.LH)]), 0)
    on("sync", p8_s0)

    def p8_v(v):
        s8l.wait(v, 0)
        for kc in range(C.KC):
            e = v.tensor_add(xr1[:, kc * C.LH:(kc + 1) * C.LH],
                             xr1[:, kc * C.LH:(kc + 1) * C.LH],
                             pme[:, kc * C.LH:(kc + 1) * C.LH])
        nsm["p8res"] = inc(e, sv)
    on("vector", p8_v)

    LNI = [0]

    def ln_feat(xap, outap, ncols, gain, bias, pre_sem, pre_val):
        li = LNI[0]
        LNI[0] += 1
        NWS = ncols // min(512, ncols)
        WS = min(512, ncols)
        sln = Stream(1)

        def l_sc(sc):
            sc.wait_ge(pre_sem, pre_val)
            for kc in range(C.KC):
                e = sc.activation(xsq[:, kc * ncols:(kc + 1) * ncols],
                                  xap[:, kc * ncols:(kc + 1) * ncols], AF.Square)
            nsm[("lnsq", li)] = inc(e, ss)
        on("scalar", l_sc)

        q1b = qc["q1"]

        def l_t(t):
            t.wait_ge(ss, nsm[("lnsq", li)])
            for w in range(NWS):
                if w >= 1:
                    t.wait_ge(Q["q1"], q1b + w)
                pa = psSa[0:1, 0:WS]
                pb = psSb[0:1, 0:WS]
                for kc in range(C.KC):
                    t.matmul(pa, ones1[:], xap[:, kc * ncols + w * WS:kc * ncols + (w + 1) * WS],
                             start=(kc == 0), stop=(kc == C.KC - 1))
                for kc in range(C.KC):
                    mm = t.matmul(pb, ones1[:],
                                  xsq[:, kc * ncols + w * WS:kc * ncols + (w + 1) * WS],
                                  start=(kc == 0), stop=(kc == C.KC - 1))
                nsm[("lnst", li, w)] = inc(mm, st)
        on("tensor", l_t)

        def l_v(v):
            for w in range(NWS):
                v.wait_ge(st, nsm[("lnst", li, w)])
                pa = psSa[0:1, 0:WS]
                pb = psSb[0:1, 0:WS]
                e = v.tensor_scalar_mul(row1[0:1, w * WS:(w + 1) * WS], pa, 1.0 / C.DM)
                ng(v, e)
                e = v.tensor_scalar_mul(row1[0:1, ncols + w * WS:ncols + (w + 1) * WS],
                                        pb, 1.0 / C.DM)
                qinc(e, "q1")
                ng(v, sem=Q["q1"], val=qc["q1"])
            t = inc(v.tensor_mul(mrow[0:1, 0:ncols], row1[0:1, 0:ncols],
                                 row1[0:1, 0:ncols]), sv)
            v.wait_ge(sv, t)
            e = v.tensor_sub(row1[0:1, ncols:2 * ncols], row1[0:1, ncols:2 * ncols],
                             mrow[0:1, 0:ncols])
            nsm[("lnvar", li)] = inc(e, sv)
        on("vector", l_v)

        def l_sc2(sc):
            sc.wait_ge(sv, nsm[("lnvar", li)])
            e = sc.activation(row1[0:1, ncols:2 * ncols], row1[0:1, ncols:2 * ncols],
                              AF.Sqrt, bias=csts[0:1, 0:1])
            nsm[("lnsd", li)] = inc(e, ss)
        on("scalar", l_sc2)

        def l_v2(v):
            v.wait_ge(ss, nsm[("lnsd", li)])
            e = v.reciprocal(row1[0:1, ncols:2 * ncols], row1[0:1, ncols:2 * ncols])
            nsm[("lnr", li)] = inc(e, sv)
        on("vector", l_v2)

        def l_s(s):
            s.wait_ge(sv, nsm[("lnr", li)])
            sln.inc(s.dma_start(out=murow_h[0:1, 0:ncols], in_=row1[0:1, 0:ncols]), 0)
            e = s.dma_start(out=murow_h[1:2, 0:ncols], in_=row1[0:1, ncols:2 * ncols])
            sln.inc(e, 0)
        on("sync", l_s)

        dgn = "dg3" if li == 0 else "dg4"

        def l_g(g):
            sln.wait(g, 0)
            qinc16(g.dma_start(out=mrow[:, 0:ncols],
                               in_=murow_h[0:1, 0:ncols].broadcast_to((128, ncols))), dgn)
            qinc16(g.dma_start(out=rrow[:, 0:ncols],
                               in_=murow_h[1:2, 0:ncols].broadcast_to((128, ncols))), dgn)
        on("gpsimd", l_g)

        def l_v3(v):
            v.wait_ge(Q[dgn], 32)
            for kc in range(C.KC):
                e = v.tensor_sub(outap[:, kc * ncols:(kc + 1) * ncols],
                                 xap[:, kc * ncols:(kc + 1) * ncols], mrow[:, 0:ncols])
                ng(v, e)
                e = v.tensor_mul(outap[:, kc * ncols:(kc + 1) * ncols],
                                 outap[:, kc * ncols:(kc + 1) * ncols], rrow[:, 0:ncols])
                if gain is not None:
                    ng(v, e)
                    e = v.tensor_scalar(outap[:, kc * ncols:(kc + 1) * ncols],
                                        outap[:, kc * ncols:(kc + 1) * ncols],
                                        gain[:, kc:kc + 1], bias[:, kc:kc + 1],
                                        OP.mult, OP.add)
            nsm[("lndone", li)] = inc(e, sv)
            return
        on("vector", l_v3)
        return li

    li1 = ln_feat(xr1, xr1, C.LH, None, None, sv, nsm["p8res"])
    barrier()
    fre("ppt", "pme")

    s9p = Stream(1)

    def p9_spill(s):
        s.wait_ge(sv, nsm[("lndone", li1)])
        for kc in range(C.KC):
            e = s.dma_start(out=xr1_h[kc * 128:(kc + 1) * 128, :],
                            in_=xr1[:, kc * C.LH:(kc + 1) * C.LH])
            s9p.inc(e, 0)
    on("sync", p9_spill)

    def p9_gate(v):
        s9p.wait(v, 0)
        inc(v.engine_nop(), sv)
    on("vector", p9_gate)
    li2 = ln_feat(xr1, xr1, C.LH, normg, normb, sv, cnt[id(sv)])
    xln = xr1
    barrier()

    seS = mksb(top, "seS", 128, C.KC * C.ES, BF16, side="right")

    sseL = Stream(1)
    sseM = Stream(1)

    def pse_s(s):
        s.wait_ge(Q["dg1"], 80)
        for kc in range(C.KC):
            sseL.inc(s.dma_start(out=seS[:, kc * C.ES:(kc + 1) * C.ES],
                                 in_=seTb[kc * 128:(kc + 1) * 128, :]), 0)
    on("sync", pse_s)

    sesq = xsq

    def pse_sq(sc):
        sseL.wait(sc, 0)
        for kc in range(C.KC):
            e = sc.activation(sesq[:, kc * C.ES:(kc + 1) * C.ES],
                              seS[:, kc * C.ES:(kc + 1) * C.ES], AF.Square)
        nsm["psesq"] = inc(e, ss)
    on("scalar", pse_sq)

    q1c = qc["q1"]

    def pse_t(t):
        t.wait_ge(ss, nsm["psesq"])
        for w in range(C.ES // 512):
            if w >= 1:
                t.wait_ge(Q["q1"], q1c + w)
            pa = psSa[0:1, 0:512]
            pb = psSb[0:1, 0:512]
            for kc in range(C.KC):
                t.matmul(pa, ones1[:], seS[:, kc * C.ES + w * 512:kc * C.ES + (w + 1) * 512],
                         start=(kc == 0), stop=(kc == C.KC - 1))
            for kc in range(C.KC):
                mm = t.matmul(pb, ones1[:], sesq[:, kc * C.ES + w * 512:kc * C.ES + (w + 1) * 512],
                              start=(kc == 0), stop=(kc == C.KC - 1))
            nsm[("psest", w)] = inc(mm, st)
    on("tensor", pse_t)

    def pse_v(v):
        for w in range(C.ES // 512):
            v.wait_ge(st, nsm[("psest", w)])
            pa = psSa[0:1, 0:512]
            pb = psSb[0:1, 0:512]
            e = v.tensor_scalar_mul(row1[0:1, w * 512:(w + 1) * 512], pa, 1.0 / C.DM)
            ng(v, e)
            e = v.tensor_scalar_mul(row1[0:1, C.ES + w * 512:C.ES + (w + 1) * 512],
                                    pb, 1.0 / C.DM)
            qinc(e, "q1")
            ng(v, sem=Q["q1"], val=qc["q1"])
        e = v.tensor_mul(mrow[0:1, 0:C.ES], row1[0:1, 0:C.ES], row1[0:1, 0:C.ES])
        ng(v, e)
        e = v.tensor_sub(row1[0:1, C.ES:2 * C.ES], row1[0:1, C.ES:2 * C.ES],
                         mrow[0:1, 0:C.ES])
        nsm["psevar"] = inc(e, sv)
    on("vector", pse_v)

    def pse_sc2(sc):
        sc.wait_ge(sv, nsm["psevar"])
        e = sc.activation(row1[0:1, C.ES:2 * C.ES], row1[0:1, C.ES:2 * C.ES],
                          AF.Sqrt, bias=csts[0:1, 0:1])
        nsm["psesd"] = inc(e, ss)
    on("scalar", pse_sc2)

    def pse_v2(v):
        v.wait_ge(ss, nsm["psesd"])
        e = v.reciprocal(row1[0:1, C.ES:2 * C.ES], row1[0:1, C.ES:2 * C.ES])
        nsm["pser"] = inc(e, sv)
    on("vector", pse_v2)

    def pse_s2(s):
        s.wait_ge(sv, nsm["pser"])
        sseM.inc(s.dma_start(out=murow_h[0:1, 0:C.ES], in_=row1[0:1, 0:C.ES]), 0)
        e = s.dma_start(out=murow_h[1:2, 0:C.ES], in_=row1[0:1, C.ES:2 * C.ES])
        sseM.inc(e, 0)
    on("sync", pse_s2)

    def pse_g2(g):
        sseM.wait(g, 0)
        qinc16(g.dma_start(out=mrow[:, 0:C.ES],
                           in_=murow_h[0:1, 0:C.ES].broadcast_to((128, C.ES))), "dg5")
        qinc16(g.dma_start(out=rrow[:, 0:C.ES],
                           in_=murow_h[1:2, 0:C.ES].broadcast_to((128, C.ES))), "dg5")
    on("gpsimd", pse_g2)

    def pse_v3(v):
        v.wait_ge(Q["dg5"], 32)
        for kc in range(C.KC):
            e = v.tensor_sub(seS[:, kc * C.ES:(kc + 1) * C.ES],
                             seS[:, kc * C.ES:(kc + 1) * C.ES], mrow[:, 0:C.ES])
            ng(v, e)
            e = v.tensor_mul(seS[:, kc * C.ES:(kc + 1) * C.ES],
                             seS[:, kc * C.ES:(kc + 1) * C.ES], rrow[:, 0:C.ES])
            ng(v, e)
            e = v.tensor_scalar(seS[:, kc * C.ES:(kc + 1) * C.ES],
                                seS[:, kc * C.ES:(kc + 1) * C.ES],
                                sgsc[:, kc:kc + 1], sbsc[:, kc:kc + 1], OP.mult, OP.add)
        nsm["psedone"] = inc(e, sv)
    on("vector", pse_v3)
    dbg_dump("seS", lambda: seS[:], sv, lambda: nsm["psedone"])

    barrier()


    NLB, LB, LTB, WB, NWB = C.NLB, C.LB, C.LTB, C.WB, C.NWB
    fre("xsq")
    disp = mksb(top, "disp", 128, C.ET * C.LH, BF16)
    lgf = mksb(top, "lgf", 128, C.ET * LB, F32)
    dstat = mksb(top, "dstat", 128, 4 * C.ET, F32)
    pstat = mksb(top, "pstat", 128, 4 * C.ET, F32)
    b2c = mksb(top, "b2c", 128, max(C.LT, 2), F32)
    b2sb = mksb(top, "b2sb", max(C.LT, 1), 128, F32)
    lrow = mksb(top, "lrow", 128, 2 * C.ES, F32)
    brow_f = mksb(top, "brow_f", 128, LB, F32)

    def pfe_v0(v):
        nsm["pfedm"] = inc(v.memset(dstat[:], 0.0), sv)
    on("vector", pfe_v0)

    for lb in range(NLB):
        PLGSS0 = cnt[id(ss)]

        def plg_t(t, _lb=lb, _b=PLGSS0):
            i = 0
            for et in range(C.ET):
                for w in range(NWB):
                    pst = psA if i % 2 == 0 else psB
                    if i >= 2:
                        t.wait_ge(ss, _b + i - 1)
                    for kc in range(C.KC):
                        mm = t.matmul(
                            pst[:, 0:WB],
                            seS[:, kc * C.ES + et * 128:kc * C.ES + (et + 1) * 128],
                            xln[:, kc * C.LH + _lb * LB + w * WB:
                                kc * C.LH + _lb * LB + (w + 1) * WB],
                            start=(kc == 0), stop=(kc == C.KC - 1))
                    nsm[("plgm", _lb, i)] = inc(mm, st)
                    i += 1
        on("tensor", plg_t)

        def plg_sc(sc, _lb=lb):
            i = 0
            for et in range(C.ET):
                for w in range(NWB):
                    pst = psA if i % 2 == 0 else psB
                    sc.wait_ge(st, nsm[("plgm", _lb, i)])
                    e = sc.copy(lgf[:, et * LB + w * WB:et * LB + (w + 1) * WB],
                                pst[:, 0:WB])
                    nsm[("plge", _lb, i)] = inc(e, ss)
                    i += 1
        on("scalar", plg_sc)

        def pds_v(v, _lb=lb):
            v.wait_ge(ss, nsm[("plge", _lb, C.ET * NWB - 1)])
            ng(v, sem=sv, val=nsm["pfedm"])
            for et in range(C.ET):
                e = v.reduce_max(dstat[:, _lb * C.ET + et:_lb * C.ET + et + 1],
                                 lgf[:, et * LB:(et + 1) * LB], axis=AX.X, negate=True)
            nsm[("pdsm", _lb)] = inc(e, sv)
        on("vector", pds_v)

        def pds_sc(sc, _lb=lb):
            sc.wait_ge(sv, nsm[("pdsm", _lb)])
            for et in range(C.ET):
                e = sc.activation(
                    disp[:, et * C.LH + _lb * LB:et * C.LH + (_lb + 1) * LB],
                    lgf[:, et * LB:(et + 1) * LB], AF.Exp,
                    bias=dstat[:, _lb * C.ET + et:_lb * C.ET + et + 1],
                    accum_out=dstat[:, 2 * C.ET + _lb * C.ET + et:
                                    2 * C.ET + _lb * C.ET + et + 1])
            nsm[("pdse", _lb)] = inc(e, ss)
        on("scalar", pds_sc)

        NG4 = C.ET // 4
        PCBSV0 = [None]
        for ltl in range(LTB):
            ltg = lb * LTB + ltl

            def pcb_t(t, _lb=lb, _ltl=ltl, _ltg=ltg):
                if _ltl == 0:
                    PCBSV0[0] = cnt[id(sv)]
                t.wait_ge(ss, nsm[("pdse", _lb)])
                for g8 in range(NG4):
                    if g8 > 0:
                        t.wait_ge(sv, PCBSV0[0] + _ltl * (NG4 + 2) + g8)
                    elif _ltl > 0:
                        t.wait_ge(sv, PCBSV0[0] + (_ltl - 1) * (NG4 + 2) + NG4 + 1)
                    for q4 in range(4):
                        et = g8 * 4 + q4
                        tr = t.transpose(psTf[:, q4 * 128:(q4 + 1) * 128],
                                         lgf[:, et * LB + _ltl * 128:
                                             et * LB + (_ltl + 1) * 128],
                                         eyef[:])
                    nsm[("pcbt", _ltg, g8)] = inc(tr, st)
            on("tensor", pcb_t)

            def pcb_v(v, _ltg=ltg):
                for g8 in range(NG4):
                    v.wait_ge(st, nsm[("pcbt", _ltg, g8)])
                    e = v.tensor_copy(lrow[:, g8 * 512:(g8 + 1) * 512], psTf[:])
                    inc(e, sv)
                ng(v, sem=sv, val=cnt[id(sv)])
                e = v.reduce_max(b2c[:, _ltg:_ltg + 1], lrow[:, 0:C.ES],
                                 axis=AX.X, negate=True)
                nsm[("pcbm", _ltg)] = inc(e, sv)
            on("vector", pcb_v)

            def pcb_sc(sc, _ltg=ltg):
                sc.wait_ge(sv, nsm[("pcbm", _ltg)])
                t = inc(sc.activation(lrow[:, C.ES:2 * C.ES], lrow[:, 0:C.ES], AF.Exp,
                                      bias=b2c[:, _ltg:_ltg + 1],
                                      accum_out=tiny[:, 40:41]), ss)
                sc.wait_ge(ss, t)
                e = sc.activation(tiny[:, 41:42], tiny[:, 40:41], AF.Ln)
                nsm[("pcbz", _ltg)] = inc(e, ss)
            on("scalar", pcb_sc)

            def pcb_v2(v, _ltg=ltg):
                v.wait_ge(ss, nsm[("pcbz", _ltg)])
                e = v.tensor_sub(b2c[:, _ltg:_ltg + 1], b2c[:, _ltg:_ltg + 1],
                                 tiny[:, 41:42])
                nsm[("pcbb", _ltg)] = inc(e, sv)
            on("vector", pcb_v2)

        def pcb_t3(t, _lb=lb):
            t.wait_ge(sv, nsm[("pcbb", (_lb + 1) * LTB - 1)])
            tr = t.transpose(psTf[0:LTB, 0:128], b2c[:, _lb * LTB:(_lb + 1) * LTB],
                             eyef[:])
            nsm[("pcbtr", _lb)] = inc(tr, st)
        on("tensor", pcb_t3)

        def pcb_v4(v, _lb=lb):
            v.wait_ge(st, nsm[("pcbtr", _lb)])
            e = v.tensor_copy(b2sb[0:LTB, :], psTf[0:LTB, 0:128])
            nsm[("pcbsb", _lb)] = inc(e, sv)
        on("vector", pcb_v4)

        scb = Stream(1)

        def pcb_s(s, _lb=lb):
            s.wait_ge(sv, nsm[("pcbsb", _lb)])
            e = s.dma_start(out=b2_h[_lb * LTB:(_lb + 1) * LTB, :], in_=b2sb[0:LTB, :])
            scb.inc(e, 0)
        on("sync", pcb_s)

        def pcb_g(g, _lb=lb):
            scb.wait(g, 0)
            qinc16(g.dma_start(
                out=brow_f[:],
                in_=b2_h[_lb * LTB:(_lb + 1) * LTB, :]
                .rearrange("a b -> (a b)").unsqueeze(0).broadcast_to((128, LB))), "dg6")
        on("gpsimd", pcb_g)

        def pcb_v5(v, _lb=lb):
            v.wait_ge(Q["dg6"], (_lb + 1) * 16)
            for et in range(C.ET):
                e = v.scalar_tensor_tensor(lgf[:, et * LB:(et + 1) * LB],
                                           lgf[:, et * LB:(et + 1) * LB], 1.0,
                                           brow_f[:], OP.mult, OP.add)
            nsm[("pcbadd", _lb)] = inc(e, sv)
        on("vector", pcb_v5)

        def pcb_sc5(sc, _lb=lb):
            sc.wait_ge(sv, nsm[("pcbadd", _lb)])
            for et in range(C.ET):
                e = sc.activation(combT[:, et * C.LH + _lb * LB:et * C.LH + (_lb + 1) * LB],
                                  lgf[:, et * LB:(et + 1) * LB], AF.Exp)
            nsm[("pcomb", _lb)] = inc(e, ss)
        on("scalar", pcb_sc5)
        barrier()

    def pdn_s(s):
        s.wait_ge(ss, nsm[("pdse", NLB - 1)])
        nsm["pdst"] = inc(s.dma_start(out=cin2[:], in_=dstat[:]), sh, 16)
    on("sync", pdn_s)

    def pdn_g(g):
        g.wait_ge(sh, nsm["pdst"])
        nsm["cc2"] = inc(g.collective_compute(
            "AllGather", OP.bypass, replica_groups=RG8, ins=[cin2[:]], outs=[cout2[:]]), scc)
    on("gpsimd", pdn_g)

    def pdn_s2(s):
        s.wait_ge(scc, nsm["cc2"])
        nsm["pdsp"] = inc(s.dma_start(out=pstat[:], in_=cout2[TS(REG["pxor"], 128), :]),
                          sh, 16)
    on("sync", pdn_s2)

    def pdn_v(v):
        v.wait_ge(sh, nsm["pdsp"])
        ET = C.ET
        if NLB == 2:
            v.tensor_tensor(tiny[:, 0:ET], dstat[:, 0:ET], dstat[:, ET:2 * ET], OP.min)
            t = inc(v.tensor_tensor(tiny[:, ET:2 * ET], pstat[:, 0:ET],
                                    pstat[:, ET:2 * ET], OP.min), sv)
            v.wait_ge(sv, t)
            t = inc(v.tensor_tensor(tiny[:, 0:ET], tiny[:, 0:ET],
                                    tiny[:, ET:2 * ET], OP.min), sv)
            v.wait_ge(sv, t)
        else:
            t = inc(v.tensor_tensor(tiny[:, 0:ET], dstat[:, 0:ET], pstat[:, 0:ET],
                                    OP.min), sv)
            v.wait_ge(sv, t)
        for lb in range(NLB):
            v.tensor_sub(tiny[:, (1 + lb) * ET:(2 + lb) * ET], tiny[:, 0:ET],
                         dstat[:, lb * ET:(lb + 1) * ET])
            e = v.tensor_sub(tiny[:, (1 + NLB + lb) * ET:(2 + NLB + lb) * ET],
                             tiny[:, 0:ET], pstat[:, lb * ET:(lb + 1) * ET])
        nsm["pdnd"] = inc(e, sv)
    on("vector", pdn_v)

    def pdn_sc(sc):
        sc.wait_ge(sv, nsm["pdnd"])
        e = sc.activation(tiny[:, C.ET:(1 + 2 * NLB) * C.ET],
                          tiny[:, C.ET:(1 + 2 * NLB) * C.ET], AF.Exp)
        nsm["pdnf"] = inc(e, ss)
    on("scalar", pdn_sc)

    def pdn_v2(v):
        v.wait_ge(ss, nsm["pdnf"])
        ET = C.ET

        def step(ins):
            t = inc(ins, sv)
            v.wait_ge(sv, t)
        step(v.tensor_mul(tiny[:, (1 + 2 * NLB) * ET:(2 + 2 * NLB) * ET],
                          dstat[:, 2 * ET:3 * ET], tiny[:, ET:2 * ET]))
        for lb in range(NLB):
            if lb > 0:
                step(v.tensor_mul(tiny[:, (2 + 2 * NLB) * ET:(3 + 2 * NLB) * ET],
                                  dstat[:, (2 + lb) * ET:(3 + lb) * ET],
                                  tiny[:, (1 + lb) * ET:(2 + lb) * ET]))
                step(v.tensor_add(tiny[:, (1 + 2 * NLB) * ET:(2 + 2 * NLB) * ET],
                                  tiny[:, (1 + 2 * NLB) * ET:(2 + 2 * NLB) * ET],
                                  tiny[:, (2 + 2 * NLB) * ET:(3 + 2 * NLB) * ET]))
            step(v.tensor_mul(tiny[:, (2 + 2 * NLB) * ET:(3 + 2 * NLB) * ET],
                              pstat[:, (2 + lb) * ET:(3 + lb) * ET],
                              tiny[:, (1 + NLB + lb) * ET:(2 + NLB + lb) * ET]))
            step(v.tensor_add(tiny[:, (1 + 2 * NLB) * ET:(2 + 2 * NLB) * ET],
                              tiny[:, (1 + 2 * NLB) * ET:(2 + 2 * NLB) * ET],
                              tiny[:, (2 + 2 * NLB) * ET:(3 + 2 * NLB) * ET]))
        step(v.reciprocal(tiny[:, (1 + 2 * NLB) * ET:(2 + 2 * NLB) * ET],
                          tiny[:, (1 + 2 * NLB) * ET:(2 + 2 * NLB) * ET]))
        for lb in range(NLB):
            step(v.tensor_mul(tiny[:, (1 + lb) * ET:(2 + lb) * ET],
                              tiny[:, (1 + lb) * ET:(2 + lb) * ET],
                              tiny[:, (1 + 2 * NLB) * ET:(2 + 2 * NLB) * ET]))
        for lb in range(NLB):
            for et in range(C.ET):
                e = v.tensor_scalar_mul(
                    disp[:, et * C.LH + lb * LB:et * C.LH + (lb + 1) * LB],
                    disp[:, et * C.LH + lb * LB:et * C.LH + (lb + 1) * LB],
                    tiny[:, (1 + lb) * ET + et:(1 + lb) * ET + et + 1])
        nsm["pdisp"] = inc(e, sv)
    on("vector", pdn_v2)

    dbg_dump("disp", lambda: disp[:], sv, lambda: nsm["pdisp"])
    dbg_dump("dispu", lambda: disp[:], ss, lambda: nsm[("pdse", NLB - 1)])
    dbg_dump("pstat", lambda: pstat[:], sh, lambda: nsm["pdsp"])
    dbg_dump("tiny", lambda: tiny[:], sv, lambda: nsm["pdisp"])
    dbg_dump("combT", lambda: combT[:], ss, lambda: nsm[("pcomb", NLB - 1)])
    dbg_dump("dstat", lambda: dstat[:], ss, lambda: nsm[("pdse", NLB - 1)])
    barrier()
    fre("brow_f", "lrow", "b2sb", "b2c", "pstat", "dstat", "lgf", "seS")

    xlnT = mksb(top, "xlnT", 128, C.LT * C.DM, BF16)
    dispT = mksb(top, "dispT", 128, C.LT * C.ES, BF16)
    q4b = qc["q4"]

    def ptr_t(t):
        gi = 0
        for (src, nt) in ((xln, C.KC), (disp, C.ET)):
            for lt in range(C.LT):
                for g4 in range(0, nt, 4):
                    w = min(4, nt - g4)
                    if gi >= 2:
                        t.wait_ge(Q["q4"], q4b + gi - 1)
                    pT = psTa if gi % 2 == 0 else psTb
                    for q4 in range(w):
                        tt = g4 + q4
                        tr = t.transpose(pT[:, q4 * 128:(q4 + 1) * 128],
                                         src[:, tt * C.LH + lt * 128:tt * C.LH + (lt + 1) * 128],
                                         eyeb[:])
                    nsm[("ptr", gi)] = inc(tr, st)
                    gi += 1
    on("tensor", ptr_t)

    def ptr_v(v):
        gi = 0
        for (dst, nt, tl) in ((xlnT, C.KC, C.DM), (dispT, C.ET, C.ES)):
            for lt in range(C.LT):
                for g4 in range(0, nt, 4):
                    w = min(4, nt - g4)
                    v.wait_ge(st, nsm[("ptr", gi)])
                    pT = psTa if gi % 2 == 0 else psTb
                    e = v.tensor_copy(dst[:, lt * tl + g4 * 128:lt * tl + (g4 + w) * 128],
                                      pT[:, 0:w * 128])
                    qinc(e, "q4")
                    gi += 1
    on("vector", ptr_v)

    barrier()
    fre("xr1")

    slotp = mksb(top, "slotp", 128, C.KC * C.ES, BF16)
    PSLSS0 = cnt[id(ss)]

    def psl_t(t):
        i = 0
        for kc in range(C.KC):
            for w in range(C.ES // 512):
                pst = psA if i % 2 == 0 else psB
                if i >= 2:
                    t.wait_ge(ss, PSLSS0 + i - 1)
                for lt in range(C.LT):
                    mm = t.matmul(pst[:, 0:512],
                                  xlnT[:, lt * C.DM + kc * 128:lt * C.DM + (kc + 1) * 128],
                                  dispT[:, lt * C.ES + w * 512:lt * C.ES + (w + 1) * 512],
                                  start=(lt == 0), stop=(lt == C.LT - 1))
                nsm[("pslm", i)] = inc(mm, st)
                i += 1
    on("tensor", psl_t)

    def psl_sc(sc):
        i = 0
        for kc in range(C.KC):
            for w in range(C.ES // 512):
                pst = psA if i % 2 == 0 else psB
                sc.wait_ge(st, nsm[("pslm", i)])
                e = sc.copy(slotp[:, kc * C.ES + w * 512:kc * C.ES + (w + 1) * 512],
                            pst[:, 0:512])
                nsm[("psle", i)] = inc(e, ss)
                i += 1
    on("scalar", psl_sc)

    def psl_s(s):
        s.wait_ge(ss, nsm[("psle", C.KC * (C.ES // 512) - 1)])
        for kc in range(C.KC):
            e = s.dma_start(out=cin3[kc * 128:(kc + 1) * 128, :],
                            in_=slotp[:, kc * C.ES:(kc + 1) * C.ES])
            inc(e, sh, 16)
        nsm["pslst"] = cnt[id(sh)]
    on("sync", psl_s)

    def psl_g(g):
        g.wait_ge(sh, nsm["pslst"])
        nsm["cc3"] = inc(g.collective_compute(
            "AllGather", OP.bypass, replica_groups=RG8, ins=[cin3[:]], outs=[cout3[:]]), scc)
    on("gpsimd", psl_g)

    barrier()
    fre("slotp", "dispT", "xlnT", "disp")

    sown = mksb(top, "sown", 128, C.KC * 512, BF16)
    srec = mksb(top, "srec", 128, 8 * C.KC * 128, BF16)

    def psg_s(s):
        s.wait_ge(scc, nsm["cc3"])
        pid = REG["pid"]
        c3r = cout3.rearrange("(g k p) e -> p g k e", g=8, k=C.KC)
        srec_r = srec[:].rearrange("p (g k e) -> p g k e", g=8, k=C.KC)
        e = s.dma_start(out=srec_r, in_=c3r[:, :, :, TS(pid, 128)])
        inc(e, sh, 16)
        nsm["psgl"] = cnt[id(sh)]
    on("sync", psg_s)

    def psg_v(v):
        v.wait_ge(sh, nsm["psgl"])
        for bb in range(4):
            for kc in range(C.KC):
                e = v.tensor_add(
                    sown[:, kc * 512 + bb * 128:kc * 512 + (bb + 1) * 128],
                    srec[:, (bb * C.KC + kc) * 128:(bb * C.KC + kc + 1) * 128],
                    srec[:, ((bb + 4) * C.KC + kc) * 128:((bb + 4) * C.KC + kc + 1) * 128])
        nsm["psown"] = inc(e, sv)
    on("vector", psg_v)
    dbg_dump("sown", lambda: sown[:], sv, lambda: nsm["psown"])

    barrier()
    fre("srec")

    wgt = mksb(top, "wgt", 128, C.KC * C.F, BF16)
    wut = mksb(top, "wut", 128, C.KC * C.F, BF16)
    wdt = mksb(top, "wdt", 128, C.FT * C.DO, BF16)
    gat = mksb(top, "gat", 128, C.FT * 256, BF16)
    upt = mksb(top, "upt", 128, C.FT * 256, BF16)
    eob = mksb(top, "eob", 64, 2 * C.DO, BF16)
    bdbc = mksb(top, "bdbc", 128, 2 * C.DO, F32)

    def slot_rhs(kc, e_):
        ap = sown[:, kc * 512:(kc + 1) * 512]
        ap = ap.rearrange("p (b s) -> p b s", b=4)
        return ap[:, :, e_ * 64:(e_ + 1) * 64]

    sBD = Stream(1)
    sWt = Stream(2)
    sE4 = Stream(1)
    sE4.preset(8, 1)

    def pex_s0(s):
        sBD.inc(s.dma_start(out=bdbc[:, 0:C.DO],
                            in_=i_bd[0:1, :].broadcast_to((128, C.DO))), 0)
        e = s.dma_start(out=bdbc[:, C.DO:2 * C.DO],
                        in_=i_bd[1:2, :].broadcast_to((128, C.DO)))
        sBD.inc(e, 0)
    on("sync", pex_s0)

    NDOW = max(1, C.DO // 512)
    WDO = min(512, C.DO)
    for e_i in range(2):
        def pex_sw(s, _e=e_i):
            if _e == 1:
                s.wait_ge(st, nsm[("peom", 0, 3)])
            for kc in range(C.KC):
                sWt.inc(s.dma_start(out=wgt[:, kc * C.F:(kc + 1) * C.F],
                                    in_=wgb[_e, kc * 128:(kc + 1) * 128, :]), _e)
                sWt.inc(s.dma_start(out=wut[:, kc * C.F:(kc + 1) * C.F],
                                    in_=wub[_e, kc * 128:(kc + 1) * 128, :]), _e)
            for ft in range(C.FT):
                sWt.inc(s.dma_start(out=wdt[:, ft * C.DO:(ft + 1) * C.DO],
                                    in_=wdb[_e, ft * 128:(ft + 1) * 128, :]), _e)
        on("sync", pex_sw)

        PEXSS0 = cnt[id(ss)]

        def pex_t(t, _e=e_i, _b=PEXSS0):
            sWt.wait(t, _e)
            t.wait_ge(sv, nsm["psown"])
            i = 0
            for wsb in (wgt, wut):
                for ft in range(C.FT):
                    pst = psA if i % 2 == 0 else psB
                    if i >= 2:
                        t.wait_ge(ss, _b + i - 1)
                    for kc in range(C.KC):
                        mm = t.matmul(pst[:, 0:256],
                                      wsb[:, kc * C.F + ft * 128:kc * C.F + (ft + 1) * 128],
                                      slot_rhs(kc, _e),
                                      start=(kc == 0), stop=(kc == C.KC - 1))
                    nsm[("pexm", _e, i)] = inc(mm, st)
                    i += 1
        on("tensor", pex_t)

        def pex_sc(sc, _e=e_i):
            i = 0
            for which in ("g", "u"):
                for ft in range(C.FT):
                    pst = psA if i % 2 == 0 else psB
                    sc.wait_ge(st, nsm[("pexm", _e, i)])
                    dst = gat if which == "g" else upt
                    bias = bgT if which == "g" else buT
                    fn = AF.Gelu if which == "g" else AF.Identity
                    e = sc.activation(dst[:, ft * 256:(ft + 1) * 256], pst[:, 0:256], fn,
                                      bias=bias[:, _e * C.FT + ft:_e * C.FT + ft + 1])
                    nsm[("pexe", _e, i)] = inc(e, ss)
                    i += 1
        on("scalar", pex_sc)

        def pex_v(v, _e=e_i):
            v.wait_ge(ss, nsm[("pexe", _e, 2 * C.FT - 1)])
            e = v.tensor_mul(gat[:], gat[:], upt[:])
            nsm[("pexgu", _e)] = inc(e, sv)
        on("vector", pex_v)

        PEXSV0 = cnt[id(sv)]

        def pex_t2(t, _e=e_i, _b=PEXSV0):
            t.wait_ge(sv, nsm[("pexgu", _e)])
            for bb in range(4):
                if bb >= 1:
                    t.wait_ge(sv, _b + bb)
                for w2 in range(NDOW):
                    pst = psA if w2 == 0 else psB
                    for ft in range(C.FT):
                        mm = t.matmul(pst[0:64, 0:WDO],
                                      gat[:, ft * 256 + bb * 64:ft * 256 + (bb + 1) * 64],
                                      wdt[:, ft * C.DO + w2 * WDO:ft * C.DO + (w2 + 1) * WDO],
                                      start=(ft == 0), stop=(ft == C.FT - 1))
                nsm[("peom", _e, bb)] = inc(mm, st)
        on("tensor", pex_t2)

        def pex_v2(v, _e=e_i):
            sBD.wait(v, 0)
            for bb in range(4):
                v.wait_ge(st, nsm[("peom", _e, bb)])
                if _e * 4 + bb >= 1:
                    sE4.wait(v, _e * 4 + bb - 1)
                for w2 in range(NDOW):
                    pst = psA if w2 == 0 else psB
                    e = v.scalar_tensor_tensor(
                        eob[0:64, w2 * WDO:(w2 + 1) * WDO], pst[0:64, 0:WDO], 1.0,
                        bdbc[0:64, _e * C.DO + w2 * WDO:_e * C.DO + (w2 + 1) * WDO],
                        OP.mult, OP.add)
                nsm[("peoe", _e, bb)] = inc(e, sv)
        on("vector", pex_v2)

        def pex_s2(s, _e=e_i):
            for bb in range(4):
                s.wait_ge(sv, nsm[("peoe", _e, bb)])
                e = s.dma_start(out=cin4[bb * 128 + _e * 64:bb * 128 + (_e + 1) * 64, :],
                                in_=eob[0:64, 0:C.DO])
                sE4.inc(e, _e * 4 + bb)
        on("sync", pex_s2)

    def pex_g(g):
        sE4.wait(g, 7)
        nsm["cc4"] = inc(g.collective_compute(
            "AllGather", OP.bypass, replica_groups=RG8, ins=[cin4[:]], outs=[cout4[:]]), scc)
    on("gpsimd", pex_g)

    barrier()
    fre("bdbc", "eob", "upt", "gat", "wdt", "wut", "wgt", "sown")

    res2 = mksb(top, "res2", 128, C.KC * C.LH, BF16)
    eoutb = mksb(top, "eoutb", 128, 8 * C.DO, BF16)
    xr1b = mksb(top, "xr1b", 128, C.KC * C.LH, BF16)

    s12x = Stream(1)

    def p12_s0(s):
        for kc in range(C.KC):
            e = s.dma_start(out=xr1b[:, kc * C.LH:(kc + 1) * C.LH],
                            in_=xr1_h[kc * 128:(kc + 1) * 128, :])
            s12x.inc(e, 0)
    on("sync", p12_s0)

    def pcm_s(s):
        s.wait_ge(scc, nsm["cc4"])
        breg = REG["breg"]
        c4r = cout4.rearrange("(g b p) o -> p g b o", g=8, b=4)
        eoutb_r = eoutb[:].rearrange("p (g o) -> p g o", g=8)
        e = s.dma_start(out=eoutb_r, in_=c4r[:, :, DS(breg, 1), :].squeeze(2))
        inc(e, sh, 16)
        nsm["pcml"] = cnt[id(sh)]
    on("sync", pcm_s)

    PCMSV0 = cnt[id(sv)]

    def pcm_t(t):
        t.wait_ge(sh, nsm["pcml"])
        i = 0
        for ot in range(C.KC):
            for w in range(C.NWH):
                pst = psA if i % 2 == 0 else psB
                if i >= 2:
                    t.wait_ge(sv, PCMSV0 + i - 1)
                for sdr in range(8):
                    mm = t.matmul(pst[:, 0:C.WH],
                                  eoutb[:, sdr * C.DO + ot * 128:sdr * C.DO + (ot + 1) * 128],
                                  combT[:, sdr * C.LH + w * C.WH:sdr * C.LH + (w + 1) * C.WH],
                                  start=(sdr == 0), stop=(sdr == 7))
                nsm[("pcmm", i)] = inc(mm, st)
                i += 1
    on("tensor", pcm_t)

    def pcm_v(v):
        s12x.wait(v, 0)
        i = 0
        for ot in range(C.KC):
            for w in range(C.NWH):
                pst = psA if i % 2 == 0 else psB
                v.wait_ge(st, nsm[("pcmm", i)])
                e = v.scalar_tensor_tensor(
                    res2[:, ot * C.LH + w * C.WH:ot * C.LH + (w + 1) * C.WH],
                    pst[:, 0:C.WH], 1.0,
                    xr1b[:, ot * C.LH + w * C.WH:ot * C.LH + (w + 1) * C.WH],
                    OP.mult, OP.add)
                nsm[("pcme", i)] = inc(e, sv)
                i += 1
    on("vector", pcm_v)
    dbg_dump("eoutb", lambda: eoutb[:], sh, lambda: nsm["pcml"])
    dbg_dump("res2", lambda: res2[:], sv, lambda: nsm[("pcme", C.KC * C.NWH - 1)])

    barrier()
    fre("xr1b", "eoutb")

    r2T = mksb(top, "r2T", 128, 2 * C.DM, BF16)
    osb = mksb(top, "osb", 128, 2 * C.DM, F32)
    sqf = mksb(top, "sqf", 128, C.DM, F32)
    q1d = qc["q1"]
    q2d = qc["q2"]
    sOo = Stream(2)

    for lt in range(C.LT):
        def pf_t(t, _lt=lt):
            for g4 in range(0, C.KC, 4):
                w = min(4, C.KC - g4)
                gi = _lt * ((C.KC + 3) // 4) + g4 // 4
                if gi >= 2:
                    t.wait_ge(Q["q1"], q1d + gi - 1)
                pT = psTa if gi % 2 == 0 else psTb
                for q4 in range(w):
                    kc = g4 + q4
                    tr = t.transpose(pT[:, q4 * 128:(q4 + 1) * 128],
                                     res2[:, kc * C.LH + _lt * 128:kc * C.LH + (_lt + 1) * 128],
                                     eyeb[:])
                nsm[("pft", _lt, g4)] = inc(tr, st)
        on("tensor", pf_t)

        def pf_v(v, _lt=lt):
            so = (_lt % 2) * C.DM
            if _lt >= 2:
                sOo.wait(v, _lt - 2)
            for g4 in range(0, C.KC, 4):
                w = min(4, C.KC - g4)
                gi = _lt * ((C.KC + 3) // 4) + g4 // 4
                v.wait_ge(st, nsm[("pft", _lt, g4)])
                pT = psTa if gi % 2 == 0 else psTb
                e = v.tensor_copy(r2T[:, so + g4 * 128:so + (g4 + w) * 128], pT[:, 0:w * 128])
                qinc(e, "q1")
            ng(v, sem=Q["q1"], val=qc["q1"])
            e = v.reduce_sum(tiny[:, 9:10], r2T[:, so:so + C.DM], axis=AX.X)
            nsm[("pfc", _lt)] = inc(e, sv)
            v.wait_ge(sv, nsm[("pfc", _lt)])
        on("vector", pf_v)

        def pf_sc(sc, _lt=lt):
            so = (_lt % 2) * C.DM
            sc.wait_ge(sv, nsm[("pfc", _lt)])
            e = sc.activation(sqf[:], r2T[:, so:so + C.DM], AF.Square,
                              accum_out=tiny[:, 8:9])
            nsm[("pfsq", _lt)] = inc(e, ss)
        on("scalar", pf_sc)

        def pf_v2(v, _lt=lt):
            v.wait_ge(ss, nsm[("pfsq", _lt)])
            v.tensor_scalar_mul(tiny[:, 9:10], tiny[:, 9:10], 1.0 / C.DM)
            t = inc(v.tensor_scalar_mul(tiny[:, 8:9], tiny[:, 8:9], 1.0 / C.DM), sv)
            v.wait_ge(sv, t)
            t = inc(v.tensor_mul(tiny[:, 10:11], tiny[:, 9:10], tiny[:, 9:10]), sv)
            v.wait_ge(sv, t)
            e = v.tensor_sub(tiny[:, 10:11], tiny[:, 8:9], tiny[:, 10:11])
            nsm[("pfv", _lt)] = inc(e, sv)
        on("vector", pf_v2)

        def pf_sc2(sc, _lt=lt):
            sc.wait_ge(sv, nsm[("pfv", _lt)])
            e = sc.activation(tiny[:, 10:11], tiny[:, 10:11], AF.Sqrt, bias=csts[:, 0:1])
            nsm[("pfsd", _lt)] = inc(e, ss)
        on("scalar", pf_sc2)

        def pf_v3(v, _lt=lt):
            so = (_lt % 2) * C.DM
            v.wait_ge(ss, nsm[("pfsd", _lt)])
            t = inc(v.reciprocal(tiny[:, 10:11], tiny[:, 10:11]), sv)
            v.wait_ge(sv, t)
            e = v.tensor_scalar(osb[:, so:so + C.DM], r2T[:, so:so + C.DM],
                                tiny[:, 9:10], tiny[:, 10:11], OP.subtract, OP.mult)
            nsm[("pfe", _lt)] = inc(e, sv)
        on("vector", pf_v3)

        def pf_s(s, _lt=lt):
            so = (_lt % 2) * C.DM
            s.wait_ge(sv, nsm[("pfe", _lt)])
            e = s.dma_start(out=o_y[_lt * 128:(_lt + 1) * 128, :], in_=osb[:, so:so + C.DM])
            sOo.inc(e, _lt)
        on("sync", pf_s)

    def fin_s(s):
        sOo.wait_all(s)
        if cnt[id(sh)] > 0:
            s.wait_ge(sh, cnt[id(sh)])
    on("sync", fin_s)

    for h in reversed(list(live.values())):
        h.__exit__(None, None, None)
    live.clear()
    top.close()
    global LAST_NSM
    LAST_NSM = dict(nsm)
    return nc


def _prep_core(C, c, I):
    b, j = c % 4, c // 4
    js, je = j * C.DH, (j + 1) * C.DH
    m = {}
    m["i_x"] = I["L0_attn_x"][b]
    m["i_xh"] = I["L0_attn_x"][b][j * C.LH:(j + 1) * C.LH]
    m["i_inw"] = np.concatenate([I["L0_attn_in_w"][js:je],
                                 I["L0_attn_in_w"][C.DI + js:C.DI + je]], 0)
    m["i_convw"] = I["L0_attn_conv_w"][js:je, 0, :].reshape(
        C.DT, 128, 4).transpose(1, 0, 2).reshape(128, C.DT * 4)
    m["i_convb"] = I["L0_attn_conv_b"][js:je].reshape(C.DT, 128).T
    m["i_xprojT"] = I["L0_attn_xproj_w"][:, js:je].T
    m["i_dtwT"] = I["L0_attn_dt_w"][js:je, :].T
    m["i_dtb"] = I["L0_attn_dt_b"][js:je].reshape(C.DT, 128).T
    m["i_alog"] = I["L0_attn_A_log"][js:je].reshape(
        C.DT, 128, C.NST).transpose(1, 0, 2).reshape(128, C.DT * C.NST)
    m["i_dv"] = I["L0_attn_D"][js:je].reshape(C.DT, 128).T
    m["i_outwT"] = I["L0_attn_out_w"][:, js:je].T
    for nm, k in (("i_normg", "L0_ffn_norm_g"), ("i_normb", "L0_ffn_norm_b"),
                  ("i_slotg", "L0_ffn_slot_g"), ("i_slotb", "L0_ffn_slot_b")):
        m[nm] = I[k].reshape(C.KC, 128).T
    m["i_scale"] = I["L0_ffn_scale"].reshape(1, 1)
    m["i_seT"] = I["L0_ffn_slot_embeds"].reshape(C.ES, C.DM).T
    m["i_wg"] = I["L0_ffn_Wg"][2 * c:2 * c + 2]
    m["i_wu"] = I["L0_ffn_Wu"][2 * c:2 * c + 2]
    m["i_wd"] = I["L0_ffn_Wd"][2 * c:2 * c + 2]
    m["i_bgT"] = I["L0_ffn_bg"][2 * c:2 * c + 2].reshape(
        2, C.FT, 128).transpose(2, 0, 1).reshape(128, 2 * C.FT)
    m["i_buT"] = I["L0_ffn_bu"][2 * c:2 * c + 2].reshape(
        2, C.FT, 128).transpose(2, 0, 1).reshape(128, 2 * C.FT)
    m["i_bd"] = I["L0_ffn_bd"][2 * c:2 * c + 2]
    m["i_eyeb"] = np.eye(128, dtype=BF)
    m["i_eyef"] = np.eye(128, dtype=np.float32)
    out = {}
    for k, v in m.items():
        v = np.asarray(v)
        if v.dtype == np.float64:
            v = v.astype(np.float32)
        out[k] = np.ascontiguousarray(v)
    return out


def kernel(**inputs):
    C = CFG_FULL
    I = {k: np.asarray(v) for k, v in inputs.items()}
    nc = build(C)
    in_maps = [_prep_core(C, c, I) for c in range(8)]
    res = run_bass_kernel_spmd(nc, in_maps, core_ids=list(range(8)))
    out = np.zeros((4, C.L, C.DM), np.float32)
    for c in range(8):
        b, j = c % 4, c // 4
        out[b, j * C.LH:(j + 1) * C.LH] = res.results[c]["o_y"]
    return out


# version 06
# speedup vs baseline: 1.1023x; 1.0312x over previous
import contextlib
import numpy as np
import ml_dtypes
import concourse.bass as bass
import concourse.mybir as mybir
from concourse.bass_utils import run_bass_kernel_spmd

LAST_NSM = {}

F32 = mybir.dt.float32
BF16 = mybir.dt.bfloat16
AF = mybir.ActivationFunctionType
OP = mybir.AluOpType
AX = mybir.AxisListType
BF = ml_dtypes.bfloat16
EPS = 1e-5
DS = bass.ds
TS = bass.ts


class CFG:
    def __init__(self, DM, L, LC, F):
        self.DM = DM
        self.DI = 2 * DM
        self.DH = DM
        self.L = L
        self.LH = L // 2
        self.LC = LC
        self.NLC = L // LC
        self.NST = 16
        self.R = 64
        self.RK2 = self.R + 2 * self.NST
        self.E = 16
        self.S = 64
        self.ES = self.E * self.S
        self.F = F
        self.DO = DM
        self.KC = DM // 128
        self.DT = self.DH // 128
        self.ET = self.ES // 128
        self.FT = F // 128
        self.LT = self.LH // 128
        self.W = min(512, L)
        self.NW = L // self.W
        self.WH = min(512, self.LH)
        self.NWH = self.LH // self.WH
        self.NLB = 2 if self.LH >= 256 else 1
        self.LB = self.LH // self.NLB
        self.LTB = self.LB // 128
        self.WB = min(512, self.LB)
        self.NWB = self.LB // self.WB
        assert DM % 128 == 0 and L % (2 * LC) == 0 and LC % 128 == 0
        assert self.LH % 128 == 0 and F % 128 == 0 and self.ET == 8


CFG_FULL = CFG(DM=1024, L=4096, LC=1024, F=2048)


def build(C: CFG, dbg=False, races=False):
    nc = bass.Bass("TRN2", target_bir_lowering=False, detect_race_conditions=races)
    dtn = nc.dram_tensor

    def ein(name, shape):
        return dtn(name, shape, F32, kind="ExternalInput").ap()

    i_x = ein("i_x", [C.L, C.DM])
    i_xh = ein("i_xh", [C.LH, C.DM])
    i_inw = ein("i_inw", [2 * C.DH, C.DM])
    i_convw = ein("i_convw", [128, C.DT * 4])
    i_convb = ein("i_convb", [128, C.DT])
    i_xprojT = ein("i_xprojT", [C.DH, C.RK2])
    i_dtwT = ein("i_dtwT", [C.R, C.DH])
    i_dtb = ein("i_dtb", [128, C.DT])
    i_alog = ein("i_alog", [128, C.DT * C.NST])
    i_dv = ein("i_dv", [128, C.DT])
    i_outwT = ein("i_outwT", [C.DH, C.DO])
    i_normg = ein("i_normg", [128, C.KC])
    i_normb = ein("i_normb", [128, C.KC])
    i_slotg = ein("i_slotg", [128, C.KC])
    i_slotb = ein("i_slotb", [128, C.KC])
    i_scale = ein("i_scale", [1, 1])
    i_seT = ein("i_seT", [C.DM, C.ES])
    i_wg = ein("i_wg", [2, C.DM, C.F])
    i_wu = ein("i_wu", [2, C.DM, C.F])
    i_wd = ein("i_wd", [2, C.F, C.DO])
    i_bgT = ein("i_bgT", [128, 2 * C.FT])
    i_buT = ein("i_buT", [128, 2 * C.FT])
    i_bd = ein("i_bd", [2, C.DO])
    i_eyeb = dtn("i_eyeb", [128, 128], BF16, kind="ExternalInput").ap()
    i_eyef = dtn("i_eyef", [128, 128], F32, kind="ExternalInput").ap()
    o_y = dtn("o_y", [C.LH, C.DM], F32, kind="ExternalOutput").ap()

    xb = dtn("xb", [C.L, C.DM], BF16).ap()
    xhb = dtn("xhb", [C.LH, C.DM], BF16).ap()
    inwb = dtn("inwb", [2 * C.DH, C.DM], BF16).ap()
    xprojTb = dtn("xprojTb", [C.DH, C.RK2], BF16).ap()
    dtwTb = dtn("dtwTb", [C.R, C.DH], BF16).ap()
    outwTb = dtn("outwTb", [C.DH, C.DO], BF16).ap()
    seTb = dtn("seTb", [C.DM, C.ES], BF16).ap()
    wgb = dtn("wgb", [2, C.DM, C.F], BF16).ap()
    wub = dtn("wub", [2, C.DM, C.F], BF16).ap()
    wdb = dtn("wdb", [2, C.F, C.DO], BF16).ap()
    dk = dict(kind="ExternalOutput") if dbg else {}
    xx_h = dtn("xx_h", [C.DH, C.L], BF16, **dk).ap()
    xr1_h = dtn("xr1_h", [C.DM, C.LH], BF16, **dk).ap()
    delta_h = dtn("delta_h", [C.DH, C.L], BF16, **dk).ap()
    u_h = dtn("u_h", [C.DH, C.L], BF16, **dk).ap()
    sres_h = dtn("sres_h", [C.DH, C.L], BF16, **dk).ap()
    y3_h = dtn("y3_h", [C.DH, C.L], BF16, **dk).ap()
    brow_h = dtn("brow_h", [2 * C.NST, C.L], BF16).ap()
    murow_h = dtn("murow_h", [2, max(C.LH, C.ES)], F32).ap()
    b2_h = dtn("b2_h", [C.LT, 128], F32).ap()
    cin0 = dtn("cin0", [C.RK2, C.L], F32).ap()
    cout0 = dtn("cout0", [C.RK2, C.L], F32).ap()
    cin1 = dtn("cin1", [C.DO, C.L], BF16).ap()
    cout1 = dtn("cout1", [C.DO, C.L], BF16).ap()
    cin2 = dtn("cin2", [128, 4 * C.ET], F32).ap()
    cout2 = dtn("cout2", [8 * 128, 4 * C.ET], F32).ap()
    cin3 = dtn("cin3", [C.DM, C.ES], BF16).ap()
    cout3 = dtn("cout3", [8 * C.DM, C.ES], BF16).ap()
    cin4 = dtn("cin4", [512, C.DO], BF16).ap()
    cout4 = dtn("cout4", [8 * 512, C.DO], BF16).ap()
    RG8 = [[0, 1, 2, 3, 4, 5, 6, 7]]
    RGP = [[0, 4], [1, 5], [2, 6], [3, 7]]

    top = contextlib.ExitStack()
    live = {}

    def mksb(st_, name, p, f, dty, side=None):
        h = nc.sbuf_tensor(name, [p, f], dty, side=side, unsafe_allow_realloc=True)
        t = h.__enter__()
        live[name] = h
        return t

    def fre(*names):
        for n in names:
            live.pop(n).__exit__(None, None, None)

    def mkps(st_, name, p, f, dty=F32):
        return st_.enter_context(nc.psum_tensor(name, [p, f], dty))

    def mksem(st_, name):
        return st_.enter_context(nc.semaphore(name))

    convw = mksb(top, "convw", 128, C.DT * 4, F32)
    convb = mksb(top, "convb", 128, C.DT, F32)
    dtb = mksb(top, "dtb", 128, C.DT, F32)
    alog = mksb(top, "alog", 128, C.DT * C.NST, F32)
    Amat = mksb(top, "Amat", 128, C.DT * C.NST, F32)
    dvs = mksb(top, "dvs", 128, C.DT, F32)
    normg = mksb(top, "normg", 128, C.KC, F32)
    normb = mksb(top, "normb", 128, C.KC, F32)
    sgsc = mksb(top, "sgsc", 128, C.KC, F32)
    sbsc = mksb(top, "sbsc", 128, C.KC, F32)
    bgT = mksb(top, "bgT", 128, 2 * C.FT, F32)
    buT = mksb(top, "buT", 128, 2 * C.FT, F32)
    eyeb = mksb(top, "eyeb", 128, 128, BF16)
    eyef = mksb(top, "eyef", 128, 128, F32)
    ones1 = mksb(top, "ones1", 128, 1, BF16)
    scale_bc = mksb(top, "scale_bc", 128, 1, F32)
    tiny = mksb(top, "tiny", 128, 64, F32)
    csts = mksb(top, "csts", 128, 2, F32)
    mrow = mksb(top, "mrow", 128, max(C.LH, C.ES), BF16)
    rrow = mksb(top, "rrow", 128, max(C.LH, C.ES), BF16)
    row1 = mksb(top, "row1", 1, 2 * max(C.LH, C.ES), F32)

    psA = mkps(top, "psA", 128, 512)
    psB = mkps(top, "psB", 128, 512)
    psSa = mkps(top, "psSa", 128, 512)
    psSb = mkps(top, "psSb", 128, 512)
    psTa = mkps(top, "psTa", 128, 512, BF16)
    psTb = mkps(top, "psTb", 128, 512, BF16)
    psTf = mkps(top, "psTf", 128, 512)

    sg = mksem(top, "sg")
    sh = mksem(top, "sh")
    st = mksem(top, "st")
    sv = mksem(top, "sv")
    ss = mksem(top, "ss")
    scc = mksem(top, "scc")
    ph = mksem(top, "ph")
    pipe_names = ["q1", "q2", "q3", "q4", "q5", "q6", "q7", "q8",
                  "dg0", "dg1", "dg2", "dg3", "dg4", "dg5", "dg6"]
    Q = {n: mksem(top, n) for n in pipe_names}
    qc = {n: 0 for n in pipe_names}

    cnt = {id(sg): 0, id(sh): 0, id(st): 0, id(sv): 0, id(ss): 0, id(scc): 0}
    nsm = {}
    nbar = [0]

    NSD = 14
    SD = [mksem(top, f"sd{i}") for i in range(NSD)]
    sdc = [0] * NSD
    sdbar = [0] * NSD
    qbar = {n: 0 for n in pipe_names}
    sdp = [0]
    srace = {n: mksem(top, f"sr_{n}") for n in
             ("Vector", "Scalar", "Tensor", "Sync", "GpSimd", "Pool")}
    cnt_sr = {n: 0 for n in srace}

    class Stream:
        def __init__(self, npar):
            self.sems = [sdp[0] + p for p in range(npar)]
            sdp[0] += npar
            assert sdp[0] <= NSD, "stream pool exhausted in phase window"
            self.npar = npar
            self.val = {}
            self._preset = False

        def preset(self, nsteps, per=1):
            shadow = {s: sdc[s] for s in self.sems}
            for k in range(nsteps):
                si = self.sems[k % self.npar]
                shadow[si] += 16 * per
                self.val[k] = shadow[si]
            self._preset = True

        def inc(self, instr, k):
            si = self.sems[k % self.npar]
            sdc[si] += 16
            if self._preset:
                assert sdc[si] <= self.val[k], "stream preset mismatch"
            else:
                self.val[k] = sdc[si]
            instr.then_inc(SD[si], 16)

        def wait(self, eng, k):
            eng.wait_ge(SD[self.sems[k % self.npar]], self.val[k])

        def wait_all(self, eng):
            for s in self.sems:
                if sdc[s] > sdbar[s]:
                    eng.wait_ge(SD[s], sdc[s])

    blk = top.enter_context(nc.Block())

    def inc(instr, s, n=1):
        instr.then_inc(s, n)
        cnt[id(s)] += n
        return cnt[id(s)]

    def qinc(instr, name):
        instr.then_inc(Q[name], 1)
        qc[name] += 1
        return qc[name]

    def qinc16(instr, name):
        instr.then_inc(Q[name], 16)
        qc[name] += 16
        return qc[name]

    def on(which, fn):
        getattr(blk, which)(fn)

    def ng(eng, instr=None, sem=None, val=None):
        if not races:
            return
        if val is None:
            en = next((n for n in srace if n in type(eng).__name__), "Sync")
            cnt_sr[en] += 1
            instr.then_inc(srace[en], 1)
            sem, val = srace[en], cnt_sr[en]
        eng.wait_ge(sem, val)

    def dbg_dump(name, ap_fn, sem_, val_fn):
        if not dbg:
            return
        p, f = ap_fn().partition_size(), ap_fn().free_size()
        od = dtn("dbg_" + name, [p, f], ap_fn().dtype, kind="ExternalOutput").ap()

        def _d(s):
            s.wait_ge(sem_, val_fn())
            inc(s.dma_start(out=od[:], in_=ap_fn()), sh, 16)
        on("sync", _d)

    def barrier():
        nbar[0] += 1
        t_ = 5 * nbar[0]
        shv, sgv = cnt[id(sh)], cnt[id(sg)]
        sdv = list(sdc)
        qv = dict(qc)
        for wn in ("tensor", "vector", "scalar", "sync", "gpsimd"):
            def f(eng, _t=t_, _wn=wn):
                if _wn == "sync":
                    eng.wait_ge(sh, shv)
                    eng.wait_ge(scc, cnt[id(scc)])
                    for i_ in range(NSD):
                        if sdv[i_] > sdbar[i_]:
                            eng.wait_ge(SD[i_], sdv[i_])
                    for qn_ in pipe_names:
                        if qv[qn_] > qbar[qn_]:
                            eng.wait_ge(Q[qn_], qv[qn_])
                if _wn == "gpsimd":
                    eng.wait_ge(sg, sgv)
                eng.drain()
                eng.sem_inc(ph, 1)
                eng.wait_ge(ph, _t)
            on(wn, f)
        sdbar[:] = sdv
        qbar.update(qv)
        sdp[0] = 0

    ph0 = contextlib.ExitStack()
    xprojT = mksb(top, "xprojT", 128, C.DT * C.RK2, BF16)
    dtwT = mksb(top, "dtwT", C.R, C.DH, BF16)
    xT = mksb(top, "xT", 128, C.KC * C.L, BF16)
    inwT = mksb(top, "inwT", 128, C.KC * 2 * C.DH, BF16)

    def p0_g(g):
        gq = C.DM // 4
        for qn in range(4):
            qinc16(g.dma_start(out=xb[:, qn * gq:(qn + 1) * gq],
                               in_=i_x[:, qn * gq:(qn + 1) * gq]), "dg0")
        qinc16(g.dma_start(out=inwb[:], in_=i_inw[:]), "dg0")
        qinc16(g.dma_start(out=xprojTb[:], in_=i_xprojT[:]), "q5")
        qinc16(g.dma_start(out=dtwTb[:], in_=i_dtwT[:]), "q5")
        qinc16(g.dma_start(out=xhb[:], in_=i_xh[:]), "q6")
        for ds_, sr in ((outwTb, i_outwT), (seTb, i_seT),
                        (wgb, i_wg), (wub, i_wu), (wdb, i_wd)):
            qinc16(g.dma_start(out=ds_[:], in_=sr[:]), "dg1")
    on("gpsimd", p0_g)

    def p0_s(s):
        for sr, ds_ in ((i_convw, convw), (i_convb, convb), (i_dtb, dtb),
                        (i_alog, alog), (i_dv, dvs), (i_normg, normg),
                        (i_normb, normb), (i_bgT, bgT), (i_buT, buT),
                        (i_eyeb, eyeb), (i_eyef, eyef)):
            inc(s.dma_start(out=ds_[:], in_=sr[:]), sh, 16)
        inc(s.dma_start(out=scale_bc[:], in_=i_scale[:].broadcast_to((128, 1))), sh, 16)
        inc(s.dma_start(out=tiny[:, 0:C.KC], in_=i_slotg[:]), sh, 16)
        inc(s.dma_start(out=tiny[:, C.KC:2 * C.KC], in_=i_slotb[:]), sh, 16)
        s.wait_ge(Q["dg0"], 80)
        for kc in range(C.KC):
            inc(s.dma_start_transpose(out=xT[:, kc * C.L:(kc + 1) * C.L],
                                      in_=xb[:, kc * 128:(kc + 1) * 128]), sh, 16)
        for kc in range(C.KC):
            inc(s.dma_start_transpose(out=inwT[:, kc * 2 * C.DH:(kc + 1) * 2 * C.DH],
                                      in_=inwb[:, kc * 128:(kc + 1) * 128]), sh, 16)
    on("sync", p0_s)
    P0H = cnt[id(sh)]

    REG = {}

    def p0_s2(s):
        pid = s.partition_id()
        REG["pid"] = pid
        REG["pxor"] = s.compute_val((pid + 4) % 8)
        REG["jj"] = s.compute_val(pid // 4)
        REG["breg"] = s.compute_val(pid % 4)
        REG["jcol"] = s.compute_val((pid // 4) * C.LH)
    on("sync", p0_s2)

    def p0_sc(sc):
        sc.wait_ge(sh, P0H)
        inc(sc.activation(Amat[:], alog[:], AF.Exp), ss)
    on("scalar", p0_sc)
    p0ss = cnt[id(ss)]

    def p0_v(v):
        v.memset(ones1[:], 1.0)
        v.memset(csts[:, 0:1], EPS)
        v.memset(csts[:, 1:2], -1.0)
        v.wait_ge(sh, P0H)
        v.tensor_scalar_mul(sgsc[:], tiny[:, 0:C.KC], scale_bc[:, 0:1])
        v.tensor_scalar_mul(sbsc[:], tiny[:, C.KC:2 * C.KC], scale_bc[:, 0:1])
        v.wait_ge(ss, p0ss)
        inc(v.tensor_scalar_mul(Amat[:], Amat[:], -1.0), sv)
    on("vector", p0_v)

    barrier()

    ev1 = mksb(top, "ev1", 128, 4 * 512, BF16)
    P1SS0 = cnt[id(ss)]
    sp1 = Stream(4)
    sp1.preset(2 * C.DT * C.NW, 1)

    def p1_t(t):
        t.wait_ge(sh, P0H)
        i = 0
        for ot in range(2 * C.DT):
            for q in range(C.NW):
                pst = psA if i % 2 == 0 else psB
                if i >= 2:
                    t.wait_ge(ss, P1SS0 + i - 1)
                for kc in range(C.KC):
                    mm = t.matmul(pst[:, 0:C.W],
                                  inwT[:, kc * 2 * C.DH + ot * 128:kc * 2 * C.DH + (ot + 1) * 128],
                                  xT[:, kc * C.L + q * C.W:kc * C.L + (q + 1) * C.W],
                                  start=(kc == 0), stop=(kc == C.KC - 1))
                nsm[("p1m", i)] = inc(mm, st)
                i += 1
    on("tensor", p1_t)

    def p1_sc(sc):
        i = 0
        for ot in range(2 * C.DT):
            for q in range(C.NW):
                pst = psA if i % 2 == 0 else psB
                sc.wait_ge(st, nsm[("p1m", i)])
                if i >= 4:
                    sp1.wait(sc, i - 4)
                so = (i % 4) * 512
                if ot < C.DT:
                    e = sc.copy(ev1[:, so:so + C.W], pst[:, 0:C.W])
                else:
                    e = sc.activation(ev1[:, so:so + C.W], pst[:, 0:C.W], AF.Silu)
                nsm[("p1e", i)] = inc(e, ss)
                i += 1
    on("scalar", p1_sc)

    def p1_s(s):
        i = 0
        for ot in range(2 * C.DT):
            for q in range(C.NW):
                s.wait_ge(ss, nsm[("p1e", i)])
                so = (i % 4) * 512
                dst = xx_h if ot < C.DT else sres_h
                rt = ot if ot < C.DT else ot - C.DT
                e = s.dma_start(out=dst[rt * 128:(rt + 1) * 128, q * C.W:(q + 1) * C.W],
                                in_=ev1[:, so:so + C.W])
                sp1.inc(e, i)
                i += 1
    on("sync", p1_s)

    def p0_s_late(s):
        s.wait_ge(Q["q5"], 32)
        for dti in range(C.DT):
            inc(s.dma_start(out=xprojT[:, dti * C.RK2:(dti + 1) * C.RK2],
                            in_=xprojTb[dti * 128:(dti + 1) * 128, :]), sh, 16)
        inc(s.dma_start(out=dtwT[:], in_=dtwTb[:]), sh, 16)
    on("sync", p0_s_late)

    barrier()
    fre("ev1", "inwT", "xT")

    uu = mksb(top, "uu", 128, 2 * C.L, BF16)
    xxs = mksb(top, "xxs", 128, 2 * (4 + C.L), BF16)

    def p2_v0(v):
        v.memset(xxs[:, 0:4], 0.0)
        inc(v.memset(xxs[:, 4 + C.L:8 + C.L], 0.0), sv)
    on("vector", p2_v0)
    p2pad = cnt[id(sv)]
    P2VB = cnt[id(sv)]

    sx2 = Stream(2)
    su2 = Stream(2)
    sx2.preset(C.DT, 1)
    su2.preset(C.DT, 1)

    def p2_v(v):
        for dti in range(C.DT):
            base = (dti % 2) * (4 + C.L) + 4
            so = (dti % 2) * C.L
            tmp = uu[:, so:so + C.L]
            sx2.wait(v, dti)
            if dti >= 2:
                su2.wait(v, dti - 2)
            e = v.tensor_scalar_mul(tmp, xxs[:, base - 2:base - 2 + C.L],
                                    convw[:, dti * 4 + 1:dti * 4 + 2])
            ng(v, e)
            e = v.scalar_tensor_tensor(tmp, xxs[:, base - 3:base - 3 + C.L],
                                       convw[:, dti * 4 + 0:dti * 4 + 1], tmp, OP.mult, OP.add)
            ng(v, e)
            e = v.scalar_tensor_tensor(tmp, xxs[:, base - 1:base - 1 + C.L],
                                       convw[:, dti * 4 + 2:dti * 4 + 3], tmp, OP.mult, OP.add)
            ng(v, e)
            e = v.scalar_tensor_tensor(tmp, xxs[:, base:base + C.L],
                                       convw[:, dti * 4 + 3:dti * 4 + 4], tmp, OP.mult, OP.add)
            nsm[("p2c", dti)] = inc(e, sv)
    on("vector", p2_v)

    def p2_sc(sc):
        for dti in range(C.DT):
            so = (dti % 2) * C.L
            tmp = uu[:, so:so + C.L]
            sc.wait_ge(sv, nsm[("p2c", dti)])
            nsm[("p2u", dti)] = inc(
                sc.activation(tmp, tmp, AF.Silu, bias=convb[:, dti:dti + 1]), ss)
    on("scalar", p2_sc)

    def p2_sync(s):
        s.wait_ge(sv, p2pad)
        for k in range(C.DT + 2):
            if k < C.DT:
                if k >= 2:
                    s.wait_ge(sv, P2VB + k - 1)
                e = s.dma_start(out=xxs[:, (k % 2) * (4 + C.L) + 4:
                                        (k % 2) * (4 + C.L) + 4 + C.L],
                                in_=xx_h[k * 128:(k + 1) * 128, :])
                sx2.inc(e, k)
            if k >= 2:
                s.wait_ge(ss, nsm[("p2u", k - 2)])
                e = s.dma_start(out=u_h[(k - 2) * 128:(k - 1) * 128, :],
                                in_=uu[:, ((k - 2) % 2) * C.L:((k - 2) % 2) * C.L + C.L])
                su2.inc(e, k - 2)
    on("sync", p2_sync)

    barrier()
    fre("xxs", "uu")

    xdbl = mksb(top, "xdbl", C.RK2, C.L, F32)
    xdbl2 = mksb(top, "xdbl2", C.RK2, C.L, F32)
    ut3 = mksb(top, "ut3", 128, 2 * C.DT * C.W, BF16)
    P3SV0 = cnt[id(sv)]
    s3l = Stream(2)

    def p3_s(s):
        for q in range(C.NW):
            so = (q % 2) * C.DT * C.W
            if q >= 2:
                s.wait_ge(sv, P3SV0 + q - 1)
            for dti in range(C.DT):
                e = s.dma_start(out=ut3[:, so + dti * C.W:so + (dti + 1) * C.W],
                                in_=u_h[dti * 128:(dti + 1) * 128, q * C.W:(q + 1) * C.W])
                s3l.inc(e, q)
    on("sync", p3_s)

    def p3_t(t):
        for q in range(C.NW):
            so = (q % 2) * C.DT * C.W
            pst = psA if q % 2 == 0 else psB
            s3l.wait(t, q)
            if q >= 2:
                t.wait_ge(sv, P3SV0 + q - 1)
            for dti in range(C.DT):
                mm = t.matmul(pst[0:C.RK2, 0:C.W],
                              xprojT[:, dti * C.RK2:(dti + 1) * C.RK2],
                              ut3[:, so + dti * C.W:so + (dti + 1) * C.W],
                              start=(dti == 0), stop=(dti == C.DT - 1))
            nsm[("p3m", q)] = inc(mm, st)
    on("tensor", p3_t)

    def p3_v(v):
        for q in range(C.NW):
            pst = psA if q % 2 == 0 else psB
            v.wait_ge(st, nsm[("p3m", q)])
            e = v.tensor_copy(xdbl[:, q * C.W:(q + 1) * C.W], pst[0:C.RK2, 0:C.W])
            nsm[("p3e", q)] = inc(e, sv)
    on("vector", p3_v)

    def p3_s2(s):
        s.wait_ge(sv, nsm[("p3e", C.NW - 1)])
        nsm["p3stage"] = inc(s.dma_start(out=cin0[:], in_=xdbl[:]), sh, 16)
    on("sync", p3_s2)

    def p3_g(g):
        g.wait_ge(sh, nsm["p3stage"])
        nsm["cc0"] = inc(g.collective_compute(
            "AllReduce", OP.add, replica_groups=RGP, ins=[cin0[:]], outs=[cout0[:]]), scc)
    on("gpsimd", p3_g)

    def p3_s3(s):
        s.wait_ge(scc, nsm["cc0"])
        nsm["p3pl"] = inc(
            s.dma_start(out=xdbl[:], in_=cout0[:]), sh, 16)
    on("sync", p3_s3)

    xdblb = mksb(top, "xdblb", C.R, C.L, BF16)

    def p3_v2(v):
        v.wait_ge(sh, nsm["p3pl"])
        nsm["p3add"] = inc(v.tensor_copy(xdblb[:], xdbl[0:C.R, :]), sv)
    on("vector", p3_v2)

    def p3_g2(g):
        g.wait_ge(sv, nsm["p3add"])
        qinc16(g.dma_start(out=brow_h[:], in_=xdbl[C.R:C.RK2, :]), "dg2")
    on("gpsimd", p3_g2)

    barrier()

    sc1 = mksb(top, "sc1", 128, C.W, F32)
    sc2 = mksb(top, "sc2", 128, C.W, F32)
    sc3 = mksb(top, "sc3", 128, 2 * C.W, BF16)
    P4SV0a = cnt[id(sv)]
    s4d = Stream(2)
    s4d.preset(C.DT * C.NW, 1)

    def p4_t(t):
        i = 0
        for dti in range(C.DT):
            for q in range(C.NW):
                pst = psA if i % 2 == 0 else psB
                if i >= 2:
                    t.wait_ge(sv, P4SV0a + 2 * (i - 1) + 1)
                mm = t.matmul(pst[:, 0:C.W], dtwT[:, dti * 128:(dti + 1) * 128],
                              xdblb[:, q * C.W:(q + 1) * C.W], start=True, stop=True)
                nsm[("p4m", i)] = inc(mm, st)
                i += 1
    on("tensor", p4_t)

    def p4_v_joint(v):
        i = 0
        for dti in range(C.DT):
            for q in range(C.NW):
                pst = psA if i % 2 == 0 else psB
                v.wait_ge(st, nsm[("p4m", i)])
                if i >= 1:
                    ng(v, sem=sv, val=nsm[("p4d", i - 1)])
                e = v.tensor_scalar_add(sc1[:, 0:C.W], pst[:, 0:C.W], dtb[:, dti:dti + 1])
                nsm[("p4z", i)] = inc(e, sv)
                v.wait_ge(ss, P4SS0[0] + i + 1)
                if i >= 2:
                    s4d.wait(v, i - 2)
                so3 = (i % 2) * C.W
                e = v.tensor_scalar_max(sc1[:, 0:C.W], sc1[:, 0:C.W], 0.0)
                ng(v, e)
                e = v.tensor_add(sc3[:, so3:so3 + C.W], sc1[:, 0:C.W], sc2[:, 0:C.W])
                nsm[("p4d", i)] = inc(e, sv)
                i += 1
    P4SS0 = [None]
    P4SV0 = [None]

    def p4_sc2(sc):
        i = 0
        for dti in range(C.DT):
            for q in range(C.NW):
                sc.wait_ge(sv, P4SV0[0] + 2 * i + 1)
                e = sc.activation(sc2[:, 0:C.W], sc1[:, 0:C.W], AF.Abs)
                ng(sc, e, sem=ss)
                e = sc.activation(sc2[:, 0:C.W], sc2[:, 0:C.W], AF.Exp, scale=csts[:, 1:2])
                ng(sc, e, sem=ss)
                inc(sc.activation(sc2[:, 0:C.W], sc2[:, 0:C.W], AF.Ln, bias=1.0), ss)
                i += 1

    P4SV0[0] = cnt[id(sv)]
    P4SS0[0] = cnt[id(ss)]
    on("scalar", p4_sc2)
    on("vector", p4_v_joint)

    def p4_s(s):
        i = 0
        for dti in range(C.DT):
            for q in range(C.NW):
                s.wait_ge(sv, nsm[("p4d", i)])
                so3 = (i % 2) * C.W
                e = s.dma_start(out=delta_h[dti * 128:(dti + 1) * 128, q * C.W:(q + 1) * C.W],
                                in_=sc3[:, so3:so3 + C.W])
                s4d.inc(e, i)
                i += 1
    on("sync", p4_s)

    barrier()
    fre("sc3", "sc2", "sc1", "xdblb", "ut3", "xdbl2", "xdbl", "dtwT", "xprojT")

    d_lq = mksb(top, "d_lq", 128, C.DT * C.LC, BF16)
    du_lq = mksb(top, "du_lq", 128, C.DT * C.LC, BF16)
    y3_lq = mksb(top, "y3_lq", 128, C.DT * C.LC, BF16)
    sr_lq = mksb(top, "sr_lq", 128, C.DT * C.LC, BF16)
    u_lq = mksb(top, "u_lq", 128, C.DT * C.LC, BF16)
    bsl = mksb(top, "bsl", 128, 2 * C.LC, BF16)
    csl = mksb(top, "csl", 128, 2 * C.LC, BF16)
    dasl = mksb(top, "dasl", 128, 2 * C.DT * C.LC, BF16)
    dbsl = mksb(top, "dbsl", 128, C.DT * C.LC, BF16)
    hsl = mksb(top, "hsl", 128, C.DT * C.LC, BF16)
    carry = mksb(top, "carry", 128, C.DT * C.NST, F32)
    outwT = mksb(top, "outwT", 128, C.DT * C.DO, BF16)
    y3s = mksb(top, "y3s", 128, C.DT * min(C.W, C.LC), BF16)
    evb = mksb(top, "evb", 128, 2 * 512, BF16)

    def r3(ap):
        return ap.rearrange("p (t l) -> p t l", t=C.DT)

    WL = min(C.W, C.LC)
    NWL = C.LC // WL
    NGQ = C.NLC * NWL
    Q7B = qc["q7"]
    Q8B = qc["q8"]
    for _i in range(NGQ * C.KC):
        nsm[("p7e", _i)] = Q8B + _i + 1
    sA6 = Stream(2)
    sB6 = Stream(2)
    sC6 = Stream(2)
    sSP = Stream(2)
    sYL = Stream(1)
    sYL.preset(C.NLC * (C.LC // min(C.W, C.LC)), C.DT)
    sST = Stream(2)
    sST.preset(NGQ * C.KC, 1)
    s7w = Stream(1)

    def p7_w(s):
        s.wait_ge(Q["dg1"], 80)
        for dti in range(C.DT):
            s7w.inc(s.dma_start(out=outwT[:, dti * C.DO:(dti + 1) * C.DO],
                                in_=outwTb[dti * 128:(dti + 1) * 128, :]), 0)
    on("sync", p7_w)

    for lq in range(C.NLC):
        SS0 = [None]

        def p6_load(s, _lq=lq):
            s.wait_ge(Q["dg2"], 16)
            if _lq >= 1:
                s.wait_ge(ss, nsm[("p6dae", _lq - 1)])
                s.wait_ge(sv, nsm[("p6pro", _lq - 1)])
            for dti in range(C.DT):
                sA6.inc(s.dma_start(out=d_lq[:, dti * C.LC:(dti + 1) * C.LC],
                                    in_=delta_h[dti * 128:(dti + 1) * 128,
                                                _lq * C.LC:(_lq + 1) * C.LC]), _lq)
            for dti in range(C.DT):
                sA6.inc(s.dma_start(out=u_lq[:, dti * C.LC:(dti + 1) * C.LC],
                                    in_=u_h[dti * 128:(dti + 1) * 128,
                                            _lq * C.LC:(_lq + 1) * C.LC]), _lq)
            for n in range(C.NST):
                sl = n % 2
                gn = _lq * C.NST + n
                if gn >= 2:
                    s.wait_ge(Q["q7"], Q7B + gn - 1)
                sC6.inc(s.dma_start(out=bsl[:, sl * C.LC:(sl + 1) * C.LC],
                                    in_=brow_h[n:n + 1, _lq * C.LC:(_lq + 1) * C.LC]
                                    .broadcast_to((128, C.LC))), gn)
                e = s.dma_start(out=csl[:, sl * C.LC:(sl + 1) * C.LC],
                                in_=brow_h[C.NST + n:C.NST + n + 1,
                                           _lq * C.LC:(_lq + 1) * C.LC]
                                .broadcast_to((128, C.LC)))
                sC6.inc(e, gn)
            if _lq >= 1:
                s.wait_ge(sv, nsm[("p6y3", _lq - 1)])
            for dti in range(C.DT):
                sB6.inc(s.dma_start(out=sr_lq[:, dti * C.LC:(dti + 1) * C.LC],
                                    in_=sres_h[dti * 128:(dti + 1) * 128,
                                               _lq * C.LC:(_lq + 1) * C.LC]), _lq)
        on("sync", p6_load)

        def p6_da(sc, _lq=lq):
            SS0[0] = cnt[id(ss)]
            sA6.wait(sc, _lq)
            for n in range(C.NST):
                sl = n % 2
                gn = _lq * C.NST + n
                if gn >= 2:
                    sc.wait_ge(Q["q7"], Q7B + gn - 1)
                for dti in range(C.DT):
                    e = sc.activation(dasl[:, (sl * C.DT + dti) * C.LC:
                                           (sl * C.DT + dti + 1) * C.LC],
                                      d_lq[:, dti * C.LC:(dti + 1) * C.LC], AF.Exp,
                                      scale=Amat[:, dti * C.NST + n:dti * C.NST + n + 1])
                    inc(e, ss)
            nsm[("p6dae", _lq)] = cnt[id(ss)]
        on("scalar", p6_da)

        if lq >= 1:
            def p7_sc(sc, _lq=lq - 1):
                for qw in range(NWL):
                    gq = _lq * NWL + qw
                    for ot in range(C.KC):
                        i = gq * C.KC + ot
                        pst = psA if i % 2 == 0 else psB
                        sc.wait_ge(st, nsm[("p7m", i)])
                        if i >= 2:
                            sST.wait(sc, i - 2)
                        so = (i % 2) * 512
                        e = sc.copy(evb[:, so:so + WL], pst[:, 0:WL])
                        nsm[("p7e", i)] = qinc(e, "q8")
            on("scalar", p7_sc)

        def p6_v(v, _lq=lq):
            sA6.wait(v, _lq)
            if _lq >= 1:
                sSP.wait(v, _lq - 1)
            e = v.tensor_mul(du_lq[:], d_lq[:], u_lq[:])
            ng(v, e)
            for dti in range(C.DT):
                e = v.tensor_scalar_mul(y3_lq[:, dti * C.LC:(dti + 1) * C.LC],
                                        u_lq[:, dti * C.LC:(dti + 1) * C.LC],
                                        dvs[:, dti:dti + 1])
                ng(v, e)
            nsm[("p6pro", _lq)] = inc(v.engine_nop(), sv)
            for n in range(C.NST):
                gn = _lq * C.NST + n
                sC6.wait(v, gn)
                sl2 = (n % 2) * C.LC
                sl = (n % 2) * C.DT
                if races and gn >= 1:
                    v.wait_ge(Q["q7"], Q7B + gn)
                bbc = bsl[:, sl2:sl2 + C.LC].unsqueeze(1).broadcast_to(
                    (128, C.DT, C.LC))
                cbc = csl[:, sl2:sl2 + C.LC].unsqueeze(1).broadcast_to(
                    (128, C.DT, C.LC))
                e = v.tensor_tensor(r3(dbsl[:]), r3(du_lq[:]), bbc, OP.mult)
                ng(v, e)
                for dti in range(C.DT):
                    ci = dti * C.NST + n
                    v.wait_ge(ss, SS0[0] + n * C.DT + dti + 1)
                    init = 0.0 if _lq == 0 else carry[:, ci:ci + 1]
                    e = v.tensor_tensor_scan(
                        hsl[:, dti * C.LC:(dti + 1) * C.LC],
                        dasl[:, (sl + dti) * C.LC:(sl + dti + 1) * C.LC],
                        dbsl[:, dti * C.LC:(dti + 1) * C.LC],
                        init, OP.mult, OP.add)
                    ng(v, e)
                e = v.tensor_copy(
                    carry[:].rearrange("p (t m) -> p t m", m=C.NST)[:, :, n:n + 1],
                    r3(hsl[:])[:, :, C.LC - 1:C.LC])
                ng(v, e)
                e = v.tensor_tensor(r3(dbsl[:]), r3(hsl[:]), cbc, OP.mult)
                ng(v, e)
                e = v.tensor_add(y3_lq[:], y3_lq[:], dbsl[:])
                qinc(e, "q7")
            sB6.wait(v, _lq)
            if races:
                v.wait_ge(Q["q7"], Q7B + (_lq + 1) * C.NST)
            e = v.tensor_mul(y3_lq[:], y3_lq[:], sr_lq[:])
            nsm[("p6y3", _lq)] = inc(e, sv)
        on("vector", p6_v)

        def p7_t(t, _lq=lq):
            if _lq == 0:
                s7w.wait(t, 0)
            for qw in range(NWL):
                gq = _lq * NWL + qw
                sYL.wait(t, gq)
                for ot in range(C.KC):
                    i = gq * C.KC + ot
                    pst = psA if i % 2 == 0 else psB
                    if i >= 2:
                        t.wait_ge(Q["q8"], Q8B + i - 1)
                    for dti in range(C.DT):
                        mm = t.matmul(pst[:, 0:WL],
                                      outwT[:, dti * C.DO + ot * 128:
                                            dti * C.DO + (ot + 1) * 128],
                                      y3s[:, dti * WL:(dti + 1) * WL],
                                      start=(dti == 0), stop=(dti == C.DT - 1))
                    nsm[("p7m", i)] = inc(mm, st)
                nsm[("p7mq", gq)] = cnt[id(st)]
        on("tensor", p7_t)

        def p6_sp(g, _lq=lq):
            g.wait_ge(sv, nsm[("p6y3", _lq)])
            for dti in range(C.DT):
                sSP.inc(g.dma_start(out=y3_h[dti * 128:(dti + 1) * 128,
                                             _lq * C.LC:(_lq + 1) * C.LC],
                                    in_=y3_lq[:, dti * C.LC:(dti + 1) * C.LC]), _lq)
            for qw in range(NWL):
                gq = _lq * NWL + qw
                sSP.wait(g, _lq)
                if gq >= 1:
                    g.wait_ge(st, nsm[("p7mq", gq - 1)])
                for dti in range(C.DT):
                    sYL.inc(g.dma_start(
                        out=y3s[:, dti * WL:(dti + 1) * WL],
                        in_=y3_h[dti * 128:(dti + 1) * 128,
                                 gq * WL:(gq + 1) * WL]), gq)
                for ot in range(C.KC):
                    i = gq * C.KC + ot
                    g.wait_ge(Q["q8"], nsm[("p7e", i)])
                    so = (i % 2) * 512
                    e = g.dma_start(out=cin1[ot * 128:(ot + 1) * 128,
                                             gq * WL:(gq + 1) * WL],
                                    in_=evb[:, so:so + WL])
                    sST.inc(e, i)
        on("gpsimd", p6_sp)

    def p7_sc_last(sc, _lq=C.NLC - 1):
        for qw in range(NWL):
            gq = _lq * NWL + qw
            for ot in range(C.KC):
                i = gq * C.KC + ot
                pst = psA if i % 2 == 0 else psB
                sc.wait_ge(st, nsm[("p7m", i)])
                if i >= 2:
                    sST.wait(sc, i - 2)
                so = (i % 2) * 512
                e = sc.copy(evb[:, so:so + WL], pst[:, 0:WL])
                nsm[("p7e", i)] = qinc(e, "q8")
    on("scalar", p7_sc_last)

    def p7_g(g):
        sST.wait_all(g)
        nsm["cc1"] = inc(g.collective_compute(
            "AllReduce", OP.add, replica_groups=RGP, ins=[cin1[:]], outs=[cout1[:]]), scc)
    on("gpsimd", p7_g)

    barrier()
    fre("evb", "y3s", "outwT", "carry", "hsl", "dbsl", "dasl", "csl", "bsl",
        "u_lq", "sr_lq", "y3_lq", "du_lq", "d_lq")

    combT = mksb(top, "combT", 128, C.ET * C.LH, BF16, side="right")
    xr1 = mksb(top, "xr1", 128, C.KC * C.LH, BF16, side="right")
    xsq = mksb(top, "xsq", 128, C.KC * max(C.LH, C.ES), BF16)
    pme = mksb(top, "pme", 128, C.KC * C.LH, BF16)
    ppt = mksb(top, "ppt", 128, C.KC * C.LH, BF16)

    s8l = Stream(1)

    def p8_s0(s):
        s.wait_ge(scc, nsm["cc1"])
        jcol = REG["jcol"]
        for kc in range(C.KC):
            s8l.inc(s.dma_start_transpose(out=xr1[:, kc * C.LH:(kc + 1) * C.LH],
                                          in_=xhb[:, kc * 128:(kc + 1) * 128]), 0)
        c1r = cout1.rearrange("(k p) l -> p k l", k=C.KC)
        pme_r = pme[:].rearrange("p (k l) -> p k l", k=C.KC)
        s8l.inc(s.dma_start(out=pme_r, in_=c1r[:, :, DS(jcol, C.LH)]), 0)
    on("sync", p8_s0)

    def p8_v(v):
        s8l.wait(v, 0)
        for kc in range(C.KC):
            e = v.tensor_add(xr1[:, kc * C.LH:(kc + 1) * C.LH],
                             xr1[:, kc * C.LH:(kc + 1) * C.LH],
                             pme[:, kc * C.LH:(kc + 1) * C.LH])
        nsm["p8res"] = inc(e, sv)
    on("vector", p8_v)

    LNI = [0]

    def ln_feat(xap, outap, ncols, gain, bias, pre_sem, pre_val):
        li = LNI[0]
        LNI[0] += 1
        NWS = ncols // min(512, ncols)
        WS = min(512, ncols)
        sln = Stream(1)

        def l_sc(sc):
            sc.wait_ge(pre_sem, pre_val)
            for kc in range(C.KC):
                e = sc.activation(xsq[:, kc * ncols:(kc + 1) * ncols],
                                  xap[:, kc * ncols:(kc + 1) * ncols], AF.Square)
            nsm[("lnsq", li)] = inc(e, ss)
        on("scalar", l_sc)

        q1b = qc["q1"]

        def l_t(t):
            t.wait_ge(ss, nsm[("lnsq", li)])
            for w in range(NWS):
                if w >= 1:
                    t.wait_ge(Q["q1"], q1b + w)
                pa = psSa[0:1, 0:WS]
                pb = psSb[0:1, 0:WS]
                for kc in range(C.KC):
                    t.matmul(pa, ones1[:], xap[:, kc * ncols + w * WS:kc * ncols + (w + 1) * WS],
                             start=(kc == 0), stop=(kc == C.KC - 1))
                for kc in range(C.KC):
                    mm = t.matmul(pb, ones1[:],
                                  xsq[:, kc * ncols + w * WS:kc * ncols + (w + 1) * WS],
                                  start=(kc == 0), stop=(kc == C.KC - 1))
                nsm[("lnst", li, w)] = inc(mm, st)
        on("tensor", l_t)

        def l_v(v):
            for w in range(NWS):
                v.wait_ge(st, nsm[("lnst", li, w)])
                pa = psSa[0:1, 0:WS]
                pb = psSb[0:1, 0:WS]
                e = v.tensor_scalar_mul(row1[0:1, w * WS:(w + 1) * WS], pa, 1.0 / C.DM)
                ng(v, e)
                e = v.tensor_scalar_mul(row1[0:1, ncols + w * WS:ncols + (w + 1) * WS],
                                        pb, 1.0 / C.DM)
                qinc(e, "q1")
                ng(v, sem=Q["q1"], val=qc["q1"])
            t = inc(v.tensor_mul(mrow[0:1, 0:ncols], row1[0:1, 0:ncols],
                                 row1[0:1, 0:ncols]), sv)
            v.wait_ge(sv, t)
            e = v.tensor_sub(row1[0:1, ncols:2 * ncols], row1[0:1, ncols:2 * ncols],
                             mrow[0:1, 0:ncols])
            nsm[("lnvar", li)] = inc(e, sv)
        on("vector", l_v)

        def l_sc2(sc):
            sc.wait_ge(sv, nsm[("lnvar", li)])
            e = sc.activation(row1[0:1, ncols:2 * ncols], row1[0:1, ncols:2 * ncols],
                              AF.Sqrt, bias=csts[0:1, 0:1])
            nsm[("lnsd", li)] = inc(e, ss)
        on("scalar", l_sc2)

        def l_v2(v):
            v.wait_ge(ss, nsm[("lnsd", li)])
            e = v.reciprocal(row1[0:1, ncols:2 * ncols], row1[0:1, ncols:2 * ncols])
            nsm[("lnr", li)] = inc(e, sv)
        on("vector", l_v2)

        def l_s(s):
            s.wait_ge(sv, nsm[("lnr", li)])
            sln.inc(s.dma_start(out=murow_h[0:1, 0:ncols], in_=row1[0:1, 0:ncols]), 0)
            e = s.dma_start(out=murow_h[1:2, 0:ncols], in_=row1[0:1, ncols:2 * ncols])
            sln.inc(e, 0)
        on("sync", l_s)

        dgn = "dg3" if li == 0 else "dg4"

        def l_g(g):
            sln.wait(g, 0)
            qinc16(g.dma_start(out=mrow[:, 0:ncols],
                               in_=murow_h[0:1, 0:ncols].broadcast_to((128, ncols))), dgn)
            qinc16(g.dma_start(out=rrow[:, 0:ncols],
                               in_=murow_h[1:2, 0:ncols].broadcast_to((128, ncols))), dgn)
        on("gpsimd", l_g)

        def l_v3(v):
            v.wait_ge(Q[dgn], 32)
            for kc in range(C.KC):
                e = v.tensor_sub(outap[:, kc * ncols:(kc + 1) * ncols],
                                 xap[:, kc * ncols:(kc + 1) * ncols], mrow[:, 0:ncols])
                ng(v, e)
                e = v.tensor_mul(outap[:, kc * ncols:(kc + 1) * ncols],
                                 outap[:, kc * ncols:(kc + 1) * ncols], rrow[:, 0:ncols])
                if gain is not None:
                    ng(v, e)
                    e = v.tensor_scalar(outap[:, kc * ncols:(kc + 1) * ncols],
                                        outap[:, kc * ncols:(kc + 1) * ncols],
                                        gain[:, kc:kc + 1], bias[:, kc:kc + 1],
                                        OP.mult, OP.add)
            nsm[("lndone", li)] = inc(e, sv)
            return
        on("vector", l_v3)
        return li

    li1 = ln_feat(xr1, xr1, C.LH, None, None, sv, nsm["p8res"])
    barrier()
    fre("ppt", "pme")

    s9p = Stream(1)

    def p9_spill(s):
        s.wait_ge(sv, nsm[("lndone", li1)])
        for kc in range(C.KC):
            e = s.dma_start(out=xr1_h[kc * 128:(kc + 1) * 128, :],
                            in_=xr1[:, kc * C.LH:(kc + 1) * C.LH])
            s9p.inc(e, 0)
    on("sync", p9_spill)

    def p9_gate(v):
        s9p.wait(v, 0)
        inc(v.engine_nop(), sv)
    on("vector", p9_gate)
    li2 = ln_feat(xr1, xr1, C.LH, normg, normb, sv, cnt[id(sv)])
    xln = xr1
    barrier()

    seS = mksb(top, "seS", 128, C.KC * C.ES, BF16, side="right")

    sseL = Stream(1)
    sseM = Stream(1)

    def pse_s(s):
        s.wait_ge(Q["dg1"], 80)
        for kc in range(C.KC):
            sseL.inc(s.dma_start(out=seS[:, kc * C.ES:(kc + 1) * C.ES],
                                 in_=seTb[kc * 128:(kc + 1) * 128, :]), 0)
    on("sync", pse_s)

    sesq = xsq

    def pse_sq(sc):
        sseL.wait(sc, 0)
        for kc in range(C.KC):
            e = sc.activation(sesq[:, kc * C.ES:(kc + 1) * C.ES],
                              seS[:, kc * C.ES:(kc + 1) * C.ES], AF.Square)
        nsm["psesq"] = inc(e, ss)
    on("scalar", pse_sq)

    q1c = qc["q1"]

    def pse_t(t):
        t.wait_ge(ss, nsm["psesq"])
        for w in range(C.ES // 512):
            if w >= 1:
                t.wait_ge(Q["q1"], q1c + w)
            pa = psSa[0:1, 0:512]
            pb = psSb[0:1, 0:512]
            for kc in range(C.KC):
                t.matmul(pa, ones1[:], seS[:, kc * C.ES + w * 512:kc * C.ES + (w + 1) * 512],
                         start=(kc == 0), stop=(kc == C.KC - 1))
            for kc in range(C.KC):
                mm = t.matmul(pb, ones1[:], sesq[:, kc * C.ES + w * 512:kc * C.ES + (w + 1) * 512],
                              start=(kc == 0), stop=(kc == C.KC - 1))
            nsm[("psest", w)] = inc(mm, st)
    on("tensor", pse_t)

    def pse_v(v):
        for w in range(C.ES // 512):
            v.wait_ge(st, nsm[("psest", w)])
            pa = psSa[0:1, 0:512]
            pb = psSb[0:1, 0:512]
            e = v.tensor_scalar_mul(row1[0:1, w * 512:(w + 1) * 512], pa, 1.0 / C.DM)
            ng(v, e)
            e = v.tensor_scalar_mul(row1[0:1, C.ES + w * 512:C.ES + (w + 1) * 512],
                                    pb, 1.0 / C.DM)
            qinc(e, "q1")
            ng(v, sem=Q["q1"], val=qc["q1"])
        e = v.tensor_mul(mrow[0:1, 0:C.ES], row1[0:1, 0:C.ES], row1[0:1, 0:C.ES])
        ng(v, e)
        e = v.tensor_sub(row1[0:1, C.ES:2 * C.ES], row1[0:1, C.ES:2 * C.ES],
                         mrow[0:1, 0:C.ES])
        nsm["psevar"] = inc(e, sv)
    on("vector", pse_v)

    def pse_sc2(sc):
        sc.wait_ge(sv, nsm["psevar"])
        e = sc.activation(row1[0:1, C.ES:2 * C.ES], row1[0:1, C.ES:2 * C.ES],
                          AF.Sqrt, bias=csts[0:1, 0:1])
        nsm["psesd"] = inc(e, ss)
    on("scalar", pse_sc2)

    def pse_v2(v):
        v.wait_ge(ss, nsm["psesd"])
        e = v.reciprocal(row1[0:1, C.ES:2 * C.ES], row1[0:1, C.ES:2 * C.ES])
        nsm["pser"] = inc(e, sv)
    on("vector", pse_v2)

    def pse_s2(s):
        s.wait_ge(sv, nsm["pser"])
        sseM.inc(s.dma_start(out=murow_h[0:1, 0:C.ES], in_=row1[0:1, 0:C.ES]), 0)
        e = s.dma_start(out=murow_h[1:2, 0:C.ES], in_=row1[0:1, C.ES:2 * C.ES])
        sseM.inc(e, 0)
    on("sync", pse_s2)

    def pse_g2(g):
        sseM.wait(g, 0)
        qinc16(g.dma_start(out=mrow[:, 0:C.ES],
                           in_=murow_h[0:1, 0:C.ES].broadcast_to((128, C.ES))), "dg5")
        qinc16(g.dma_start(out=rrow[:, 0:C.ES],
                           in_=murow_h[1:2, 0:C.ES].broadcast_to((128, C.ES))), "dg5")
    on("gpsimd", pse_g2)

    def pse_v3(v):
        v.wait_ge(Q["dg5"], 32)
        for kc in range(C.KC):
            e = v.tensor_sub(seS[:, kc * C.ES:(kc + 1) * C.ES],
                             seS[:, kc * C.ES:(kc + 1) * C.ES], mrow[:, 0:C.ES])
            ng(v, e)
            e = v.tensor_mul(seS[:, kc * C.ES:(kc + 1) * C.ES],
                             seS[:, kc * C.ES:(kc + 1) * C.ES], rrow[:, 0:C.ES])
            ng(v, e)
            e = v.tensor_scalar(seS[:, kc * C.ES:(kc + 1) * C.ES],
                                seS[:, kc * C.ES:(kc + 1) * C.ES],
                                sgsc[:, kc:kc + 1], sbsc[:, kc:kc + 1], OP.mult, OP.add)
        nsm["psedone"] = inc(e, sv)
    on("vector", pse_v3)
    dbg_dump("seS", lambda: seS[:], sv, lambda: nsm["psedone"])

    barrier()


    NLB, LB, LTB, WB, NWB = C.NLB, C.LB, C.LTB, C.WB, C.NWB
    fre("xsq")
    disp = mksb(top, "disp", 128, C.ET * C.LH, BF16)
    lgf = mksb(top, "lgf", 128, C.ET * LB, F32)
    dstat = mksb(top, "dstat", 128, 4 * C.ET, F32)
    pstat = mksb(top, "pstat", 128, 4 * C.ET, F32)
    b2c = mksb(top, "b2c", 128, max(C.LT, 2), F32)
    b2sb = mksb(top, "b2sb", max(C.LT, 1), 128, F32)
    lrow = mksb(top, "lrow", 128, 2 * C.ES, F32)
    brow_f = mksb(top, "brow_f", 128, LB, F32)

    def pfe_v0(v):
        nsm["pfedm"] = inc(v.memset(dstat[:], 0.0), sv)
    on("vector", pfe_v0)

    for lb in range(NLB):
        PLGSS0 = cnt[id(ss)]

        def plg_t(t, _lb=lb, _b=PLGSS0):
            i = 0
            for et in range(C.ET):
                for w in range(NWB):
                    pst = psA if i % 2 == 0 else psB
                    if i >= 2:
                        t.wait_ge(ss, _b + i - 1)
                    for kc in range(C.KC):
                        mm = t.matmul(
                            pst[:, 0:WB],
                            seS[:, kc * C.ES + et * 128:kc * C.ES + (et + 1) * 128],
                            xln[:, kc * C.LH + _lb * LB + w * WB:
                                kc * C.LH + _lb * LB + (w + 1) * WB],
                            start=(kc == 0), stop=(kc == C.KC - 1))
                    nsm[("plgm", _lb, i)] = inc(mm, st)
                    i += 1
        on("tensor", plg_t)

        def plg_sc(sc, _lb=lb):
            i = 0
            for et in range(C.ET):
                for w in range(NWB):
                    pst = psA if i % 2 == 0 else psB
                    sc.wait_ge(st, nsm[("plgm", _lb, i)])
                    e = sc.copy(lgf[:, et * LB + w * WB:et * LB + (w + 1) * WB],
                                pst[:, 0:WB])
                    nsm[("plge", _lb, i)] = inc(e, ss)
                    i += 1
        on("scalar", plg_sc)

        def pds_v(v, _lb=lb):
            v.wait_ge(ss, nsm[("plge", _lb, C.ET * NWB - 1)])
            ng(v, sem=sv, val=nsm["pfedm"])
            for et in range(C.ET):
                e = v.reduce_max(dstat[:, _lb * C.ET + et:_lb * C.ET + et + 1],
                                 lgf[:, et * LB:(et + 1) * LB], axis=AX.X, negate=True)
            nsm[("pdsm", _lb)] = inc(e, sv)
        on("vector", pds_v)

        def pds_sc(sc, _lb=lb):
            sc.wait_ge(sv, nsm[("pdsm", _lb)])
            for et in range(C.ET):
                e = sc.activation(
                    disp[:, et * C.LH + _lb * LB:et * C.LH + (_lb + 1) * LB],
                    lgf[:, et * LB:(et + 1) * LB], AF.Exp,
                    bias=dstat[:, _lb * C.ET + et:_lb * C.ET + et + 1],
                    accum_out=dstat[:, 2 * C.ET + _lb * C.ET + et:
                                    2 * C.ET + _lb * C.ET + et + 1])
            nsm[("pdse", _lb)] = inc(e, ss)
        on("scalar", pds_sc)

        NG4 = C.ET // 4
        PCBSV0 = [None]
        for ltl in range(LTB):
            ltg = lb * LTB + ltl

            def pcb_t(t, _lb=lb, _ltl=ltl, _ltg=ltg):
                if _ltl == 0:
                    PCBSV0[0] = cnt[id(sv)]
                t.wait_ge(ss, nsm[("pdse", _lb)])
                for g8 in range(NG4):
                    if g8 > 0:
                        t.wait_ge(sv, PCBSV0[0] + _ltl * (NG4 + 2) + g8)
                    elif _ltl > 0:
                        t.wait_ge(sv, PCBSV0[0] + (_ltl - 1) * (NG4 + 2) + NG4 + 1)
                    for q4 in range(4):
                        et = g8 * 4 + q4
                        tr = t.transpose(psTf[:, q4 * 128:(q4 + 1) * 128],
                                         lgf[:, et * LB + _ltl * 128:
                                             et * LB + (_ltl + 1) * 128],
                                         eyef[:])
                    nsm[("pcbt", _ltg, g8)] = inc(tr, st)
            on("tensor", pcb_t)

            def pcb_v(v, _ltg=ltg):
                for g8 in range(NG4):
                    v.wait_ge(st, nsm[("pcbt", _ltg, g8)])
                    e = v.tensor_copy(lrow[:, g8 * 512:(g8 + 1) * 512], psTf[:])
                    inc(e, sv)
                ng(v, sem=sv, val=cnt[id(sv)])
                e = v.reduce_max(b2c[:, _ltg:_ltg + 1], lrow[:, 0:C.ES],
                                 axis=AX.X, negate=True)
                nsm[("pcbm", _ltg)] = inc(e, sv)
            on("vector", pcb_v)

            def pcb_sc(sc, _ltg=ltg):
                sc.wait_ge(sv, nsm[("pcbm", _ltg)])
                t = inc(sc.activation(lrow[:, C.ES:2 * C.ES], lrow[:, 0:C.ES], AF.Exp,
                                      bias=b2c[:, _ltg:_ltg + 1],
                                      accum_out=tiny[:, 40:41]), ss)
                sc.wait_ge(ss, t)
                e = sc.activation(tiny[:, 41:42], tiny[:, 40:41], AF.Ln)
                nsm[("pcbz", _ltg)] = inc(e, ss)
            on("scalar", pcb_sc)

            def pcb_v2(v, _ltg=ltg):
                v.wait_ge(ss, nsm[("pcbz", _ltg)])
                e = v.tensor_sub(b2c[:, _ltg:_ltg + 1], b2c[:, _ltg:_ltg + 1],
                                 tiny[:, 41:42])
                nsm[("pcbb", _ltg)] = inc(e, sv)
            on("vector", pcb_v2)

        def pcb_t3(t, _lb=lb):
            t.wait_ge(sv, nsm[("pcbb", (_lb + 1) * LTB - 1)])
            tr = t.transpose(psTf[0:LTB, 0:128], b2c[:, _lb * LTB:(_lb + 1) * LTB],
                             eyef[:])
            nsm[("pcbtr", _lb)] = inc(tr, st)
        on("tensor", pcb_t3)

        def pcb_v4(v, _lb=lb):
            v.wait_ge(st, nsm[("pcbtr", _lb)])
            e = v.tensor_copy(b2sb[0:LTB, :], psTf[0:LTB, 0:128])
            nsm[("pcbsb", _lb)] = inc(e, sv)
        on("vector", pcb_v4)

        scb = Stream(1)

        def pcb_s(s, _lb=lb):
            s.wait_ge(sv, nsm[("pcbsb", _lb)])
            e = s.dma_start(out=b2_h[_lb * LTB:(_lb + 1) * LTB, :], in_=b2sb[0:LTB, :])
            scb.inc(e, 0)
        on("sync", pcb_s)

        def pcb_g(g, _lb=lb):
            scb.wait(g, 0)
            qinc16(g.dma_start(
                out=brow_f[:],
                in_=b2_h[_lb * LTB:(_lb + 1) * LTB, :]
                .rearrange("a b -> (a b)").unsqueeze(0).broadcast_to((128, LB))), "dg6")
        on("gpsimd", pcb_g)

        def pcb_v5(v, _lb=lb):
            v.wait_ge(Q["dg6"], (_lb + 1) * 16)
            for et in range(C.ET):
                e = v.scalar_tensor_tensor(lgf[:, et * LB:(et + 1) * LB],
                                           lgf[:, et * LB:(et + 1) * LB], 1.0,
                                           brow_f[:], OP.mult, OP.add)
            nsm[("pcbadd", _lb)] = inc(e, sv)
        on("vector", pcb_v5)

        def pcb_sc5(sc, _lb=lb):
            sc.wait_ge(sv, nsm[("pcbadd", _lb)])
            for et in range(C.ET):
                e = sc.activation(combT[:, et * C.LH + _lb * LB:et * C.LH + (_lb + 1) * LB],
                                  lgf[:, et * LB:(et + 1) * LB], AF.Exp)
            nsm[("pcomb", _lb)] = inc(e, ss)
        on("scalar", pcb_sc5)
        barrier()

    def pdn_s(s):
        s.wait_ge(ss, nsm[("pdse", NLB - 1)])
        nsm["pdst"] = inc(s.dma_start(out=cin2[:], in_=dstat[:]), sh, 16)
    on("sync", pdn_s)

    def pdn_g(g):
        g.wait_ge(sh, nsm["pdst"])
        nsm["cc2"] = inc(g.collective_compute(
            "AllGather", OP.bypass, replica_groups=RG8, ins=[cin2[:]], outs=[cout2[:]]), scc)
    on("gpsimd", pdn_g)

    def pdn_s2(s):
        s.wait_ge(scc, nsm["cc2"])
        nsm["pdsp"] = inc(s.dma_start(out=pstat[:], in_=cout2[TS(REG["pxor"], 128), :]),
                          sh, 16)
    on("sync", pdn_s2)

    def pdn_v(v):
        v.wait_ge(sh, nsm["pdsp"])
        ET = C.ET
        if NLB == 2:
            v.tensor_tensor(tiny[:, 0:ET], dstat[:, 0:ET], dstat[:, ET:2 * ET], OP.min)
            t = inc(v.tensor_tensor(tiny[:, ET:2 * ET], pstat[:, 0:ET],
                                    pstat[:, ET:2 * ET], OP.min), sv)
            v.wait_ge(sv, t)
            t = inc(v.tensor_tensor(tiny[:, 0:ET], tiny[:, 0:ET],
                                    tiny[:, ET:2 * ET], OP.min), sv)
            v.wait_ge(sv, t)
        else:
            t = inc(v.tensor_tensor(tiny[:, 0:ET], dstat[:, 0:ET], pstat[:, 0:ET],
                                    OP.min), sv)
            v.wait_ge(sv, t)
        for lb in range(NLB):
            v.tensor_sub(tiny[:, (1 + lb) * ET:(2 + lb) * ET], tiny[:, 0:ET],
                         dstat[:, lb * ET:(lb + 1) * ET])
            e = v.tensor_sub(tiny[:, (1 + NLB + lb) * ET:(2 + NLB + lb) * ET],
                             tiny[:, 0:ET], pstat[:, lb * ET:(lb + 1) * ET])
        nsm["pdnd"] = inc(e, sv)
    on("vector", pdn_v)

    def pdn_sc(sc):
        sc.wait_ge(sv, nsm["pdnd"])
        e = sc.activation(tiny[:, C.ET:(1 + 2 * NLB) * C.ET],
                          tiny[:, C.ET:(1 + 2 * NLB) * C.ET], AF.Exp)
        nsm["pdnf"] = inc(e, ss)
    on("scalar", pdn_sc)

    def pdn_v2(v):
        v.wait_ge(ss, nsm["pdnf"])
        ET = C.ET

        def step(ins):
            t = inc(ins, sv)
            v.wait_ge(sv, t)
        step(v.tensor_mul(tiny[:, (1 + 2 * NLB) * ET:(2 + 2 * NLB) * ET],
                          dstat[:, 2 * ET:3 * ET], tiny[:, ET:2 * ET]))
        for lb in range(NLB):
            if lb > 0:
                step(v.tensor_mul(tiny[:, (2 + 2 * NLB) * ET:(3 + 2 * NLB) * ET],
                                  dstat[:, (2 + lb) * ET:(3 + lb) * ET],
                                  tiny[:, (1 + lb) * ET:(2 + lb) * ET]))
                step(v.tensor_add(tiny[:, (1 + 2 * NLB) * ET:(2 + 2 * NLB) * ET],
                                  tiny[:, (1 + 2 * NLB) * ET:(2 + 2 * NLB) * ET],
                                  tiny[:, (2 + 2 * NLB) * ET:(3 + 2 * NLB) * ET]))
            step(v.tensor_mul(tiny[:, (2 + 2 * NLB) * ET:(3 + 2 * NLB) * ET],
                              pstat[:, (2 + lb) * ET:(3 + lb) * ET],
                              tiny[:, (1 + NLB + lb) * ET:(2 + NLB + lb) * ET]))
            step(v.tensor_add(tiny[:, (1 + 2 * NLB) * ET:(2 + 2 * NLB) * ET],
                              tiny[:, (1 + 2 * NLB) * ET:(2 + 2 * NLB) * ET],
                              tiny[:, (2 + 2 * NLB) * ET:(3 + 2 * NLB) * ET]))
        step(v.reciprocal(tiny[:, (1 + 2 * NLB) * ET:(2 + 2 * NLB) * ET],
                          tiny[:, (1 + 2 * NLB) * ET:(2 + 2 * NLB) * ET]))
        for lb in range(NLB):
            step(v.tensor_mul(tiny[:, (1 + lb) * ET:(2 + lb) * ET],
                              tiny[:, (1 + lb) * ET:(2 + lb) * ET],
                              tiny[:, (1 + 2 * NLB) * ET:(2 + 2 * NLB) * ET]))
        for lb in range(NLB):
            for et in range(C.ET):
                e = v.tensor_scalar_mul(
                    disp[:, et * C.LH + lb * LB:et * C.LH + (lb + 1) * LB],
                    disp[:, et * C.LH + lb * LB:et * C.LH + (lb + 1) * LB],
                    tiny[:, (1 + lb) * ET + et:(1 + lb) * ET + et + 1])
        nsm["pdisp"] = inc(e, sv)
    on("vector", pdn_v2)

    dbg_dump("disp", lambda: disp[:], sv, lambda: nsm["pdisp"])
    dbg_dump("dispu", lambda: disp[:], ss, lambda: nsm[("pdse", NLB - 1)])
    dbg_dump("pstat", lambda: pstat[:], sh, lambda: nsm["pdsp"])
    dbg_dump("tiny", lambda: tiny[:], sv, lambda: nsm["pdisp"])
    dbg_dump("combT", lambda: combT[:], ss, lambda: nsm[("pcomb", NLB - 1)])
    dbg_dump("dstat", lambda: dstat[:], ss, lambda: nsm[("pdse", NLB - 1)])
    barrier()
    fre("brow_f", "lrow", "b2sb", "b2c", "pstat", "dstat", "lgf", "seS")

    xlnT = mksb(top, "xlnT", 128, C.LT * C.DM, BF16)
    dispT = mksb(top, "dispT", 128, C.LT * C.ES, BF16)
    q4b = qc["q4"]

    def ptr_t(t):
        gi = 0
        for (src, nt) in ((xln, C.KC), (disp, C.ET)):
            for lt in range(C.LT):
                for g4 in range(0, nt, 4):
                    w = min(4, nt - g4)
                    if gi >= 2:
                        t.wait_ge(Q["q4"], q4b + gi - 1)
                    pT = psTa if gi % 2 == 0 else psTb
                    for q4 in range(w):
                        tt = g4 + q4
                        tr = t.transpose(pT[:, q4 * 128:(q4 + 1) * 128],
                                         src[:, tt * C.LH + lt * 128:tt * C.LH + (lt + 1) * 128],
                                         eyeb[:])
                    nsm[("ptr", gi)] = inc(tr, st)
                    gi += 1
    on("tensor", ptr_t)

    def ptr_v(v):
        gi = 0
        for (dst, nt, tl) in ((xlnT, C.KC, C.DM), (dispT, C.ET, C.ES)):
            for lt in range(C.LT):
                for g4 in range(0, nt, 4):
                    w = min(4, nt - g4)
                    v.wait_ge(st, nsm[("ptr", gi)])
                    pT = psTa if gi % 2 == 0 else psTb
                    e = v.tensor_copy(dst[:, lt * tl + g4 * 128:lt * tl + (g4 + w) * 128],
                                      pT[:, 0:w * 128])
                    qinc(e, "q4")
                    gi += 1
    on("vector", ptr_v)

    barrier()
    fre("xr1")

    slotp = mksb(top, "slotp", 128, C.KC * C.ES, BF16)
    PSLSS0 = cnt[id(ss)]

    def psl_t(t):
        i = 0
        for kc in range(C.KC):
            for w in range(C.ES // 512):
                pst = psA if i % 2 == 0 else psB
                if i >= 2:
                    t.wait_ge(ss, PSLSS0 + i - 1)
                for lt in range(C.LT):
                    mm = t.matmul(pst[:, 0:512],
                                  xlnT[:, lt * C.DM + kc * 128:lt * C.DM + (kc + 1) * 128],
                                  dispT[:, lt * C.ES + w * 512:lt * C.ES + (w + 1) * 512],
                                  start=(lt == 0), stop=(lt == C.LT - 1))
                nsm[("pslm", i)] = inc(mm, st)
                i += 1
    on("tensor", psl_t)

    def psl_sc(sc):
        i = 0
        for kc in range(C.KC):
            for w in range(C.ES // 512):
                pst = psA if i % 2 == 0 else psB
                sc.wait_ge(st, nsm[("pslm", i)])
                e = sc.copy(slotp[:, kc * C.ES + w * 512:kc * C.ES + (w + 1) * 512],
                            pst[:, 0:512])
                nsm[("psle", i)] = inc(e, ss)
                i += 1
    on("scalar", psl_sc)

    def psl_s(s):
        s.wait_ge(ss, nsm[("psle", C.KC * (C.ES // 512) - 1)])
        for kc in range(C.KC):
            e = s.dma_start(out=cin3[kc * 128:(kc + 1) * 128, :],
                            in_=slotp[:, kc * C.ES:(kc + 1) * C.ES])
            inc(e, sh, 16)
        nsm["pslst"] = cnt[id(sh)]
    on("sync", psl_s)

    def psl_g(g):
        g.wait_ge(sh, nsm["pslst"])
        nsm["cc3"] = inc(g.collective_compute(
            "AllGather", OP.bypass, replica_groups=RG8, ins=[cin3[:]], outs=[cout3[:]]), scc)
    on("gpsimd", psl_g)

    barrier()
    fre("slotp", "dispT", "xlnT", "disp")

    sown = mksb(top, "sown", 128, C.KC * 512, BF16)
    srec = mksb(top, "srec", 128, 8 * C.KC * 128, BF16)

    def psg_s(s):
        s.wait_ge(scc, nsm["cc3"])
        pid = REG["pid"]
        c3r = cout3.rearrange("(g k p) e -> p g k e", g=8, k=C.KC)
        srec_r = srec[:].rearrange("p (g k e) -> p g k e", g=8, k=C.KC)
        e = s.dma_start(out=srec_r, in_=c3r[:, :, :, TS(pid, 128)])
        inc(e, sh, 16)
        nsm["psgl"] = cnt[id(sh)]
    on("sync", psg_s)

    def psg_v(v):
        v.wait_ge(sh, nsm["psgl"])
        for bb in range(4):
            for kc in range(C.KC):
                e = v.tensor_add(
                    sown[:, kc * 512 + bb * 128:kc * 512 + (bb + 1) * 128],
                    srec[:, (bb * C.KC + kc) * 128:(bb * C.KC + kc + 1) * 128],
                    srec[:, ((bb + 4) * C.KC + kc) * 128:((bb + 4) * C.KC + kc + 1) * 128])
        nsm["psown"] = inc(e, sv)
    on("vector", psg_v)
    dbg_dump("sown", lambda: sown[:], sv, lambda: nsm["psown"])

    barrier()
    fre("srec")

    wgt = mksb(top, "wgt", 128, C.KC * C.F, BF16)
    wut = mksb(top, "wut", 128, C.KC * C.F, BF16)
    wdt = mksb(top, "wdt", 128, C.FT * C.DO, BF16)
    gat = mksb(top, "gat", 128, C.FT * 256, BF16)
    upt = mksb(top, "upt", 128, C.FT * 256, BF16)
    eob = mksb(top, "eob", 64, 2 * C.DO, BF16)
    bdbc = mksb(top, "bdbc", 128, 2 * C.DO, F32)

    def slot_rhs(kc, e_):
        ap = sown[:, kc * 512:(kc + 1) * 512]
        ap = ap.rearrange("p (b s) -> p b s", b=4)
        return ap[:, :, e_ * 64:(e_ + 1) * 64]

    sBD = Stream(1)
    sWt = Stream(2)
    sE4 = Stream(1)
    sE4.preset(8, 1)

    def pex_s0(s):
        sBD.inc(s.dma_start(out=bdbc[:, 0:C.DO],
                            in_=i_bd[0:1, :].broadcast_to((128, C.DO))), 0)
        e = s.dma_start(out=bdbc[:, C.DO:2 * C.DO],
                        in_=i_bd[1:2, :].broadcast_to((128, C.DO)))
        sBD.inc(e, 0)
    on("sync", pex_s0)

    NDOW = max(1, C.DO // 512)
    WDO = min(512, C.DO)
    for e_i in range(2):
        def pex_sw(s, _e=e_i):
            if _e == 1:
                s.wait_ge(st, nsm[("peom", 0, 3)])
            for kc in range(C.KC):
                sWt.inc(s.dma_start(out=wgt[:, kc * C.F:(kc + 1) * C.F],
                                    in_=wgb[_e, kc * 128:(kc + 1) * 128, :]), _e)
                sWt.inc(s.dma_start(out=wut[:, kc * C.F:(kc + 1) * C.F],
                                    in_=wub[_e, kc * 128:(kc + 1) * 128, :]), _e)
            for ft in range(C.FT):
                sWt.inc(s.dma_start(out=wdt[:, ft * C.DO:(ft + 1) * C.DO],
                                    in_=wdb[_e, ft * 128:(ft + 1) * 128, :]), _e)
        on("sync", pex_sw)

        PEXSS0 = cnt[id(ss)]

        def pex_t(t, _e=e_i, _b=PEXSS0):
            sWt.wait(t, _e)
            t.wait_ge(sv, nsm["psown"])
            i = 0
            for wsb in (wgt, wut):
                for ft in range(C.FT):
                    pst = psA if i % 2 == 0 else psB
                    if i >= 2:
                        t.wait_ge(ss, _b + i - 1)
                    for kc in range(C.KC):
                        mm = t.matmul(pst[:, 0:256],
                                      wsb[:, kc * C.F + ft * 128:kc * C.F + (ft + 1) * 128],
                                      slot_rhs(kc, _e),
                                      start=(kc == 0), stop=(kc == C.KC - 1))
                    nsm[("pexm", _e, i)] = inc(mm, st)
                    i += 1
        on("tensor", pex_t)

        def pex_sc(sc, _e=e_i):
            i = 0
            for which in ("g", "u"):
                for ft in range(C.FT):
                    pst = psA if i % 2 == 0 else psB
                    sc.wait_ge(st, nsm[("pexm", _e, i)])
                    dst = gat if which == "g" else upt
                    bias = bgT if which == "g" else buT
                    fn = AF.Gelu if which == "g" else AF.Identity
                    e = sc.activation(dst[:, ft * 256:(ft + 1) * 256], pst[:, 0:256], fn,
                                      bias=bias[:, _e * C.FT + ft:_e * C.FT + ft + 1])
                    nsm[("pexe", _e, i)] = inc(e, ss)
                    i += 1
        on("scalar", pex_sc)

        def pex_v(v, _e=e_i):
            v.wait_ge(ss, nsm[("pexe", _e, 2 * C.FT - 1)])
            e = v.tensor_mul(gat[:], gat[:], upt[:])
            nsm[("pexgu", _e)] = inc(e, sv)
        on("vector", pex_v)

        PEXSV0 = cnt[id(sv)]

        def pex_t2(t, _e=e_i, _b=PEXSV0):
            t.wait_ge(sv, nsm[("pexgu", _e)])
            for bb in range(4):
                if bb >= 1:
                    t.wait_ge(sv, _b + bb)
                for w2 in range(NDOW):
                    pst = psA if w2 == 0 else psB
                    for ft in range(C.FT):
                        mm = t.matmul(pst[0:64, 0:WDO],
                                      gat[:, ft * 256 + bb * 64:ft * 256 + (bb + 1) * 64],
                                      wdt[:, ft * C.DO + w2 * WDO:ft * C.DO + (w2 + 1) * WDO],
                                      start=(ft == 0), stop=(ft == C.FT - 1))
                nsm[("peom", _e, bb)] = inc(mm, st)
        on("tensor", pex_t2)

        def pex_v2(v, _e=e_i):
            sBD.wait(v, 0)
            for bb in range(4):
                v.wait_ge(st, nsm[("peom", _e, bb)])
                if _e * 4 + bb >= 1:
                    sE4.wait(v, _e * 4 + bb - 1)
                for w2 in range(NDOW):
                    pst = psA if w2 == 0 else psB
                    e = v.scalar_tensor_tensor(
                        eob[0:64, w2 * WDO:(w2 + 1) * WDO], pst[0:64, 0:WDO], 1.0,
                        bdbc[0:64, _e * C.DO + w2 * WDO:_e * C.DO + (w2 + 1) * WDO],
                        OP.mult, OP.add)
                nsm[("peoe", _e, bb)] = inc(e, sv)
        on("vector", pex_v2)

        def pex_s2(s, _e=e_i):
            for bb in range(4):
                s.wait_ge(sv, nsm[("peoe", _e, bb)])
                e = s.dma_start(out=cin4[bb * 128 + _e * 64:bb * 128 + (_e + 1) * 64, :],
                                in_=eob[0:64, 0:C.DO])
                sE4.inc(e, _e * 4 + bb)
        on("sync", pex_s2)

    def pex_g(g):
        sE4.wait(g, 7)
        nsm["cc4"] = inc(g.collective_compute(
            "AllGather", OP.bypass, replica_groups=RG8, ins=[cin4[:]], outs=[cout4[:]]), scc)
    on("gpsimd", pex_g)

    barrier()
    fre("bdbc", "eob", "upt", "gat", "wdt", "wut", "wgt", "sown")

    res2 = mksb(top, "res2", 128, C.KC * C.LH, BF16)
    eoutb = mksb(top, "eoutb", 128, 8 * C.DO, BF16)
    xr1b = mksb(top, "xr1b", 128, C.KC * C.LH, BF16)

    s12x = Stream(1)

    def p12_s0(s):
        for kc in range(C.KC):
            e = s.dma_start(out=xr1b[:, kc * C.LH:(kc + 1) * C.LH],
                            in_=xr1_h[kc * 128:(kc + 1) * 128, :])
            s12x.inc(e, 0)
    on("sync", p12_s0)

    def pcm_s(s):
        s.wait_ge(scc, nsm["cc4"])
        breg = REG["breg"]
        c4r = cout4.rearrange("(g b p) o -> p g b o", g=8, b=4)
        eoutb_r = eoutb[:].rearrange("p (g o) -> p g o", g=8)
        e = s.dma_start(out=eoutb_r, in_=c4r[:, :, DS(breg, 1), :].squeeze(2))
        inc(e, sh, 16)
        nsm["pcml"] = cnt[id(sh)]
    on("sync", pcm_s)

    PCMSV0 = cnt[id(sv)]

    def pcm_t(t):
        t.wait_ge(sh, nsm["pcml"])
        i = 0
        for ot in range(C.KC):
            for w in range(C.NWH):
                pst = psA if i % 2 == 0 else psB
                if i >= 2:
                    t.wait_ge(sv, PCMSV0 + i - 1)
                for sdr in range(8):
                    mm = t.matmul(pst[:, 0:C.WH],
                                  eoutb[:, sdr * C.DO + ot * 128:sdr * C.DO + (ot + 1) * 128],
                                  combT[:, sdr * C.LH + w * C.WH:sdr * C.LH + (w + 1) * C.WH],
                                  start=(sdr == 0), stop=(sdr == 7))
                nsm[("pcmm", i)] = inc(mm, st)
                i += 1
    on("tensor", pcm_t)

    def pcm_v(v):
        s12x.wait(v, 0)
        i = 0
        for ot in range(C.KC):
            for w in range(C.NWH):
                pst = psA if i % 2 == 0 else psB
                v.wait_ge(st, nsm[("pcmm", i)])
                e = v.scalar_tensor_tensor(
                    res2[:, ot * C.LH + w * C.WH:ot * C.LH + (w + 1) * C.WH],
                    pst[:, 0:C.WH], 1.0,
                    xr1b[:, ot * C.LH + w * C.WH:ot * C.LH + (w + 1) * C.WH],
                    OP.mult, OP.add)
                nsm[("pcme", i)] = inc(e, sv)
                i += 1
    on("vector", pcm_v)
    dbg_dump("eoutb", lambda: eoutb[:], sh, lambda: nsm["pcml"])
    dbg_dump("res2", lambda: res2[:], sv, lambda: nsm[("pcme", C.KC * C.NWH - 1)])

    barrier()
    fre("xr1b", "eoutb")

    r2T = mksb(top, "r2T", 128, 2 * C.DM, BF16)
    osb = mksb(top, "osb", 128, 2 * C.DM, F32)
    sqf = mksb(top, "sqf", 128, C.DM, F32)
    q1d = qc["q1"]
    q2d = qc["q2"]
    sOo = Stream(2)

    for lt in range(C.LT):
        def pf_t(t, _lt=lt):
            for g4 in range(0, C.KC, 4):
                w = min(4, C.KC - g4)
                gi = _lt * ((C.KC + 3) // 4) + g4 // 4
                if gi >= 2:
                    t.wait_ge(Q["q1"], q1d + gi - 1)
                pT = psTa if gi % 2 == 0 else psTb
                for q4 in range(w):
                    kc = g4 + q4
                    tr = t.transpose(pT[:, q4 * 128:(q4 + 1) * 128],
                                     res2[:, kc * C.LH + _lt * 128:kc * C.LH + (_lt + 1) * 128],
                                     eyeb[:])
                nsm[("pft", _lt, g4)] = inc(tr, st)
        on("tensor", pf_t)

        def pf_v(v, _lt=lt):
            so = (_lt % 2) * C.DM
            if _lt >= 2:
                sOo.wait(v, _lt - 2)
            for g4 in range(0, C.KC, 4):
                w = min(4, C.KC - g4)
                gi = _lt * ((C.KC + 3) // 4) + g4 // 4
                v.wait_ge(st, nsm[("pft", _lt, g4)])
                pT = psTa if gi % 2 == 0 else psTb
                e = v.tensor_copy(r2T[:, so + g4 * 128:so + (g4 + w) * 128], pT[:, 0:w * 128])
                qinc(e, "q1")
            ng(v, sem=Q["q1"], val=qc["q1"])
            e = v.reduce_sum(tiny[:, 9:10], r2T[:, so:so + C.DM], axis=AX.X)
            nsm[("pfc", _lt)] = inc(e, sv)
            v.wait_ge(sv, nsm[("pfc", _lt)])
        on("vector", pf_v)

        def pf_sc(sc, _lt=lt):
            so = (_lt % 2) * C.DM
            sc.wait_ge(sv, nsm[("pfc", _lt)])
            e = sc.activation(sqf[:], r2T[:, so:so + C.DM], AF.Square,
                              accum_out=tiny[:, 8:9])
            nsm[("pfsq", _lt)] = inc(e, ss)
        on("scalar", pf_sc)

        def pf_v2(v, _lt=lt):
            v.wait_ge(ss, nsm[("pfsq", _lt)])
            v.tensor_scalar_mul(tiny[:, 9:10], tiny[:, 9:10], 1.0 / C.DM)
            t = inc(v.tensor_scalar_mul(tiny[:, 8:9], tiny[:, 8:9], 1.0 / C.DM), sv)
            v.wait_ge(sv, t)
            t = inc(v.tensor_mul(tiny[:, 10:11], tiny[:, 9:10], tiny[:, 9:10]), sv)
            v.wait_ge(sv, t)
            e = v.tensor_sub(tiny[:, 10:11], tiny[:, 8:9], tiny[:, 10:11])
            nsm[("pfv", _lt)] = inc(e, sv)
        on("vector", pf_v2)

        def pf_sc2(sc, _lt=lt):
            sc.wait_ge(sv, nsm[("pfv", _lt)])
            e = sc.activation(tiny[:, 10:11], tiny[:, 10:11], AF.Sqrt, bias=csts[:, 0:1])
            nsm[("pfsd", _lt)] = inc(e, ss)
        on("scalar", pf_sc2)

        def pf_v3(v, _lt=lt):
            so = (_lt % 2) * C.DM
            v.wait_ge(ss, nsm[("pfsd", _lt)])
            t = inc(v.reciprocal(tiny[:, 10:11], tiny[:, 10:11]), sv)
            v.wait_ge(sv, t)
            e = v.tensor_scalar(osb[:, so:so + C.DM], r2T[:, so:so + C.DM],
                                tiny[:, 9:10], tiny[:, 10:11], OP.subtract, OP.mult)
            nsm[("pfe", _lt)] = inc(e, sv)
        on("vector", pf_v3)

        def pf_s(s, _lt=lt):
            so = (_lt % 2) * C.DM
            s.wait_ge(sv, nsm[("pfe", _lt)])
            e = s.dma_start(out=o_y[_lt * 128:(_lt + 1) * 128, :], in_=osb[:, so:so + C.DM])
            sOo.inc(e, _lt)
        on("sync", pf_s)

    def fin_s(s):
        sOo.wait_all(s)
        if cnt[id(sh)] > 0:
            s.wait_ge(sh, cnt[id(sh)])
    on("sync", fin_s)

    for h in reversed(list(live.values())):
        h.__exit__(None, None, None)
    live.clear()
    top.close()
    global LAST_NSM
    LAST_NSM = dict(nsm)
    return nc


def _prep_core(C, c, I):
    b, j = c % 4, c // 4
    js, je = j * C.DH, (j + 1) * C.DH
    m = {}
    m["i_x"] = I["L0_attn_x"][b]
    m["i_xh"] = I["L0_attn_x"][b][j * C.LH:(j + 1) * C.LH]
    m["i_inw"] = np.concatenate([I["L0_attn_in_w"][js:je],
                                 I["L0_attn_in_w"][C.DI + js:C.DI + je]], 0)
    m["i_convw"] = I["L0_attn_conv_w"][js:je, 0, :].reshape(
        C.DT, 128, 4).transpose(1, 0, 2).reshape(128, C.DT * 4)
    m["i_convb"] = I["L0_attn_conv_b"][js:je].reshape(C.DT, 128).T
    m["i_xprojT"] = I["L0_attn_xproj_w"][:, js:je].T
    m["i_dtwT"] = I["L0_attn_dt_w"][js:je, :].T
    m["i_dtb"] = I["L0_attn_dt_b"][js:je].reshape(C.DT, 128).T
    m["i_alog"] = I["L0_attn_A_log"][js:je].reshape(
        C.DT, 128, C.NST).transpose(1, 0, 2).reshape(128, C.DT * C.NST)
    m["i_dv"] = I["L0_attn_D"][js:je].reshape(C.DT, 128).T
    m["i_outwT"] = I["L0_attn_out_w"][:, js:je].T
    for nm, k in (("i_normg", "L0_ffn_norm_g"), ("i_normb", "L0_ffn_norm_b"),
                  ("i_slotg", "L0_ffn_slot_g"), ("i_slotb", "L0_ffn_slot_b")):
        m[nm] = I[k].reshape(C.KC, 128).T
    m["i_scale"] = I["L0_ffn_scale"].reshape(1, 1)
    m["i_seT"] = I["L0_ffn_slot_embeds"].reshape(C.ES, C.DM).T
    m["i_wg"] = I["L0_ffn_Wg"][2 * c:2 * c + 2]
    m["i_wu"] = I["L0_ffn_Wu"][2 * c:2 * c + 2]
    m["i_wd"] = I["L0_ffn_Wd"][2 * c:2 * c + 2]
    m["i_bgT"] = I["L0_ffn_bg"][2 * c:2 * c + 2].reshape(
        2, C.FT, 128).transpose(2, 0, 1).reshape(128, 2 * C.FT)
    m["i_buT"] = I["L0_ffn_bu"][2 * c:2 * c + 2].reshape(
        2, C.FT, 128).transpose(2, 0, 1).reshape(128, 2 * C.FT)
    m["i_bd"] = I["L0_ffn_bd"][2 * c:2 * c + 2]
    m["i_eyeb"] = np.eye(128, dtype=BF)
    m["i_eyef"] = np.eye(128, dtype=np.float32)
    out = {}
    for k, v in m.items():
        v = np.asarray(v)
        if v.dtype == np.float64:
            v = v.astype(np.float32)
        out[k] = np.ascontiguousarray(v)
    return out


def kernel(**inputs):
    C = CFG_FULL
    I = {k: np.asarray(v) for k, v in inputs.items()}
    nc = build(C)
    in_maps = [_prep_core(C, c, I) for c in range(8)]
    res = run_bass_kernel_spmd(nc, in_maps, core_ids=list(range(8)))
    out = np.zeros((4, C.L, C.DM), np.float32)
    for c in range(8):
        b, j = c % 4, c // 4
        out[b, j * C.LH:(j + 1) * C.LH] = res.results[c]["o_y"]
    return out


# version 07
# speedup vs baseline: 1.1074x; 1.0046x over previous
import contextlib
import numpy as np
import ml_dtypes
import concourse.bass as bass
import concourse.mybir as mybir
from concourse.bass_utils import run_bass_kernel_spmd

LAST_NSM = {}

F32 = mybir.dt.float32
BF16 = mybir.dt.bfloat16
AF = mybir.ActivationFunctionType
OP = mybir.AluOpType
AX = mybir.AxisListType
BF = ml_dtypes.bfloat16
EPS = 1e-5
DS = bass.ds
TS = bass.ts


class CFG:
    def __init__(self, DM, L, LC, F):
        self.DM = DM
        self.DI = 2 * DM
        self.DH = DM
        self.L = L
        self.LH = L // 2
        self.LC = LC
        self.NLC = L // LC
        self.NST = 16
        self.R = 64
        self.RK2 = self.R + 2 * self.NST
        self.E = 16
        self.S = 64
        self.ES = self.E * self.S
        self.F = F
        self.DO = DM
        self.KC = DM // 128
        self.DT = self.DH // 128
        self.ET = self.ES // 128
        self.FT = F // 128
        self.LT = self.LH // 128
        self.W = min(512, L)
        self.NW = L // self.W
        self.WH = min(512, self.LH)
        self.NWH = self.LH // self.WH
        self.NLB = 2 if self.LH >= 256 else 1
        self.LB = self.LH // self.NLB
        self.LTB = self.LB // 128
        self.WB = min(512, self.LB)
        self.NWB = self.LB // self.WB
        assert DM % 128 == 0 and L % (2 * LC) == 0 and LC % 128 == 0
        assert self.LH % 128 == 0 and F % 128 == 0 and self.ET == 8


CFG_FULL = CFG(DM=1024, L=4096, LC=1024, F=2048)


def build(C: CFG, dbg=False, races=False):
    nc = bass.Bass("TRN2", target_bir_lowering=False, detect_race_conditions=races)
    dtn = nc.dram_tensor

    def ein(name, shape):
        return dtn(name, shape, F32, kind="ExternalInput").ap()

    i_x = ein("i_x", [C.L, C.DM])
    i_xh = ein("i_xh", [C.LH, C.DM])
    i_inw = ein("i_inw", [2 * C.DH, C.DM])
    i_convw = ein("i_convw", [128, C.DT * 4])
    i_convb = ein("i_convb", [128, C.DT])
    i_xprojT = ein("i_xprojT", [C.DH, C.RK2])
    i_dtwT = ein("i_dtwT", [C.R, C.DH])
    i_dtb = ein("i_dtb", [128, C.DT])
    i_alog = ein("i_alog", [128, C.DT * C.NST])
    i_dv = ein("i_dv", [128, C.DT])
    i_outwT = ein("i_outwT", [C.DH, C.DO])
    i_normg = ein("i_normg", [128, C.KC])
    i_normb = ein("i_normb", [128, C.KC])
    i_slotg = ein("i_slotg", [128, C.KC])
    i_slotb = ein("i_slotb", [128, C.KC])
    i_scale = ein("i_scale", [1, 1])
    i_seT = ein("i_seT", [C.DM, C.ES])
    i_wg = ein("i_wg", [2, C.DM, C.F])
    i_wu = ein("i_wu", [2, C.DM, C.F])
    i_wd = ein("i_wd", [2, C.F, C.DO])
    i_bgT = ein("i_bgT", [128, 2 * C.FT])
    i_buT = ein("i_buT", [128, 2 * C.FT])
    i_bd = ein("i_bd", [2, C.DO])
    i_eyeb = dtn("i_eyeb", [128, 128], BF16, kind="ExternalInput").ap()
    i_eyef = dtn("i_eyef", [128, 128], F32, kind="ExternalInput").ap()
    o_y = dtn("o_y", [C.LH, C.DM], F32, kind="ExternalOutput").ap()

    xb = dtn("xb", [C.L, C.DM], BF16).ap()
    xhb = dtn("xhb", [C.LH, C.DM], BF16).ap()
    inwb = dtn("inwb", [2 * C.DH, C.DM], BF16).ap()
    xprojTb = dtn("xprojTb", [C.DH, C.RK2], BF16).ap()
    dtwTb = dtn("dtwTb", [C.R, C.DH], BF16).ap()
    outwTb = dtn("outwTb", [C.DH, C.DO], BF16).ap()
    seTb = dtn("seTb", [C.DM, C.ES], BF16).ap()
    wgb = dtn("wgb", [2, C.DM, C.F], BF16).ap()
    wub = dtn("wub", [2, C.DM, C.F], BF16).ap()
    wdb = dtn("wdb", [2, C.F, C.DO], BF16).ap()
    dk = dict(kind="ExternalOutput") if dbg else {}
    xx_h = dtn("xx_h", [C.DH, C.L], BF16, **dk).ap()
    xr1_h = dtn("xr1_h", [C.DM, C.LH], BF16, **dk).ap()
    delta_h = dtn("delta_h", [C.DH, C.L], BF16, **dk).ap()
    u_h = dtn("u_h", [C.DH, C.L], BF16, **dk).ap()
    sres_h = dtn("sres_h", [C.DH, C.L], BF16, **dk).ap()
    y3_h = dtn("y3_h", [C.DH, C.L], BF16, **dk).ap()
    brow_h = dtn("brow_h", [2 * C.NST, C.L], BF16).ap()
    murow_h = dtn("murow_h", [3, max(C.LH, C.ES)], F32).ap()
    b2_h = dtn("b2_h", [C.LT, 128], F32).ap()
    cin0 = dtn("cin0", [C.RK2, C.L], F32).ap()
    cout0 = dtn("cout0", [C.RK2, C.L], F32).ap()
    cin1 = dtn("cin1", [C.DO, C.L], BF16).ap()
    cout1 = dtn("cout1", [2 * C.DO, C.L], BF16).ap()
    cin2 = dtn("cin2", [128, 4 * C.ET], F32).ap()
    cout2 = dtn("cout2", [8 * 128, 4 * C.ET], F32).ap()
    cin3 = dtn("cin3", [C.DM, C.ES], BF16).ap()
    cout3 = dtn("cout3", [8 * C.DM, C.ES], BF16).ap()
    cin4 = dtn("cin4", [512, C.DO], BF16).ap()
    cout4 = dtn("cout4", [8 * 512, C.DO], BF16).ap()
    RG8 = [[0, 1, 2, 3, 4, 5, 6, 7]]
    RGP = [[0, 4], [1, 5], [2, 6], [3, 7]]

    top = contextlib.ExitStack()
    live = {}

    def mksb(st_, name, p, f, dty, side=None):
        h = nc.sbuf_tensor(name, [p, f], dty, side=side, unsafe_allow_realloc=True)
        t = h.__enter__()
        live[name] = h
        return t

    def fre(*names):
        for n in names:
            live.pop(n).__exit__(None, None, None)

    def mkps(st_, name, p, f, dty=F32):
        return st_.enter_context(nc.psum_tensor(name, [p, f], dty))

    def mksem(st_, name):
        return st_.enter_context(nc.semaphore(name))

    convw = mksb(top, "convw", 128, C.DT * 4, F32)
    convb = mksb(top, "convb", 128, C.DT, F32)
    dtb = mksb(top, "dtb", 128, C.DT, F32)
    alog = mksb(top, "alog", 128, C.DT * C.NST, F32)
    Amat = mksb(top, "Amat", 128, C.DT * C.NST, F32)
    dvs = mksb(top, "dvs", 128, C.DT, F32)
    normg = mksb(top, "normg", 128, C.KC, F32)
    normb = mksb(top, "normb", 128, C.KC, F32)
    sgsc = mksb(top, "sgsc", 128, C.KC, F32)
    sbsc = mksb(top, "sbsc", 128, C.KC, F32)
    bgT = mksb(top, "bgT", 128, 2 * C.FT, F32)
    buT = mksb(top, "buT", 128, 2 * C.FT, F32)
    eyeb = mksb(top, "eyeb", 128, 128, BF16)
    eyef = mksb(top, "eyef", 128, 128, F32)
    ones1 = mksb(top, "ones1", 128, 1, BF16)
    scale_bc = mksb(top, "scale_bc", 128, 1, F32)
    tiny = mksb(top, "tiny", 128, 64, F32)
    csts = mksb(top, "csts", 128, 2, F32)
    mrow = mksb(top, "mrow", 128, max(C.LH, C.ES), BF16)
    rrow = mksb(top, "rrow", 128, max(C.LH, C.ES), BF16)
    row1 = mksb(top, "row1", 1, 2 * max(C.LH, C.ES), F32)

    psA = mkps(top, "psA", 128, 512)
    psB = mkps(top, "psB", 128, 512)
    psSa = mkps(top, "psSa", 128, 512)
    psSb = mkps(top, "psSb", 128, 512)
    psTa = mkps(top, "psTa", 128, 512, BF16)
    psTb = mkps(top, "psTb", 128, 512, BF16)
    psTf = mkps(top, "psTf", 128, 512)

    sg = mksem(top, "sg")
    sh = mksem(top, "sh")
    st = mksem(top, "st")
    sv = mksem(top, "sv")
    ss = mksem(top, "ss")
    scc = mksem(top, "scc")
    ph = mksem(top, "ph")
    pipe_names = ["q1", "q2", "q3", "q4", "q5", "q6", "q7", "q8",
                  "dg0", "dg1", "dg2", "dg3", "dg4", "dg5", "dg6"]
    Q = {n: mksem(top, n) for n in pipe_names}
    qc = {n: 0 for n in pipe_names}

    cnt = {id(sg): 0, id(sh): 0, id(st): 0, id(sv): 0, id(ss): 0, id(scc): 0}
    nsm = {}
    nbar = [0]

    NSD = 14
    SD = [mksem(top, f"sd{i}") for i in range(NSD)]
    sdc = [0] * NSD
    sdbar = [0] * NSD
    qbar = {n: 0 for n in pipe_names}
    sdp = [0]
    srace = {n: mksem(top, f"sr_{n}") for n in
             ("Vector", "Scalar", "Tensor", "Sync", "GpSimd", "Pool")}
    cnt_sr = {n: 0 for n in srace}

    class Stream:
        def __init__(self, npar):
            self.sems = [sdp[0] + p for p in range(npar)]
            sdp[0] += npar
            assert sdp[0] <= NSD, "stream pool exhausted in phase window"
            self.npar = npar
            self.val = {}
            self._preset = False

        def preset(self, nsteps, per=1):
            shadow = {s: sdc[s] for s in self.sems}
            for k in range(nsteps):
                si = self.sems[k % self.npar]
                shadow[si] += 16 * per
                self.val[k] = shadow[si]
            self._preset = True

        def inc(self, instr, k):
            si = self.sems[k % self.npar]
            sdc[si] += 16
            if self._preset:
                assert sdc[si] <= self.val[k], "stream preset mismatch"
            else:
                self.val[k] = sdc[si]
            instr.then_inc(SD[si], 16)

        def wait(self, eng, k):
            eng.wait_ge(SD[self.sems[k % self.npar]], self.val[k])

        def wait_all(self, eng):
            for s in self.sems:
                if sdc[s] > sdbar[s]:
                    eng.wait_ge(SD[s], sdc[s])

    blk = top.enter_context(nc.Block())

    def inc(instr, s, n=1):
        instr.then_inc(s, n)
        cnt[id(s)] += n
        return cnt[id(s)]

    def qinc(instr, name):
        instr.then_inc(Q[name], 1)
        qc[name] += 1
        return qc[name]

    def qinc16(instr, name):
        instr.then_inc(Q[name], 16)
        qc[name] += 16
        return qc[name]

    def on(which, fn):
        getattr(blk, which)(fn)

    def ng(eng, instr=None, sem=None, val=None):
        if not races:
            return
        if val is None:
            en = next((n for n in srace if n in type(eng).__name__), "Sync")
            cnt_sr[en] += 1
            instr.then_inc(srace[en], 1)
            sem, val = srace[en], cnt_sr[en]
        eng.wait_ge(sem, val)

    def dbg_dump(name, ap_fn, sem_, val_fn):
        if not dbg:
            return
        p, f = ap_fn().partition_size(), ap_fn().free_size()
        od = dtn("dbg_" + name, [p, f], ap_fn().dtype, kind="ExternalOutput").ap()

        def _d(s):
            s.wait_ge(sem_, val_fn())
            inc(s.dma_start(out=od[:], in_=ap_fn()), sh, 16)
        on("sync", _d)

    def barrier():
        nbar[0] += 1
        t_ = 5 * nbar[0]
        shv, sgv = cnt[id(sh)], cnt[id(sg)]
        sdv = list(sdc)
        qv = dict(qc)
        for wn in ("tensor", "vector", "scalar", "sync", "gpsimd"):
            def f(eng, _t=t_, _wn=wn):
                if _wn == "sync":
                    eng.wait_ge(sh, shv)
                    eng.wait_ge(scc, cnt[id(scc)])
                    for i_ in range(NSD):
                        if sdv[i_] > sdbar[i_]:
                            eng.wait_ge(SD[i_], sdv[i_])
                    for qn_ in pipe_names:
                        if qv[qn_] > qbar[qn_]:
                            eng.wait_ge(Q[qn_], qv[qn_])
                if _wn == "gpsimd":
                    eng.wait_ge(sg, sgv)
                eng.drain()
                eng.sem_inc(ph, 1)
                eng.wait_ge(ph, _t)
            on(wn, f)
        sdbar[:] = sdv
        qbar.update(qv)
        sdp[0] = 0

    ph0 = contextlib.ExitStack()
    xprojT = mksb(top, "xprojT", 128, C.DT * C.RK2, BF16)
    dtwT = mksb(top, "dtwT", C.R, C.DH, BF16)
    xT = mksb(top, "xT", 128, C.KC * C.L, BF16)
    inwT = mksb(top, "inwT", 128, C.KC * 2 * C.DH, BF16)

    def p0_g(g):
        gq = C.DM // 4
        for qn in range(4):
            qinc16(g.dma_start(out=xb[:, qn * gq:(qn + 1) * gq],
                               in_=i_x[:, qn * gq:(qn + 1) * gq]), "dg0")
        qinc16(g.dma_start(out=inwb[:], in_=i_inw[:]), "dg0")
        qinc16(g.dma_start(out=xprojTb[:], in_=i_xprojT[:]), "q5")
        qinc16(g.dma_start(out=dtwTb[:], in_=i_dtwT[:]), "q5")
        qinc16(g.dma_start(out=xhb[:], in_=i_xh[:]), "q6")
        for ds_, sr in ((outwTb, i_outwT), (seTb, i_seT),
                        (wgb, i_wg), (wub, i_wu), (wdb, i_wd)):
            qinc16(g.dma_start(out=ds_[:], in_=sr[:]), "dg1")
    on("gpsimd", p0_g)

    def p0_s(s):
        for sr, ds_ in ((i_convw, convw), (i_convb, convb), (i_dtb, dtb),
                        (i_alog, alog), (i_dv, dvs), (i_normg, normg),
                        (i_normb, normb), (i_bgT, bgT), (i_buT, buT),
                        (i_eyeb, eyeb), (i_eyef, eyef)):
            inc(s.dma_start(out=ds_[:], in_=sr[:]), sh, 16)
        inc(s.dma_start(out=scale_bc[:], in_=i_scale[:].broadcast_to((128, 1))), sh, 16)
        inc(s.dma_start(out=tiny[:, 0:C.KC], in_=i_slotg[:]), sh, 16)
        inc(s.dma_start(out=tiny[:, C.KC:2 * C.KC], in_=i_slotb[:]), sh, 16)
        s.wait_ge(Q["dg0"], 80)
        for kc in range(C.KC):
            inc(s.dma_start_transpose(out=xT[:, kc * C.L:(kc + 1) * C.L],
                                      in_=xb[:, kc * 128:(kc + 1) * 128]), sh, 16)
        for kc in range(C.KC):
            inc(s.dma_start_transpose(out=inwT[:, kc * 2 * C.DH:(kc + 1) * 2 * C.DH],
                                      in_=inwb[:, kc * 128:(kc + 1) * 128]), sh, 16)
    on("sync", p0_s)
    P0H = cnt[id(sh)]

    REG = {}

    def p0_s2(s):
        pid = s.partition_id()
        REG["pid"] = pid
        REG["pxor"] = s.compute_val((pid + 4) % 8)
        REG["jj"] = s.compute_val(pid // 4)
        REG["jx"] = s.compute_val(1 - pid // 4)
        REG["breg"] = s.compute_val(pid % 4)
        REG["jcol"] = s.compute_val((pid // 4) * C.LH)
    on("sync", p0_s2)

    def p0_sc(sc):
        sc.wait_ge(sh, P0H)
        inc(sc.activation(Amat[:], alog[:], AF.Exp), ss)
    on("scalar", p0_sc)
    p0ss = cnt[id(ss)]

    def p0_v(v):
        v.memset(ones1[:], 1.0)
        v.memset(csts[:, 0:1], EPS)
        v.memset(csts[:, 1:2], -1.0)
        v.wait_ge(sh, P0H)
        v.tensor_scalar_mul(sgsc[:], tiny[:, 0:C.KC], scale_bc[:, 0:1])
        v.tensor_scalar_mul(sbsc[:], tiny[:, C.KC:2 * C.KC], scale_bc[:, 0:1])
        v.wait_ge(ss, p0ss)
        inc(v.tensor_scalar_mul(Amat[:], Amat[:], -1.0), sv)
    on("vector", p0_v)

    barrier()

    ev1 = mksb(top, "ev1", 128, 4 * 512, BF16)
    P1SS0 = cnt[id(ss)]
    sp1 = Stream(4)
    sp1.preset(2 * C.DT * C.NW, 1)

    def p1_t(t):
        t.wait_ge(sh, P0H)
        i = 0
        for ot in range(2 * C.DT):
            for q in range(C.NW):
                pst = psA if i % 2 == 0 else psB
                if i >= 2:
                    t.wait_ge(ss, P1SS0 + i - 1)
                for kc in range(C.KC):
                    mm = t.matmul(pst[:, 0:C.W],
                                  inwT[:, kc * 2 * C.DH + ot * 128:kc * 2 * C.DH + (ot + 1) * 128],
                                  xT[:, kc * C.L + q * C.W:kc * C.L + (q + 1) * C.W],
                                  start=(kc == 0), stop=(kc == C.KC - 1))
                nsm[("p1m", i)] = inc(mm, st)
                i += 1
    on("tensor", p1_t)

    def p1_sc(sc):
        i = 0
        for ot in range(2 * C.DT):
            for q in range(C.NW):
                pst = psA if i % 2 == 0 else psB
                sc.wait_ge(st, nsm[("p1m", i)])
                if i >= 4:
                    sp1.wait(sc, i - 4)
                so = (i % 4) * 512
                if ot < C.DT:
                    e = sc.copy(ev1[:, so:so + C.W], pst[:, 0:C.W])
                else:
                    e = sc.activation(ev1[:, so:so + C.W], pst[:, 0:C.W], AF.Silu)
                nsm[("p1e", i)] = inc(e, ss)
                i += 1
    on("scalar", p1_sc)

    def p1_s(s):
        i = 0
        for ot in range(2 * C.DT):
            for q in range(C.NW):
                s.wait_ge(ss, nsm[("p1e", i)])
                so = (i % 4) * 512
                dst = xx_h if ot < C.DT else sres_h
                rt = ot if ot < C.DT else ot - C.DT
                e = s.dma_start(out=dst[rt * 128:(rt + 1) * 128, q * C.W:(q + 1) * C.W],
                                in_=ev1[:, so:so + C.W])
                sp1.inc(e, i)
                i += 1
    on("sync", p1_s)

    def p0_s_late(s):
        s.wait_ge(Q["q5"], 32)
        for dti in range(C.DT):
            inc(s.dma_start(out=xprojT[:, dti * C.RK2:(dti + 1) * C.RK2],
                            in_=xprojTb[dti * 128:(dti + 1) * 128, :]), sh, 16)
        inc(s.dma_start(out=dtwT[:], in_=dtwTb[:]), sh, 16)
    on("sync", p0_s_late)

    barrier()
    fre("ev1", "inwT", "xT")

    uu = mksb(top, "uu", 128, 2 * C.L, BF16)
    xxs = mksb(top, "xxs", 128, 2 * (4 + C.L), BF16)

    def p2_v0(v):
        v.memset(xxs[:, 0:4], 0.0)
        inc(v.memset(xxs[:, 4 + C.L:8 + C.L], 0.0), sv)
    on("vector", p2_v0)
    p2pad = cnt[id(sv)]
    P2VB = cnt[id(sv)]

    sx2 = Stream(2)
    su2 = Stream(2)
    sx2.preset(C.DT, 1)
    su2.preset(C.DT, 1)

    def p2_v(v):
        for dti in range(C.DT):
            base = (dti % 2) * (4 + C.L) + 4
            so = (dti % 2) * C.L
            tmp = uu[:, so:so + C.L]
            sx2.wait(v, dti)
            if dti >= 2:
                su2.wait(v, dti - 2)
            e = v.tensor_scalar_mul(tmp, xxs[:, base - 2:base - 2 + C.L],
                                    convw[:, dti * 4 + 1:dti * 4 + 2])
            ng(v, e)
            e = v.scalar_tensor_tensor(tmp, xxs[:, base - 3:base - 3 + C.L],
                                       convw[:, dti * 4 + 0:dti * 4 + 1], tmp, OP.mult, OP.add)
            ng(v, e)
            e = v.scalar_tensor_tensor(tmp, xxs[:, base - 1:base - 1 + C.L],
                                       convw[:, dti * 4 + 2:dti * 4 + 3], tmp, OP.mult, OP.add)
            ng(v, e)
            e = v.scalar_tensor_tensor(tmp, xxs[:, base:base + C.L],
                                       convw[:, dti * 4 + 3:dti * 4 + 4], tmp, OP.mult, OP.add)
            nsm[("p2c", dti)] = inc(e, sv)
    on("vector", p2_v)

    def p2_sc(sc):
        for dti in range(C.DT):
            so = (dti % 2) * C.L
            tmp = uu[:, so:so + C.L]
            sc.wait_ge(sv, nsm[("p2c", dti)])
            nsm[("p2u", dti)] = inc(
                sc.activation(tmp, tmp, AF.Silu, bias=convb[:, dti:dti + 1]), ss)
    on("scalar", p2_sc)

    def p2_sync(s):
        s.wait_ge(sv, p2pad)
        for k in range(C.DT + 2):
            if k < C.DT:
                if k >= 2:
                    s.wait_ge(sv, P2VB + k - 1)
                e = s.dma_start(out=xxs[:, (k % 2) * (4 + C.L) + 4:
                                        (k % 2) * (4 + C.L) + 4 + C.L],
                                in_=xx_h[k * 128:(k + 1) * 128, :])
                sx2.inc(e, k)
            if k >= 2:
                s.wait_ge(ss, nsm[("p2u", k - 2)])
                e = s.dma_start(out=u_h[(k - 2) * 128:(k - 1) * 128, :],
                                in_=uu[:, ((k - 2) % 2) * C.L:((k - 2) % 2) * C.L + C.L])
                su2.inc(e, k - 2)
    on("sync", p2_sync)

    barrier()
    fre("xxs", "uu")

    xdbl = mksb(top, "xdbl", C.RK2, C.L, F32)
    xdbl2 = mksb(top, "xdbl2", C.RK2, C.L, F32)
    ut3 = mksb(top, "ut3", 128, 2 * C.DT * C.W, BF16)
    P3SV0 = cnt[id(sv)]
    s3l = Stream(2)

    def p3_s(s):
        for q in range(C.NW):
            so = (q % 2) * C.DT * C.W
            if q >= 2:
                s.wait_ge(sv, P3SV0 + q - 1)
            for dti in range(C.DT):
                e = s.dma_start(out=ut3[:, so + dti * C.W:so + (dti + 1) * C.W],
                                in_=u_h[dti * 128:(dti + 1) * 128, q * C.W:(q + 1) * C.W])
                s3l.inc(e, q)
    on("sync", p3_s)

    def p3_t(t):
        for q in range(C.NW):
            so = (q % 2) * C.DT * C.W
            pst = psA if q % 2 == 0 else psB
            s3l.wait(t, q)
            if q >= 2:
                t.wait_ge(sv, P3SV0 + q - 1)
            for dti in range(C.DT):
                mm = t.matmul(pst[0:C.RK2, 0:C.W],
                              xprojT[:, dti * C.RK2:(dti + 1) * C.RK2],
                              ut3[:, so + dti * C.W:so + (dti + 1) * C.W],
                              start=(dti == 0), stop=(dti == C.DT - 1))
            nsm[("p3m", q)] = inc(mm, st)
    on("tensor", p3_t)

    def p3_v(v):
        for q in range(C.NW):
            pst = psA if q % 2 == 0 else psB
            v.wait_ge(st, nsm[("p3m", q)])
            e = v.tensor_copy(xdbl[:, q * C.W:(q + 1) * C.W], pst[0:C.RK2, 0:C.W])
            nsm[("p3e", q)] = inc(e, sv)
    on("vector", p3_v)

    def p3_s2(s):
        s.wait_ge(sv, nsm[("p3e", C.NW - 1)])
        nsm["p3stage"] = inc(s.dma_start(out=cin0[:], in_=xdbl[:]), sh, 16)
    on("sync", p3_s2)

    def p3_g(g):
        g.wait_ge(sh, nsm["p3stage"])
        nsm["cc0"] = inc(g.collective_compute(
            "AllReduce", OP.add, replica_groups=RGP, ins=[cin0[:]], outs=[cout0[:]]), scc)
    on("gpsimd", p3_g)

    def p3_s3(s):
        s.wait_ge(scc, nsm["cc0"])
        nsm["p3pl"] = inc(
            s.dma_start(out=xdbl[:], in_=cout0[:]), sh, 16)
    on("sync", p3_s3)

    xdblb = mksb(top, "xdblb", C.R, C.L, BF16)

    def p3_v2(v):
        v.wait_ge(sh, nsm["p3pl"])
        nsm["p3add"] = inc(v.tensor_copy(xdblb[:], xdbl[0:C.R, :]), sv)
    on("vector", p3_v2)

    def p3_g2(g):
        g.wait_ge(sv, nsm["p3add"])
        qinc16(g.dma_start(out=brow_h[:], in_=xdbl[C.R:C.RK2, :]), "dg2")
    on("gpsimd", p3_g2)

    barrier()

    sc1 = mksb(top, "sc1", 128, C.W, F32)
    sc2 = mksb(top, "sc2", 128, C.W, F32)
    sc3 = mksb(top, "sc3", 128, 2 * C.W, BF16)
    P4SV0a = cnt[id(sv)]
    s4d = Stream(2)
    s4d.preset(C.DT * C.NW, 1)

    def p4_t(t):
        i = 0
        for dti in range(C.DT):
            for q in range(C.NW):
                pst = psA if i % 2 == 0 else psB
                if i >= 2:
                    t.wait_ge(sv, P4SV0a + 2 * (i - 1) + 1)
                mm = t.matmul(pst[:, 0:C.W], dtwT[:, dti * 128:(dti + 1) * 128],
                              xdblb[:, q * C.W:(q + 1) * C.W], start=True, stop=True)
                nsm[("p4m", i)] = inc(mm, st)
                i += 1
    on("tensor", p4_t)

    def p4_v_joint(v):
        i = 0
        for dti in range(C.DT):
            for q in range(C.NW):
                pst = psA if i % 2 == 0 else psB
                v.wait_ge(st, nsm[("p4m", i)])
                if i >= 1:
                    ng(v, sem=sv, val=nsm[("p4d", i - 1)])
                e = v.tensor_scalar_add(sc1[:, 0:C.W], pst[:, 0:C.W], dtb[:, dti:dti + 1])
                nsm[("p4z", i)] = inc(e, sv)
                v.wait_ge(ss, P4SS0[0] + i + 1)
                if i >= 2:
                    s4d.wait(v, i - 2)
                so3 = (i % 2) * C.W
                e = v.tensor_scalar_max(sc1[:, 0:C.W], sc1[:, 0:C.W], 0.0)
                ng(v, e)
                e = v.tensor_add(sc3[:, so3:so3 + C.W], sc1[:, 0:C.W], sc2[:, 0:C.W])
                nsm[("p4d", i)] = inc(e, sv)
                i += 1
    P4SS0 = [None]
    P4SV0 = [None]

    def p4_sc2(sc):
        i = 0
        for dti in range(C.DT):
            for q in range(C.NW):
                sc.wait_ge(sv, P4SV0[0] + 2 * i + 1)
                e = sc.activation(sc2[:, 0:C.W], sc1[:, 0:C.W], AF.Abs)
                ng(sc, e, sem=ss)
                e = sc.activation(sc2[:, 0:C.W], sc2[:, 0:C.W], AF.Exp, scale=csts[:, 1:2])
                ng(sc, e, sem=ss)
                inc(sc.activation(sc2[:, 0:C.W], sc2[:, 0:C.W], AF.Ln, bias=1.0), ss)
                i += 1

    P4SV0[0] = cnt[id(sv)]
    P4SS0[0] = cnt[id(ss)]
    on("scalar", p4_sc2)
    on("vector", p4_v_joint)

    def p4_s(s):
        i = 0
        for dti in range(C.DT):
            for q in range(C.NW):
                s.wait_ge(sv, nsm[("p4d", i)])
                so3 = (i % 2) * C.W
                e = s.dma_start(out=delta_h[dti * 128:(dti + 1) * 128, q * C.W:(q + 1) * C.W],
                                in_=sc3[:, so3:so3 + C.W])
                s4d.inc(e, i)
                i += 1
    on("sync", p4_s)

    barrier()
    fre("sc3", "sc2", "sc1", "xdblb", "ut3", "xdbl2", "xdbl", "dtwT", "xprojT")

    d_lq = mksb(top, "d_lq", 128, C.DT * C.LC, BF16)
    du_lq = mksb(top, "du_lq", 128, C.DT * C.LC, BF16)
    y3_lq = mksb(top, "y3_lq", 128, C.DT * C.LC, BF16)
    sr_lq = mksb(top, "sr_lq", 128, C.DT * C.LC, BF16)
    u_lq = mksb(top, "u_lq", 128, C.DT * C.LC, BF16)
    bsl = mksb(top, "bsl", 128, 2 * C.LC, BF16)
    csl = mksb(top, "csl", 128, 2 * C.LC, BF16)
    dasl = mksb(top, "dasl", 128, 2 * C.DT * C.LC, BF16)
    dbsl = mksb(top, "dbsl", 128, C.DT * C.LC, BF16)
    hsl = mksb(top, "hsl", 128, C.DT * C.LC, BF16)
    carry = mksb(top, "carry", 128, C.DT * C.NST, F32)
    outwT = mksb(top, "outwT", 128, C.DT * C.DO, BF16)
    y3s = mksb(top, "y3s", 128, C.DT * min(C.W, C.LC), BF16)
    evb = mksb(top, "evb", 128, 2 * 512, BF16)

    def r3(ap):
        return ap.rearrange("p (t l) -> p t l", t=C.DT)

    WL = min(C.W, C.LC)
    NWL = C.LC // WL
    NGQ = C.NLC * NWL
    Q7B = qc["q7"]
    Q8B = qc["q8"]
    for _i in range(NGQ * C.KC):
        nsm[("p7e", _i)] = Q8B + _i + 1
    sA6 = Stream(2)
    sB6 = Stream(2)
    sC6 = Stream(2)
    sSP = Stream(2)
    sYL = Stream(1)
    sYL.preset(C.NLC * (C.LC // min(C.W, C.LC)), C.DT)
    sST = Stream(2)
    sST.preset(NGQ * C.KC, 1)
    s7w = Stream(1)

    def p7_w(s):
        s.wait_ge(Q["dg1"], 80)
        for dti in range(C.DT):
            s7w.inc(s.dma_start(out=outwT[:, dti * C.DO:(dti + 1) * C.DO],
                                in_=outwTb[dti * 128:(dti + 1) * 128, :]), 0)
    on("sync", p7_w)

    for lq in range(C.NLC):
        SS0 = [None]

        def p6_load(s, _lq=lq):
            s.wait_ge(Q["dg2"], 16)
            if _lq >= 1:
                s.wait_ge(ss, nsm[("p6dae", _lq - 1)])
                s.wait_ge(sv, nsm[("p6pro", _lq - 1)])
            for dti in range(C.DT):
                sA6.inc(s.dma_start(out=d_lq[:, dti * C.LC:(dti + 1) * C.LC],
                                    in_=delta_h[dti * 128:(dti + 1) * 128,
                                                _lq * C.LC:(_lq + 1) * C.LC]), _lq)
            for dti in range(C.DT):
                sA6.inc(s.dma_start(out=u_lq[:, dti * C.LC:(dti + 1) * C.LC],
                                    in_=u_h[dti * 128:(dti + 1) * 128,
                                            _lq * C.LC:(_lq + 1) * C.LC]), _lq)
            for n in range(C.NST):
                sl = n % 2
                gn = _lq * C.NST + n
                if gn >= 2:
                    s.wait_ge(Q["q7"], Q7B + gn - 1)
                sC6.inc(s.dma_start(out=bsl[:, sl * C.LC:(sl + 1) * C.LC],
                                    in_=brow_h[n:n + 1, _lq * C.LC:(_lq + 1) * C.LC]
                                    .broadcast_to((128, C.LC))), gn)
                e = s.dma_start(out=csl[:, sl * C.LC:(sl + 1) * C.LC],
                                in_=brow_h[C.NST + n:C.NST + n + 1,
                                           _lq * C.LC:(_lq + 1) * C.LC]
                                .broadcast_to((128, C.LC)))
                sC6.inc(e, gn)
            if _lq >= 1:
                s.wait_ge(sv, nsm[("p6y3", _lq - 1)])
            for dti in range(C.DT):
                sB6.inc(s.dma_start(out=sr_lq[:, dti * C.LC:(dti + 1) * C.LC],
                                    in_=sres_h[dti * 128:(dti + 1) * 128,
                                               _lq * C.LC:(_lq + 1) * C.LC]), _lq)
        on("sync", p6_load)

        def p6_da(sc, _lq=lq):
            SS0[0] = cnt[id(ss)]
            sA6.wait(sc, _lq)
            for n in range(C.NST):
                sl = n % 2
                gn = _lq * C.NST + n
                if gn >= 2:
                    sc.wait_ge(Q["q7"], Q7B + gn - 1)
                for dti in range(C.DT):
                    e = sc.activation(dasl[:, (sl * C.DT + dti) * C.LC:
                                           (sl * C.DT + dti + 1) * C.LC],
                                      d_lq[:, dti * C.LC:(dti + 1) * C.LC], AF.Exp,
                                      scale=Amat[:, dti * C.NST + n:dti * C.NST + n + 1])
                    inc(e, ss)
            nsm[("p6dae", _lq)] = cnt[id(ss)]
        on("scalar", p6_da)

        if lq >= 1:
            def p7_sc(sc, _lq=lq - 1):
                for qw in range(NWL):
                    gq = _lq * NWL + qw
                    for ot in range(C.KC):
                        i = gq * C.KC + ot
                        pst = psA if i % 2 == 0 else psB
                        sc.wait_ge(st, nsm[("p7m", i)])
                        if i >= 2:
                            sST.wait(sc, i - 2)
                        so = (i % 2) * 512
                        e = sc.copy(evb[:, so:so + WL], pst[:, 0:WL])
                        nsm[("p7e", i)] = qinc(e, "q8")
            on("scalar", p7_sc)

        def p6_v(v, _lq=lq):
            sA6.wait(v, _lq)
            if _lq >= 1:
                sSP.wait(v, _lq - 1)
            e = v.tensor_mul(du_lq[:], d_lq[:], u_lq[:])
            ng(v, e)
            for dti in range(C.DT):
                e = v.tensor_scalar_mul(y3_lq[:, dti * C.LC:(dti + 1) * C.LC],
                                        u_lq[:, dti * C.LC:(dti + 1) * C.LC],
                                        dvs[:, dti:dti + 1])
                ng(v, e)
            nsm[("p6pro", _lq)] = inc(v.engine_nop(), sv)
            for n in range(C.NST):
                gn = _lq * C.NST + n
                sC6.wait(v, gn)
                sl2 = (n % 2) * C.LC
                sl = (n % 2) * C.DT
                if races and gn >= 1:
                    v.wait_ge(Q["q7"], Q7B + gn)
                bbc = bsl[:, sl2:sl2 + C.LC].unsqueeze(1).broadcast_to(
                    (128, C.DT, C.LC))
                cbc = csl[:, sl2:sl2 + C.LC].unsqueeze(1).broadcast_to(
                    (128, C.DT, C.LC))
                e = v.tensor_tensor(r3(dbsl[:]), r3(du_lq[:]), bbc, OP.mult)
                ng(v, e)
                for dti in range(C.DT):
                    ci = dti * C.NST + n
                    v.wait_ge(ss, SS0[0] + n * C.DT + dti + 1)
                    init = 0.0 if _lq == 0 else carry[:, ci:ci + 1]
                    e = v.tensor_tensor_scan(
                        hsl[:, dti * C.LC:(dti + 1) * C.LC],
                        dasl[:, (sl + dti) * C.LC:(sl + dti + 1) * C.LC],
                        dbsl[:, dti * C.LC:(dti + 1) * C.LC],
                        init, OP.mult, OP.add)
                    ng(v, e)
                e = v.tensor_copy(
                    carry[:].rearrange("p (t m) -> p t m", m=C.NST)[:, :, n:n + 1],
                    r3(hsl[:])[:, :, C.LC - 1:C.LC])
                ng(v, e)
                e = v.tensor_tensor(r3(dbsl[:]), r3(hsl[:]), cbc, OP.mult)
                ng(v, e)
                e = v.tensor_add(y3_lq[:], y3_lq[:], dbsl[:])
                qinc(e, "q7")
            sB6.wait(v, _lq)
            if races:
                v.wait_ge(Q["q7"], Q7B + (_lq + 1) * C.NST)
            e = v.tensor_mul(y3_lq[:], y3_lq[:], sr_lq[:])
            nsm[("p6y3", _lq)] = inc(e, sv)
        on("vector", p6_v)

        def p7_t(t, _lq=lq):
            if _lq == 0:
                s7w.wait(t, 0)
            for qw in range(NWL):
                gq = _lq * NWL + qw
                sYL.wait(t, gq)
                for ot in range(C.KC):
                    i = gq * C.KC + ot
                    pst = psA if i % 2 == 0 else psB
                    if i >= 2:
                        t.wait_ge(Q["q8"], Q8B + i - 1)
                    for dti in range(C.DT):
                        mm = t.matmul(pst[:, 0:WL],
                                      outwT[:, dti * C.DO + ot * 128:
                                            dti * C.DO + (ot + 1) * 128],
                                      y3s[:, dti * WL:(dti + 1) * WL],
                                      start=(dti == 0), stop=(dti == C.DT - 1))
                    nsm[("p7m", i)] = inc(mm, st)
                nsm[("p7mq", gq)] = cnt[id(st)]
        on("tensor", p7_t)

        def p6_sp(g, _lq=lq):
            g.wait_ge(sv, nsm[("p6y3", _lq)])
            for dti in range(C.DT):
                sSP.inc(g.dma_start(out=y3_h[dti * 128:(dti + 1) * 128,
                                             _lq * C.LC:(_lq + 1) * C.LC],
                                    in_=y3_lq[:, dti * C.LC:(dti + 1) * C.LC]), _lq)
            for qw in range(NWL):
                gq = _lq * NWL + qw
                sSP.wait(g, _lq)
                if gq >= 1:
                    g.wait_ge(st, nsm[("p7mq", gq - 1)])
                for dti in range(C.DT):
                    sYL.inc(g.dma_start(
                        out=y3s[:, dti * WL:(dti + 1) * WL],
                        in_=y3_h[dti * 128:(dti + 1) * 128,
                                 gq * WL:(gq + 1) * WL]), gq)
                for ot in range(C.KC):
                    i = gq * C.KC + ot
                    g.wait_ge(Q["q8"], nsm[("p7e", i)])
                    so = (i % 2) * 512
                    e = g.dma_start(out=cin1[ot * 128:(ot + 1) * 128,
                                             gq * WL:(gq + 1) * WL],
                                    in_=evb[:, so:so + WL])
                    sST.inc(e, i)
        on("gpsimd", p6_sp)

    def p7_sc_last(sc, _lq=C.NLC - 1):
        for qw in range(NWL):
            gq = _lq * NWL + qw
            for ot in range(C.KC):
                i = gq * C.KC + ot
                pst = psA if i % 2 == 0 else psB
                sc.wait_ge(st, nsm[("p7m", i)])
                if i >= 2:
                    sST.wait(sc, i - 2)
                so = (i % 2) * 512
                e = sc.copy(evb[:, so:so + WL], pst[:, 0:WL])
                nsm[("p7e", i)] = qinc(e, "q8")
    on("scalar", p7_sc_last)

    def p7_g(g):
        sST.wait_all(g)
        nsm["cc1"] = inc(g.collective_compute(
            "AllGather", OP.bypass, replica_groups=RGP, ins=[cin1[:]], outs=[cout1[:]]), scc)
    on("gpsimd", p7_g)

    barrier()
    fre("evb", "y3s", "outwT", "carry", "hsl", "dbsl", "dasl", "csl", "bsl",
        "u_lq", "sr_lq", "y3_lq", "du_lq", "d_lq")

    combT = mksb(top, "combT", 128, C.ET * C.LH, BF16, side="right")
    xr1 = mksb(top, "xr1", 128, C.KC * C.LH, BF16, side="right")
    xsq = mksb(top, "xsq", 128, C.KC * max(C.LH, C.ES), BF16)
    pme = mksb(top, "pme", 128, C.KC * C.LH, BF16)
    ppt = mksb(top, "ppt", 128, C.KC * C.LH, BF16)

    s8l = Stream(1)

    def p8_s0(s):
        s.wait_ge(scc, nsm["cc1"])
        jcol, jj, jx = REG["jcol"], REG["jj"], REG["jx"]
        for kc in range(C.KC):
            s8l.inc(s.dma_start_transpose(out=xr1[:, kc * C.LH:(kc + 1) * C.LH],
                                          in_=xhb[:, kc * 128:(kc + 1) * 128]), 0)
        c1r = cout1.rearrange("(c k p) l -> c p k l", c=2, k=C.KC)
        pme_r = pme[:].rearrange("p (k l) -> p k l", k=C.KC)
        ppt_r = ppt[:].rearrange("p (k l) -> p k l", k=C.KC)
        s8l.inc(s.dma_start(out=pme_r,
                            in_=c1r[DS(jj, 1), :, :, DS(jcol, C.LH)].squeeze(0)), 0)
        s8l.inc(s.dma_start(out=ppt_r,
                            in_=c1r[DS(jx, 1), :, :, DS(jcol, C.LH)].squeeze(0)), 0)
    on("sync", p8_s0)

    def p8_v(v):
        s8l.wait(v, 0)
        for kc in range(C.KC):
            e = v.tensor_add(pme[:, kc * C.LH:(kc + 1) * C.LH],
                             pme[:, kc * C.LH:(kc + 1) * C.LH],
                             ppt[:, kc * C.LH:(kc + 1) * C.LH])
            ng(v, e)
            e = v.tensor_add(xr1[:, kc * C.LH:(kc + 1) * C.LH],
                             xr1[:, kc * C.LH:(kc + 1) * C.LH],
                             pme[:, kc * C.LH:(kc + 1) * C.LH])
        nsm["p8res"] = inc(e, sv)
    on("vector", p8_v)

    LNI = [0]

    def ln_feat(xap, outap, ncols, gain, bias, pre_sem, pre_val):
        li = LNI[0]
        LNI[0] += 1
        NWS = ncols // min(512, ncols)
        WS = min(512, ncols)
        sln = Stream(1)

        def l_sc(sc):
            sc.wait_ge(pre_sem, pre_val)
            for kc in range(C.KC):
                e = sc.activation(xsq[:, kc * ncols:(kc + 1) * ncols],
                                  xap[:, kc * ncols:(kc + 1) * ncols], AF.Square)
            nsm[("lnsq", li)] = inc(e, ss)
        on("scalar", l_sc)

        q1b = qc["q1"]

        def l_t(t):
            t.wait_ge(ss, nsm[("lnsq", li)])
            for w in range(NWS):
                if w >= 1:
                    t.wait_ge(Q["q1"], q1b + w)
                pa = psSa[0:1, 0:WS]
                pb = psSb[0:1, 0:WS]
                for kc in range(C.KC):
                    t.matmul(pa, ones1[:], xap[:, kc * ncols + w * WS:kc * ncols + (w + 1) * WS],
                             start=(kc == 0), stop=(kc == C.KC - 1))
                for kc in range(C.KC):
                    mm = t.matmul(pb, ones1[:],
                                  xsq[:, kc * ncols + w * WS:kc * ncols + (w + 1) * WS],
                                  start=(kc == 0), stop=(kc == C.KC - 1))
                nsm[("lnst", li, w)] = inc(mm, st)
        on("tensor", l_t)

        def l_v(v):
            for w in range(NWS):
                v.wait_ge(st, nsm[("lnst", li, w)])
                pa = psSa[0:1, 0:WS]
                pb = psSb[0:1, 0:WS]
                e = v.tensor_scalar_mul(row1[0:1, w * WS:(w + 1) * WS], pa, 1.0 / C.DM)
                ng(v, e)
                e = v.tensor_scalar_mul(row1[0:1, ncols + w * WS:ncols + (w + 1) * WS],
                                        pb, 1.0 / C.DM)
                qinc(e, "q1")
                ng(v, sem=Q["q1"], val=qc["q1"])
            t = inc(v.tensor_mul(mrow[0:1, 0:ncols], row1[0:1, 0:ncols],
                                 row1[0:1, 0:ncols]), sv)
            v.wait_ge(sv, t)
            e = v.tensor_sub(row1[0:1, ncols:2 * ncols], row1[0:1, ncols:2 * ncols],
                             mrow[0:1, 0:ncols])
            nsm[("lnvar", li)] = inc(e, sv)
        on("vector", l_v)

        def l_sc2(sc):
            sc.wait_ge(sv, nsm[("lnvar", li)])
            e = sc.activation(row1[0:1, ncols:2 * ncols], row1[0:1, ncols:2 * ncols],
                              AF.Sqrt, bias=csts[0:1, 0:1])
            nsm[("lnsd", li)] = inc(e, ss)
        on("scalar", l_sc2)

        def l_v2(v):
            v.wait_ge(ss, nsm[("lnsd", li)])
            e = v.reciprocal(row1[0:1, ncols:2 * ncols], row1[0:1, ncols:2 * ncols])
            nsm[("lnr", li)] = inc(e, sv)
        on("vector", l_v2)

        def l_s(s):
            s.wait_ge(sv, nsm[("lnr", li)])
            sln.inc(s.dma_start(out=murow_h[0:1, 0:ncols], in_=row1[0:1, 0:ncols]), 0)
            e = s.dma_start(out=murow_h[1:2, 0:ncols], in_=row1[0:1, ncols:2 * ncols])
            sln.inc(e, 0)
        on("sync", l_s)

        dgn = "dg3" if li == 0 else "dg4"

        def l_g(g):
            sln.wait(g, 0)
            qinc16(g.dma_start(out=mrow[:, 0:ncols],
                               in_=murow_h[0:1, 0:ncols].broadcast_to((128, ncols))), dgn)
            qinc16(g.dma_start(out=rrow[:, 0:ncols],
                               in_=murow_h[1:2, 0:ncols].broadcast_to((128, ncols))), dgn)
        on("gpsimd", l_g)

        def l_v3(v):
            v.wait_ge(Q[dgn], 32)
            for kc in range(C.KC):
                e = v.tensor_sub(outap[:, kc * ncols:(kc + 1) * ncols],
                                 xap[:, kc * ncols:(kc + 1) * ncols], mrow[:, 0:ncols])
                ng(v, e)
                e = v.tensor_mul(outap[:, kc * ncols:(kc + 1) * ncols],
                                 outap[:, kc * ncols:(kc + 1) * ncols], rrow[:, 0:ncols])
                if gain is not None:
                    ng(v, e)
                    e = v.tensor_scalar(outap[:, kc * ncols:(kc + 1) * ncols],
                                        outap[:, kc * ncols:(kc + 1) * ncols],
                                        gain[:, kc:kc + 1], bias[:, kc:kc + 1],
                                        OP.mult, OP.add)
            nsm[("lndone", li)] = inc(e, sv)
            return
        on("vector", l_v3)
        return li

    nln = C.LH
    rowf = mksb(top, "rowf", 1, 2 * C.LH, F32)
    sfl = Stream(1)
    s9p = Stream(1)

    def fl_sc(sc):
        sc.wait_ge(sv, nsm["p8res"])
        for kc in range(C.KC):
            e = sc.activation(xsq[:, kc * nln:(kc + 1) * nln],
                              xr1[:, kc * nln:(kc + 1) * nln], AF.Square)
        nsm["flsq"] = inc(e, ss)
    on("scalar", fl_sc)

    q1b = qc["q1"]
    NWS = nln // min(512, nln)
    WS = min(512, nln)

    def fl_t(t):
        t.wait_ge(ss, nsm["flsq"])
        for w in range(NWS):
            if w >= 1:
                t.wait_ge(Q["q1"], q1b + w)
            pa = psSa[0:1, 0:WS]
            pb = psSb[0:1, 0:WS]
            for kc in range(C.KC):
                t.matmul(pa, ones1[:], xr1[:, kc * nln + w * WS:kc * nln + (w + 1) * WS],
                         start=(kc == 0), stop=(kc == C.KC - 1))
            for kc in range(C.KC):
                mm = t.matmul(pb, ones1[:],
                              xsq[:, kc * nln + w * WS:kc * nln + (w + 1) * WS],
                              start=(kc == 0), stop=(kc == C.KC - 1))
            nsm[("flst", w)] = inc(mm, st)
    on("tensor", fl_t)

    def fl_v(v):
        for w in range(NWS):
            v.wait_ge(st, nsm[("flst", w)])
            e = v.tensor_scalar_mul(row1[0:1, w * WS:(w + 1) * WS],
                                    psSa[0:1, 0:WS], 1.0 / C.DM)
            ng(v, e)
            e = v.tensor_scalar_mul(row1[0:1, nln + w * WS:nln + (w + 1) * WS],
                                    psSb[0:1, 0:WS], 1.0 / C.DM)
            qinc(e, "q1")
            ng(v, sem=Q["q1"], val=qc["q1"])
        e = v.tensor_mul(rowf[0:1, 0:nln], row1[0:1, 0:nln], row1[0:1, 0:nln])
        ng(v, e)
        e = v.tensor_sub(row1[0:1, nln:2 * nln], row1[0:1, nln:2 * nln],
                         rowf[0:1, 0:nln])
        nsm["flv"] = inc(e, sv)
    on("vector", fl_v)

    def fl_sc2(sc):
        sc.wait_ge(sv, nsm["flv"])
        e = sc.activation(rowf[0:1, 0:nln], row1[0:1, nln:2 * nln],
                          AF.Sqrt, bias=csts[0:1, 0:1])
        nsm["flsd"] = inc(e, ss)
    on("scalar", fl_sc2)

    def fl_v2(v):
        v.wait_ge(ss, nsm["flsd"])
        e = v.reciprocal(rowf[0:1, 0:nln], rowf[0:1, 0:nln])
        ng(v, e)
        e = v.tensor_mul(rowf[0:1, nln:2 * nln], rowf[0:1, 0:nln],
                         rowf[0:1, 0:nln])
        ng(v, e)
        e = v.tensor_mul(rowf[0:1, nln:2 * nln], rowf[0:1, nln:2 * nln],
                         row1[0:1, nln:2 * nln])
        nsm["flvz"] = inc(e, sv)
    on("vector", fl_v2)

    def fl_sc3(sc):
        sc.wait_ge(sv, nsm["flvz"])
        e = sc.activation(rowf[0:1, nln:2 * nln], rowf[0:1, nln:2 * nln],
                          AF.Sqrt, bias=csts[0:1, 0:1])
        nsm["flsd2"] = inc(e, ss)
    on("scalar", fl_sc3)

    def fl_v3(v):
        v.wait_ge(ss, nsm["flsd2"])
        e = v.reciprocal(rowf[0:1, nln:2 * nln], rowf[0:1, nln:2 * nln])
        nsm["flf2"] = inc(e, sv)
    on("vector", fl_v3)

    def fl_s(s):
        s.wait_ge(sv, nsm["flf2"])
        sfl.inc(s.dma_start(out=murow_h[0:1, 0:nln], in_=row1[0:1, 0:nln]), 0)
        sfl.inc(s.dma_start(out=murow_h[1:2, 0:nln],
                            in_=rowf[0:1, 0:nln]), 0)
        sfl.inc(s.dma_start(out=murow_h[2:3, 0:nln],
                            in_=rowf[0:1, nln:2 * nln]), 0)
    on("sync", fl_s)

    frow = ppt

    def fl_g(g):
        sfl.wait(g, 0)
        qinc16(g.dma_start(out=mrow[:, 0:nln],
                           in_=murow_h[0:1, 0:nln].broadcast_to((128, nln))), "dg3")
        qinc16(g.dma_start(out=rrow[:, 0:nln],
                           in_=murow_h[1:2, 0:nln].broadcast_to((128, nln))), "dg3")
        qinc16(g.dma_start(out=frow[:, 0:nln],
                           in_=murow_h[2:3, 0:nln].broadcast_to((128, nln))), "dg3")
        nsm["flbc"] = qc["dg3"]
    on("gpsimd", fl_g)

    def fl_v4(v):
        v.wait_ge(Q["dg3"], nsm["flbc"])
        for kc in range(C.KC):
            e = v.tensor_sub(xr1[:, kc * nln:(kc + 1) * nln],
                             xr1[:, kc * nln:(kc + 1) * nln], mrow[:, 0:nln])
            ng(v, e)
            e = v.tensor_mul(xr1[:, kc * nln:(kc + 1) * nln],
                             xr1[:, kc * nln:(kc + 1) * nln], rrow[:, 0:nln])
        nsm["flz"] = inc(e, sv)
    on("vector", fl_v4)

    def p9_spill(s):
        s.wait_ge(sv, nsm["flz"])
        for kc in range(C.KC):
            e = s.dma_start(out=xr1_h[kc * 128:(kc + 1) * 128, :],
                            in_=xr1[:, kc * C.LH:(kc + 1) * C.LH])
            s9p.inc(e, 0)
    on("sync", p9_spill)

    def fl_v5(v):
        s9p.wait(v, 0)
        for kc in range(C.KC):
            e = v.tensor_mul(xr1[:, kc * nln:(kc + 1) * nln],
                             xr1[:, kc * nln:(kc + 1) * nln], frow[:, 0:nln])
            ng(v, e)
            e = v.tensor_scalar(xr1[:, kc * nln:(kc + 1) * nln],
                                xr1[:, kc * nln:(kc + 1) * nln],
                                normg[:, kc:kc + 1], normb[:, kc:kc + 1],
                                OP.mult, OP.add)
        inc(e, sv)
    on("vector", fl_v5)
    xln = xr1
    barrier()
    fre("rowf", "ppt", "pme")

    seS = mksb(top, "seS", 128, C.KC * C.ES, BF16, side="right")

    sseL = Stream(1)
    sseM = Stream(1)

    def pse_s(s):
        s.wait_ge(Q["dg1"], 80)
        for kc in range(C.KC):
            sseL.inc(s.dma_start(out=seS[:, kc * C.ES:(kc + 1) * C.ES],
                                 in_=seTb[kc * 128:(kc + 1) * 128, :]), 0)
    on("sync", pse_s)

    sesq = xsq

    def pse_sq(sc):
        sseL.wait(sc, 0)
        for kc in range(C.KC):
            e = sc.activation(sesq[:, kc * C.ES:(kc + 1) * C.ES],
                              seS[:, kc * C.ES:(kc + 1) * C.ES], AF.Square)
        nsm["psesq"] = inc(e, ss)
    on("scalar", pse_sq)

    q1c = qc["q1"]

    def pse_t(t):
        t.wait_ge(ss, nsm["psesq"])
        for w in range(C.ES // 512):
            if w >= 1:
                t.wait_ge(Q["q1"], q1c + w)
            pa = psSa[0:1, 0:512]
            pb = psSb[0:1, 0:512]
            for kc in range(C.KC):
                t.matmul(pa, ones1[:], seS[:, kc * C.ES + w * 512:kc * C.ES + (w + 1) * 512],
                         start=(kc == 0), stop=(kc == C.KC - 1))
            for kc in range(C.KC):
                mm = t.matmul(pb, ones1[:], sesq[:, kc * C.ES + w * 512:kc * C.ES + (w + 1) * 512],
                              start=(kc == 0), stop=(kc == C.KC - 1))
            nsm[("psest", w)] = inc(mm, st)
    on("tensor", pse_t)

    def pse_v(v):
        for w in range(C.ES // 512):
            v.wait_ge(st, nsm[("psest", w)])
            pa = psSa[0:1, 0:512]
            pb = psSb[0:1, 0:512]
            e = v.tensor_scalar_mul(row1[0:1, w * 512:(w + 1) * 512], pa, 1.0 / C.DM)
            ng(v, e)
            e = v.tensor_scalar_mul(row1[0:1, C.ES + w * 512:C.ES + (w + 1) * 512],
                                    pb, 1.0 / C.DM)
            qinc(e, "q1")
            ng(v, sem=Q["q1"], val=qc["q1"])
        e = v.tensor_mul(mrow[0:1, 0:C.ES], row1[0:1, 0:C.ES], row1[0:1, 0:C.ES])
        ng(v, e)
        e = v.tensor_sub(row1[0:1, C.ES:2 * C.ES], row1[0:1, C.ES:2 * C.ES],
                         mrow[0:1, 0:C.ES])
        nsm["psevar"] = inc(e, sv)
    on("vector", pse_v)

    def pse_sc2(sc):
        sc.wait_ge(sv, nsm["psevar"])
        e = sc.activation(row1[0:1, C.ES:2 * C.ES], row1[0:1, C.ES:2 * C.ES],
                          AF.Sqrt, bias=csts[0:1, 0:1])
        nsm["psesd"] = inc(e, ss)
    on("scalar", pse_sc2)

    def pse_v2(v):
        v.wait_ge(ss, nsm["psesd"])
        e = v.reciprocal(row1[0:1, C.ES:2 * C.ES], row1[0:1, C.ES:2 * C.ES])
        nsm["pser"] = inc(e, sv)
    on("vector", pse_v2)

    def pse_s2(s):
        s.wait_ge(sv, nsm["pser"])
        sseM.inc(s.dma_start(out=murow_h[0:1, 0:C.ES], in_=row1[0:1, 0:C.ES]), 0)
        e = s.dma_start(out=murow_h[1:2, 0:C.ES], in_=row1[0:1, C.ES:2 * C.ES])
        sseM.inc(e, 0)
    on("sync", pse_s2)

    def pse_g2(g):
        sseM.wait(g, 0)
        qinc16(g.dma_start(out=mrow[:, 0:C.ES],
                           in_=murow_h[0:1, 0:C.ES].broadcast_to((128, C.ES))), "dg5")
        qinc16(g.dma_start(out=rrow[:, 0:C.ES],
                           in_=murow_h[1:2, 0:C.ES].broadcast_to((128, C.ES))), "dg5")
    on("gpsimd", pse_g2)

    def pse_v3(v):
        v.wait_ge(Q["dg5"], 32)
        for kc in range(C.KC):
            e = v.tensor_sub(seS[:, kc * C.ES:(kc + 1) * C.ES],
                             seS[:, kc * C.ES:(kc + 1) * C.ES], mrow[:, 0:C.ES])
            ng(v, e)
            e = v.tensor_mul(seS[:, kc * C.ES:(kc + 1) * C.ES],
                             seS[:, kc * C.ES:(kc + 1) * C.ES], rrow[:, 0:C.ES])
            ng(v, e)
            e = v.tensor_scalar(seS[:, kc * C.ES:(kc + 1) * C.ES],
                                seS[:, kc * C.ES:(kc + 1) * C.ES],
                                sgsc[:, kc:kc + 1], sbsc[:, kc:kc + 1], OP.mult, OP.add)
        nsm["psedone"] = inc(e, sv)
    on("vector", pse_v3)
    dbg_dump("seS", lambda: seS[:], sv, lambda: nsm["psedone"])

    barrier()


    NLB, LB, LTB, WB, NWB = C.NLB, C.LB, C.LTB, C.WB, C.NWB
    fre("xsq")
    disp = mksb(top, "disp", 128, C.ET * C.LH, BF16)
    lgf = mksb(top, "lgf", 128, C.ET * LB, F32)
    dstat = mksb(top, "dstat", 128, 4 * C.ET, F32)
    pstat = mksb(top, "pstat", 128, 4 * C.ET, F32)
    b2c = mksb(top, "b2c", 128, max(C.LT, 2), F32)
    b2sb = mksb(top, "b2sb", max(C.LT, 1), 128, F32)
    lrow = mksb(top, "lrow", 128, 2 * C.ES, F32)
    brow_f = mksb(top, "brow_f", 128, LB, F32)

    def pfe_v0(v):
        nsm["pfedm"] = inc(v.memset(dstat[:], 0.0), sv)
    on("vector", pfe_v0)

    for lb in range(NLB):
        PLGSS0 = cnt[id(ss)]

        def plg_t(t, _lb=lb, _b=PLGSS0):
            i = 0
            for et in range(C.ET):
                for w in range(NWB):
                    pst = psA if i % 2 == 0 else psB
                    if i >= 2:
                        t.wait_ge(ss, _b + i - 1)
                    for kc in range(C.KC):
                        mm = t.matmul(
                            pst[:, 0:WB],
                            seS[:, kc * C.ES + et * 128:kc * C.ES + (et + 1) * 128],
                            xln[:, kc * C.LH + _lb * LB + w * WB:
                                kc * C.LH + _lb * LB + (w + 1) * WB],
                            start=(kc == 0), stop=(kc == C.KC - 1))
                    nsm[("plgm", _lb, i)] = inc(mm, st)
                    i += 1
        on("tensor", plg_t)

        def plg_sc(sc, _lb=lb):
            i = 0
            for et in range(C.ET):
                for w in range(NWB):
                    pst = psA if i % 2 == 0 else psB
                    sc.wait_ge(st, nsm[("plgm", _lb, i)])
                    e = sc.copy(lgf[:, et * LB + w * WB:et * LB + (w + 1) * WB],
                                pst[:, 0:WB])
                    nsm[("plge", _lb, i)] = inc(e, ss)
                    i += 1
        on("scalar", plg_sc)

        def pds_v(v, _lb=lb):
            v.wait_ge(ss, nsm[("plge", _lb, C.ET * NWB - 1)])
            ng(v, sem=sv, val=nsm["pfedm"])
            for et in range(C.ET):
                e = v.reduce_max(dstat[:, _lb * C.ET + et:_lb * C.ET + et + 1],
                                 lgf[:, et * LB:(et + 1) * LB], axis=AX.X, negate=True)
            nsm[("pdsm", _lb)] = inc(e, sv)
        on("vector", pds_v)

        def pds_sc(sc, _lb=lb):
            sc.wait_ge(sv, nsm[("pdsm", _lb)])
            for et in range(C.ET):
                e = sc.activation(
                    disp[:, et * C.LH + _lb * LB:et * C.LH + (_lb + 1) * LB],
                    lgf[:, et * LB:(et + 1) * LB], AF.Exp,
                    bias=dstat[:, _lb * C.ET + et:_lb * C.ET + et + 1],
                    accum_out=dstat[:, 2 * C.ET + _lb * C.ET + et:
                                    2 * C.ET + _lb * C.ET + et + 1])
            nsm[("pdse", _lb)] = inc(e, ss)
        on("scalar", pds_sc)

        NG4 = C.ET // 4
        PCBSV0 = [None]
        for ltl in range(LTB):
            ltg = lb * LTB + ltl

            def pcb_t(t, _lb=lb, _ltl=ltl, _ltg=ltg):
                if _ltl == 0:
                    PCBSV0[0] = cnt[id(sv)]
                t.wait_ge(ss, nsm[("pdse", _lb)])
                for g8 in range(NG4):
                    if g8 > 0:
                        t.wait_ge(sv, PCBSV0[0] + _ltl * (NG4 + 2) + g8)
                    elif _ltl > 0:
                        t.wait_ge(sv, PCBSV0[0] + (_ltl - 1) * (NG4 + 2) + NG4 + 1)
                    for q4 in range(4):
                        et = g8 * 4 + q4
                        tr = t.transpose(psTf[:, q4 * 128:(q4 + 1) * 128],
                                         lgf[:, et * LB + _ltl * 128:
                                             et * LB + (_ltl + 1) * 128],
                                         eyef[:])
                    nsm[("pcbt", _ltg, g8)] = inc(tr, st)
            on("tensor", pcb_t)

            def pcb_v(v, _ltg=ltg):
                for g8 in range(NG4):
                    v.wait_ge(st, nsm[("pcbt", _ltg, g8)])
                    e = v.tensor_copy(lrow[:, g8 * 512:(g8 + 1) * 512], psTf[:])
                    inc(e, sv)
                ng(v, sem=sv, val=cnt[id(sv)])
                e = v.reduce_max(b2c[:, _ltg:_ltg + 1], lrow[:, 0:C.ES],
                                 axis=AX.X, negate=True)
                nsm[("pcbm", _ltg)] = inc(e, sv)
            on("vector", pcb_v)

            def pcb_sc(sc, _ltg=ltg):
                sc.wait_ge(sv, nsm[("pcbm", _ltg)])
                t = inc(sc.activation(lrow[:, C.ES:2 * C.ES], lrow[:, 0:C.ES], AF.Exp,
                                      bias=b2c[:, _ltg:_ltg + 1],
                                      accum_out=tiny[:, 40:41]), ss)
                sc.wait_ge(ss, t)
                e = sc.activation(tiny[:, 41:42], tiny[:, 40:41], AF.Ln)
                nsm[("pcbz", _ltg)] = inc(e, ss)
            on("scalar", pcb_sc)

            def pcb_v2(v, _ltg=ltg):
                v.wait_ge(ss, nsm[("pcbz", _ltg)])
                e = v.tensor_sub(b2c[:, _ltg:_ltg + 1], b2c[:, _ltg:_ltg + 1],
                                 tiny[:, 41:42])
                nsm[("pcbb", _ltg)] = inc(e, sv)
            on("vector", pcb_v2)

        def pcb_t3(t, _lb=lb):
            t.wait_ge(sv, nsm[("pcbb", (_lb + 1) * LTB - 1)])
            tr = t.transpose(psTf[0:LTB, 0:128], b2c[:, _lb * LTB:(_lb + 1) * LTB],
                             eyef[:])
            nsm[("pcbtr", _lb)] = inc(tr, st)
        on("tensor", pcb_t3)

        def pcb_v4(v, _lb=lb):
            v.wait_ge(st, nsm[("pcbtr", _lb)])
            e = v.tensor_copy(b2sb[0:LTB, :], psTf[0:LTB, 0:128])
            nsm[("pcbsb", _lb)] = inc(e, sv)
        on("vector", pcb_v4)

        scb = Stream(1)

        def pcb_s(s, _lb=lb):
            s.wait_ge(sv, nsm[("pcbsb", _lb)])
            e = s.dma_start(out=b2_h[_lb * LTB:(_lb + 1) * LTB, :], in_=b2sb[0:LTB, :])
            scb.inc(e, 0)
        on("sync", pcb_s)

        def pcb_g(g, _lb=lb):
            scb.wait(g, 0)
            qinc16(g.dma_start(
                out=brow_f[:],
                in_=b2_h[_lb * LTB:(_lb + 1) * LTB, :]
                .rearrange("a b -> (a b)").unsqueeze(0).broadcast_to((128, LB))), "dg6")
        on("gpsimd", pcb_g)

        def pcb_v5(v, _lb=lb):
            v.wait_ge(Q["dg6"], (_lb + 1) * 16)
            for et in range(C.ET):
                e = v.scalar_tensor_tensor(lgf[:, et * LB:(et + 1) * LB],
                                           lgf[:, et * LB:(et + 1) * LB], 1.0,
                                           brow_f[:], OP.mult, OP.add)
            nsm[("pcbadd", _lb)] = inc(e, sv)
        on("vector", pcb_v5)

        def pcb_sc5(sc, _lb=lb):
            sc.wait_ge(sv, nsm[("pcbadd", _lb)])
            for et in range(C.ET):
                e = sc.activation(combT[:, et * C.LH + _lb * LB:et * C.LH + (_lb + 1) * LB],
                                  lgf[:, et * LB:(et + 1) * LB], AF.Exp)
            nsm[("pcomb", _lb)] = inc(e, ss)
        on("scalar", pcb_sc5)
        barrier()

    def pdn_s(s):
        s.wait_ge(ss, nsm[("pdse", NLB - 1)])
        nsm["pdst"] = inc(s.dma_start(out=cin2[:], in_=dstat[:]), sh, 16)
    on("sync", pdn_s)

    def pdn_g(g):
        g.wait_ge(sh, nsm["pdst"])
        nsm["cc2"] = inc(g.collective_compute(
            "AllGather", OP.bypass, replica_groups=RG8, ins=[cin2[:]], outs=[cout2[:]]), scc)
    on("gpsimd", pdn_g)

    def pdn_s2(s):
        s.wait_ge(scc, nsm["cc2"])
        nsm["pdsp"] = inc(s.dma_start(out=pstat[:], in_=cout2[TS(REG["pxor"], 128), :]),
                          sh, 16)
    on("sync", pdn_s2)

    def pdn_v(v):
        v.wait_ge(sh, nsm["pdsp"])
        ET = C.ET
        if NLB == 2:
            v.tensor_tensor(tiny[:, 0:ET], dstat[:, 0:ET], dstat[:, ET:2 * ET], OP.min)
            t = inc(v.tensor_tensor(tiny[:, ET:2 * ET], pstat[:, 0:ET],
                                    pstat[:, ET:2 * ET], OP.min), sv)
            v.wait_ge(sv, t)
            t = inc(v.tensor_tensor(tiny[:, 0:ET], tiny[:, 0:ET],
                                    tiny[:, ET:2 * ET], OP.min), sv)
            v.wait_ge(sv, t)
        else:
            t = inc(v.tensor_tensor(tiny[:, 0:ET], dstat[:, 0:ET], pstat[:, 0:ET],
                                    OP.min), sv)
            v.wait_ge(sv, t)
        for lb in range(NLB):
            v.tensor_sub(tiny[:, (1 + lb) * ET:(2 + lb) * ET], tiny[:, 0:ET],
                         dstat[:, lb * ET:(lb + 1) * ET])
            e = v.tensor_sub(tiny[:, (1 + NLB + lb) * ET:(2 + NLB + lb) * ET],
                             tiny[:, 0:ET], pstat[:, lb * ET:(lb + 1) * ET])
        nsm["pdnd"] = inc(e, sv)
    on("vector", pdn_v)

    def pdn_sc(sc):
        sc.wait_ge(sv, nsm["pdnd"])
        e = sc.activation(tiny[:, C.ET:(1 + 2 * NLB) * C.ET],
                          tiny[:, C.ET:(1 + 2 * NLB) * C.ET], AF.Exp)
        nsm["pdnf"] = inc(e, ss)
    on("scalar", pdn_sc)

    def pdn_v2(v):
        v.wait_ge(ss, nsm["pdnf"])
        ET = C.ET

        def step(ins):
            t = inc(ins, sv)
            v.wait_ge(sv, t)
        step(v.tensor_mul(tiny[:, (1 + 2 * NLB) * ET:(2 + 2 * NLB) * ET],
                          dstat[:, 2 * ET:3 * ET], tiny[:, ET:2 * ET]))
        for lb in range(NLB):
            if lb > 0:
                step(v.tensor_mul(tiny[:, (2 + 2 * NLB) * ET:(3 + 2 * NLB) * ET],
                                  dstat[:, (2 + lb) * ET:(3 + lb) * ET],
                                  tiny[:, (1 + lb) * ET:(2 + lb) * ET]))
                step(v.tensor_add(tiny[:, (1 + 2 * NLB) * ET:(2 + 2 * NLB) * ET],
                                  tiny[:, (1 + 2 * NLB) * ET:(2 + 2 * NLB) * ET],
                                  tiny[:, (2 + 2 * NLB) * ET:(3 + 2 * NLB) * ET]))
            step(v.tensor_mul(tiny[:, (2 + 2 * NLB) * ET:(3 + 2 * NLB) * ET],
                              pstat[:, (2 + lb) * ET:(3 + lb) * ET],
                              tiny[:, (1 + NLB + lb) * ET:(2 + NLB + lb) * ET]))
            step(v.tensor_add(tiny[:, (1 + 2 * NLB) * ET:(2 + 2 * NLB) * ET],
                              tiny[:, (1 + 2 * NLB) * ET:(2 + 2 * NLB) * ET],
                              tiny[:, (2 + 2 * NLB) * ET:(3 + 2 * NLB) * ET]))
        step(v.reciprocal(tiny[:, (1 + 2 * NLB) * ET:(2 + 2 * NLB) * ET],
                          tiny[:, (1 + 2 * NLB) * ET:(2 + 2 * NLB) * ET]))
        for lb in range(NLB):
            step(v.tensor_mul(tiny[:, (1 + lb) * ET:(2 + lb) * ET],
                              tiny[:, (1 + lb) * ET:(2 + lb) * ET],
                              tiny[:, (1 + 2 * NLB) * ET:(2 + 2 * NLB) * ET]))
        for lb in range(NLB):
            for et in range(C.ET):
                e = v.tensor_scalar_mul(
                    disp[:, et * C.LH + lb * LB:et * C.LH + (lb + 1) * LB],
                    disp[:, et * C.LH + lb * LB:et * C.LH + (lb + 1) * LB],
                    tiny[:, (1 + lb) * ET + et:(1 + lb) * ET + et + 1])
        nsm["pdisp"] = inc(e, sv)
    on("vector", pdn_v2)

    dbg_dump("disp", lambda: disp[:], sv, lambda: nsm["pdisp"])
    dbg_dump("dispu", lambda: disp[:], ss, lambda: nsm[("pdse", NLB - 1)])
    dbg_dump("pstat", lambda: pstat[:], sh, lambda: nsm["pdsp"])
    dbg_dump("tiny", lambda: tiny[:], sv, lambda: nsm["pdisp"])
    dbg_dump("combT", lambda: combT[:], ss, lambda: nsm[("pcomb", NLB - 1)])
    dbg_dump("dstat", lambda: dstat[:], ss, lambda: nsm[("pdse", NLB - 1)])
    barrier()
    fre("brow_f", "lrow", "b2sb", "b2c", "pstat", "dstat", "lgf", "seS")

    xlnT = mksb(top, "xlnT", 128, C.LT * C.DM, BF16)
    dispT = mksb(top, "dispT", 128, C.LT * C.ES, BF16)
    q4b = qc["q4"]

    def ptr_t(t):
        gi = 0
        for (src, nt) in ((xln, C.KC), (disp, C.ET)):
            for lt in range(C.LT):
                for g4 in range(0, nt, 4):
                    w = min(4, nt - g4)
                    if gi >= 2:
                        t.wait_ge(Q["q4"], q4b + gi - 1)
                    pT = psTa if gi % 2 == 0 else psTb
                    for q4 in range(w):
                        tt = g4 + q4
                        tr = t.transpose(pT[:, q4 * 128:(q4 + 1) * 128],
                                         src[:, tt * C.LH + lt * 128:tt * C.LH + (lt + 1) * 128],
                                         eyeb[:])
                    nsm[("ptr", gi)] = inc(tr, st)
                    gi += 1
    on("tensor", ptr_t)

    def ptr_v(v):
        gi = 0
        for (dst, nt, tl) in ((xlnT, C.KC, C.DM), (dispT, C.ET, C.ES)):
            for lt in range(C.LT):
                for g4 in range(0, nt, 4):
                    w = min(4, nt - g4)
                    v.wait_ge(st, nsm[("ptr", gi)])
                    pT = psTa if gi % 2 == 0 else psTb
                    e = v.tensor_copy(dst[:, lt * tl + g4 * 128:lt * tl + (g4 + w) * 128],
                                      pT[:, 0:w * 128])
                    qinc(e, "q4")
                    gi += 1
    on("vector", ptr_v)

    barrier()
    fre("xr1")

    slotp = mksb(top, "slotp", 128, C.KC * C.ES, BF16)
    PSLSS0 = cnt[id(ss)]

    def psl_t(t):
        i = 0
        for kc in range(C.KC):
            for w in range(C.ES // 512):
                pst = psA if i % 2 == 0 else psB
                if i >= 2:
                    t.wait_ge(ss, PSLSS0 + i - 1)
                for lt in range(C.LT):
                    mm = t.matmul(pst[:, 0:512],
                                  xlnT[:, lt * C.DM + kc * 128:lt * C.DM + (kc + 1) * 128],
                                  dispT[:, lt * C.ES + w * 512:lt * C.ES + (w + 1) * 512],
                                  start=(lt == 0), stop=(lt == C.LT - 1))
                nsm[("pslm", i)] = inc(mm, st)
                i += 1
    on("tensor", psl_t)

    def psl_sc(sc):
        i = 0
        for kc in range(C.KC):
            for w in range(C.ES // 512):
                pst = psA if i % 2 == 0 else psB
                sc.wait_ge(st, nsm[("pslm", i)])
                e = sc.copy(slotp[:, kc * C.ES + w * 512:kc * C.ES + (w + 1) * 512],
                            pst[:, 0:512])
                nsm[("psle", i)] = inc(e, ss)
                i += 1
    on("scalar", psl_sc)

    def psl_s(s):
        s.wait_ge(ss, nsm[("psle", C.KC * (C.ES // 512) - 1)])
        for kc in range(C.KC):
            e = s.dma_start(out=cin3[kc * 128:(kc + 1) * 128, :],
                            in_=slotp[:, kc * C.ES:(kc + 1) * C.ES])
            inc(e, sh, 16)
        nsm["pslst"] = cnt[id(sh)]
    on("sync", psl_s)

    def psl_g(g):
        g.wait_ge(sh, nsm["pslst"])
        nsm["cc3"] = inc(g.collective_compute(
            "AllGather", OP.bypass, replica_groups=RG8, ins=[cin3[:]], outs=[cout3[:]]), scc)
    on("gpsimd", psl_g)

    barrier()
    fre("slotp", "dispT", "xlnT", "disp")

    sown = mksb(top, "sown", 128, C.KC * 512, BF16)
    srec = mksb(top, "srec", 128, 8 * C.KC * 128, BF16)

    def psg_s(s):
        s.wait_ge(scc, nsm["cc3"])
        pid = REG["pid"]
        c3r = cout3.rearrange("(g k p) e -> p g k e", g=8, k=C.KC)
        srec_r = srec[:].rearrange("p (g k e) -> p g k e", g=8, k=C.KC)
        e = s.dma_start(out=srec_r, in_=c3r[:, :, :, TS(pid, 128)])
        inc(e, sh, 16)
        nsm["psgl"] = cnt[id(sh)]
    on("sync", psg_s)

    def psg_v(v):
        v.wait_ge(sh, nsm["psgl"])
        for bb in range(4):
            for kc in range(C.KC):
                e = v.tensor_add(
                    sown[:, kc * 512 + bb * 128:kc * 512 + (bb + 1) * 128],
                    srec[:, (bb * C.KC + kc) * 128:(bb * C.KC + kc + 1) * 128],
                    srec[:, ((bb + 4) * C.KC + kc) * 128:((bb + 4) * C.KC + kc + 1) * 128])
        nsm["psown"] = inc(e, sv)
    on("vector", psg_v)
    dbg_dump("sown", lambda: sown[:], sv, lambda: nsm["psown"])

    barrier()
    fre("srec")

    wgt = mksb(top, "wgt", 128, C.KC * C.F, BF16)
    wut = mksb(top, "wut", 128, C.KC * C.F, BF16)
    wdt = mksb(top, "wdt", 128, C.FT * C.DO, BF16)
    gat = mksb(top, "gat", 128, C.FT * 256, BF16)
    upt = mksb(top, "upt", 128, C.FT * 256, BF16)
    eob = mksb(top, "eob", 64, 2 * C.DO, BF16)
    bdbc = mksb(top, "bdbc", 128, 2 * C.DO, F32)

    def slot_rhs(kc, e_):
        ap = sown[:, kc * 512:(kc + 1) * 512]
        ap = ap.rearrange("p (b s) -> p b s", b=4)
        return ap[:, :, e_ * 64:(e_ + 1) * 64]

    sBD = Stream(1)
    sWt = Stream(2)
    sE4 = Stream(1)
    sE4.preset(8, 1)

    def pex_s0(s):
        sBD.inc(s.dma_start(out=bdbc[:, 0:C.DO],
                            in_=i_bd[0:1, :].broadcast_to((128, C.DO))), 0)
        e = s.dma_start(out=bdbc[:, C.DO:2 * C.DO],
                        in_=i_bd[1:2, :].broadcast_to((128, C.DO)))
        sBD.inc(e, 0)
    on("sync", pex_s0)

    NDOW = max(1, C.DO // 512)
    WDO = min(512, C.DO)
    for e_i in range(2):
        def pex_sw(s, _e=e_i):
            if _e == 1:
                s.wait_ge(st, nsm[("peom", 0, 3)])
            for kc in range(C.KC):
                sWt.inc(s.dma_start(out=wgt[:, kc * C.F:(kc + 1) * C.F],
                                    in_=wgb[_e, kc * 128:(kc + 1) * 128, :]), _e)
                sWt.inc(s.dma_start(out=wut[:, kc * C.F:(kc + 1) * C.F],
                                    in_=wub[_e, kc * 128:(kc + 1) * 128, :]), _e)
            for ft in range(C.FT):
                sWt.inc(s.dma_start(out=wdt[:, ft * C.DO:(ft + 1) * C.DO],
                                    in_=wdb[_e, ft * 128:(ft + 1) * 128, :]), _e)
        on("sync", pex_sw)

        PEXSS0 = cnt[id(ss)]

        def pex_t(t, _e=e_i, _b=PEXSS0):
            sWt.wait(t, _e)
            t.wait_ge(sv, nsm["psown"])
            i = 0
            for wsb in (wgt, wut):
                for ft in range(C.FT):
                    pst = psA if i % 2 == 0 else psB
                    if i >= 2:
                        t.wait_ge(ss, _b + i - 1)
                    for kc in range(C.KC):
                        mm = t.matmul(pst[:, 0:256],
                                      wsb[:, kc * C.F + ft * 128:kc * C.F + (ft + 1) * 128],
                                      slot_rhs(kc, _e),
                                      start=(kc == 0), stop=(kc == C.KC - 1))
                    nsm[("pexm", _e, i)] = inc(mm, st)
                    i += 1
        on("tensor", pex_t)

        def pex_sc(sc, _e=e_i):
            i = 0
            for which in ("g", "u"):
                for ft in range(C.FT):
                    pst = psA if i % 2 == 0 else psB
                    sc.wait_ge(st, nsm[("pexm", _e, i)])
                    dst = gat if which == "g" else upt
                    bias = bgT if which == "g" else buT
                    fn = AF.Gelu if which == "g" else AF.Identity
                    e = sc.activation(dst[:, ft * 256:(ft + 1) * 256], pst[:, 0:256], fn,
                                      bias=bias[:, _e * C.FT + ft:_e * C.FT + ft + 1])
                    nsm[("pexe", _e, i)] = inc(e, ss)
                    i += 1
        on("scalar", pex_sc)

        def pex_v(v, _e=e_i):
            v.wait_ge(ss, nsm[("pexe", _e, 2 * C.FT - 1)])
            e = v.tensor_mul(gat[:], gat[:], upt[:])
            nsm[("pexgu", _e)] = inc(e, sv)
        on("vector", pex_v)

        PEXSV0 = cnt[id(sv)]

        def pex_t2(t, _e=e_i, _b=PEXSV0):
            t.wait_ge(sv, nsm[("pexgu", _e)])
            for bb in range(4):
                if bb >= 1:
                    t.wait_ge(sv, _b + bb)
                for w2 in range(NDOW):
                    pst = psA if w2 == 0 else psB
                    for ft in range(C.FT):
                        mm = t.matmul(pst[0:64, 0:WDO],
                                      gat[:, ft * 256 + bb * 64:ft * 256 + (bb + 1) * 64],
                                      wdt[:, ft * C.DO + w2 * WDO:ft * C.DO + (w2 + 1) * WDO],
                                      start=(ft == 0), stop=(ft == C.FT - 1))
                nsm[("peom", _e, bb)] = inc(mm, st)
        on("tensor", pex_t2)

        def pex_v2(v, _e=e_i):
            sBD.wait(v, 0)
            for bb in range(4):
                v.wait_ge(st, nsm[("peom", _e, bb)])
                if _e * 4 + bb >= 1:
                    sE4.wait(v, _e * 4 + bb - 1)
                for w2 in range(NDOW):
                    pst = psA if w2 == 0 else psB
                    e = v.scalar_tensor_tensor(
                        eob[0:64, w2 * WDO:(w2 + 1) * WDO], pst[0:64, 0:WDO], 1.0,
                        bdbc[0:64, _e * C.DO + w2 * WDO:_e * C.DO + (w2 + 1) * WDO],
                        OP.mult, OP.add)
                nsm[("peoe", _e, bb)] = inc(e, sv)
        on("vector", pex_v2)

        def pex_s2(s, _e=e_i):
            for bb in range(4):
                s.wait_ge(sv, nsm[("peoe", _e, bb)])
                e = s.dma_start(out=cin4[bb * 128 + _e * 64:bb * 128 + (_e + 1) * 64, :],
                                in_=eob[0:64, 0:C.DO])
                sE4.inc(e, _e * 4 + bb)
        on("sync", pex_s2)

    def pex_g(g):
        sE4.wait(g, 7)
        nsm["cc4"] = inc(g.collective_compute(
            "AllGather", OP.bypass, replica_groups=RG8, ins=[cin4[:]], outs=[cout4[:]]), scc)
    on("gpsimd", pex_g)

    barrier()
    fre("bdbc", "eob", "upt", "gat", "wdt", "wut", "wgt", "sown")

    res2 = mksb(top, "res2", 128, C.KC * C.LH, BF16)
    eoutb = mksb(top, "eoutb", 128, 8 * C.DO, BF16)
    xr1b = mksb(top, "xr1b", 128, C.KC * C.LH, BF16)

    s12x = Stream(1)

    def p12_s0(s):
        for kc in range(C.KC):
            e = s.dma_start(out=xr1b[:, kc * C.LH:(kc + 1) * C.LH],
                            in_=xr1_h[kc * 128:(kc + 1) * 128, :])
            s12x.inc(e, 0)
    on("sync", p12_s0)

    def pcm_s(s):
        s.wait_ge(scc, nsm["cc4"])
        breg = REG["breg"]
        c4r = cout4.rearrange("(g b p) o -> p g b o", g=8, b=4)
        eoutb_r = eoutb[:].rearrange("p (g o) -> p g o", g=8)
        e = s.dma_start(out=eoutb_r, in_=c4r[:, :, DS(breg, 1), :].squeeze(2))
        inc(e, sh, 16)
        nsm["pcml"] = cnt[id(sh)]
    on("sync", pcm_s)

    PCMSV0 = cnt[id(sv)]

    def pcm_t(t):
        t.wait_ge(sh, nsm["pcml"])
        i = 0
        for ot in range(C.KC):
            for w in range(C.NWH):
                pst = psA if i % 2 == 0 else psB
                if i >= 2:
                    t.wait_ge(sv, PCMSV0 + i - 1)
                for sdr in range(8):
                    mm = t.matmul(pst[:, 0:C.WH],
                                  eoutb[:, sdr * C.DO + ot * 128:sdr * C.DO + (ot + 1) * 128],
                                  combT[:, sdr * C.LH + w * C.WH:sdr * C.LH + (w + 1) * C.WH],
                                  start=(sdr == 0), stop=(sdr == 7))
                nsm[("pcmm", i)] = inc(mm, st)
                i += 1
    on("tensor", pcm_t)

    def pcm_v(v):
        s12x.wait(v, 0)
        i = 0
        for ot in range(C.KC):
            for w in range(C.NWH):
                pst = psA if i % 2 == 0 else psB
                v.wait_ge(st, nsm[("pcmm", i)])
                e = v.scalar_tensor_tensor(
                    res2[:, ot * C.LH + w * C.WH:ot * C.LH + (w + 1) * C.WH],
                    pst[:, 0:C.WH], 1.0,
                    xr1b[:, ot * C.LH + w * C.WH:ot * C.LH + (w + 1) * C.WH],
                    OP.mult, OP.add)
                nsm[("pcme", i)] = inc(e, sv)
                i += 1
    on("vector", pcm_v)
    dbg_dump("eoutb", lambda: eoutb[:], sh, lambda: nsm["pcml"])
    dbg_dump("res2", lambda: res2[:], sv, lambda: nsm[("pcme", C.KC * C.NWH - 1)])

    barrier()
    fre("xr1b", "eoutb")

    r2T = mksb(top, "r2T", 128, 2 * C.DM, BF16)
    osb = mksb(top, "osb", 128, 2 * C.DM, F32)
    sqf = mksb(top, "sqf", 128, C.DM, F32)
    q1d = qc["q1"]
    q2d = qc["q2"]
    sOo = Stream(2)

    for lt in range(C.LT):
        def pf_t(t, _lt=lt):
            for g4 in range(0, C.KC, 4):
                w = min(4, C.KC - g4)
                gi = _lt * ((C.KC + 3) // 4) + g4 // 4
                if gi >= 2:
                    t.wait_ge(Q["q1"], q1d + gi - 1)
                pT = psTa if gi % 2 == 0 else psTb
                for q4 in range(w):
                    kc = g4 + q4
                    tr = t.transpose(pT[:, q4 * 128:(q4 + 1) * 128],
                                     res2[:, kc * C.LH + _lt * 128:kc * C.LH + (_lt + 1) * 128],
                                     eyeb[:])
                nsm[("pft", _lt, g4)] = inc(tr, st)
        on("tensor", pf_t)

        def pf_v(v, _lt=lt):
            so = (_lt % 2) * C.DM
            if _lt >= 2:
                sOo.wait(v, _lt - 2)
            for g4 in range(0, C.KC, 4):
                w = min(4, C.KC - g4)
                gi = _lt * ((C.KC + 3) // 4) + g4 // 4
                v.wait_ge(st, nsm[("pft", _lt, g4)])
                pT = psTa if gi % 2 == 0 else psTb
                e = v.tensor_copy(r2T[:, so + g4 * 128:so + (g4 + w) * 128], pT[:, 0:w * 128])
                qinc(e, "q1")
            ng(v, sem=Q["q1"], val=qc["q1"])
            e = v.reduce_sum(tiny[:, 9:10], r2T[:, so:so + C.DM], axis=AX.X)
            nsm[("pfc", _lt)] = inc(e, sv)
            v.wait_ge(sv, nsm[("pfc", _lt)])
        on("vector", pf_v)

        def pf_sc(sc, _lt=lt):
            so = (_lt % 2) * C.DM
            sc.wait_ge(sv, nsm[("pfc", _lt)])
            e = sc.activation(sqf[:], r2T[:, so:so + C.DM], AF.Square,
                              accum_out=tiny[:, 8:9])
            nsm[("pfsq", _lt)] = inc(e, ss)
        on("scalar", pf_sc)

        def pf_v2(v, _lt=lt):
            v.wait_ge(ss, nsm[("pfsq", _lt)])
            v.tensor_scalar_mul(tiny[:, 9:10], tiny[:, 9:10], 1.0 / C.DM)
            t = inc(v.tensor_scalar_mul(tiny[:, 8:9], tiny[:, 8:9], 1.0 / C.DM), sv)
            v.wait_ge(sv, t)
            t = inc(v.tensor_mul(tiny[:, 10:11], tiny[:, 9:10], tiny[:, 9:10]), sv)
            v.wait_ge(sv, t)
            e = v.tensor_sub(tiny[:, 10:11], tiny[:, 8:9], tiny[:, 10:11])
            nsm[("pfv", _lt)] = inc(e, sv)
        on("vector", pf_v2)

        def pf_sc2(sc, _lt=lt):
            sc.wait_ge(sv, nsm[("pfv", _lt)])
            e = sc.activation(tiny[:, 10:11], tiny[:, 10:11], AF.Sqrt, bias=csts[:, 0:1])
            nsm[("pfsd", _lt)] = inc(e, ss)
        on("scalar", pf_sc2)

        def pf_v3(v, _lt=lt):
            so = (_lt % 2) * C.DM
            v.wait_ge(ss, nsm[("pfsd", _lt)])
            t = inc(v.reciprocal(tiny[:, 10:11], tiny[:, 10:11]), sv)
            v.wait_ge(sv, t)
            e = v.tensor_scalar(osb[:, so:so + C.DM], r2T[:, so:so + C.DM],
                                tiny[:, 9:10], tiny[:, 10:11], OP.subtract, OP.mult)
            nsm[("pfe", _lt)] = inc(e, sv)
        on("vector", pf_v3)

        def pf_s(s, _lt=lt):
            so = (_lt % 2) * C.DM
            s.wait_ge(sv, nsm[("pfe", _lt)])
            e = s.dma_start(out=o_y[_lt * 128:(_lt + 1) * 128, :], in_=osb[:, so:so + C.DM])
            sOo.inc(e, _lt)
        on("sync", pf_s)

    def fin_s(s):
        sOo.wait_all(s)
        if cnt[id(sh)] > 0:
            s.wait_ge(sh, cnt[id(sh)])
    on("sync", fin_s)

    for h in reversed(list(live.values())):
        h.__exit__(None, None, None)
    live.clear()
    top.close()
    global LAST_NSM
    LAST_NSM = dict(nsm)
    return nc


def _prep_core(C, c, I):
    b, j = c % 4, c // 4
    js, je = j * C.DH, (j + 1) * C.DH
    m = {}
    m["i_x"] = I["L0_attn_x"][b]
    m["i_xh"] = I["L0_attn_x"][b][j * C.LH:(j + 1) * C.LH]
    m["i_inw"] = np.concatenate([I["L0_attn_in_w"][js:je],
                                 I["L0_attn_in_w"][C.DI + js:C.DI + je]], 0)
    m["i_convw"] = I["L0_attn_conv_w"][js:je, 0, :].reshape(
        C.DT, 128, 4).transpose(1, 0, 2).reshape(128, C.DT * 4)
    m["i_convb"] = I["L0_attn_conv_b"][js:je].reshape(C.DT, 128).T
    m["i_xprojT"] = I["L0_attn_xproj_w"][:, js:je].T
    m["i_dtwT"] = I["L0_attn_dt_w"][js:je, :].T
    m["i_dtb"] = I["L0_attn_dt_b"][js:je].reshape(C.DT, 128).T
    m["i_alog"] = I["L0_attn_A_log"][js:je].reshape(
        C.DT, 128, C.NST).transpose(1, 0, 2).reshape(128, C.DT * C.NST)
    m["i_dv"] = I["L0_attn_D"][js:je].reshape(C.DT, 128).T
    m["i_outwT"] = I["L0_attn_out_w"][:, js:je].T
    for nm, k in (("i_normg", "L0_ffn_norm_g"), ("i_normb", "L0_ffn_norm_b"),
                  ("i_slotg", "L0_ffn_slot_g"), ("i_slotb", "L0_ffn_slot_b")):
        m[nm] = I[k].reshape(C.KC, 128).T
    m["i_scale"] = I["L0_ffn_scale"].reshape(1, 1)
    m["i_seT"] = I["L0_ffn_slot_embeds"].reshape(C.ES, C.DM).T
    m["i_wg"] = I["L0_ffn_Wg"][2 * c:2 * c + 2]
    m["i_wu"] = I["L0_ffn_Wu"][2 * c:2 * c + 2]
    m["i_wd"] = I["L0_ffn_Wd"][2 * c:2 * c + 2]
    m["i_bgT"] = I["L0_ffn_bg"][2 * c:2 * c + 2].reshape(
        2, C.FT, 128).transpose(2, 0, 1).reshape(128, 2 * C.FT)
    m["i_buT"] = I["L0_ffn_bu"][2 * c:2 * c + 2].reshape(
        2, C.FT, 128).transpose(2, 0, 1).reshape(128, 2 * C.FT)
    m["i_bd"] = I["L0_ffn_bd"][2 * c:2 * c + 2]
    m["i_eyeb"] = np.eye(128, dtype=BF)
    m["i_eyef"] = np.eye(128, dtype=np.float32)
    out = {}
    for k, v in m.items():
        v = np.asarray(v)
        if v.dtype == np.float64:
            v = v.astype(np.float32)
        out[k] = np.ascontiguousarray(v)
    return out


def kernel(**inputs):
    C = CFG_FULL
    I = {k: np.asarray(v) for k, v in inputs.items()}
    nc = build(C)
    in_maps = [_prep_core(C, c, I) for c in range(8)]
    res = run_bass_kernel_spmd(nc, in_maps, core_ids=list(range(8)))
    out = np.zeros((4, C.L, C.DM), np.float32)
    for c in range(8):
        b, j = c % 4, c // 4
        out[b, j * C.LH:(j + 1) * C.LH] = res.results[c]["o_y"]
    return out
